# version 01
# baseline (speedup 1.0000x reference)
import contextlib

import numpy as np
import ml_dtypes

import concourse.bass as bass
import concourse.mybir as mybir
from concourse.bass_utils import run_bass_kernel_spmd

F32 = mybir.dt.float32
BF16 = mybir.dt.bfloat16
AF = mybir.ActivationFunctionType
ALU = mybir.AluOpType
AX = mybir.AxisListType
BF = ml_dtypes.bfloat16

D = 1024
DC = 8
TSEQ = 8192
NB = 2
TC = 2048
NCORE = 8
ENGINES = ("sync", "tensor", "vector", "scalar", "gpsimd")
DMA_ENG = ("sync", "gpsimd", "scalar")



class BSP:

    def __init__(self):
        self.nc = bass.Bass("TRN2", target_bir_lowering=False, debug=False)
        self._es = contextlib.ExitStack()
        self.steps = []
        self._cur = None
        self.dram = {}

    def sbuf(self, name, shape, dtype):
        return self._es.enter_context(self.nc.sbuf_tensor(name, list(shape), dtype))

    def psum(self, name, shape, dtype=F32):
        return self._es.enter_context(self.nc.psum_tensor(name, list(shape), dtype))

    def din(self, name, shape, dtype=F32):
        ap = self.nc.dram_tensor(name, list(shape), dtype, kind="ExternalInput").ap()
        self.dram[name] = ap
        return ap

    def dout(self, name, shape, dtype=F32):
        ap = self.nc.dram_tensor(name, list(shape), dtype, kind="ExternalOutput").ap()
        self.dram[name] = ap
        return ap

    def step(self):
        self._cur = {e: [] for e in ENGINES}
        self.steps.append(self._cur)

    def on(self, eng, fn):
        assert self._cur is not None
        self._cur[eng].append(("op", fn))

    def dma(self, eng, out, in_):
        assert self._cur is not None
        self._cur[eng].append(("dma", (out, in_)))

    def finalize(self):
        nc = self.nc
        with self._es:
            bar = self._es.enter_context(nc.semaphore("bar"))
            dsems = {e: self._es.enter_context(nc.semaphore(f"ds_{e}")) for e in ENGINES}
            dcnt = {e: 0 for e in ENGINES}
            with nc.Block() as block:
                def run_engine(e, eng):
                    for k, st in enumerate(self.steps):
                        if k > 0:
                            e.wait_ge(bar, len(ENGINES) * k)
                        last = None
                        had_dma = False
                        for kind, payload in st[eng]:
                            if kind == "op":
                                r = payload(e)
                                if r is not None:
                                    last = r
                            else:
                                out, in_ = payload
                                e.dma_start(out=out, in_=in_).then_inc(dsems[eng], 16)
                                dcnt[eng] += 16
                                had_dma = True
                        if had_dma:
                            e.wait_ge(dsems[eng], dcnt[eng])
                            e.nop().then_inc(bar, 1)
                        elif last is not None:
                            last.then_inc(bar, 1)
                        else:
                            e.nop().then_inc(bar, 1)
                for eng in ENGINES:
                    getattr(block, eng)(lambda e, eng=eng: run_engine(e, eng))
        return nc


class Arena:

    def __init__(self, b, name, cols, dtype):
        self.t = b.sbuf(name, [128, cols], dtype)
        self.cols = cols
        self.off = 0
        self.name = name
        self.hi = 0

    def alloc(self, n, parts=128):
        assert self.off + n <= self.cols, \
            f"{self.name} arena overflow: {self.off}+{n}>{self.cols}"
        ap = self.t[0:parts, self.off:self.off + n]
        self.off += n
        self.hi = max(self.hi, self.off)
        return ap

    def alloc8(self, n):
        return [self.alloc(n) for _ in range(DC)]

    def mark(self):
        return self.off

    def reset(self, mark):
        self.off = mark




class Lay:
    def __init__(self, b: BSP, f32_cols, bf_cols, n_ps=6, pt_cols=512, n_stg=6,
                 pq_cols=0):
        self.b = b
        self.fa = Arena(b, "af32", f32_cols, F32)
        self.ba = Arena(b, "abf", bf_cols, BF16)
        self.ps = [b.psum(f"ps{i}", [128, 512], F32) for i in range(n_ps)]
        self.pt = b.psum("pst", [128, pt_cols], BF16) if pt_cols else None
        self.pq = b.psum("psq", [128, pq_cols], F32) if pq_cols else None
        self.ones_bf = b.sbuf("ones_bf", [128, 128], BF16)
        self.ident_bf = b.sbuf("ident_bf", [128, 128], BF16)
        self.stg = [b.sbuf(f"stg{i}", [128, 512], BF16)[:, :] for i in range(max(n_stg, 1))]
        self._stg_i = 0
        self.cb = b.sbuf("constb", [128, 16], F32)
        self._consts = {}
        c_ident = b.din("c_ident", [128, 128], F32)
        self.ident_f32 = b.sbuf("identf", [128, 128], F32)
        b.step()
        b.dma("sync", self.ident_f32[:, :], c_ident[:, :])
        b.on("gpsimd", lambda e: e.memset(self.ones_bf[:], 1.0))
        b.step()
        b.on("vector", lambda e: e.tensor_copy(self.ident_bf[:], self.ident_f32[:, :]))

    def stg_next(self):
        s = self.stg[self._stg_i % len(self.stg)]
        self._stg_i += 1
        return s

    def stgf_next(self):
        if not hasattr(self, "stgf"):
            self.stgf = [self.b.sbuf(f"stgf{i}", [128, 512], F32)[:, :] for i in range(6)]
            self._stgf_i = 0
        s = self.stgf[self._stgf_i % len(self.stgf)]
        self._stgf_i += 1
        return s

    def const(self, v):
        v = float(v)
        if v in self._consts:
            return self._consts[v]
        i = len(self._consts)
        ap = self.cb[0:128, i:i + 1]
        self.b.on("gpsimd", lambda e: e.memset(ap, v))
        self._consts[v] = ap
        return ap

    @staticmethod
    def const_inputs():
        return {"c_ident": np.eye(128, dtype=np.float32)}

    def load_w_bf(self, dram_ap, din, dout, w32):
        b = self.b
        kc = din // 128
        wbf = self.ba.alloc(kc * dout)
        for k in range(kc):
            b.step()
            b.dma("sync", w32, dram_ap[k * 128:(k + 1) * 128, :])
            b.step()
            b.on("vector" if k % 2 == 0 else "gpsimd",
                 lambda e, k=k: e.tensor_copy(wbf[:, k * dout:(k + 1) * dout], w32))
        return wbf

    def wslice(self, wbf, w_dout, k, o0, on):
        return wbf[:, k * w_dout + o0: k * w_dout + o0 + on]

    def ln(self, x_tiles, scr_bf, mb, rb, tb, *, T, x_is_bf=False, eps=1e-5,
           out_bf=None, out_f32=None, resid=None, gain_bias=None):
        b, ps = self.b, self.ps
        blks = [(i, min(512, T - i)) for i in range(0, T, 512)]
        assert len(blks) <= 4
        epsap = self.const(eps)
        if not x_is_bf:
            b.step()
            for k in range(DC):
                b.on("vector" if k % 2 == 0 else "gpsimd",
                     lambda e, k=k: e.tensor_copy(scr_bf[k], x_tiles[k]))
            src = scr_bf
        else:
            src = x_tiles
        b.step()
        def mm_sums(e):
            r = None
            for bi, (o, n) in enumerate(blks):
                for k in range(DC):
                    r = e.matmul(ps[bi][:, 0:n], self.ones_bf[:],
                                 src[k][:, o:o + n],
                                 start=(k == 0), stop=(k == DC - 1))
            return r
        b.on("tensor", mm_sums)
        b.step()
        for bi, (o, n) in enumerate(blks):
            b.on("vector", lambda e, bi=bi, o=o, n=n: e.tensor_scalar(
                mb[:, o:o + n], ps[bi][:, 0:n], 1.0 / D, None, ALU.mult))
        for k in range(DC):
            b.on("scalar", lambda e, k=k, s=src[k]: e.activation(scr_bf[k], s, AF.Square))
        for wave in range(2):
            wb = blks[wave * 2:wave * 2 + 2]
            if not wb:
                break
            b.step()
            def mm_sq(e, wb=wb, wave=wave):
                r = None
                for bi, (o, n) in enumerate(wb):
                    for k in range(DC):
                        r = e.matmul(ps[4 + bi][:, 0:n], self.ones_bf[:],
                                     scr_bf[k][:, o:o + n],
                                     start=(k == 0), stop=(k == DC - 1))
                return r
            b.on("tensor", mm_sq)
            b.step()
            for bi, (o, n) in enumerate(wb):
                b.on("vector", lambda e, bi=bi, o=o, n=n: e.tensor_scalar(
                    rb[:, o:o + n], ps[4 + bi][:, 0:n], 1.0 / D, None, ALU.mult))
        b.step()
        b.on("vector", lambda e: e.tensor_mul(tb[:, 0:T], mb[:, 0:T], mb[:, 0:T]))
        b.step()
        b.on("vector", lambda e: e.tensor_sub(rb[:, 0:T], rb[:, 0:T], tb[:, 0:T]))
        b.step()
        b.on("scalar", lambda e: e.activation(rb[:, 0:T], rb[:, 0:T], AF.Ln, bias=epsap))
        b.step()
        b.on("scalar", lambda e: e.activation(rb[:, 0:T], rb[:, 0:T], AF.Exp, scale=-0.5))
        b.step()
        for k in range(DC):
            b.on("vector" if k % 2 == 0 else "gpsimd",
                 lambda e, k=k: e.tensor_sub(scr_bf[k], x_tiles[k], mb[:, 0:T]))
        b.step()
        for k in range(DC):
            eng = "vector" if k % 2 == 0 else "gpsimd"
            dst = (out_bf or out_f32 or scr_bf)[k]
            b.on(eng, lambda e, k=k, dst=dst: e.tensor_mul(dst, scr_bf[k], rb[:, 0:T]))
        if gain_bias is not None:
            gl, bl = gain_bias
            b.step()
            for k in range(DC):
                dst = (out_bf or out_f32 or scr_bf)[k]
                b.on("vector", lambda e, k=k, dst=dst: e.tensor_scalar(
                    dst, dst, gl[k], bl[k], ALU.mult, ALU.add))
        if resid is not None:
            assert out_bf is None and out_f32 is None
            b.step()
            for k in range(DC):
                b.on("vector" if k % 2 == 0 else "gpsimd",
                     lambda e, k=k: e.tensor_add(resid[k], resid[k], scr_bf[k]))

    def ln_resid_stream(self, x_tiles, scr_bf, mb, rb, tb, *, T, x_in, x_out,
                        xblk, x_is_bf=True, eps=1e-5):
        self.ln(x_tiles, scr_bf, mb, rb, tb, T=T, x_is_bf=x_is_bf, eps=eps)
        b = self.b
        blks = [(i, min(512, T - i)) for i in range(0, T, 512)]
        for (o, n) in blks:
            b.step()
            for k in range(DC):
                b.dma(DMA_ENG[k % 3], xblk[k][:, 0:n],
                      x_in[k * 128:(k + 1) * 128, o:o + n])
            b.step()
            for k in range(DC):
                b.on("vector" if k % 2 == 0 else "gpsimd",
                     lambda e, k=k, o=o, n=n: e.tensor_add(
                         xblk[k][:, 0:n], xblk[k][:, 0:n], scr_bf[k][:, o:o + n]))
            b.step()
            for k in range(DC):
                b.dma(DMA_ENG[k % 3], x_out[k * 128:(k + 1) * 128, o:o + n],
                      xblk[k][:, 0:n])

    def proj_f(self, wbf, x_bf, T, dout, dst, *, w_dout=None, o_base=0,
               din_c=DC, scale=1.0, bias=None, act=None, store_fn=None,
               out_f32=False):
        b, ps = self.b, self.ps
        w_dout = w_dout or dout
        oc_n = dout // 128
        resident = isinstance(dst, list)
        blks = [(i, min(512, T - i)) for i in range(0, T, 512)]
        jobs = [(o, n, oc) for (o, n) in blks for oc in range(oc_n)]
        groups = [jobs[i:i + 3] for i in range(0, len(jobs), 3)]
        pend = []
        def evac(bank, o, n, oc):
            if resident:
                tgt = dst[oc][:, o:o + n]
            elif out_f32:
                tgt = self.stgf_next()[:, 0:n]
            else:
                tgt = self.stg_next()[:, 0:n]
            bias_ap = bias[oc] if bias is not None else None
            if act is not None or bias_ap is not None:
                fn = act if act is not None else AF.Identity
                kw = {}
                if bias_ap is not None:
                    kw["bias"] = bias_ap
                if scale != 1.0:
                    kw["scale"] = scale
                b.on("scalar", lambda e, tgt=tgt, bank=bank, fn=fn, kw=kw, n=n:
                     e.activation(tgt, bank[:, 0:n], fn, **kw))
            elif scale != 1.0:
                b.on("vector", lambda e, tgt=tgt, bank=bank, n=n: e.tensor_scalar(
                    tgt, bank[:, 0:n], scale, None, ALU.mult))
            else:
                b.on("vector", lambda e, tgt=tgt, bank=bank, n=n: e.tensor_copy(
                    tgt, bank[:, 0:n]))
            if resident:
                return None
            if store_fn is not None:
                return lambda: store_fn(oc, o, n, tgt)
            dram_slice = dst[oc * 128:(oc + 1) * 128, o:o + n]
            eng = DMA_ENG[oc % 2]
            return lambda: b.dma(eng, dram_slice, tgt)
        for gi, g in enumerate(groups):
            banks = [ps[3 * (gi % 2) + i] for i in range(len(g))]
            b.step()
            def mms(e, g=g, banks=banks):
                r = None
                for i, (o, n, oc) in enumerate(g):
                    for k in range(din_c):
                        r = e.matmul(banks[i][:, 0:n],
                                     self.wslice(wbf, w_dout, k, o_base + oc * 128, 128),
                                     x_bf[k][:, o:o + n],
                                     start=(k == 0), stop=(k == din_c - 1))
                return r
            b.on("tensor", mms)
            new_pend = []
            if gi > 0:
                prevg = groups[gi - 1]
                prevbanks = [ps[3 * ((gi - 1) % 2) + i] for i in range(len(prevg))]
                for i, (o, n, oc) in enumerate(prevg):
                    new_pend.append(evac(prevbanks[i], o, n, oc))
            for st in pend:
                st()
            pend = [p for p in new_pend if p is not None]
        b.step()
        lastg = groups[-1]
        lastbanks = [ps[3 * ((len(groups) - 1) % 2) + i] for i in range(len(lastg))]
        new_pend = []
        for i, (o, n, oc) in enumerate(lastg):
            new_pend.append(evac(lastbanks[i], o, n, oc))
        for st in pend:
            st()
        b.step()
        for st in new_pend:
            if st is not None:
                st()
        b.step()

    def proj_t(self, wbf, x_bf, T, dout, dst, *, w_dout=None, o_base=0,
               din_c=DC, scale=1.0):
        b, ps = self.b, self.ps
        w_dout = w_dout or dout
        resident = isinstance(dst, list)
        obs = [(i, min(512, dout - i)) for i in range(0, dout, 512)]
        jobs = [(tt, o, n) for tt in range(T // 128) for (o, n) in obs]
        groups = [jobs[i:i + 3] for i in range(0, len(jobs), 3)]
        pend = []
        def evac(bank, tt, o, n):
            if resident:
                tgt = dst[tt][:, o:o + n]
            else:
                tgt = self.stg_next()[:, 0:n]
            if scale != 1.0:
                b.on("vector", lambda e, tgt=tgt, bank=bank, n=n: e.tensor_scalar(
                    tgt, bank[:, 0:n], scale, None, ALU.mult))
            else:
                b.on("vector", lambda e, tgt=tgt, bank=bank, n=n: e.tensor_copy(
                    tgt, bank[:, 0:n]))
            if resident:
                return None
            dram_slice = dst[tt * 128:(tt + 1) * 128, o:o + n]
            eng = DMA_ENG[tt % 2]
            return lambda: b.dma(eng, dram_slice, tgt)
        for gi, g in enumerate(groups):
            banks = [ps[3 * (gi % 2) + i] for i in range(len(g))]
            b.step()
            def mms(e, g=g, banks=banks):
                r = None
                for i, (tt, o, n) in enumerate(g):
                    for k in range(din_c):
                        r = e.matmul(banks[i][:, 0:n],
                                     x_bf[k][:, tt * 128:(tt + 1) * 128],
                                     self.wslice(wbf, w_dout, k, o_base + o, n),
                                     start=(k == 0), stop=(k == din_c - 1))
                return r
            b.on("tensor", mms)
            new_pend = []
            if gi > 0:
                prevg = groups[gi - 1]
                prevbanks = [ps[3 * ((gi - 1) % 2) + i] for i in range(len(prevg))]
                for i, (tt, o, n) in enumerate(prevg):
                    new_pend.append(evac(prevbanks[i], tt, o, n))
            for st in pend:
                st()
            pend = [p for p in new_pend if p is not None]
        b.step()
        lastg = groups[-1]
        lastbanks = [ps[3 * ((len(groups) - 1) % 2) + i] for i in range(len(lastg))]
        new_pend = []
        for i, (tt, o, n) in enumerate(lastg):
            new_pend.append(evac(lastbanks[i], tt, o, n))
        for st in pend:
            st()
        b.step()
        for st in new_pend:
            if st is not None:
                st()
        b.step()




def xT_chunks(x):
    out = []
    for core in range(NCORE):
        bb, cc = divmod(core, 4)
        out.append(np.ascontiguousarray(
            x[bb, cc * TC:(cc + 1) * TC, :].T.astype(np.float32)))
    return out


def run(b, in_maps):
    nc = b.finalize()
    return run_bass_kernel_spmd(nc, in_maps, core_ids=list(range(NCORE)))




def build_l0_proj():
    b = BSP()
    xT = b.din("xT", [D, TC], F32)
    wqkv = b.din("wqkv", [D, 3 * D], F32)
    qT_o = b.dout("qT", [D, TC], BF16)
    kT_o = b.dout("kT", [D, TC], BF16)
    v_o = b.dout("v", [TC, D], BF16)

    lay = Lay(b, f32_cols=16384 + 6144, bf_cols=16384 + 8 * 3072)
    fa, ba = lay.fa, lay.ba
    x = fa.alloc8(TC)
    m0 = fa.mark()
    mb = fa.alloc(TC); rb = fa.alloc(TC); tb = fa.alloc(TC)
    xln = ba.alloc8(TC)
    m1 = ba.mark()
    scr = ba.alloc8(TC)

    b.step()
    for k in range(DC):
        b.dma("sync", x[k], xT[k * 128:(k + 1) * 128, :])
    lay.ln(x, scr, mb, rb, tb, T=TC, out_bf=xln)

    ba.reset(m1)
    fa.reset(m0)
    w32 = fa.alloc(3 * D)
    wbf = lay.load_w_bf(b.dram["wqkv"], D, 3 * D, w32)

    lay.proj_f(wbf, xln, TC, D, qT_o, w_dout=3 * D, o_base=0, scale=1.0 / 32.0)
    lay.proj_f(wbf, xln, TC, D, kT_o, w_dout=3 * D, o_base=D)
    lay.proj_t(wbf, xln, TC, D, v_o, w_dout=3 * D, o_base=2 * D)

    kn2_o = b.dout("kn2loc", [1, 1], F32)
    ktmp = xln
    kn2b = mb
    ps = lay.ps
    b.step()
    for k in range(DC):
        b.dma(DMA_ENG[k % 3], ktmp[k], kT_o[k * 128:(k + 1) * 128, :])
    b.step()
    for k in range(DC):
        b.on("scalar", lambda e, k=k: e.activation(ktmp[k], ktmp[k], AF.Square))
    b.step()
    def mm_kn2(e):
        r = None
        for blk in range(4):
            for k in range(DC):
                r = e.matmul(ps[blk][:, 0:512], lay.ones_bf[:],
                             ktmp[k][:, blk * 512:(blk + 1) * 512],
                             start=(k == 0), stop=(k == DC - 1))
        return r
    b.on("tensor", mm_kn2)
    b.step()
    for blk in range(4):
        b.on("vector", lambda e, blk=blk: e.tensor_copy(
            kn2b[:, blk * 512:(blk + 1) * 512], ps[blk][:, 0:512]))
    kmax = rb[:, 0:1]
    b.step()
    b.on("vector", lambda e: e.reduce_max(kmax, kn2b[:, 0:TC], AX.X))
    b.step()
    b.dma("gpsimd", kn2_o[0:1, 0:1], kmax[0:1, 0:1])
    b.step()
    return b


def kernel(**inputs):
    raise NotImplementedError("under construction")




def build_l0_attend(n_qt=16, do_epi=True, sweep_stage=5, debug=False):
    b = BSP()
    qT = b.din("qT", [D, TC], BF16)
    kTf = b.din("kTf", [D, TSEQ], BF16)
    vf = b.din("vf", [TSEQ, D], BF16)
    xT = b.din("xT", [D, TC], F32)
    wo_t = b.din("wo_t", [D, D], F32)
    out_b = b.din("out_b", [D, 1], F32)
    kn24 = b.din("kn24", [1, 4], F32)
    x1_o = b.dout("x1T", [D, TC], F32)

    lay = Lay(b, f32_cols=1024 + 16384 + 3 * TC,
              bf_cols=16384 + 2 + 8 * D + 33280, n_ps=6, pt_cols=0, n_stg=1)
    fa, ba, ps = lay.fa, lay.ba, lay.ps
    ptA = ps[4][:, 0:512].bitcast(BF16)
    ptB = ps[5][:, 0:512].bitcast(BF16)

    sm = fa.alloc(1024)
    def smcol(i, n=1):
        return sm[:, i:i + n]
    kn2_sb = sm[0:1, 0:4]
    k2m = sm[0:1, 4:5]
    lnk_bf = ba.alloc(2)[0:1, 0:1]
    lnk_b = smcol(8)
    qn2tok = smcol(9)
    Bneg = smcol(10)
    ell = smcol(11)
    ellp = [smcol(12), smcol(13)]
    rec = smcol(14)
    outb_sb = sm[:, 16:24]

    mreg = fa.mark()
    w32 = fa.alloc(D)
    fa.reset(mreg)
    x = fa.alloc8(TC)
    mb = fa.alloc(TC); rb = fa.alloc(TC); tb = fa.alloc(TC)

    yT = ba.alloc8(TC)
    b.step()
    wo_bf = None
    msweep = None

    b.step()
    b.dma("sync", kn2_sb, kn24[0:1, 0:4])
    for oc in range(DC):
        b.dma("gpsimd", outb_sb[0:128, oc:oc + 1], out_b[oc * 128:(oc + 1) * 128, 0:1])
    b.step()
    b.on("vector", lambda e: e.reduce_max(k2m, kn2_sb, AX.X))
    b.step()
    b.on("scalar", lambda e: e.activation(k2m, k2m, AF.Ln))
    b.step()
    b.on("vector", lambda e: e.tensor_copy(lnk_bf, k2m))
    b.step()
    b.on("tensor", lambda e: e.matmul(ps[0][:, 0:1], lay.ones_bf[0:1, 0:128],
                                      lnk_bf, start=True, stop=True))
    b.step()
    b.on("vector", lambda e: e.tensor_copy(lnk_b, ps[0][:, 0:1]))

    wo_bf = lay.load_w_bf(b.dram["wo_t"], D, D, w32)

    msweep = ba.mark()
    qsl = ba.alloc8(128)
    qsq = ba.alloc8(128)
    kbuf = [[ba.alloc(512) for _ in range(DC)] for _ in range(2)]
    vbuf = [[ba.alloc(D) for _ in range(4)] for _ in range(3)]
    probs = [ba.alloc(512) for _ in range(2)]
    probsT = [ba.alloc(512) for _ in range(2)]
    ybf = ba.alloc(D)

    NKB = TSEQ // 512

    for qt in range(n_qt):
        qoff = qt * 128
        b.step()
        for k in range(DC):
            b.dma("sync", qsl[k], qT[k * 128:(k + 1) * 128, qoff:qoff + 128])
        b.on("vector", lambda e: e.memset(ell, 0.0))
        b.step()
        for k in range(DC):
            b.on("scalar", lambda e, k=k: e.activation(qsq[k], qsl[k], AF.Square))
        b.step()
        def mm_qn2(e):
            r = None
            for k in range(DC):
                r = e.matmul(ps[0][:, 0:128], lay.ones_bf[:], qsq[k],
                             start=(k == 0), stop=(k == DC - 1))
            return r
        b.on("tensor", mm_qn2)
        b.step()
        b.on("vector", lambda e: e.tensor_copy(mb[:, 0:128], ps[0][:, 0:128]))
        b.step()
        b.on("tensor", lambda e: e.transpose(ps[1][:, 0:128], mb[:, 0:128],
                                             lay.ident_f32[:]))
        b.step()
        b.on("vector", lambda e: e.tensor_copy(qn2tok, ps[1][:, 0:1]))
        b.step()
        b.on("scalar", lambda e: e.activation(qn2tok, qn2tok, AF.Ln))
        b.step()
        b.on("vector", lambda e: e.tensor_add(qn2tok, qn2tok, lnk_b))
        b.step()
        b.on("scalar", lambda e: e.activation(Bneg, qn2tok, AF.Exp, scale=0.5))
        b.step()
        b.on("vector", lambda e: e.tensor_scalar(Bneg, Bneg, -1.0, None, ALU.mult))

        for i in (range(NKB + 6) if sweep_stage >= 2 else []):
            b.step()
            if i < NKB:
                for k in range(DC):
                    b.dma("sync", kbuf[i % 2][k],
                          kTf[k * 128:(k + 1) * 128, i * 512:(i + 1) * 512])
            jv = i - 3
            if 0 <= jv < NKB:
                for c in range(4):
                    b.dma("gpsimd", vbuf[jv % 3][c],
                          vf[jv * 512 + c * 128: jv * 512 + (c + 1) * 128, :])
            def tensor_work(e, i=i):
                r = None
                js = i - 1
                if 0 <= js < NKB:
                    bank = ps[2 + js % 2]
                    for k in range(DC):
                        r = e.matmul(bank[:, 0:512], qsl[k], kbuf[js % 2][k],
                                     start=(k == 0), stop=(k == DC - 1))
                jt = i - 3
                if 0 <= jt < NKB:
                    ptX = ptA if jt % 2 == 0 else ptB
                    for c in range(4):
                        r = e.transpose(ptX[:, c * 128:(c + 1) * 128],
                                        probs[jt % 2][:, c * 128:(c + 1) * 128],
                                        lay.ident_bf[:])
                jp = i - 5
                if 0 <= jp < NKB:
                    for c in range(4):
                        for h in range(2):
                            r = e.matmul(ps[h][:, 0:512],
                                         probsT[jp % 2][:, c * 128:(c + 1) * 128],
                                         vbuf[jp % 3][c][:, h * 512:(h + 1) * 512],
                                         start=(jp == 0 and c == 0),
                                         stop=(jp == NKB - 1 and c == 3))
                return r
            b.on("tensor", tensor_work)
            je = i - 2
            if 0 <= je < NKB:
                b.on("scalar", lambda e, je=je: e.activation(
                    probs[je % 2], ps[2 + je % 2][:, 0:512], AF.Exp,
                    bias=Bneg, accum_out=ellp[je % 2]))
            jv2 = i - 4
            if 0 <= jv2 < NKB:
                ptX = ptA if jv2 % 2 == 0 else ptB
                b.on("vector", lambda e, jv2=jv2, ptX=ptX: e.tensor_copy(
                    probsT[jv2 % 2], ptX[:, 0:512]))
            ja = i - 3
            if 0 <= ja < NKB:
                b.on("vector", lambda e, ja=ja: e.tensor_add(ell, ell, ellp[ja % 2]))

        if debug and qt == 0:
            dbg_ell = b.dout("dbg_ell", [2, 128], F32)
            dbg_p0 = b.dout("dbg_p0", [128, 512], BF16)
            b.step()
            b.dma("sync", dbg_ell[0:1, :].rearrange("a b -> b a"), ell)
            b.dma("sync", dbg_ell[1:2, :].rearrange("a b -> b a"), Bneg)
            b.dma("gpsimd", dbg_p0[:, :], probs[0])
        if sweep_stage < 3:
            continue
        b.step()
        b.on("vector", lambda e: e.reciprocal(rec, ell))
        b.step()
        for h in range(2):
            b.on("vector", lambda e, h=h: e.tensor_scalar(
                ybf[:, h * 512:(h + 1) * 512], ps[h][:, 0:512], rec, None, ALU.mult))
        if sweep_stage < 4:
            continue
        if debug and qt == 0:
            dbg_ybf = b.dout("dbg_ybf", [128, D], BF16)
            b.step()
            b.dma("sync", dbg_ybf[:, :], ybf)
        b.step()
        def ytrans(e):
            r = None
            for c in range(DC):
                r = e.transpose(ptA[:, c * 128:(c + 1) * 128],
                                ybf[:, c * 128:(c + 1) * 128], lay.ident_bf[:])
            return r
        b.on("tensor", ytrans)
        if sweep_stage < 5:
            continue
        b.step()
        for c in range(DC):
            b.on("vector", lambda e, c=c, qoff=qoff: e.tensor_copy(
                yT[c][:, qoff:qoff + 128], ptA[:, c * 128:(c + 1) * 128]))

    if not do_epi:
        b.step()
        return b
    ba.reset(msweep)
    attn = ba.alloc8(TC)
    scr = ba.alloc8(TC)
    outb_list = [outb_sb[0:128, oc:oc + 1] for oc in range(DC)]
    if debug:
        yT_dbg = b.dout("yT_dbg", [D, TC], BF16)
        b.step()
        for k in range(DC):
            b.dma(DMA_ENG[k % 3], yT_dbg[k * 128:(k + 1) * 128, :], yT[k])
    lay.proj_f(wo_bf, yT, TC, D, attn, bias=outb_list)
    if debug:
        at_dbg = b.dout("at_dbg", [D, TC], BF16)
        b.step()
        for k in range(DC):
            b.dma(DMA_ENG[k % 3], at_dbg[k * 128:(k + 1) * 128, :], attn[k])
    b.step()
    for k in range(DC):
        b.dma(DMA_ENG[k % 3], x[k], xT[k * 128:(k + 1) * 128, :])
    lay.ln(attn, scr, mb, rb, tb, T=TC, x_is_bf=True, resid=x)
    b.step()
    for k in range(DC):
        b.dma(DMA_ENG[k % 3], x1_o[k * 128:(k + 1) * 128, :], x[k])
    b.step()
    return b




def build_l0_ffn(debug=False):
    b = BSP()
    xT = b.din("xT", [D, TC], F32)
    halo = b.din("halo", [D, 2], F32)
    dwk = b.din("dwk", [D, 3], F32)
    dwb = b.din("dwb", [D, 1], F32)
    pw_t = b.din("pw_t", [D, D], F32)
    pwb = b.din("pwb", [D, 1], F32)
    rw_t = b.din("rw_t", [D, 8], F32)
    rb_in = b.din("rb_in", [8, 1], F32)
    ew_t = b.din("ew_t", [8 * D, D], F32)
    eb_t = b.din("eb_t", [D, 8], F32)
    wout_t = b.din("wout_t", [D, D], F32)
    woutb = b.din("woutb", [D, 1], F32)
    nw_in = b.din("nw_in", [D, 1], F32)
    ffd_t = b.din("ffd_t", [D, 256], F32)
    ffdb = b.din("ffdb", [256, 1], F32)
    ffu_t = b.din("ffu_t", [256, D], F32)
    ffub = b.din("ffub", [D, 1], F32)
    xo = b.dout("xoT", [D, TC], F32)

    lay = Lay(b, f32_cols=9600, bf_cols=79500, n_ps=6, pt_cols=1024, n_stg=6)
    fa, ba, ps, pt = lay.fa, lay.ba, lay.ps, lay.pt

    sm = fa.alloc(16)
    mb = fa.alloc(TC); rb = fa.alloc(TC); tb_ = fa.alloc(TC)
    rwbc = fa.alloc(TC)
    beT = fa.alloc(64)
    pbias = fa.alloc(40)
    kert = fa.alloc(3 * DC)
    nw = fa.alloc(DC)
    hx = fa.alloc(2 * DC)
    ltokf = [fa.alloc(8) for _ in range(16)]
    rwtok = [fa.alloc(8) for _ in range(16)]
    m1 = fa.alloc(1); nm1 = fa.alloc(1); m2 = fa.alloc(1); den = fa.alloc(1)
    msk = fa.alloc(8)
    rrow = mb[0:1, 1024:1024 + TC // 2]
    rrow = tb_[0:1, 0:TC]
    xblk = ([rwbc[:, i * 512:(i + 1) * 512] for i in range(4)]
            + [tb_[:, i * 512:(i + 1) * 512] for i in range(4)])

    R1 = ba.alloc8(2 + TC)
    R2 = ba.alloc8(TC)
    WR = ba.alloc(8 * D)
    scr = ba.alloc8(TC)
    acc = ba.alloc8(TC)
    d1 = [ba.alloc(TC) for _ in range(2)]
    stg2 = [ba.alloc(512) for _ in range(3)]
    rrow_bf = d1[0][0:1, 0:TC]

    xext = R1
    xln = [R1[k][:, 2:2 + TC] for k in range(DC)]

    b.step()
    xbf = acc
    for k in range(DC):
        b.dma("gpsimd", xbf[k], xT[k * 128:(k + 1) * 128, :])
    b.dma("sync", sm[0:8, 0:1], rb_in[:, :])
    for k in range(DC):
        b.dma("gpsimd", kert[0:128, 3 * k:3 * k + 3], dwk[k * 128:(k + 1) * 128, :])
        b.dma("scalar", nw[0:128, k:k + 1], nw_in[k * 128:(k + 1) * 128, :])
        b.dma("gpsimd", pbias[0:128, k:k + 1], pwb[k * 128:(k + 1) * 128, :])
        b.dma("scalar", pbias[0:128, 8 + k:9 + k], woutb[k * 128:(k + 1) * 128, :])
        b.dma("gpsimd", pbias[0:128, 16 + k:17 + k], ffub[k * 128:(k + 1) * 128, :])
        b.dma("scalar", pbias[0:128, 24 + k:25 + k], dwb[k * 128:(k + 1) * 128, :])
        b.dma("sync", beT[0:128, k * 8:(k + 1) * 8], eb_t[k * 128:(k + 1) * 128, :])
        b.dma("sync", hx[0:128, 2 * k:2 * k + 2], halo[k * 128:(k + 1) * 128, :])
    for k in range(2):
        b.dma("sync", pbias[0:128, 32 + k:33 + k], ffdb[k * 128:(k + 1) * 128, :])

    lay.ln(xbf, scr, mb, rb, tb_, T=TC, x_is_bf=True, out_bf=xln)
    hxt = [hx[0:128, 2 * k:2 * k + 2] for k in range(DC)]
    hscr = [stg2[0][:, k * 2:k * 2 + 2] for k in range(DC)]
    lay.ln(hxt, hscr, mb, rb, tb_, T=2, out_bf=[xext[k][:, 0:2] for k in range(DC)])

    xc = R2
    for j in range(3):
        b.step()
        for k in range(DC):
            eng = "vector"
            kap = kert[0:128, 3 * k + j:3 * k + j + 1]
            if j == 0:
                b.on(eng, lambda e, k=k, kap=kap: e.tensor_scalar(
                    xc[k], xext[k][:, 0:TC], kap, None, ALU.mult))
            else:
                b.on(eng, lambda e, k=k, kap=kap, j=j: e.scalar_tensor_tensor(
                    xc[k], xext[k][:, j:j + TC], kap, xc[k], ALU.mult, ALU.add))
    b.step()
    for k in range(DC):
        b.on("vector",
             lambda e, k=k: e.tensor_scalar(xc[k], xc[k],
                                            pbias[0:128, 24 + k:25 + k], None, ALU.add))

    w32 = mb
    pw_bf = WR[:, 0:8 * D]
    for k in range(DC):
        b.step()
        b.dma("sync", w32[:, 0:D], pw_t[k * 128:(k + 1) * 128, :])
        b.step()
        b.on("vector", lambda e, k=k: e.tensor_copy(pw_bf[:, k * D:(k + 1) * D], w32[:, 0:D]))
    x1i = scr
    pwb_list = [pbias[0:128, k:k + 1] for k in range(DC)]
    def pw_store(oc, o, n, stgap):
        b.on("vector", lambda e: e.tensor_add(x1i[oc][:, o:o + n],
                                              xln[oc][:, o:o + n], stgap))
    lay.proj_f(pw_bf, xc, TC, D, None, act=AF.Silu, bias=pwb_list, store_fn=pw_store)

    if debug:
        d_x1i = b.dout("d_x1i", [D, TC], BF16)
        b.step()
        for k in range(DC):
            b.dma(DMA_ENG[k % 3], d_x1i[k * 128:(k + 1) * 128, :], x1i[k])
    rW_bf = stg2[1][:, 0:64]
    for k in range(DC):
        b.step()
        b.dma("sync", w32[:, 0:8], rw_t[k * 128:(k + 1) * 128, :])
        b.step()
        b.on("vector", lambda e, k=k: e.tensor_copy(rW_bf[:, k * 8:(k + 1) * 8], w32[:, 0:8]))
    lg = rb
    b.step()
    def mm_router(e):
        r = None
        for blk in range(4):
            for k in range(DC):
                r = e.matmul(ps[blk][0:8, 0:512], rW_bf[:, k * 8:(k + 1) * 8],
                             x1i[k][:, blk * 512:(blk + 1) * 512],
                             start=(k == 0), stop=(k == DC - 1))
        return r
    b.on("tensor", mm_router)
    b.step()
    for blk in range(4):
        b.on("vector", lambda e, blk=blk: e.tensor_scalar(
            lg[0:8, blk * 512:(blk + 1) * 512], ps[blk][0:8, 0:512],
            sm[0:8, 0:1], None, ALU.add))
    for tt in range(16):
        b.step()
        b.on("tensor", lambda e, tt=tt: e.transpose(
            ps[4][:, 0:8], lg[0:8, tt * 128:(tt + 1) * 128], lay.ident_f32[0:8, 0:8]))
        b.step()
        b.on("vector", lambda e, tt=tt: e.tensor_copy(ltokf[tt][0:128, 0:8], ps[4][:, 0:8]))
    BIGNEG = -1.0e30
    for tt in range(16):
        lt = ltokf[tt][0:128, 0:8]
        rw_ = rwtok[tt][0:128, 0:8]
        b.step()
        b.on("vector", lambda e, lt=lt: e.reduce_max(m1[:, 0:1], lt, AX.X))
        b.step()
        b.on("vector", lambda e: e.tensor_scalar(nm1[:, 0:1], m1[:, 0:1], -1.0, None, ALU.mult))
        b.on("vector", lambda e, lt=lt: e.tensor_scalar(msk[:, 0:8], lt, m1[:, 0:1], None, ALU.is_ge))
        b.step()
        b.on("vector", lambda e, lt=lt: e.scalar_tensor_tensor(
            msk[:, 0:8], msk[:, 0:8], BIGNEG, lt, ALU.mult, ALU.add))
        b.step()
        b.on("vector", lambda e: e.reduce_max(m2[:, 0:1], msk[:, 0:8], AX.X))
        b.step()
        b.on("vector", lambda e, lt=lt: e.tensor_scalar(msk[:, 0:8], lt, m2[:, 0:1], None, ALU.is_ge))
        b.step()
        b.on("scalar", lambda e: e.activation(den[:, 0:1], m2[:, 0:1], AF.Exp, bias=nm1[:, 0:1]))
        b.step()
        b.on("vector", lambda e: e.tensor_scalar(den[:, 0:1], den[:, 0:1], 1.0, None, ALU.add))
        b.step()
        b.on("vector", lambda e: e.reciprocal(den[:, 0:1], den[:, 0:1]))
        b.step()
        b.on("scalar", lambda e, lt=lt, rw_=rw_: e.activation(rw_, lt, AF.Exp, bias=nm1[:, 0:1]))
        b.step()
        b.on("vector", lambda e, rw_=rw_: e.tensor_mul(rw_, rw_, msk[:, 0:8]))
        b.step()
        b.on("vector", lambda e, rw_=rw_: e.tensor_scalar(rw_, rw_, den[:, 0:1], None, ALU.mult))
    for tt in range(16):
        b.step()
        b.on("tensor", lambda e, tt=tt: e.transpose(
            ps[4][0:8, 0:128], rwtok[tt][0:128, 0:8], lay.ident_f32[:, :]))
        b.step()
        b.on("vector", lambda e, tt=tt: e.tensor_copy(
            lg[0:8, tt * 128:(tt + 1) * 128], ps[4][0:8, 0:128]))

    if debug:
        d_lg = b.dout("d_lg", [8, TC], F32)
        b.step()
        b.dma("sync", d_lg[:, :], lg[0:8, 0:TC])
    b.step()
    for k in range(DC):
        b.on("vector" if k % 2 == 0 else "gpsimd", lambda e, k=k: e.memset(acc[k], 0.0))
    for ex in range(8):
        we_bf = WR[:, 0:8 * D]
        for k in range(DC):
            b.step()
            b.dma("sync", w32[:, 0:D], ew_t[ex * D + k * 128: ex * D + (k + 1) * 128, :])
            b.step()
            b.on("vector", lambda e, k=k, we_bf=we_bf: e.tensor_copy(
                we_bf[:, k * D:(k + 1) * D], w32[:, 0:D]))
        b.step()
        b.dma("gpsimd", rrow[0:1, 0:TC], lg[ex:ex + 1, 0:TC])
        b.step()
        b.on("vector", lambda e: e.tensor_copy(rrow_bf[0:1, 0:TC], rrow[0:1, 0:TC]))
        b.step()
        def mm_gate(e):
            r = None
            for blk in range(4):
                r = e.matmul(ps[blk][:, 0:512], lay.ones_bf[0:1, 0:128],
                             rrow_bf[0:1, blk * 512:(blk + 1) * 512],
                             start=True, stop=True)
            return r
        b.on("tensor", mm_gate)
        b.step()
        for blk in range(4):
            b.on("vector", lambda e, blk=blk: e.tensor_copy(
                rwbc[:, blk * 512:(blk + 1) * 512], ps[blk][:, 0:512]))
        bias_list = [beT[0:128, k * 8 + ex:k * 8 + ex + 1] for k in range(DC)]
        def est(oc, o, n, stgap, ex=ex):
            def fn(e):
                e.tensor_mul(stg2[oc % 3][:, 0:n], stgap, rwbc[:, o:o + n])
                return e.tensor_add(acc[oc][:, o:o + n], acc[oc][:, o:o + n],
                                    stg2[oc % 3][:, 0:n])
            b.on("vector", fn)
        lay.proj_f(we_bf, x1i, TC, D, None, act=AF.Silu, bias=bias_list, store_fn=est)

    if debug:
        d_acc = b.dout("d_acc", [D, TC], BF16)
        b.step()
        for k in range(DC):
            b.dma(DMA_ENG[k % 3], d_acc[k * 128:(k + 1) * 128, :], acc[k])
    wout_bf = WR[:, 0:8 * D]
    for k in range(DC):
        b.step()
        b.dma("sync", w32[:, 0:D], wout_t[k * 128:(k + 1) * 128, :])
        b.step()
        b.on("vector", lambda e, k=k: e.tensor_copy(wout_bf[:, k * D:(k + 1) * D], w32[:, 0:D]))
    s_bf = xln
    wob_list = [pbias[0:128, 8 + k:9 + k] for k in range(DC)]
    def mo_store(oc, o, n, stgap):
        b.on("vector", lambda e: e.tensor_add(s_bf[oc][:, o:o + n],
                                              x1i[oc][:, o:o + n], stgap))
    lay.proj_f(wout_bf, acc, TC, D, None, bias=wob_list, act=AF.Identity,
               store_fn=mo_store)

    if debug:
        d_s = b.dout("d_s", [D, TC], BF16)
        b.step()
        for k in range(DC):
            b.dma(DMA_ENG[k % 3], d_s[k * 128:(k + 1) * 128, :], s_bf[k])
    b.step()
    for k in range(DC):
        b.on("scalar", lambda e, k=k: e.activation(R2[k], s_bf[k], AF.Square))
    b.step()
    def mm_rms(e):
        r = None
        for blk in range(4):
            for k in range(DC):
                r = e.matmul(ps[blk][:, 0:512], lay.ones_bf[:],
                             R2[k][:, blk * 512:(blk + 1) * 512],
                             start=(k == 0), stop=(k == DC - 1))
        return r
    b.on("tensor", mm_rms)
    epsap = lay.const(1e-6)
    b.step()
    for blk in range(4):
        b.on("vector", lambda e, blk=blk: e.tensor_scalar(
            rb[:, blk * 512:(blk + 1) * 512], ps[blk][:, 0:512], 1.0 / D, None, ALU.mult))
    b.step()
    b.on("scalar", lambda e: e.activation(rb[:, 0:TC], rb[:, 0:TC], AF.Ln, bias=epsap))
    b.step()
    b.on("scalar", lambda e: e.activation(rb[:, 0:TC], rb[:, 0:TC], AF.Exp, scale=-0.5))
    h_bf = R2
    b.step()
    for k in range(DC):
        b.on("vector",
             lambda e, k=k: e.scalar_tensor_tensor(
                 h_bf[k], s_bf[k], nw[0:128, k:k + 1], rb[:, 0:TC], ALU.mult, ALU.mult))

    if debug:
        d_h = b.dout("d_h", [D, TC], BF16)
        b.step()
        for k in range(DC):
            b.dma(DMA_ENG[k % 3], d_h[k * 128:(k + 1) * 128, :], h_bf[k])
    ffd_bf = WR[:, 0:2048]
    for k in range(DC):
        b.step()
        b.dma("sync", w32[:, 0:256], ffd_t[k * 128:(k + 1) * 128, :])
        b.step()
        b.on("vector", lambda e, k=k: e.tensor_copy(ffd_bf[:, k * 256:(k + 1) * 256], w32[:, 0:256]))
    ffu_bf = WR[:, 2048:4096]
    for k in range(2):
        b.step()
        b.dma("sync", w32[:, 0:D], ffu_t[k * 128:(k + 1) * 128, :])
        b.step()
        b.on("vector", lambda e, k=k: e.tensor_copy(ffu_bf[:, k * D:(k + 1) * D], w32[:, 0:D]))
    fdb_list = [pbias[0:128, 32 + k:33 + k] for k in range(2)]
    lay.proj_f(ffd_bf, h_bf, TC, 256, d1, w_dout=256, act=AF.Gelu, bias=fdb_list)
    out_sub = acc
    fub_list = [pbias[0:128, 16 + k:17 + k] for k in range(DC)]
    def fu_store(oc, o, n, stgap):
        b.on("vector", lambda e: e.tensor_add(out_sub[oc][:, o:o + n],
                                              h_bf[oc][:, o:o + n], stgap))
    lay.proj_f(ffu_bf, d1, TC, D, None, w_dout=D, din_c=2, bias=fub_list,
               act=AF.Identity, store_fn=fu_store)

    if debug:
        d_os = b.dout("d_os", [D, TC], BF16)
        b.step()
        for k in range(DC):
            b.dma(DMA_ENG[k % 3], d_os[k * 128:(k + 1) * 128, :], out_sub[k])
    lay.ln_resid_stream(out_sub, scr, mb, rb, tb_, T=TC, x_in=xT, x_out=xo,
                        xblk=xblk, x_is_bf=True)
    b.step()
    return b




def build_chanmix():
    b = BSP()
    xT = b.din("xT", [D, TC], F32)
    halo = b.din("halo", [D, 1], F32)
    wr_t = b.din("wr_t", [D, D], F32)
    wk_t = b.din("wk_t", [D, D], F32)
    wv_t = b.din("wv_t", [D, D], F32)
    mur = b.din("mur", [D, 1], F32)
    muk = b.din("muk", [D, 1], F32)
    xo = b.dout("xoT", [D, TC], F32)

    lay = Lay(b, f32_cols=9000, bf_cols=77000, n_ps=6, pt_cols=0, n_stg=6)
    fa, ba, ps = lay.fa, lay.ba, lay.ps

    mb = fa.alloc(TC); rb = fa.alloc(TC); tb_ = fa.alloc(TC)
    mu = fa.alloc(2 * DC)
    hx = fa.alloc(DC)
    w32 = mb
    xblk = ([rb[:, i * 512:(i + 1) * 512] for i in range(0)]
            + [tb_[:, i * 512:(i + 1) * 512] for i in range(4)]
            + [fa.alloc(512) for _ in range(4)])

    A = ba.alloc8(1 + TC)
    B_ = ba.alloc8(TC)
    Cb = ba.alloc8(TC)
    Db = ba.alloc8(TC)
    WR = ba.alloc(8 * D)
    mext = A
    m_t = [A[k][:, 1:1 + TC] for k in range(DC)]

    b.step()
    xbf = Cb
    for k in range(DC):
        b.dma("gpsimd", xbf[k], xT[k * 128:(k + 1) * 128, :])
    for k in range(DC):
        b.dma("sync", mu[0:128, k:k + 1], mur[k * 128:(k + 1) * 128, :])
        b.dma("sync", mu[0:128, DC + k:DC + k + 1], muk[k * 128:(k + 1) * 128, :])
        b.dma("scalar", hx[0:128, k:k + 1], halo[k * 128:(k + 1) * 128, :])

    lay.ln(xbf, B_, mb, rb, tb_, T=TC, x_is_bf=True, out_bf=m_t)
    hxt = [hx[0:128, k:k + 1] for k in range(DC)]
    hscr = [lay.stg[0][:, k:k + 1] for k in range(DC)]
    lay.ln(hxt, hscr, mb, rb, tb_, T=1, out_bf=[mext[k][:, 0:1] for k in range(DC)])

    def timemix(dst, mucol):
        b.step()
        for k in range(DC):
            b.on("vector", lambda e, k=k: e.tensor_sub(
                B_[k], m_t[k], mext[k][:, 0:TC]))
        b.step()
        for k in range(DC):
            b.on("vector", lambda e, k=k: e.scalar_tensor_tensor(
                dst[k], B_[k], mu[0:128, mucol + k:mucol + k + 1],
                mext[k][:, 0:TC], ALU.mult, ALU.add))
    timemix(Cb, 0)
    wrbf = WR[:, 0:8 * D]
    for k in range(DC):
        b.step()
        b.dma("sync", w32[:, 0:D], wr_t[k * 128:(k + 1) * 128, :])
        b.step()
        b.on("vector", lambda e, k=k: e.tensor_copy(wrbf[:, k * D:(k + 1) * D], w32[:, 0:D]))
    lay.proj_f(wrbf, Cb, TC, D, Db, act=AF.Sigmoid)

    timemix(Cb, DC)
    for k in range(DC):
        b.step()
        b.dma("sync", w32[:, 0:D], wk_t[k * 128:(k + 1) * 128, :])
        b.step()
        b.on("vector", lambda e, k=k: e.tensor_copy(wrbf[:, k * D:(k + 1) * D], w32[:, 0:D]))
    k2 = [A[k][:, 0:TC] for k in range(DC)]
    lay.proj_f(wrbf, Cb, TC, D, k2, act=AF.Relu)
    b.step()
    for k in range(DC):
        b.on("scalar", lambda e, k=k: e.activation(k2[k], k2[k], AF.Square))

    for k in range(DC):
        b.step()
        b.dma("sync", w32[:, 0:D], wv_t[k * 128:(k + 1) * 128, :])
        b.step()
        b.on("vector", lambda e, k=k: e.tensor_copy(wrbf[:, k * D:(k + 1) * D], w32[:, 0:D]))
    out_sub = Cb
    def vstore(oc, o, n, stgap):
        b.on("vector", lambda e: e.tensor_mul(out_sub[oc][:, o:o + n],
                                              Db[oc][:, o:o + n], stgap))
    lay.proj_f(wrbf, k2, TC, D, None, store_fn=vstore)

    lay.ln_resid_stream(out_sub, B_, mb, rb, tb_, T=TC, x_in=xT, x_out=xo,
                        xblk=xblk, x_is_bf=True)
    b.step()
    return b




def build_l1a_A():
    b = BSP()
    xT = b.din("xT", [D, TC], F32)
    halo = b.din("halo", [D, 1], F32)
    wk_t = b.din("wk_t", [D, D], F32)
    wv_t = b.din("wv_t", [D, D], F32)
    wr_t = b.din("wr_t", [D, D], F32)
    tmk = b.din("tmk", [D, 1], F32)
    tmv = b.din("tmv", [D, 1], F32)
    tmr = b.din("tmr", [D, 1], F32)
    w_in = b.din("w_in", [D, 1], F32)
    ko = b.dout("kTT", [D, TC], F32)
    vo = b.dout("vTT", [D, TC], F32)
    ro = b.dout("rTT", [D, TC], F32)
    kmo = b.dout("kmaxl", [D, 1], F32)
    tlo = b.dout("tailsA", [D, 1], F32)
    tlo2 = b.dout("tailsB", [D, 1], F32)

    lay = Lay(b, f32_cols=16700, bf_cols=57400, n_ps=6, pt_cols=0, n_stg=6)
    fa, ba, ps = lay.fa, lay.ba, lay.ps

    mb = fa.alloc(TC); rb = fa.alloc(TC); tb_ = fa.alloc(TC)
    tm = fa.alloc(4 * DC)
    hx = fa.alloc(DC)
    sm = fa.alloc(16)
    zc = fa.alloc(TC); vc = fa.alloc(TC); sa = fa.alloc(TC); sb2 = fa.alloc(TC)
    gam = fa.alloc(TC)
    onesf = tb_
    w32 = mb

    A = ba.alloc8(1 + TC)
    B_ = ba.alloc8(TC)
    Cb = ba.alloc8(TC)
    WR = ba.alloc(8 * D)
    mext = A
    m_t = [A[k][:, 1:1 + TC] for k in range(DC)]

    b.step()
    xbf = Cb
    for k in range(DC):
        b.dma("gpsimd", xbf[k], xT[k * 128:(k + 1) * 128, :])
    for k in range(DC):
        b.dma("sync", tm[0:128, k:k + 1], tmk[k * 128:(k + 1) * 128, :])
        b.dma("sync", tm[0:128, DC + k:DC + k + 1], tmv[k * 128:(k + 1) * 128, :])
        b.dma("sync", tm[0:128, 2 * DC + k:2 * DC + k + 1], tmr[k * 128:(k + 1) * 128, :])
        b.dma("sync", tm[0:128, 3 * DC + k:3 * DC + k + 1], w_in[k * 128:(k + 1) * 128, :])
        b.dma("scalar", hx[0:128, k:k + 1], halo[k * 128:(k + 1) * 128, :])

    lay.ln(xbf, B_, mb, rb, tb_, T=TC, x_is_bf=True, out_bf=m_t)
    hxt = [hx[0:128, k:k + 1] for k in range(DC)]
    hscr = [lay.stg[0][:, k:k + 1] for k in range(DC)]
    lay.ln(hxt, hscr, mb, rb, tb_, T=1, out_bf=[mext[k][:, 0:1] for k in range(DC)])

    def timemix(dst, col0):
        b.step()
        for k in range(DC):
            b.on("vector", lambda e, k=k: e.tensor_sub(B_[k], m_t[k], mext[k][:, 0:TC]))
        b.step()
        for k in range(DC):
            b.on("vector", lambda e, k=k, col0=col0: e.scalar_tensor_tensor(
                dst[k], B_[k], tm[0:128, col0 + k:col0 + k + 1],
                mext[k][:, 0:TC], ALU.mult, ALU.add))

    def loadw(dram):
        for k in range(DC):
            b.step()
            b.dma("sync", w32[:, 0:D], dram[k * 128:(k + 1) * 128, :])
            b.step()
            b.on("vector", lambda e, k=k: e.tensor_copy(WR[:, k * D:(k + 1) * D], w32[:, 0:D]))
        return WR[:, 0:8 * D]

    b.step()
    b.on("vector", lambda e: e.memset(onesf[:, 0:TC], 1.0))
    timemix(Cb, 0)
    wbf = loadw(wk_t)
    lay.proj_f(wbf, Cb, TC, D, ko, out_f32=True)
    timemix(Cb, DC)
    wbf = loadw(wv_t)
    lay.proj_f(wbf, Cb, TC, D, vo, out_f32=True)
    timemix(Cb, 2 * DC)
    wbf = loadw(wr_t)
    lay.proj_f(wbf, Cb, TC, D, ro, out_f32=True)

    for k in range(DC):
        b.step()
        b.dma("sync", zc[:, 0:TC], ko[k * 128:(k + 1) * 128, :])
        b.dma("gpsimd", vc[:, 0:TC], vo[k * 128:(k + 1) * 128, :])
        b.step()
        b.on("vector", lambda e: e.reduce_max(sm[:, 0:1], zc[:, 0:TC], AX.X))
        b.step()
        b.on("vector", lambda e: e.tensor_scalar(sm[:, 1:2], sm[:, 0:1], -1.0, None, ALU.mult))
        b.on("scalar", lambda e, k=k: e.activation(sm[:, 2:3], tm[0:128, 3 * DC + k:3 * DC + k + 1], AF.Exp))
        b.step()
        b.on("scalar", lambda e: e.activation(zc[:, 0:TC], zc[:, 0:TC], AF.Exp, bias=sm[:, 1:2]))
        b.on("vector", lambda e: e.tensor_scalar(sm[:, 3:4], sm[:, 2:3], -1.0, None, ALU.mult))
        b.step()
        b.on("scalar", lambda e: e.activation(sm[:, 4:5], sm[:, 3:4], AF.Exp))
        b.step()
        b.on("vector", lambda e: e.tensor_mul(vc[:, 0:TC], zc[:, 0:TC], vc[:, 0:TC]))
        b.on("vector", lambda e: e.tensor_scalar(gam[:, 0:TC], onesf[:, 0:TC], sm[:, 4:5], None, ALU.mult))
        b.step()
        b.on("vector", lambda e: e.tensor_tensor_scan(sa[:, 0:TC], gam[:, 0:TC], vc[:, 0:TC], 0.0, ALU.mult, ALU.add))
        b.step()
        b.on("vector", lambda e: e.tensor_tensor_scan(sb2[:, 0:TC], gam[:, 0:TC], zc[:, 0:TC], 0.0, ALU.mult, ALU.add))
        b.step()
        b.dma("sync", kmo[k * 128:(k + 1) * 128, 0:1], sm[:, 0:1])
        b.dma("gpsimd", tlo[k * 128:(k + 1) * 128, 0:1], sa[:, TC - 1:TC])
        b.dma("scalar", tlo2[k * 128:(k + 1) * 128, 0:1], sb2[:, TC - 1:TC])
    b.step()
    return b




def build_l1a_C(debug=False):
    b = BSP()
    xT = b.din("xT", [D, TC], F32)
    kTT = b.din("kTT", [D, TC], F32)
    vTT = b.din("vTT", [D, TC], F32)
    rTT = b.din("rTT", [D, TC], F32)
    kmax4 = b.din("kmax4", [D, 4], F32)
    tails8 = b.din("tails8", [D, 8], F32)
    tjs = b.din("tjs", [128, 4], F32)
    msk4 = b.din("msk4", [128, 4], F32)
    w_in = b.din("w_in", [D, 1], F32)
    u_in = b.din("u_in", [D, 1], F32)
    wo_t = b.din("wo_t", [D, D], F32)
    xo = b.dout("xoT", [D, TC], F32)

    lay = Lay(b, f32_cols=22800, bf_cols=41000, n_ps=6, pt_cols=0, n_stg=6)
    fa, ba, ps = lay.fa, lay.ba, lay.ps

    mb = fa.alloc(TC); rb = fa.alloc(TC); tb_ = fa.alloc(TC)
    wu = fa.alloc(2 * DC)
    km = fa.alloc(4 * DC)
    tl = fa.alloc(8 * DC)
    tjc = fa.alloc(4); mkc = fa.alloc(4)
    sm = fa.alloc(16)
    zc = fa.alloc(TC); vc = fa.alloc(TC)
    saf = fa.alloc(1 + TC); sbf = fa.alloc(1 + TC)
    gam = fa.alloc(TC)
    onesf = tb_
    rampg = fa.alloc(TC)
    num = fa.alloc(TC); den = fa.alloc(TC)
    xblk = ([rampg[:, i * 512:(i + 1) * 512] for i in range(4)]
            + [num[:, i * 512:(i + 1) * 512] for i in range(4)])
    w32 = mb

    rwkv = ba.alloc8(TC)
    scr = ba.alloc8(TC)
    WR = ba.alloc(8 * D)

    b.step()
    b.dma("sync", tjc[0:128, 0:4], tjs[:, :])
    b.dma("sync", mkc[0:128, 0:4], msk4[:, :])
    for k in range(DC):
        b.dma("sync", wu[0:128, k:k + 1], w_in[k * 128:(k + 1) * 128, :])
        b.dma("sync", wu[0:128, DC + k:DC + k + 1], u_in[k * 128:(k + 1) * 128, :])
        b.dma("gpsimd", km[0:128, 4 * k:4 * k + 4], kmax4[k * 128:(k + 1) * 128, :])
        b.dma("scalar", tl[0:128, 8 * k:8 * k + 8], tails8[k * 128:(k + 1) * 128, :])
    b.on("vector", lambda e: e.memset(onesf[:, 0:TC], 1.0))
    b.on("gpsimd", lambda e: e.iota(gam[:, 0:TC], [[1, TC]], channel_multiplier=0,
                                    allow_small_or_imprecise_dtypes=True))

    wbf = WR[:, 0:8 * D]
    for k in range(DC):
        b.step()
        b.dma("sync", w32[:, 0:D], wo_t[k * 128:(k + 1) * 128, :])
        b.step()
        b.on("vector", lambda e, k=k: e.tensor_copy(wbf[:, k * D:(k + 1) * D], w32[:, 0:D]))
    rampi = rb
    b.step()
    b.on("vector", lambda e: e.tensor_copy(rampi[:, 0:TC], gam[:, 0:TC]))

    for k in range(DC):
        wcol = wu[0:128, k:k + 1]
        ucol = wu[0:128, DC + k:DC + k + 1]
        b.step()
        b.dma("sync", zc[:, 0:TC], kTT[k * 128:(k + 1) * 128, :])
        b.dma("gpsimd", vc[:, 0:TC], vTT[k * 128:(k + 1) * 128, :])
        b.on("vector", lambda e, k=k: e.reduce_max(sm[:, 0:1], km[0:128, 4 * k:4 * k + 4], AX.X))
        b.step()
        b.on("vector", lambda e: e.tensor_scalar(sm[:, 1:2], sm[:, 0:1], -1.0, None, ALU.mult))
        b.on("scalar", lambda e, wcol=wcol: e.activation(sm[:, 2:3], wcol, AF.Exp))
        b.step()
        b.on("scalar", lambda e: e.activation(zc[:, 0:TC], zc[:, 0:TC], AF.Exp, bias=sm[:, 1:2]))
        b.on("vector", lambda e: e.tensor_scalar(sm[:, 3:4], sm[:, 2:3], -1.0, None, ALU.mult))
        b.step()
        b.on("scalar", lambda e: e.activation(sm[:, 4:5], sm[:, 3:4], AF.Exp))
        b.on("scalar", lambda e, ucol=ucol: e.activation(sm[:, 5:6], ucol, AF.Exp))
        b.step()
        b.on("vector", lambda e: e.tensor_mul(vc[:, 0:TC], zc[:, 0:TC], vc[:, 0:TC]))
        b.on("vector", lambda e: e.tensor_scalar(gam[:, 0:TC], onesf[:, 0:TC], sm[:, 4:5], None, ALU.mult))
        b.on("vector", lambda e: e.memset(sm[:, 6:8], 0.0))
        t8 = sm[:, 8:12]
        t9 = sm[:, 12:16]
        b.step()
        b.on("vector", lambda e: e.tensor_scalar(t8, tjc[0:128, 0:4], sm[:, 2:3], None, ALU.mult))
        b.on("vector", lambda e, k=k: e.tensor_scalar(
            t9, km[0:128, 4 * k:4 * k + 4], sm[:, 0:1], None, ALU.subtract))
        b.step()
        b.on("scalar", lambda e: e.activation(t8, t8, AF.Exp))
        b.on("scalar", lambda e: e.activation(t9, t9, AF.Exp))
        b.step()
        b.on("vector", lambda e: e.tensor_mul(t8, t8, t9))
        b.step()
        b.on("vector", lambda e: e.tensor_mul(t8, t8, mkc[0:128, 0:4]))
        b.step()
        b.on("vector", lambda e, k=k: e.tensor_mul(t9, t8, tl[0:128, 8 * k:8 * k + 4]))
        b.step()
        b.on("vector", lambda e: e.reduce_sum(sm[:, 6:7], t9, AX.X))
        b.on("vector", lambda e, k=k: e.tensor_mul(t9, t8, tl[0:128, 8 * k + 4:8 * k + 8]))
        b.step()
        b.on("vector", lambda e: e.reduce_sum(sm[:, 7:8], t9, AX.X))
        b.step()
        b.on("vector", lambda e: e.tensor_tensor_scan(saf[:, 1:1 + TC], gam[:, 0:TC],
                                                      vc[:, 0:TC], 0.0, ALU.mult, ALU.add))
        b.step()
        b.on("vector", lambda e: e.tensor_tensor_scan(sbf[:, 1:1 + TC], gam[:, 0:TC],
                                                      zc[:, 0:TC], 0.0, ALU.mult, ALU.add))
        b.step()
        b.on("vector", lambda e: e.tensor_scalar(rampg[:, 0:TC], rampi[:, 0:TC],
                                                 sm[:, 3:4], None, ALU.mult))
        b.step()
        b.on("scalar", lambda e: e.activation(rampg[:, 0:TC], rampg[:, 0:TC], AF.Exp,
                                              bias=sm[:, 3:4]))
        b.step()
        b.on("vector", lambda e: e.scalar_tensor_tensor(
            saf[:, 1:1 + TC], rampg[:, 0:TC], sm[:, 6:7], saf[:, 1:1 + TC],
            ALU.mult, ALU.add))
        b.on("vector", lambda e: e.tensor_copy(saf[:, 0:1], sm[:, 6:7]))
        b.step()
        b.on("vector", lambda e: e.scalar_tensor_tensor(
            sbf[:, 1:1 + TC], rampg[:, 0:TC], sm[:, 7:8], sbf[:, 1:1 + TC],
            ALU.mult, ALU.add))
        b.on("vector", lambda e: e.tensor_copy(sbf[:, 0:1], sm[:, 7:8]))
        b.step()
        b.on("vector", lambda e: e.scalar_tensor_tensor(
            num[:, 0:TC], vc[:, 0:TC], sm[:, 5:6], saf[:, 0:TC], ALU.mult, ALU.add))
        b.on("vector", lambda e: e.scalar_tensor_tensor(
            den[:, 0:TC], zc[:, 0:TC], sm[:, 5:6], sbf[:, 0:TC], ALU.mult, ALU.add))
        b.step()
        b.on("vector", lambda e: e.reciprocal(den[:, 0:TC], den[:, 0:TC]))
        b.step()
        b.on("vector", lambda e: e.tensor_mul(num[:, 0:TC], num[:, 0:TC], den[:, 0:TC]))
        if debug and k == 0:
            d_z = b.dout("d_z", [128, TC], F32)
            d_sa = b.dout("d_sa", [128, 1 + TC], F32)
            d_num = b.dout("d_num", [128, TC], F32)
            d_ramp = b.dout("d_ramp", [128, TC], F32)
            b.step()
            b.dma("sync", d_z[:, :], zc[:, 0:TC])
            b.dma("gpsimd", d_sa[:, :], saf[:, 0:1 + TC])
            b.dma("scalar", d_num[:, :], num[:, 0:TC])
            b.dma("sync", d_ramp[:, :], rampi[:, 0:TC])
        b.step()
        b.dma("sync", zc[:, 0:TC], rTT[k * 128:(k + 1) * 128, :])
        b.step()
        b.on("scalar", lambda e: e.activation(zc[:, 0:TC], zc[:, 0:TC], AF.Sigmoid))
        b.step()
        b.on("vector", lambda e, k=k: e.tensor_mul(rwkv[k], zc[:, 0:TC], num[:, 0:TC]))

    if debug:
        d_rw = b.dout("d_rw", [D, TC], BF16)
        b.step()
        for k in range(DC):
            b.dma(DMA_ENG[k % 3], d_rw[k * 128:(k + 1) * 128, :], rwkv[k])
    out_sub = scr
    def wstore(oc, o, n, stgap):
        b.on("vector", lambda e: e.tensor_copy(out_sub[oc][:, o:o + n], stgap))
    lay.proj_f(wbf, rwkv, TC, D, out_sub)
    lay.ln_resid_stream(out_sub, rwkv, mb, rb, tb_, T=TC, x_in=xT, x_out=xo,
                        xblk=xblk, x_is_bf=True)
    b.step()
    return b




def build_l2a_A(debug=False):
    b = BSP()
    xT = b.din("xT", [D, TC], F32)
    wq_t = b.din("wq_t", [D, D], F32)
    wk_t = b.din("wk_t", [D, D], F32)
    wv_t = b.din("wv_t", [D, D], F32)
    bq_r = b.din("bq_r", [1, D], F32)
    bk_r = b.din("bk_r", [1, D], F32)
    bv_r = b.din("bv_r", [1, D], F32)
    cs16 = b.din("cs16", [TC, 512], F32)
    sn16 = b.din("sn16", [TC, 512], F32)
    qT_o = b.dout("qT", [D, TC], BF16)
    kT_o = b.dout("kT", [D, TC], BF16)
    v_o = b.dout("v", [TC, D], BF16)

    lay = Lay(b, f32_cols=16500, bf_cols=59500, n_ps=6, pt_cols=1024, n_stg=6)
    fa, ba, ps, pt = lay.fa, lay.ba, lay.ps, lay.pt

    mb = fa.alloc(TC); rb = fa.alloc(TC); tb_ = fa.alloc(TC)
    qf = fa.alloc(D); qr = fa.alloc(D); tmp1 = fa.alloc(512)
    cs_f = fa.alloc(512); sn_f = fa.alloc(512)
    bb3 = fa.alloc(3 * D)
    w32 = mb

    xln = ba.alloc8(TC)
    scr = ba.alloc8(TC)
    WR = ba.alloc(8 * D)
    qT_sb = ba.alloc8(TC)
    qrb = ba.alloc(D)
    brow = ba.alloc(D)

    b.step()
    xbf = qT_sb
    for k in range(DC):
        b.dma("gpsimd", xbf[k], xT[k * 128:(k + 1) * 128, :])
    lay.ln(xbf, scr, mb, rb, tb_, T=TC, x_is_bf=True, out_bf=xln)

    for bi, brd in enumerate((bq_r, bk_r, bv_r)):
        b.step()
        b.dma("sync", tmp1[0:1, 0:512], brd[0:1, 0:512])
        b.dma("gpsimd", cs_f[0:1, 0:512], brd[0:1, 512:1024])
        b.step()
        b.on("vector", lambda e: e.tensor_copy(brow[0:1, 0:512], tmp1[0:1, 0:512]))
        b.on("vector", lambda e: e.tensor_copy(brow[0:1, 512:1024], cs_f[0:1, 0:512]))
        b.step()
        def bmm(e):
            r = None
            for h in range(2):
                r = e.matmul(ps[h][:, 0:512], lay.ones_bf[0:1, 0:128],
                             brow[0:1, h * 512:(h + 1) * 512], start=True, stop=True)
            return r
        b.on("tensor", bmm)
        b.step()
        for h in range(2):
            b.on("vector", lambda e, bi=bi, h=h: e.tensor_copy(
                bb3[:, bi * D + h * 512: bi * D + (h + 1) * 512], ps[h][:, 0:512]))

    def loadw(dram):
        for k in range(DC):
            b.step()
            b.dma("sync", w32[:, 0:D], dram[k * 128:(k + 1) * 128, :])
            b.step()
            b.on("vector", lambda e, k=k: e.tensor_copy(WR[:, k * D:(k + 1) * D], w32[:, 0:D]))
        return WR[:, 0:8 * D]

    wbf = loadw(wv_t)
    def vevac_store(tt, o, n, stgap):
        b.on("vector", lambda e, tt=tt, o=o, n=n, stgap=stgap: None)
    jobs = [(tt, ob) for tt in range(16) for ob in range(2)]
    groups = [jobs[i:i + 3] for i in range(0, len(jobs), 3)]
    pend = []
    for gi, g in enumerate(groups):
        banks = [ps[3 * (gi % 2) + i] for i in range(len(g))]
        b.step()
        def mms(e, g=g, banks=banks):
            r = None
            for i, (tt, ob) in enumerate(g):
                for k in range(DC):
                    r = e.matmul(banks[i][:, 0:512],
                                 xln[k][:, tt * 128:(tt + 1) * 128],
                                 wbf[k * D + ob * 512: k * D + ob * 512 + 512] if False else
                                 WR[:, k * D + ob * 512: k * D + ob * 512 + 512],
                                 start=(k == 0), stop=(k == DC - 1))
            return r
        b.on("tensor", mms)
        new_pend = []
        if gi > 0:
            prevg = groups[gi - 1]
            prevbanks = [ps[3 * ((gi - 1) % 2) + i] for i in range(len(prevg))]
            for i, (tt, ob) in enumerate(prevg):
                s = lay.stg_next()[:, 0:512]
                b.on("vector", lambda e, s=s, bank=prevbanks[i], ob=ob: e.scalar_tensor_tensor(
                    s, bank[:, 0:512], 1.0, bb3[:, 2 * D + ob * 512:2 * D + (ob + 1) * 512],
                    ALU.mult, ALU.add))
                new_pend.append((tt, ob, s))
        for (tt, ob, s) in pend:
            b.dma(DMA_ENG[(tt + ob) % 3], v_o[tt * 128:(tt + 1) * 128, ob * 512:(ob + 1) * 512], s)
        pend = new_pend
    b.step()
    lastg = groups[-1]
    lastbanks = [ps[3 * ((len(groups) - 1) % 2) + i] for i in range(len(lastg))]
    new_pend = []
    for i, (tt, ob) in enumerate(lastg):
        s = lay.stg_next()[:, 0:512]
        b.on("vector", lambda e, s=s, bank=lastbanks[i], ob=ob: e.scalar_tensor_tensor(
            s, bank[:, 0:512], 1.0, bb3[:, 2 * D + ob * 512:2 * D + (ob + 1) * 512],
            ALU.mult, ALU.add))
        new_pend.append((tt, ob, s))
    for (tt, ob, s) in pend:
        b.dma(DMA_ENG[(tt + ob) % 3], v_o[tt * 128:(tt + 1) * 128, ob * 512:(ob + 1) * 512], s)
    b.step()
    for (tt, ob, s) in new_pend:
        b.dma(DMA_ENG[(tt + ob) % 3], v_o[tt * 128:(tt + 1) * 128, ob * 512:(ob + 1) * 512], s)
    b.step()

    def qk_path(wdram, bias_off, scale, dst_dram):
        wbf = loadw(wdram)
        for tt in range(16):
            b.step()
            b.dma("sync", cs_f[:, 0:512], cs16[tt * 128:(tt + 1) * 128, :])
            b.dma("gpsimd", sn_f[:, 0:512], sn16[tt * 128:(tt + 1) * 128, :])
            def mmq(e, tt=tt):
                r = None
                for ob in range(2):
                    for k in range(DC):
                        r = e.matmul(ps[ob][:, 0:512],
                                     xln[k][:, tt * 128:(tt + 1) * 128],
                                     WR[:, k * D + ob * 512: k * D + ob * 512 + 512],
                                     start=(k == 0), stop=(k == DC - 1))
                return r
            b.on("tensor", mmq)
            b.step()
            for ob in range(2):
                b.on("vector", lambda e, ob=ob: e.scalar_tensor_tensor(
                    qf[:, ob * 512:(ob + 1) * 512], ps[ob][:, 0:512], 1.0,
                    bb3[:, bias_off + ob * 512:bias_off + (ob + 1) * 512],
                    ALU.mult, ALU.add))
            qe = qf.rearrange("p (h j two) -> p two h j", h=16, j=32, two=2)
            qev = qe[:, 0, :, :]
            qov = qe[:, 1, :, :]
            orr = qr.rearrange("p (h half j) -> p half h j", h=16, half=2, j=32)
            orre = orr[:, 0, :, :]
            orim = orr[:, 1, :, :]
            b.step()
            b.on("vector", lambda e: e.tensor_mul(orre, qev, cs_f[:, 0:512]))
            b.on("vector", lambda e: e.tensor_mul(tmp1[:, 0:512], qov, sn_f[:, 0:512]))
            b.step()
            b.on("vector", lambda e: e.tensor_sub(orre, orre, tmp1[:, 0:512]))
            b.on("vector", lambda e: e.tensor_mul(tmp1[:, 0:512], qov, cs_f[:, 0:512]))
            b.step()
            b.on("vector", lambda e: e.tensor_mul(orim, qev, sn_f[:, 0:512]))
            b.step()
            b.on("vector", lambda e: e.tensor_add(orim, orim, tmp1[:, 0:512]))
            if debug and tt == 0 and bias_off == 0:
                d_qf = b.dout("d_qf", [128, D], F32)
                d_qr = b.dout("d_qr", [128, D], F32)
                b.step()
                b.dma("sync", d_qf[:, :], qf[:, 0:D])
                b.dma("gpsimd", d_qr[:, :], qr[:, 0:D])
            b.step()
            b.on("vector", lambda e, scale=scale: e.tensor_scalar(
                qrb[:, 0:D], qr[:, 0:D], scale, None, ALU.mult))
            b.step()
            def qtr(e):
                r = None
                for c in range(DC):
                    r = e.transpose(pt[:, c * 128:(c + 1) * 128],
                                    qrb[:, c * 128:(c + 1) * 128], lay.ident_bf[:])
                return r
            b.on("tensor", qtr)
            b.step()
            for c in range(DC):
                b.on("vector", lambda e, c=c, tt=tt: e.tensor_copy(
                    qT_sb[c][:, tt * 128:(tt + 1) * 128], pt[:, c * 128:(c + 1) * 128]))
        b.step()
        for c in range(DC):
            b.dma(DMA_ENG[c % 3], dst_dram[c * 128:(c + 1) * 128, :], qT_sb[c])
        b.step()

    qk_path(wq_t, 0, 1.0 / 8.0, qT_o)
    qk_path(wk_t, D, 1.0, kT_o)

    kn_o = b.dout("kn2seg", [D, 1], F32)
    sego = b.din("sego", [128, 128], F32)
    sego_bf = qrb
    b.step()
    b.dma("sync", qf[:, 0:128], sego[:, :])
    b.step()
    b.on("vector", lambda e: e.tensor_copy(sego_bf[:, 0:128], qf[:, 0:128]))
    for c in range(DC):
        b.step()
        b.on("scalar", lambda e, c=c: e.activation(scr[c], qT_sb[c], AF.Square))
        b.step()
        def mmseg(e, c=c):
            r = None
            for blk in range(4):
                r = e.matmul(ps[blk][:, 0:512], sego_bf[:, 0:128],
                             scr[c][:, blk * 512:(blk + 1) * 512],
                             start=True, stop=True)
            return r
        b.on("tensor", mmseg)
        b.step()
        for blk in range(4):
            b.on("vector", lambda e, blk=blk: e.tensor_copy(
                mb[:, blk * 512:(blk + 1) * 512], ps[blk][:, 0:512]))
        b.step()
        b.on("vector", lambda e: e.reduce_max(rb[:, 0:1], mb[:, 0:TC], AX.X))
        b.step()
        b.dma("sync", kn_o[c * 128:(c + 1) * 128, 0:1], rb[:, 0:1])
    b.step()
    return b




def build_l2a_B(debug=False):
    b = BSP()
    qTi = b.din("qTi", [D, TC], BF16)
    kTf = b.din("kTf", [D, TSEQ], BF16)
    vf = b.din("vf", [TSEQ, D], BF16)
    mtd = b.din("mtd", [128, 16 * 512], F32)
    wo_t = b.din("wo_t", [D, D], F32)
    bo_c = b.din("bo_c", [D, 1], F32)
    at_o = b.dout("attnTi", [D, TC], BF16)

    lay = Lay(b, f32_cols=10400, bf_cols=75100, n_ps=6, pt_cols=1024, n_stg=6)
    fa, ba, ps, pt = lay.fa, lay.ba, lay.ps, lay.pt

    mb = fa.alloc(TC)
    mtd_f = fa.alloc(16 * 512)
    boc = fa.alloc(DC)
    rec2 = fa.alloc(2)
    w32 = mb

    qT_sb = ba.alloc8(TC)
    kpair = ba.alloc(TSEQ)
    vpair = [ba.alloc(128) for _ in range(64)]
    probs = [ba.alloc(512) for _ in range(2)]
    ytok = [ba.alloc(D) for _ in range(16)]
    yT = ba.alloc8(TC)
    WR = ba.alloc(8 * D)

    b.step()
    for k in range(DC):
        b.dma("gpsimd", qT_sb[k], qTi[k * 128:(k + 1) * 128, :])
        b.dma("sync", boc[0:128, k:k + 1], bo_c[k * 128:(k + 1) * 128, :])
    b.dma("scalar", mtd_f[:, :], mtd[:, :])
    for k in range(DC):
        b.step()
        b.dma("sync", w32[:, 0:D], wo_t[k * 128:(k + 1) * 128, :])
        b.step()
        b.on("vector", lambda e, k=k: e.tensor_copy(WR[:, k * D:(k + 1) * D], w32[:, 0:D]))

    for pair in range(8):
        b.step()
        b.dma("sync", kpair[:, :], kTf[pair * 128:(pair + 1) * 128, :])
        for ch in range(64):
            b.dma(("gpsimd", "scalar")[ch % 2], vpair[ch],
                  vf[ch * 128:(ch + 1) * 128, pair * 128:(pair + 1) * 128])
        for j in range(16):
            ngrp = j + 1
            qslA = qT_sb[pair][0:64, j * 128:(j + 1) * 128]
            qslB = qT_sb[pair][64:128, j * 128:(j + 1) * 128]
            for g in range(ngrp):
                par = g % 2
                diag = (g == ngrp - 1)
                b.step()
                def mms(e, g=g, par=par, qslA=qslA, qslB=qslB):
                    r = None
                    for h, qsl in ((0, qslA), (1, qslB)):
                        bank = ps[2 + 2 * h + par]
                        for c in range(4):
                            kcol = (g * 4 + c) * 128
                            r = e.matmul(bank[:, c * 128:(c + 1) * 128],
                                         kpair[h * 64:(h + 1) * 64, kcol:kcol + 128],
                                         qsl, start=True, stop=True)
                    return r
                b.on("tensor", mms)
                if diag:
                    b.step()
                    for h in range(2):
                        b.on("vector", lambda e, h=h, par=par, j=j: e.tensor_tensor(
                            ps[2 + 2 * h + par][:, 0:512], ps[2 + 2 * h + par][:, 0:512],
                            mtd_f[:, j * 512:(j + 1) * 512], ALU.add))
                b.step()
                for h in range(2):
                    b.on("scalar", lambda e, h=h, par=par: e.activation(
                        probs[h], ps[2 + 2 * h + par][:, 0:512], AF.Exp))
                b.step()
                def pv(e, g=g, ngrp=ngrp):
                    r = None
                    first = (g == 0)
                    last = (g == ngrp - 1)
                    for h in range(2):
                        for c in range(4):
                            ch = g * 4 + c
                            r = e.matmul(ps[0][:, h * 64:(h + 1) * 64],
                                         probs[h][:, c * 128:(c + 1) * 128],
                                         vpair[ch][:, h * 64:(h + 1) * 64],
                                         start=(first and c == 0 and h == 0),
                                         stop=(last and c == 3))
                    for h in range(2):
                        for c in range(4):
                            r = e.matmul(ps[1][:, h:h + 1],
                                         probs[h][:, c * 128:(c + 1) * 128],
                                         lay.ones_bf[:, 0:1],
                                         start=(first and c == 0 and h == 0),
                                         stop=(last and c == 3))
                    return r
                b.on("tensor", pv)
            b.step()
            b.on("vector", lambda e: e.tensor_copy(rec2[:, 0:2], ps[1][:, 0:2]))
            b.step()
            b.on("vector", lambda e: e.reciprocal(rec2[:, 0:2], rec2[:, 0:2]))
            b.step()
            for h in range(2):
                b.on("vector", lambda e, h=h, j=j, pair=pair: e.tensor_scalar(
                    ytok[j][:, pair * 128 + h * 64: pair * 128 + (h + 1) * 64],
                    ps[0][:, h * 64:(h + 1) * 64], rec2[:, h:h + 1], None, ALU.mult))
            b.step()

    for j in range(16):
        b.step()
        def ytr(e, j=j):
            r = None
            for c in range(DC):
                r = e.transpose(pt[:, c * 128:(c + 1) * 128],
                                ytok[j][:, c * 128:(c + 1) * 128], lay.ident_bf[:])
            return r
        b.on("tensor", ytr)
        b.step()
        for c in range(DC):
            b.on("vector", lambda e, c=c, j=j: e.tensor_copy(
                yT[c][:, j * 128:(j + 1) * 128], pt[:, c * 128:(c + 1) * 128]))
    boc_list = [boc[0:128, k:k + 1] for k in range(DC)]
    lay.proj_f(WR[:, 0:8 * D], yT, TC, D, at_o, bias=boc_list, act=AF.Identity)
    b.step()
    return b




def build_sandwich():
    b = BSP()
    xT = b.din("xT", [D, TC], F32)
    mixT = b.din("mixT", [D, TC], BF16)
    xo = b.dout("xoT", [D, TC], F32)
    lay = Lay(b, f32_cols=10300, bf_cols=33000, n_ps=6, pt_cols=0, n_stg=2)
    fa, ba = lay.fa, lay.ba
    mb = fa.alloc(TC); rb = fa.alloc(TC); tb_ = fa.alloc(TC)
    xblk = ([mb[:, i * 512:(i + 1) * 512] for i in range(0)]
            + [fa.alloc(512) for _ in range(8)])
    mix = ba.alloc8(TC)
    scr = ba.alloc8(TC)
    b.step()
    for k in range(DC):
        b.dma(DMA_ENG[k % 3], mix[k], mixT[k * 128:(k + 1) * 128, :])
    lay.ln_resid_stream(mix, scr, mb, rb, tb_, T=TC, x_in=xT, x_out=xo,
                        xblk=xblk, x_is_bf=True)
    b.step()
    return b


def build_l3a_A():
    b = BSP()
    xT = b.din("xT", [D, TC], F32)
    wq = b.din("wq", [D, D], F32)
    wv = b.din("wv", [D, D], F32)
    bq_r = b.din("bq_r", [1, D], F32)
    bv_r = b.din("bv_r", [1, D], F32)
    qsm_o = b.dout("qsmT", [D, TC], BF16)
    v_o = b.dout("vtok", [TC, D], BF16)
    ml_o = b.dout("mloc", [D, 1], F32)

    lay = Lay(b, f32_cols=16000, bf_cols=60000, n_ps=6, pt_cols=1024, n_stg=6)
    fa, ba, ps, pt = lay.fa, lay.ba, lay.ps, lay.pt
    SCONST = float(64.0 ** 0.5) ** (-0.25)

    mb = fa.alloc(TC); rb = fa.alloc(TC); tb_ = fa.alloc(TC)
    qf = fa.alloc(D)
    bb2 = fa.alloc(2 * D)
    sm = fa.alloc(8)
    w32 = mb

    xln = ba.alloc8(TC)
    scr = ba.alloc8(TC)
    WR = ba.alloc(8 * D)
    qT_sb = ba.alloc8(TC)
    qrb = ba.alloc(D)
    brow = ba.alloc(D)

    b.step()
    xbf = qT_sb
    for k in range(DC):
        b.dma("gpsimd", xbf[k], xT[k * 128:(k + 1) * 128, :])
    lay.ln(xbf, scr, mb, rb, tb_, T=TC, x_is_bf=True, out_bf=xln)

    for bi, brd in enumerate((bq_r, bv_r)):
        b.step()
        b.dma("sync", qf[0:1, 0:D], brd[0:1, :])
        b.step()
        b.on("vector", lambda e: e.tensor_copy(brow[0:1, 0:D], qf[0:1, 0:D]))
        b.step()
        def bmm(e):
            r = None
            for h in range(2):
                r = e.matmul(ps[h][:, 0:512], lay.ones_bf[0:1, 0:128],
                             brow[0:1, h * 512:(h + 1) * 512], start=True, stop=True)
            return r
        b.on("tensor", bmm)
        b.step()
        for h in range(2):
            b.on("vector", lambda e, bi=bi, h=h: e.tensor_copy(
                bb2[:, bi * D + h * 512: bi * D + (h + 1) * 512], ps[h][:, 0:512]))

    def loadw(dram):
        for k in range(DC):
            b.step()
            b.dma("sync", w32[:, 0:D], dram[k * 128:(k + 1) * 128, :])
            b.step()
            b.on("vector", lambda e, k=k: e.tensor_copy(WR[:, k * D:(k + 1) * D], w32[:, 0:D]))
        return WR[:, 0:8 * D]

    wbf = loadw(wv)
    jobs = [(tt, ob) for tt in range(16) for ob in range(2)]
    groups = [jobs[i:i + 3] for i in range(0, len(jobs), 3)]
    pend = []
    for gi, g in enumerate(groups):
        banks = [ps[3 * (gi % 2) + i] for i in range(len(g))]
        b.step()
        def mms(e, g=g, banks=banks):
            r = None
            for i, (tt, ob) in enumerate(g):
                for k in range(DC):
                    r = e.matmul(banks[i][:, 0:512],
                                 xln[k][:, tt * 128:(tt + 1) * 128],
                                 WR[:, k * D + ob * 512: k * D + ob * 512 + 512],
                                 start=(k == 0), stop=(k == DC - 1))
            return r
        b.on("tensor", mms)
        new_pend = []
        if gi > 0:
            prevg = groups[gi - 1]
            prevbanks = [ps[3 * ((gi - 1) % 2) + i] for i in range(len(prevg))]
            for i, (tt, ob) in enumerate(prevg):
                s = lay.stg_next()[:, 0:512]
                b.on("vector", lambda e, s=s, bank=prevbanks[i], ob=ob: e.scalar_tensor_tensor(
                    s, bank[:, 0:512], 1.0, bb2[:, D + ob * 512:D + (ob + 1) * 512],
                    ALU.mult, ALU.add))
                new_pend.append((tt, ob, s))
        for (tt, ob, s) in pend:
            b.dma(DMA_ENG[(tt + ob) % 3], v_o[tt * 128:(tt + 1) * 128, ob * 512:(ob + 1) * 512], s)
        pend = new_pend
    b.step()
    lastg = groups[-1]
    lastbanks = [ps[3 * ((len(groups) - 1) % 2) + i] for i in range(len(lastg))]
    new_pend = []
    for i, (tt, ob) in enumerate(lastg):
        s = lay.stg_next()[:, 0:512]
        b.on("vector", lambda e, s=s, bank=lastbanks[i], ob=ob: e.scalar_tensor_tensor(
            s, bank[:, 0:512], 1.0, bb2[:, D + ob * 512:D + (ob + 1) * 512],
            ALU.mult, ALU.add))
        new_pend.append((tt, ob, s))
    for (tt, ob, s) in pend:
        b.dma(DMA_ENG[(tt + ob) % 3], v_o[tt * 128:(tt + 1) * 128, ob * 512:(ob + 1) * 512], s)
    b.step()
    for (tt, ob, s) in new_pend:
        b.dma(DMA_ENG[(tt + ob) % 3], v_o[tt * 128:(tt + 1) * 128, ob * 512:(ob + 1) * 512], s)
    b.step()

    wbf = loadw(wq)
    for tt in range(16):
        b.step()
        def mmq(e, tt=tt):
            r = None
            for ob in range(2):
                for k in range(DC):
                    r = e.matmul(ps[ob][:, 0:512],
                                 xln[k][:, tt * 128:(tt + 1) * 128],
                                 WR[:, k * D + ob * 512: k * D + ob * 512 + 512],
                                 start=(k == 0), stop=(k == DC - 1))
            return r
        b.on("tensor", mmq)
        b.step()
        for ob in range(2):
            b.on("vector", lambda e, ob=ob: e.scalar_tensor_tensor(
                qf[:, ob * 512:(ob + 1) * 512], ps[ob][:, 0:512], 1.0,
                bb2[:, ob * 512:(ob + 1) * 512], ALU.mult, ALU.add))
        for h in range(16):
            hs = qf[:, h * 64:(h + 1) * 64]
            b.step()
            b.on("vector", lambda e, hs=hs: e.reduce_max(sm[:, 0:1], hs, AX.X))
            b.step()
            b.on("vector", lambda e: e.tensor_scalar(sm[:, 1:2], sm[:, 0:1], -1.0, None, ALU.mult))
            b.step()
            b.on("scalar", lambda e, hs=hs: e.activation(hs, hs, AF.Exp, bias=sm[:, 1:2],
                                                         accum_out=sm[:, 2:3]))
            b.step()
            b.on("vector", lambda e: e.reciprocal(sm[:, 2:3], sm[:, 2:3]))
            b.step()
            b.on("vector", lambda e, hs=hs: e.tensor_scalar(hs, hs, sm[:, 2:3],
                                                            SCONST, ALU.mult, ALU.mult))
        b.step()
        b.on("vector", lambda e: e.tensor_copy(qrb[:, 0:D], qf[:, 0:D]))
        b.step()
        def qtr(e):
            r = None
            for c in range(DC):
                r = e.transpose(pt[:, c * 128:(c + 1) * 128],
                                qrb[:, c * 128:(c + 1) * 128], lay.ident_bf[:])
            return r
        b.on("tensor", qtr)
        b.step()
        for c in range(DC):
            b.on("vector", lambda e, c=c, tt=tt: e.tensor_copy(
                qT_sb[c][:, tt * 128:(tt + 1) * 128], pt[:, c * 128:(c + 1) * 128]))
    b.step()
    for c in range(DC):
        b.dma(DMA_ENG[c % 3], qsm_o[c * 128:(c + 1) * 128, :], qT_sb[c])
    b.step()
    for c in range(DC):
        b.on("vector", lambda e, c=c: e.reduce_max(sm[:, 3:4], qT_sb[c], AX.X))
        b.on("vector", lambda e, c=c: e.tensor_copy(mb[:, c:c + 1], sm[:, 3:4])) if False else None
    b.step()
    for c in range(DC):
        b.on("vector", lambda e, c=c: e.reduce_max(mb[:, c:c + 1], qT_sb[c], AX.X))
    b.step()
    for c in range(DC):
        b.dma(DMA_ENG[c % 3], ml_o[c * 128:(c + 1) * 128, 0:1], mb[:, c:c + 1])
    b.step()
    return b


def build_l3a_B():
    b = BSP()
    qsmT = b.din("qsmT", [D, TC], BF16)
    vtok = b.din("vtok", [TC, D], BF16)
    ml4 = b.din("mloc4", [D, 4], F32)
    s_o = b.dout("spart", [D, 1], F32)
    cx_o = b.dout("ctxp", [D, 64], F32)

    lay = Lay(b, f32_cols=8000, bf_cols=70000, n_ps=6, pt_cols=1024, n_stg=6)
    fa, ba, ps, pt = lay.fa, lay.ba, lay.ps, lay.pt

    m4 = fa.alloc(4 * DC)
    sm = fa.alloc(8)
    cstg = fa.alloc(64)
    qT = ba.alloc8(TC)
    ek = ba.alloc8(TC)
    ektok = [ba.alloc(D) for _ in range(16)]
    vt = [ba.alloc(D) for _ in range(16)]

    b.step()
    for k in range(DC):
        b.dma("gpsimd", qT[k], qsmT[k * 128:(k + 1) * 128, :])
        b.dma("sync", m4[0:128, 4 * k:4 * k + 4], ml4[k * 128:(k + 1) * 128, :])
    for tt in range(16):
        b.dma("scalar", vt[tt], vtok[tt * 128:(tt + 1) * 128, :])
    for k in range(DC):
        b.step()
        b.on("vector", lambda e, k=k: e.reduce_max(sm[:, 0:1], m4[0:128, 4 * k:4 * k + 4], AX.X))
        b.step()
        b.on("vector", lambda e: e.tensor_scalar(sm[:, 1:2], sm[:, 0:1], -1.0, None, ALU.mult))
        b.step()
        b.on("scalar", lambda e, k=k: e.activation(ek[k], qT[k], AF.Exp, bias=sm[:, 1:2]))
        b.step()
        b.on("vector", lambda e, k=k: e.reduce_sum(sm[:, 2:3], ek[k], AX.X))
        b.step()
        b.dma("sync", s_o[k * 128:(k + 1) * 128, 0:1], sm[:, 2:3])
    for tt in range(16):
        b.step()
        def etr(e, tt=tt):
            r = None
            for c in range(DC):
                r = e.transpose(pt[:, c * 128:(c + 1) * 128],
                                ek[c][:, tt * 128:(tt + 1) * 128], lay.ident_bf[:])
            return r
        b.on("tensor", etr)
        b.step()
        for c in range(DC):
            b.on("vector", lambda e, c=c, tt=tt: e.tensor_copy(
                ektok[tt][:, c * 128:(c + 1) * 128], pt[:, c * 128:(c + 1) * 128]))
    for p2 in range(8):
        b.step()
        def cmm(e, p2=p2):
            r = None
            for hh in range(2):
                h = 2 * p2 + hh
                for tt in range(16):
                    r = e.matmul(ps[hh][0:64, 0:64],
                                 ektok[tt][:, h * 64:(h + 1) * 64],
                                 vt[tt][:, h * 64:(h + 1) * 64],
                                 start=(tt == 0), stop=(tt == 15))
            return r
        b.on("tensor", cmm)
        b.step()
        for hh in range(2):
            b.on("vector", lambda e, hh=hh: e.tensor_copy(cstg[0:64, 0:64], ps[hh][0:64, 0:64])) \
                if False else None
        for hh in range(2):
            b.on("vector" if hh == 0 else "scalar",
                 (lambda e, hh=hh: e.tensor_copy(cstg[0:64, 0:64], ps[hh][0:64, 0:64])) if hh == 0
                 else (lambda e, hh=hh: e.copy(cstg[64:128, 0:64], ps[hh][0:64, 0:64])))
        b.step()
        b.dma("sync", cx_o[(2 * p2) * 64:(2 * p2) * 64 + 64, :], cstg[0:64, 0:64])
        b.dma("gpsimd", cx_o[(2 * p2 + 1) * 64:(2 * p2 + 1) * 64 + 64, :], cstg[64:128, 0:64])
    b.step()
    return b


def build_l3a_C():
    b = BSP()
    xT = b.din("xT", [D, TC], F32)
    qsmT = b.din("qsmT", [D, TC], BF16)
    cx4 = b.din("ctxp4", [D, 4 * 64], F32)
    s4 = b.din("spart4", [D, 4], F32)
    ml4 = b.din("mloc4", [D, 4], F32)
    wo = b.din("wo", [D, D], F32)
    bo_c = b.din("bo_c", [D, 1], F32)
    xo = b.dout("xoT", [D, TC], F32)

    lay = Lay(b, f32_cols=14000, bf_cols=60000, n_ps=6, pt_cols=0, n_stg=6)
    fa, ba, ps = lay.fa, lay.ba, lay.ps
    SCONST = float(64.0 ** 0.5) ** (-0.25)

    mb = fa.alloc(TC); rb = fa.alloc(TC); tb_ = fa.alloc(TC)
    cxf = fa.alloc(4 * 64)
    s4f = fa.alloc(4 * DC)
    m4f = fa.alloc(4 * DC)
    boc = fa.alloc(DC)
    sm = fa.alloc(8)
    xblk = ([rb[:, i * 512:(i + 1) * 512] for i in range(0)]
            + [fa.alloc(512) for _ in range(8)])
    w32 = mb

    qT = ba.alloc8(TC)
    scr = ba.alloc8(TC)
    attnT = ba.alloc8(TC)
    ctxbf = ba.alloc(64)
    WR = ba.alloc(8 * D)

    b.step()
    for k in range(DC):
        b.dma("gpsimd", qT[k], qsmT[k * 128:(k + 1) * 128, :])
        b.dma("sync", s4f[0:128, 4 * k:4 * k + 4], s4[k * 128:(k + 1) * 128, :])
        b.dma("sync", m4f[0:128, 4 * k:4 * k + 4], ml4[k * 128:(k + 1) * 128, :])
        b.dma("scalar", boc[0:128, k:k + 1], bo_c[k * 128:(k + 1) * 128, :])
    for k in range(DC):
        b.step()
        b.dma("sync", w32[:, 0:D], wo[k * 128:(k + 1) * 128, :])
        b.step()
        b.on("vector", lambda e, k=k: e.tensor_copy(WR[:, k * D:(k + 1) * D], w32[:, 0:D]))

    for k in range(DC):
        b.step()
        b.dma("sync", cxf[:, 0:256], cx4[k * 128:(k + 1) * 128, :].rearrange("p (r e) -> p (r e)", r=4) if False else cx4[k * 128:(k + 1) * 128, :])
        b.step()
        b.on("vector", lambda e, k=k: e.reduce_sum(sm[:, 0:1], s4f[0:128, 4 * k:4 * k + 4], AX.X))
        b.step()
        b.on("vector", lambda e: e.reciprocal(sm[:, 0:1], sm[:, 0:1]))
        b.step()
        b.on("vector", lambda e: e.tensor_add(cxf[:, 0:64], cxf[:, 0:64], cxf[:, 64:128]))
        b.step()
        b.on("vector", lambda e: e.tensor_add(cxf[:, 128:192], cxf[:, 128:192], cxf[:, 192:256]))
        b.step()
        b.on("vector", lambda e: e.tensor_add(cxf[:, 0:64], cxf[:, 0:64], cxf[:, 128:192]))
        b.step()
        b.on("vector", lambda e: e.tensor_scalar(cxf[:, 0:64], cxf[:, 0:64], sm[:, 0:1],
                                                 SCONST, ALU.mult, ALU.mult))
        b.step()
        b.on("vector", lambda e: e.tensor_copy(ctxbf[:, 0:64], cxf[:, 0:64]))
        b.step()
        def amm(e, k=k):
            r = None
            for blk in range(4):
                for hh in range(2):
                    r = e.matmul(ps[blk][hh * 64:(hh + 1) * 64, 0:512],
                                 ctxbf[hh * 64:(hh + 1) * 64, 0:64],
                                 qT[k][hh * 64:(hh + 1) * 64, blk * 512:(blk + 1) * 512],
                                 start=True, stop=True)
            return r
        b.on("tensor", amm)
        b.step()
        for blk in range(4):
            b.on("vector", lambda e, k=k, blk=blk: e.tensor_copy(
                attnT[k][:, blk * 512:(blk + 1) * 512], ps[blk][:, 0:512]))
    mixo = scr
    boc_list = [boc[0:128, k:k + 1] for k in range(DC)]
    lay.proj_f(WR[:, 0:8 * D], attnT, TC, D, mixo, bias=boc_list, act=AF.Identity)
    xsc = qT
    lay.ln_resid_stream(mixo, xsc, mb, rb, tb_, T=TC, x_in=xT, x_out=xo,
                        xblk=xblk, x_is_bf=True)
    b.step()
    return b


def build_l3f_1():
    b = BSP()
    xT = b.din("xT", [D, TC], F32)
    g1 = b.din("g1", [D, 1], F32)
    b1 = b.din("b1c", [D, 1], F32)
    wq = b.din("wq", [D, D], F32)
    wk = b.din("wk", [D, D], F32)
    wv = b.din("wv", [D, D], F32)
    qT_o = b.dout("qT", [D, TC], BF16)
    kT_o = b.dout("kT", [D, TC], BF16)
    v_o = b.dout("v", [TC, D], BF16)
    m0_o = b.dout("m0T", [D, TC], BF16)

    lay = Lay(b, f32_cols=13000, bf_cols=59500, n_ps=6, pt_cols=0, n_stg=6)
    fa, ba, ps = lay.fa, lay.ba, lay.ps

    mb = fa.alloc(TC); rb = fa.alloc(TC); tb_ = fa.alloc(TC)
    gb = fa.alloc(2 * DC)
    w32 = mb

    m0 = ba.alloc8(TC)
    scr = ba.alloc8(TC)
    xbf = ba.alloc8(TC)
    h = xbf
    WR = ba.alloc(8 * D)

    b.step()
    for k in range(DC):
        b.dma("gpsimd", xbf[k], xT[k * 128:(k + 1) * 128, :])
        b.dma("sync", gb[0:128, k:k + 1], g1[k * 128:(k + 1) * 128, :])
        b.dma("sync", gb[0:128, DC + k:DC + k + 1], b1[k * 128:(k + 1) * 128, :])
    lay.ln(xbf, scr, mb, rb, tb_, T=TC, x_is_bf=True, out_bf=m0)
    gl = [gb[0:128, k:k + 1] for k in range(DC)]
    bl = [gb[0:128, DC + k:DC + k + 1] for k in range(DC)]
    lay.ln(m0, scr, mb, rb, tb_, T=TC, x_is_bf=True, out_bf=h, gain_bias=(gl, bl))
    b.step()
    for k in range(DC):
        b.dma(DMA_ENG[k % 3], m0_o[k * 128:(k + 1) * 128, :], m0[k])

    def loadw(dram):
        for k in range(DC):
            b.step()
            b.dma("sync", w32[:, 0:D], dram[k * 128:(k + 1) * 128, :])
            b.step()
            b.on("vector", lambda e, k=k: e.tensor_copy(WR[:, k * D:(k + 1) * D], w32[:, 0:D]))
        return WR[:, 0:8 * D]

    wbf = loadw(wq)
    lay.proj_f(wbf, h, TC, D, qT_o, scale=1.0 / 8.0)
    wbf = loadw(wk)
    lay.proj_f(wbf, h, TC, D, kT_o)
    wbf = loadw(wv)
    lay.proj_t(wbf, h, TC, D, v_o)
    b.step()
    return b


def build_l3f_2():
    b = BSP()
    qT_i = b.din("qT", [D, TC], BF16)
    kTf = b.din("kTf", [D, TSEQ], BF16)
    vf = b.din("vf", [TSEQ, D], BF16)
    m0T = b.din("m0T", [D, TC], BF16)
    wo = b.din("wo", [D, D], F32)
    xi_o = b.dout("xintT", [D, TC], BF16)

    lay = Lay(b, f32_cols=5200, bf_cols=78900, n_ps=6, pt_cols=1024, n_stg=6)
    fa, ba, ps, pt = lay.fa, lay.ba, lay.ps, lay.pt

    mb = fa.alloc(TC)
    rec2 = fa.alloc(2)
    w32 = mb

    m0blk = [ba.alloc(512) for _ in range(DC)]
    qT_sb = ba.alloc8(TC)
    kpair = ba.alloc(TSEQ)
    vpair = [ba.alloc(128) for _ in range(64)]
    probs = [ba.alloc(512) for _ in range(2)]
    yT = ba.alloc8(TC)
    WR = ba.alloc(8 * D)
    mk = ba.mark()
    ytok = [ba.alloc(D) for _ in range(16)]

    b.step()
    for k in range(DC):
        b.dma("gpsimd", qT_sb[k], qT_i[k * 128:(k + 1) * 128, :])
    for k in range(DC):
        b.step()
        b.dma("sync", w32[:, 0:D], wo[k * 128:(k + 1) * 128, :])
        b.step()
        b.on("vector", lambda e, k=k: e.tensor_copy(WR[:, k * D:(k + 1) * D], w32[:, 0:D]))

    for pair in range(8):
        b.step()
        b.dma("sync", kpair[:, :], kTf[pair * 128:(pair + 1) * 128, :])
        for ch in range(64):
            b.dma(("gpsimd", "scalar")[ch % 2], vpair[ch],
                  vf[ch * 128:(ch + 1) * 128, pair * 128:(pair + 1) * 128])
        for j in range(16):
            qslA = qT_sb[pair][0:64, j * 128:(j + 1) * 128]
            qslB = qT_sb[pair][64:128, j * 128:(j + 1) * 128]
            for g in range(16):
                par = g % 2
                b.step()
                def mms(e, g=g, par=par, qslA=qslA, qslB=qslB):
                    r = None
                    for h, qsl in ((0, qslA), (1, qslB)):
                        bank = ps[2 + 2 * h + par]
                        for c in range(4):
                            kcol = (g * 4 + c) * 128
                            r = e.matmul(bank[:, c * 128:(c + 1) * 128],
                                         kpair[h * 64:(h + 1) * 64, kcol:kcol + 128],
                                         qsl, start=True, stop=True)
                    return r
                b.on("tensor", mms)
                b.step()
                for h in range(2):
                    b.on("scalar", lambda e, h=h, par=par: e.activation(
                        probs[h], ps[2 + 2 * h + par][:, 0:512], AF.Exp))
                b.step()
                def pv(e, g=g):
                    r = None
                    first = (g == 0)
                    last = (g == 15)
                    for h in range(2):
                        for c in range(4):
                            ch = g * 4 + c
                            r = e.matmul(ps[0][:, h * 64:(h + 1) * 64],
                                         probs[h][:, c * 128:(c + 1) * 128],
                                         vpair[ch][:, h * 64:(h + 1) * 64],
                                         start=(first and c == 0 and h == 0),
                                         stop=(last and c == 3))
                    for h in range(2):
                        for c in range(4):
                            r = e.matmul(ps[1][:, h:h + 1],
                                         probs[h][:, c * 128:(c + 1) * 128],
                                         lay.ones_bf[:, 0:1],
                                         start=(first and c == 0 and h == 0),
                                         stop=(last and c == 3))
                    return r
                b.on("tensor", pv)
            b.step()
            b.on("vector", lambda e: e.tensor_copy(rec2[:, 0:2], ps[1][:, 0:2]))
            b.step()
            b.on("vector", lambda e: e.reciprocal(rec2[:, 0:2], rec2[:, 0:2]))
            b.step()
            for h in range(2):
                b.on("vector", lambda e, h=h, j=j, pair=pair: e.tensor_scalar(
                    ytok[j][:, pair * 128 + h * 64: pair * 128 + (h + 1) * 64],
                    ps[0][:, h * 64:(h + 1) * 64], rec2[:, h:h + 1], None, ALU.mult))
            b.step()

    for j in range(16):
        b.step()
        def ytr(e, j=j):
            r = None
            for c in range(DC):
                r = e.transpose(pt[:, c * 128:(c + 1) * 128],
                                ytok[j][:, c * 128:(c + 1) * 128], lay.ident_bf[:])
            return r
        b.on("tensor", ytr)
        b.step()
        for c in range(DC):
            b.on("vector", lambda e, c=c, j=j: e.tensor_copy(
                yT[c][:, j * 128:(j + 1) * 128], pt[:, c * 128:(c + 1) * 128]))
    ba.reset(mk)
    xint8 = ba.alloc8(TC)
    lay.proj_f(WR[:, 0:8 * D], yT, TC, D, xint8)
    for blk in range(4):
        b.step()
        for k in range(DC):
            b.dma(DMA_ENG[k % 3], m0blk[k][:, 0:512],
                  m0T[k * 128:(k + 1) * 128, blk * 512:(blk + 1) * 512])
        b.step()
        for k in range(DC):
            b.on("vector", lambda e, k=k, blk=blk: e.tensor_add(
                xint8[k][:, blk * 512:(blk + 1) * 512],
                xint8[k][:, blk * 512:(blk + 1) * 512], m0blk[k][:, 0:512]))
        b.step()
        for k in range(DC):
            b.dma(DMA_ENG[k % 3], xi_o[k * 128:(k + 1) * 128, blk * 512:(blk + 1) * 512],
                  xint8[k][:, blk * 512:(blk + 1) * 512])
    b.step()
    return b


def build_l3f_3(debug=False):
    b = BSP()
    xT = b.din("xT", [D, TC], F32)
    xintT = b.din("xintT", [D, TC], BF16)
    g2 = b.din("g2", [D, 1], F32)
    b2c = b.din("b2c", [D, 1], F32)
    wg = b.din("wg", [D, 8], F32)
    w1f = b.din("w1f", [8 * D, 4096], F32)
    b1f = b.din("b1f", [8 * 4096, 1], F32)
    w2f = b.din("w2f", [8 * 4096, D], F32)
    b2f = b.din("b2f", [8 * D, 1], F32)
    xo = b.dout("xoT", [D, TC], F32)

    lay = Lay(b, f32_cols=8600, bf_cols=82500, n_ps=6, pt_cols=1024, n_stg=6)
    fa, ba, ps, pt = lay.fa, lay.ba, lay.ps, lay.pt

    mb = fa.alloc(TC); rb = fa.alloc(TC); tb_ = fa.alloc(TC)
    gb = fa.alloc(2 * DC)
    rwbc = fa.alloc(TC)
    b1s = fa.alloc(16 * 2)
    b2s = fa.alloc(DC)
    ltokf = [fa.alloc(8) for _ in range(16)]
    rwtok = [fa.alloc(8) for _ in range(16)]
    m1 = fa.alloc(1); nm1 = fa.alloc(1); m2 = fa.alloc(1); den = fa.alloc(1)
    msk = fa.alloc(8)
    xblk = [tb_[:, i * 512:(i + 1) * 512] for i in range(4)] + \
           [rwbc[:, i * 512:(i + 1) * 512] for i in range(4)]
    rrow = tb_[0:1, 0:TC]
    w32 = mb

    m_bf = ba.alloc8(TC)
    acc = ba.alloc8(TC)
    WR = ba.alloc(2 * 16 * D)
    scr = ba.alloc8(TC)
    h1h = [scr[i // 4][:, (i % 4) * 512:(i % 4 + 1) * 512] for i in range(16)]
    yeac = [scr[4 + i // 4][:, (i % 4) * 512:(i % 4 + 1) * 512] for i in range(DC)]
    stg2 = [scr[6][:, i * 512:(i + 1) * 512] for i in range(3)]
    rrow_bf = scr[7][0:1, 0:512]
    lg = rb

    b.step()
    xibf = acc
    for k in range(DC):
        b.dma("gpsimd", xibf[k], xintT[k * 128:(k + 1) * 128, :])
        b.dma("sync", gb[0:128, k:k + 1], g2[k * 128:(k + 1) * 128, :])
        b.dma("sync", gb[0:128, DC + k:DC + k + 1], b2c[k * 128:(k + 1) * 128, :])
    gl = [gb[0:128, k:k + 1] for k in range(DC)]
    bl = [gb[0:128, DC + k:DC + k + 1] for k in range(DC)]
    lay.ln(xibf, scr, mb, rb, tb_, T=TC, x_is_bf=True, out_bf=m_bf,
           gain_bias=(gl, bl))

    for k in range(DC):
        b.step()
        b.dma("sync", w32[:, 0:8], wg[k * 128:(k + 1) * 128, :])
        b.step()
        b.on("vector", lambda e, k=k: e.tensor_copy(rW_bf[:, k * 8:(k + 1) * 8], w32[:, 0:8])) \
            if False else None
        b.on("vector", lambda e, k=k: e.tensor_copy(stg2[1][:, k * 8:(k + 1) * 8], w32[:, 0:8]))
    b.step()
    def mm_router(e):
        r = None
        for blk in range(4):
            for k in range(DC):
                r = e.matmul(ps[blk][0:8, 0:512], stg2[1][:, k * 8:(k + 1) * 8],
                             m_bf[k][:, blk * 512:(blk + 1) * 512],
                             start=(k == 0), stop=(k == DC - 1))
        return r
    b.on("tensor", mm_router)
    b.step()
    for blk in range(4):
        b.on("vector", lambda e, blk=blk: e.tensor_copy(
            lg[0:8, blk * 512:(blk + 1) * 512], ps[blk][0:8, 0:512]))
    for tt in range(16):
        b.step()
        b.on("tensor", lambda e, tt=tt: e.transpose(
            ps[4][:, 0:8], lg[0:8, tt * 128:(tt + 1) * 128], lay.ident_f32[0:8, 0:8]))
        b.step()
        b.on("vector", lambda e, tt=tt: e.tensor_copy(ltokf[tt][0:128, 0:8], ps[4][:, 0:8]))
    BIGNEG = -1.0e30
    for tt in range(16):
        lt = ltokf[tt][0:128, 0:8]
        rw_ = rwtok[tt][0:128, 0:8]
        b.step()
        b.on("vector", lambda e, lt=lt: e.reduce_max(m1[:, 0:1], lt, AX.X))
        b.step()
        b.on("vector", lambda e: e.tensor_scalar(nm1[:, 0:1], m1[:, 0:1], -1.0, None, ALU.mult))
        b.on("vector", lambda e, lt=lt: e.tensor_scalar(msk[:, 0:8], lt, m1[:, 0:1], None, ALU.is_ge))
        b.step()
        b.on("vector", lambda e, lt=lt: e.scalar_tensor_tensor(
            msk[:, 0:8], msk[:, 0:8], BIGNEG, lt, ALU.mult, ALU.add))
        b.step()
        b.on("vector", lambda e: e.reduce_max(m2[:, 0:1], msk[:, 0:8], AX.X))
        b.step()
        b.on("vector", lambda e, lt=lt: e.tensor_scalar(msk[:, 0:8], lt, m2[:, 0:1], None, ALU.is_ge))
        b.step()
        b.on("scalar", lambda e: e.activation(den[:, 0:1], m2[:, 0:1], AF.Exp, bias=nm1[:, 0:1]))
        b.step()
        b.on("vector", lambda e: e.tensor_scalar(den[:, 0:1], den[:, 0:1], 1.0, None, ALU.add))
        b.step()
        b.on("vector", lambda e: e.reciprocal(den[:, 0:1], den[:, 0:1]))
        b.step()
        b.on("scalar", lambda e, lt=lt, rw_=rw_: e.activation(rw_, lt, AF.Exp, bias=nm1[:, 0:1]))
        b.step()
        b.on("vector", lambda e, rw_=rw_: e.tensor_mul(rw_, rw_, msk[:, 0:8]))
        b.step()
        b.on("vector", lambda e, rw_=rw_: e.tensor_scalar(rw_, rw_, den[:, 0:1], None, ALU.mult))
    for tt in range(16):
        b.step()
        b.on("tensor", lambda e, tt=tt: e.transpose(
            ps[4][0:8, 0:128], rwtok[tt][0:128, 0:8], lay.ident_f32[:, :]))
        b.step()
        b.on("vector", lambda e, tt=tt: e.tensor_copy(
            lg[0:8, tt * 128:(tt + 1) * 128], ps[4][0:8, 0:128]))

    if debug:
        d_m = b.dout("d_m", [D, TC], BF16)
        d_lg = b.dout("d_lg", [8, TC], F32)
        b.step()
        for k in range(DC):
            b.dma(DMA_ENG[k % 3], d_m[k * 128:(k + 1) * 128, :], m_bf[k])
        b.dma("sync", d_lg[:, :], lg[0:8, 0:TC])
    b.step()
    for k in range(DC):
        b.on("vector", lambda e, k=k: e.memset(acc[k], 0.0))
    W1h = WR[:, 0:16 * D]
    W2h = WR[:, 16 * D:32 * D]
    for ex in range(8):
        b.step()
        b.dma("gpsimd", rrow[0:1, 0:TC], lg[ex:ex + 1, 0:TC])
        for k in range(DC):
            b.dma("sync", b2s[0:128, k:k + 1], b2f[ex * D + k * 128: ex * D + (k + 1) * 128, :])
        b.step()
        b.on("vector", lambda e: e.tensor_copy(rrow_bf[0:1, 0:512], rrow[0:1, 0:512]))
        b.on("vector", lambda e: e.tensor_copy(h1h[1][0:1, 0:512], rrow[0:1, 512:1024]))
        b.on("vector", lambda e: e.tensor_copy(h1h[2][0:1, 0:512], rrow[0:1, 1024:1536]))
        b.on("vector", lambda e: e.tensor_copy(h1h[3][0:1, 0:512], rrow[0:1, 1536:2048]))
        b.step()
        def mm_gate(e):
            r = None
            srcs = [rrow_bf, h1h[1][0:1, 0:512], h1h[2][0:1, 0:512], h1h[3][0:1, 0:512]]
            for blk in range(4):
                r = e.matmul(ps[blk][:, 0:512], lay.ones_bf[0:1, 0:128],
                             srcs[blk][0:1, 0:512], start=True, stop=True)
            return r
        b.on("tensor", mm_gate)
        b.step()
        for blk in range(4):
            b.on("vector", lambda e, blk=blk: e.tensor_copy(
                rwbc[:, blk * 512:(blk + 1) * 512], ps[blk][:, 0:512]))
        for half in range(2):
            for k in range(DC):
                b.step()
                b.dma("sync", w32[:, 0:2048], w1f[ex * D + k * 128: ex * D + (k + 1) * 128,
                                                  half * 2048:(half + 1) * 2048])
                b.step()
                b.on("vector", lambda e, k=k: e.tensor_copy(
                    W1h[:, k * 2048:(k + 1) * 2048], w32[:, 0:2048]))
            for k in range(16):
                b.step()
                b.dma("sync", w32[:, 0:D], w2f[ex * 4096 + half * 2048 + k * 128:
                                               ex * 4096 + half * 2048 + (k + 1) * 128, :])
                b.step()
                b.on("vector", lambda e, k=k: e.tensor_copy(
                    W2h[:, k * D:(k + 1) * D], w32[:, 0:D]))
            b.step()
            for hc in range(16):
                b.dma(DMA_ENG[hc % 3], b1s[0:128, hc:hc + 1],
                      b1f[ex * 4096 + half * 2048 + hc * 128:
                          ex * 4096 + half * 2048 + (hc + 1) * 128, :])
            b1l = [b1s[0:128, hc:hc + 1] for hc in range(16)]
            for tb in range(4):
                mslice = [m_bf[k][:, tb * 512:(tb + 1) * 512] for k in range(DC)]
                lay.proj_f(W1h, mslice, 512, 2048, h1h, w_dout=2048,
                           act=AF.Gelu, bias=b1l)
                if debug and ex == 0 and half == 0 and tb == 0:
                    d_h1 = b.dout("d_h1", [16 * 128, 512], BF16)
                    d_rw = b.dout("d_rw", [128, TC], F32)
                    b.step()
                    for hc in range(16):
                        b.dma(DMA_ENG[hc % 3], d_h1[hc * 128:(hc + 1) * 128, :], h1h[hc])
                    b.dma("sync", d_rw[:, :], rwbc[:, 0:TC])
                def yst(oc, o, n, stgap, tb=tb):
                    def fn(e, oc=oc, o=o, n=n, stgap=stgap):
                        e.tensor_mul(stg2[0][:, 0:n], stgap,
                                     rwbc[:, tb * 512 + o: tb * 512 + o + n])
                        return e.tensor_add(acc[oc][:, tb * 512 + o: tb * 512 + o + n],
                                            acc[oc][:, tb * 512 + o: tb * 512 + o + n],
                                            stg2[0][:, 0:n])
                    b.on("vector", fn)
                lay.proj_f(W2h, h1h, 512, D, None, w_dout=D, din_c=16, store_fn=yst)
        b.step()
        for k in range(DC):
            b.on("vector", lambda e, k=k: e.scalar_tensor_tensor(
                acc[k], rwbc[:, 0:TC], b2s[0:128, k:k + 1], acc[k], ALU.mult, ALU.add))

    if debug:
        d_acc = b.dout("d_acc", [D, TC], BF16)
        b.step()
        for k in range(DC):
            b.dma(DMA_ENG[k % 3], d_acc[k * 128:(k + 1) * 128, :], acc[k])
    for blk in range(4):
        b.step()
        for k in range(DC):
            b.dma("gpsimd", xblk[k][:, 0:512],
                  xintT[k * 128:(k + 1) * 128, blk * 512:(blk + 1) * 512])
        b.step()
        for k in range(DC):
            b.on("vector", lambda e, k=k, blk=blk: e.tensor_add(
                acc[k][:, blk * 512:(blk + 1) * 512],
                acc[k][:, blk * 512:(blk + 1) * 512], xblk[k][:, 0:512]))
    lay.ln_resid_stream(acc, scr, mb, rb, tb_, T=TC, x_in=xT, x_out=xo,
                        xblk=xblk, x_is_bf=True)
    b.step()
    return b


def _rope_tables(pos0):
    j = np.arange(32)
    theta = 10000.0 ** (-2 * j / 64.0)
    pos = pos0 + np.arange(TC)
    f = pos[:, None] * theta[None, :]
    return (np.ascontiguousarray(np.tile(np.cos(f), (1, 16)).astype(np.float32)),
            np.ascontiguousarray(np.tile(np.sin(f), (1, 16)).astype(np.float32)))


def kernel(**I):
    C = np.ascontiguousarray
    f32 = np.float32
    const = Lay.const_inputs()
    col = lambda a: C(np.asarray(a).reshape(-1, 1).astype(f32))
    row = lambda a: C(np.asarray(a).reshape(1, -1).astype(f32))
    mat = lambda a: C(np.asarray(a).astype(f32))
    matT = lambda a: C(np.asarray(a).T.astype(f32))

    x = np.asarray(I["L0_attn_x"], dtype=f32)
    xts = xT_chunks(x)

    def halos(xts_list, n):
        out = []
        for c in range(NCORE):
            bb, cc = divmod(c, 4)
            if cc == 0:
                out.append(np.zeros((D, n), f32))
            else:
                out.append(C(xts_list[c - 1][:, TC - n:TC]))
        return out

    wqkv = mat(I["L0_attn_weight_qkv"])
    im = [{"xT": xts[c], "wqkv": wqkv, **const} for c in range(NCORE)]
    r = run(build_l0_proj(), im)
    qT = [np.asarray(r.results[c]["qT"]) for c in range(NCORE)]
    kT = [np.asarray(r.results[c]["kT"]) for c in range(NCORE)]
    v = [np.asarray(r.results[c]["v"]) for c in range(NCORE)]
    kn2 = [np.asarray(r.results[c]["kn2loc"]) for c in range(NCORE)]
    kTf = [np.concatenate(kT[bb * 4:(bb + 1) * 4], axis=1) for bb in range(2)]
    vf = [np.concatenate(v[bb * 4:(bb + 1) * 4], axis=0) for bb in range(2)]
    kn24 = [C(np.concatenate(kn2[bb * 4:(bb + 1) * 4], axis=1)) for bb in range(2)]
    wo_t = matT(I["L0_attn_out_w"])
    outb = col(I["L0_attn_out_b"])
    im = [{"qT": qT[c], "kTf": kTf[c // 4], "vf": vf[c // 4], "xT": xts[c],
           "wo_t": wo_t, "out_b": outb, "kn24": kn24[c // 4], **const}
          for c in range(NCORE)]
    r = run(build_l0_attend(), im)
    xts = [C(np.asarray(r.results[c]["x1T"])) for c in range(NCORE)]

    w = {"dwk": C(I["L0_ffn_dw_kernel"][:, 0, :].astype(f32)),
         "dwb": col(I["L0_ffn_dw_bias"]),
         "pw_t": matT(I["L0_ffn_pw_w"]), "pwb": col(I["L0_ffn_pw_b"]),
         "rw_t": matT(I["L0_ffn_router_w"]), "rb_in": col(I["L0_ffn_router_b"]),
         "ew_t": C(np.asarray(I["L0_ffn_expert_w"]).transpose(0, 2, 1)
                   .reshape(8 * D, D).astype(f32)),
         "eb_t": matT(I["L0_ffn_expert_b"]),
         "wout_t": matT(I["L0_ffn_out_w"]), "woutb": col(I["L0_ffn_out_b"]),
         "nw_in": col(I["L0_ffn_norm_w"]),
         "ffd_t": matT(I["L0_ffn_ff_down_w"]), "ffdb": col(I["L0_ffn_ff_down_b"]),
         "ffu_t": matT(I["L0_ffn_ff_up_w"]), "ffub": col(I["L0_ffn_ff_up_b"])}
    hl = halos(xts, 2)
    im = [{"xT": xts[c], "halo": hl[c], **w, **const} for c in range(NCORE)]
    r = run(build_l0_ffn(), im)
    xts = [C(np.asarray(r.results[c]["xoT"])) for c in range(NCORE)]

    wA = {"wk_t": matT(I["L1_attn_Wk"]), "wv_t": matT(I["L1_attn_Wv"]),
          "wr_t": matT(I["L1_attn_Wr"]),
          "tmk": col(I["L1_attn_time_mix_k"]), "tmv": col(I["L1_attn_time_mix_v"]),
          "tmr": col(I["L1_attn_time_mix_r"]), "w_in": col(I["L1_attn_w"])}
    hl = halos(xts, 1)
    im = [{"xT": xts[c], "halo": hl[c], **wA, **const} for c in range(NCORE)]
    r = run(build_l1a_A(), im)
    kTT = [np.asarray(r.results[c]["kTT"]) for c in range(NCORE)]
    vTT = [np.asarray(r.results[c]["vTT"]) for c in range(NCORE)]
    rTT = [np.asarray(r.results[c]["rTT"]) for c in range(NCORE)]
    kmx = [np.asarray(r.results[c]["kmaxl"]) for c in range(NCORE)]
    tls = [np.concatenate([np.asarray(r.results[c]["tailsA"]),
                           np.asarray(r.results[c]["tailsB"])], axis=1)
           for c in range(NCORE)]
    wo1 = matT(I["L1_attn_Wo"])
    u_in = col(I["L1_attn_u"])
    im = []
    for c in range(NCORE):
        bb, cc = divmod(c, 4)
        kmax4 = C(np.concatenate([kmx[bb * 4 + j] for j in range(4)], axis=1))
        tails8 = C(np.concatenate([tls[bb * 4 + j][:, 0:1] for j in range(4)]
                                  + [tls[bb * 4 + j][:, 1:2] for j in range(4)], axis=1))
        tjs = np.zeros((128, 4), f32)
        msk = np.zeros((128, 4), f32)
        for j in range(4):
            if j < cc:
                tjs[:, j] = -2048.0 * (cc - 1 - j)
                msk[:, j] = 1.0
        im.append({"xT": xts[c], "kTT": kTT[c], "vTT": vTT[c], "rTT": rTT[c],
                   "kmax4": kmax4, "tails8": tails8, "tjs": tjs, "msk4": msk,
                   "w_in": wA["w_in"], "u_in": u_in, "wo_t": wo1, **const})
    r = run(build_l1a_C(), im)
    xts = [C(np.asarray(r.results[c]["xoT"])) for c in range(NCORE)]

    w = {"wr_t": matT(I["L1_ffn_W_r"]), "wk_t": matT(I["L1_ffn_W_k"]),
         "wv_t": matT(I["L1_ffn_W_v"]),
         "mur": col(I["L1_ffn_mu_r"]), "muk": col(I["L1_ffn_mu_k"])}
    hl = halos(xts, 1)
    im = [{"xT": xts[c], "halo": hl[c], **w, **const} for c in range(NCORE)]
    r = run(build_chanmix(), im)
    xts = [C(np.asarray(r.results[c]["xoT"])) for c in range(NCORE)]

    segones = np.zeros((128, 128), f32)
    for s in range(2):
        segones[s * 64:(s + 1) * 64, s * 64:(s + 1) * 64] = 1.0
    wA = {"wq_t": matT(I["L2_attn_Wq"]), "wk_t": matT(I["L2_attn_Wk"]),
          "wv_t": matT(I["L2_attn_Wv"]),
          "bq_r": row(I["L2_attn_bq"]), "bk_r": row(I["L2_attn_bk"]),
          "bv_r": row(I["L2_attn_bv"]), "sego": segones}
    im = []
    for c in range(NCORE):
        bb, cc = divmod(c, 4)
        cs, sn = _rope_tables(cc * TC)
        im.append({"xT": xts[c], "cs16": cs, "sn16": sn, **wA, **const})
    r = run(build_l2a_A(), im)
    qT = [np.asarray(r.results[c]["qT"]) for c in range(NCORE)]
    kT = [np.asarray(r.results[c]["kT"]) for c in range(NCORE)]
    v = [np.asarray(r.results[c]["v"]) for c in range(NCORE)]
    kTf = [np.concatenate(kT[bb * 4:(bb + 1) * 4], axis=1) for bb in range(2)]
    vf = [np.concatenate(v[bb * 4:(bb + 1) * 4], axis=0) for bb in range(2)]
    wo2 = matT(I["L2_attn_Wo"])
    bo2 = col(I["L2_attn_bo"])
    im = []
    for c in range(NCORE):
        bb, cc = divmod(c, 4)
        cols = []
        for j in range(16):
            g = 4 * j + cc
            src = bb * 4 + g // 16
            lt = g % 16
            cols.append(qT[src][:, lt * 128:(lt + 1) * 128])
        qTi = C(np.concatenate(cols, axis=1))
        mtd = np.zeros((128, 16 * 512), f32)
        for j in range(16):
            i = 4 * j + cc
            for cch in range(4):
                key = 512 * j + cch * 128 + np.arange(128)[:, None]
                qq = i * 128 + np.arange(128)[None, :]
                mtd[:, j * 512 + cch * 128: j * 512 + (cch + 1) * 128] = \
                    np.where(key <= qq, 0.0, -1e9)
        im.append({"qTi": qTi, "kTf": kTf[bb], "vf": vf[bb], "mtd": mtd,
                   "wo_t": wo2, "bo_c": bo2, **const})
    r = run(build_l2a_B(), im)
    at = [np.asarray(r.results[c]["attnTi"]) for c in range(NCORE)]
    im = []
    for c in range(NCORE):
        bb, cc = divmod(c, 4)
        cols = []
        for lt in range(16):
            g = cc * 16 + lt
            src = bb * 4 + g % 4
            j = g // 4
            cols.append(at[src][:, j * 128:(j + 1) * 128])
        attnT = C(np.concatenate(cols, axis=1))
        im.append({"xT": xts[c], "mixT": attnT, **const})
    r = run(build_sandwich(), im)
    xts = [C(np.asarray(r.results[c]["xoT"])) for c in range(NCORE)]

    w = {"wr_t": matT(I["L2_ffn_W_r"]), "wk_t": matT(I["L2_ffn_W_k"]),
         "wv_t": matT(I["L2_ffn_W_v"]),
         "mur": col(I["L2_ffn_mu_r"]), "muk": col(I["L2_ffn_mu_k"])}
    hl = halos(xts, 1)
    im = [{"xT": xts[c], "halo": hl[c], **w, **const} for c in range(NCORE)]
    r = run(build_chanmix(), im)
    xts = [C(np.asarray(r.results[c]["xoT"])) for c in range(NCORE)]

    wA = {"wq": mat(I["L3_attn_Wq"]), "wv": mat(I["L3_attn_Wv"]),
          "bq_r": row(I["L3_attn_bq"]), "bv_r": row(I["L3_attn_bv"])}
    im = [{"xT": xts[c], **wA, **const} for c in range(NCORE)]
    r = run(build_l3a_A(), im)
    qsm = [np.asarray(r.results[c]["qsmT"]) for c in range(NCORE)]
    vtk = [np.asarray(r.results[c]["vtok"]) for c in range(NCORE)]
    ml = [np.asarray(r.results[c]["mloc"]) for c in range(NCORE)]
    im = []
    for c in range(NCORE):
        bb, cc = divmod(c, 4)
        ml4 = C(np.concatenate([ml[bb * 4 + j] for j in range(4)], axis=1))
        im.append({"qsmT": qsm[c], "vtok": vtk[c], "mloc4": ml4, **const})
    r = run(build_l3a_B(), im)
    sp = [np.asarray(r.results[c]["spart"]) for c in range(NCORE)]
    cxp = [np.asarray(r.results[c]["ctxp"]) for c in range(NCORE)]
    im = []
    for c in range(NCORE):
        bb, cc = divmod(c, 4)
        ml4 = C(np.concatenate([ml[bb * 4 + j] for j in range(4)], axis=1))
        s4 = C(np.concatenate([sp[bb * 4 + j] for j in range(4)], axis=1))
        cx4 = C(np.concatenate([cxp[bb * 4 + j] for j in range(4)], axis=1))
        im.append({"xT": xts[c], "qsmT": qsm[c], "ctxp4": cx4, "spart4": s4,
                   "mloc4": ml4, "wo": mat(I["L3_attn_Wo"]),
                   "bo_c": col(I["L3_attn_bo"]), **const})
    r = run(build_l3a_C(), im)
    xts = [C(np.asarray(r.results[c]["xoT"])) for c in range(NCORE)]

    w1 = {"g1": col(I["L3_ffn_ln1_g"]), "b1c": col(I["L3_ffn_ln1_b"]),
          "wq": mat(I["L3_ffn_Wq"]), "wk": mat(I["L3_ffn_Wk"]),
          "wv": mat(I["L3_ffn_Wv"])}
    im = [{"xT": xts[c], **w1, **const} for c in range(NCORE)]
    r = run(build_l3f_1(), im)
    qT = [np.asarray(r.results[c]["qT"]) for c in range(NCORE)]
    kT = [np.asarray(r.results[c]["kT"]) for c in range(NCORE)]
    v = [np.asarray(r.results[c]["v"]) for c in range(NCORE)]
    m0 = [np.asarray(r.results[c]["m0T"]) for c in range(NCORE)]
    kTf = [np.concatenate(kT[bb * 4:(bb + 1) * 4], axis=1) for bb in range(2)]
    vf = [np.concatenate(v[bb * 4:(bb + 1) * 4], axis=0) for bb in range(2)]
    im = [{"qT": qT[c], "kTf": kTf[c // 4], "vf": vf[c // 4], "m0T": m0[c],
           "wo": mat(I["L3_ffn_Wo"]), **const} for c in range(NCORE)]
    r = run(build_l3f_2(), im)
    xint = [np.asarray(r.results[c]["xintT"]) for c in range(NCORE)]
    im = [{"xT": xts[c], "xintT": xint[c],
           "g2": col(I["L3_ffn_ln2_g"]), "b2c": col(I["L3_ffn_ln2_b"]),
           "wg": mat(I["L3_ffn_Wg"]),
           "w1f": C(np.asarray(I["L3_ffn_W1"]).reshape(8 * D, 4096).astype(f32)),
           "b1f": col(I["L3_ffn_b1"]),
           "w2f": C(np.asarray(I["L3_ffn_W2"]).reshape(8 * 4096, D).astype(f32)),
           "b2f": col(I["L3_ffn_b2"]), **const} for c in range(NCORE)]
    r = run(build_l3f_3(), im)

    out = np.empty((NB, TSEQ, D), dtype=f32)
    for c in range(NCORE):
        bb, cc = divmod(c, 4)
        out[bb, cc * TC:(cc + 1) * TC, :] = np.asarray(r.results[c]["xoT"]).T
    return out


# version 02
# speedup vs baseline: 1.1001x; 1.1001x over previous
import contextlib

import numpy as np
import ml_dtypes

import concourse.bass as bass
import concourse.mybir as mybir
from concourse.bass_utils import run_bass_kernel_spmd

F32 = mybir.dt.float32
BF16 = mybir.dt.bfloat16
AF = mybir.ActivationFunctionType
ALU = mybir.AluOpType
AX = mybir.AxisListType
BF = ml_dtypes.bfloat16

D = 1024
DC = 8
TSEQ = 8192
NB = 2
TC = 2048
NCORE = 8
ENGINES = ("sync", "tensor", "vector", "scalar", "gpsimd")
DMA_ENG = ("sync", "gpsimd", "scalar")



class BSP:

    def __init__(self):
        self.nc = bass.Bass("TRN2", target_bir_lowering=False, debug=False)
        self._es = contextlib.ExitStack()
        self.steps = []
        self._cur = None
        self.dram = {}

    def sbuf(self, name, shape, dtype):
        return self._es.enter_context(self.nc.sbuf_tensor(name, list(shape), dtype))

    def psum(self, name, shape, dtype=F32):
        return self._es.enter_context(self.nc.psum_tensor(name, list(shape), dtype))

    def din(self, name, shape, dtype=F32):
        ap = self.nc.dram_tensor(name, list(shape), dtype, kind="ExternalInput").ap()
        self.dram[name] = ap
        return ap

    def dout(self, name, shape, dtype=F32):
        ap = self.nc.dram_tensor(name, list(shape), dtype, kind="ExternalOutput").ap()
        self.dram[name] = ap
        return ap

    def step(self):
        self._cur = {e: [] for e in ENGINES}
        self.steps.append(self._cur)

    def on(self, eng, fn):
        assert self._cur is not None
        self._cur[eng].append(("op", fn))

    def dma(self, eng, out, in_):
        assert self._cur is not None
        self._cur[eng].append(("dma", (out, in_)))

    def finalize(self):
        nc = self.nc
        with self._es:
            bar = self._es.enter_context(nc.semaphore("bar"))
            dsems = {e: self._es.enter_context(nc.semaphore(f"ds_{e}")) for e in ENGINES}
            dcnt = {e: 0 for e in ENGINES}
            with nc.Block() as block:
                def run_engine(e, eng):
                    for k, st in enumerate(self.steps):
                        if k > 0:
                            e.wait_ge(bar, len(ENGINES) * k)
                        last = None
                        had_dma = False
                        for kind, payload in st[eng]:
                            if kind == "op":
                                r = payload(e)
                                if r is not None:
                                    last = r
                            else:
                                out, in_ = payload
                                e.dma_start(out=out, in_=in_).then_inc(dsems[eng], 16)
                                dcnt[eng] += 16
                                had_dma = True
                        if had_dma:
                            e.wait_ge(dsems[eng], dcnt[eng])
                            e.nop().then_inc(bar, 1)
                        elif last is not None:
                            last.then_inc(bar, 1)
                        else:
                            e.nop().then_inc(bar, 1)
                for eng in ENGINES:
                    getattr(block, eng)(lambda e, eng=eng: run_engine(e, eng))
        return nc


class Arena:

    def __init__(self, b, name, cols, dtype):
        self.t = b.sbuf(name, [128, cols], dtype)
        self.cols = cols
        self.off = 0
        self.name = name
        self.hi = 0

    def alloc(self, n, parts=128):
        assert self.off + n <= self.cols, \
            f"{self.name} arena overflow: {self.off}+{n}>{self.cols}"
        ap = self.t[0:parts, self.off:self.off + n]
        self.off += n
        self.hi = max(self.hi, self.off)
        return ap

    def alloc8(self, n):
        return [self.alloc(n) for _ in range(DC)]

    def mark(self):
        return self.off

    def reset(self, mark):
        self.off = mark




class Lay:
    def __init__(self, b: BSP, f32_cols, bf_cols, n_ps=6, pt_cols=512, n_stg=6,
                 pq_cols=0):
        self.b = b
        self.fa = Arena(b, "af32", f32_cols, F32)
        self.ba = Arena(b, "abf", bf_cols, BF16)
        self.psL = [b.psum(f"psL{i}", [128, 1024], F32) for i in range((n_ps + 1) // 2)]
        self.ps = [self.psL[i // 2][:, (i % 2) * 512:(i % 2 + 1) * 512]
                   for i in range(n_ps)]
        self.pt = b.psum("pst", [128, pt_cols], BF16) if pt_cols else None
        self.pq = b.psum("psq", [128, pq_cols], F32) if pq_cols else None
        self.ones_bf = b.sbuf("ones_bf", [128, 128], BF16)
        self.ident_bf = b.sbuf("ident_bf", [128, 128], BF16)
        self.stg = [b.sbuf(f"stg{i}", [128, 512], BF16)[:, :] for i in range(max(n_stg, 1))]
        self._stg_i = 0
        self.cb = b.sbuf("constb", [128, 16], F32)
        self._consts = {}
        c_ident = b.din("c_ident", [128, 128], F32)
        self.ident_f32 = b.sbuf("identf", [128, 128], F32)
        b.step()
        b.dma("sync", self.ident_f32[:, :], c_ident[:, :])
        b.on("gpsimd", lambda e: e.memset(self.ones_bf[:], 1.0))
        b.step()
        b.on("vector", lambda e: e.tensor_copy(self.ident_bf[:], self.ident_f32[:, :]))

    def stg_next(self):
        s = self.stg[self._stg_i % len(self.stg)]
        self._stg_i += 1
        return s

    def stgf_next(self):
        if not hasattr(self, "stgf"):
            self.stgf = [self.b.sbuf(f"stgf{i}", [128, 512], F32)[:, :] for i in range(6)]
            self._stgf_i = 0
        s = self.stgf[self._stgf_i % len(self.stgf)]
        self._stgf_i += 1
        return s

    def const(self, v):
        v = float(v)
        if v in self._consts:
            return self._consts[v]
        i = len(self._consts)
        ap = self.cb[0:128, i:i + 1]
        self.b.on("gpsimd", lambda e: e.memset(ap, v))
        self._consts[v] = ap
        return ap

    @staticmethod
    def const_inputs():
        return {"c_ident": np.eye(128, dtype=np.float32)}

    def load_w_bf(self, dram_ap, din, dout, w32):
        b = self.b
        kc = din // 128
        wbf = self.ba.alloc(kc * dout)
        b.step()
        for k in range(kc):
            b.dma("gpsimd", wbf[:, k * dout:(k + 1) * dout],
                  dram_ap[k * 128:(k + 1) * 128, :])
        return wbf

    def wslice(self, wbf, w_dout, k, o0, on):
        return wbf[:, k * w_dout + o0: k * w_dout + o0 + on]

    def ln(self, x_tiles, scr_bf, mb, rb, tb, *, T, x_is_bf=False, eps=1e-5,
           out_bf=None, out_f32=None, resid=None, gain_bias=None):
        b, ps = self.b, self.ps
        blks = [(i, min(512, T - i)) for i in range(0, T, 512)]
        assert len(blks) <= 4
        epsap = self.const(eps)
        if not x_is_bf:
            b.step()
            for k in range(DC):
                b.on("vector" if k % 2 == 0 else "gpsimd",
                     lambda e, k=k: e.tensor_copy(scr_bf[k], x_tiles[k]))
            src = scr_bf
        else:
            src = x_tiles
        b.step()
        def mm_sums(e):
            r = None
            for bi, (o, n) in enumerate(blks):
                for k in range(DC):
                    r = e.matmul(ps[bi][:, 0:n], self.ones_bf[:],
                                 src[k][:, o:o + n],
                                 start=(k == 0), stop=(k == DC - 1))
            return r
        b.on("tensor", mm_sums)
        b.step()
        for bi, (o, n) in enumerate(blks):
            b.on("vector", lambda e, bi=bi, o=o, n=n: e.tensor_scalar(
                mb[:, o:o + n], ps[bi][:, 0:n], 1.0 / D, None, ALU.mult))
        for k in range(DC):
            b.on("scalar", lambda e, k=k, s=src[k]: e.activation(scr_bf[k], s, AF.Square))
        for wave in range(2):
            wb = blks[wave * 2:wave * 2 + 2]
            if not wb:
                break
            b.step()
            def mm_sq(e, wb=wb, wave=wave):
                r = None
                for bi, (o, n) in enumerate(wb):
                    for k in range(DC):
                        r = e.matmul(ps[4 + bi][:, 0:n], self.ones_bf[:],
                                     scr_bf[k][:, o:o + n],
                                     start=(k == 0), stop=(k == DC - 1))
                return r
            b.on("tensor", mm_sq)
            b.step()
            for bi, (o, n) in enumerate(wb):
                b.on("vector", lambda e, bi=bi, o=o, n=n: e.tensor_scalar(
                    rb[:, o:o + n], ps[4 + bi][:, 0:n], 1.0 / D, None, ALU.mult))
        b.step()
        b.on("vector", lambda e: e.tensor_mul(tb[:, 0:T], mb[:, 0:T], mb[:, 0:T]))
        b.step()
        b.on("vector", lambda e: e.tensor_sub(rb[:, 0:T], rb[:, 0:T], tb[:, 0:T]))
        b.step()
        b.on("scalar", lambda e: e.activation(rb[:, 0:T], rb[:, 0:T], AF.Ln, bias=epsap))
        b.step()
        b.on("scalar", lambda e: e.activation(rb[:, 0:T], rb[:, 0:T], AF.Exp, scale=-0.5))
        b.step()
        for k in range(DC):
            b.on("vector" if k % 2 == 0 else "gpsimd",
                 lambda e, k=k: e.tensor_sub(scr_bf[k], x_tiles[k], mb[:, 0:T]))
        b.step()
        for k in range(DC):
            eng = "vector" if k % 2 == 0 else "gpsimd"
            dst = (out_bf or out_f32 or scr_bf)[k]
            b.on(eng, lambda e, k=k, dst=dst: e.tensor_mul(dst, scr_bf[k], rb[:, 0:T]))
        if gain_bias is not None:
            gl, bl = gain_bias
            b.step()
            for k in range(DC):
                dst = (out_bf or out_f32 or scr_bf)[k]
                b.on("vector", lambda e, k=k, dst=dst: e.tensor_scalar(
                    dst, dst, gl[k], bl[k], ALU.mult, ALU.add))
        if resid is not None:
            assert out_bf is None and out_f32 is None
            b.step()
            for k in range(DC):
                b.on("vector" if k % 2 == 0 else "gpsimd",
                     lambda e, k=k: e.tensor_add(resid[k], resid[k], scr_bf[k]))

    def ln_resid_stream(self, x_tiles, scr_bf, mb, rb, tb, *, T, x_in, x_out,
                        xblk, x_is_bf=True, eps=1e-5):
        self.ln(x_tiles, scr_bf, mb, rb, tb, T=T, x_is_bf=x_is_bf, eps=eps)
        b = self.b
        blks = [(i, min(512, T - i)) for i in range(0, T, 512)]
        for (o, n) in blks:
            b.step()
            for k in range(DC):
                b.dma(DMA_ENG[k % 3], xblk[k][:, 0:n],
                      x_in[k * 128:(k + 1) * 128, o:o + n])
            b.step()
            for k in range(DC):
                b.on("vector" if k % 2 == 0 else "gpsimd",
                     lambda e, k=k, o=o, n=n: e.tensor_add(
                         xblk[k][:, 0:n], xblk[k][:, 0:n], scr_bf[k][:, o:o + n]))
            b.step()
            for k in range(DC):
                b.dma(DMA_ENG[k % 3], x_out[k * 128:(k + 1) * 128, o:o + n],
                      xblk[k][:, 0:n])

    def proj_f(self, wbf, x_bf, T, dout, dst, *, w_dout=None, o_base=0,
               din_c=DC, scale=1.0, bias=None, act=None, store_fn=None,
               out_f32=False):
        b, ps = self.b, self.ps
        w_dout = w_dout or dout
        oc_n = dout // 128
        resident = isinstance(dst, list)
        blks = [(i, min(512, T - i)) for i in range(0, T, 512)]
        jobs = [(o, n, oc) for (o, n) in blks for oc in range(oc_n)]
        groups = [jobs[i:i + 3] for i in range(0, len(jobs), 3)]
        pend = []
        def evac(bank, o, n, oc):
            if resident:
                tgt = dst[oc][:, o:o + n]
            elif out_f32:
                tgt = self.stgf_next()[:, 0:n]
            else:
                tgt = self.stg_next()[:, 0:n]
            bias_ap = bias[oc] if bias is not None else None
            if act is not None or bias_ap is not None:
                fn = act if act is not None else AF.Identity
                kw = {}
                if bias_ap is not None:
                    kw["bias"] = bias_ap
                if scale != 1.0:
                    kw["scale"] = scale
                b.on("scalar", lambda e, tgt=tgt, bank=bank, fn=fn, kw=kw, n=n:
                     e.activation(tgt, bank[:, 0:n], fn, **kw))
            elif scale != 1.0:
                b.on("vector", lambda e, tgt=tgt, bank=bank, n=n: e.tensor_scalar(
                    tgt, bank[:, 0:n], scale, None, ALU.mult))
            else:
                b.on("vector", lambda e, tgt=tgt, bank=bank, n=n: e.tensor_copy(
                    tgt, bank[:, 0:n]))
            if resident:
                return None
            if store_fn is not None:
                return lambda: store_fn(oc, o, n, tgt)
            dram_slice = dst[oc * 128:(oc + 1) * 128, o:o + n]
            eng = DMA_ENG[oc % 2]
            return lambda: b.dma(eng, dram_slice, tgt)
        for gi, g in enumerate(groups):
            banks = [ps[3 * (gi % 2) + i] for i in range(len(g))]
            b.step()
            def mms(e, g=g, banks=banks):
                r = None
                for i, (o, n, oc) in enumerate(g):
                    for k in range(din_c):
                        r = e.matmul(banks[i][:, 0:n],
                                     self.wslice(wbf, w_dout, k, o_base + oc * 128, 128),
                                     x_bf[k][:, o:o + n],
                                     start=(k == 0), stop=(k == din_c - 1))
                return r
            b.on("tensor", mms)
            new_pend = []
            if gi > 0:
                prevg = groups[gi - 1]
                prevbanks = [ps[3 * ((gi - 1) % 2) + i] for i in range(len(prevg))]
                for i, (o, n, oc) in enumerate(prevg):
                    new_pend.append(evac(prevbanks[i], o, n, oc))
            for st in pend:
                st()
            pend = [p for p in new_pend if p is not None]
        b.step()
        lastg = groups[-1]
        lastbanks = [ps[3 * ((len(groups) - 1) % 2) + i] for i in range(len(lastg))]
        new_pend = []
        for i, (o, n, oc) in enumerate(lastg):
            new_pend.append(evac(lastbanks[i], o, n, oc))
        for st in pend:
            st()
        b.step()
        for st in new_pend:
            if st is not None:
                st()
        b.step()

    def proj_t(self, wbf, x_bf, T, dout, dst, *, w_dout=None, o_base=0,
               din_c=DC, scale=1.0):
        b, ps = self.b, self.ps
        w_dout = w_dout or dout
        resident = isinstance(dst, list)
        obs = [(i, min(512, dout - i)) for i in range(0, dout, 512)]
        jobs = [(tt, o, n) for tt in range(T // 128) for (o, n) in obs]
        groups = [jobs[i:i + 3] for i in range(0, len(jobs), 3)]
        pend = []
        def evac(bank, tt, o, n):
            if resident:
                tgt = dst[tt][:, o:o + n]
            else:
                tgt = self.stg_next()[:, 0:n]
            if scale != 1.0:
                b.on("vector", lambda e, tgt=tgt, bank=bank, n=n: e.tensor_scalar(
                    tgt, bank[:, 0:n], scale, None, ALU.mult))
            else:
                b.on("vector", lambda e, tgt=tgt, bank=bank, n=n: e.tensor_copy(
                    tgt, bank[:, 0:n]))
            if resident:
                return None
            dram_slice = dst[tt * 128:(tt + 1) * 128, o:o + n]
            eng = DMA_ENG[tt % 2]
            return lambda: b.dma(eng, dram_slice, tgt)
        for gi, g in enumerate(groups):
            banks = [ps[3 * (gi % 2) + i] for i in range(len(g))]
            b.step()
            def mms(e, g=g, banks=banks):
                r = None
                for i, (tt, o, n) in enumerate(g):
                    for k in range(din_c):
                        r = e.matmul(banks[i][:, 0:n],
                                     x_bf[k][:, tt * 128:(tt + 1) * 128],
                                     self.wslice(wbf, w_dout, k, o_base + o, n),
                                     start=(k == 0), stop=(k == din_c - 1))
                return r
            b.on("tensor", mms)
            new_pend = []
            if gi > 0:
                prevg = groups[gi - 1]
                prevbanks = [ps[3 * ((gi - 1) % 2) + i] for i in range(len(prevg))]
                for i, (tt, o, n) in enumerate(prevg):
                    new_pend.append(evac(prevbanks[i], tt, o, n))
            for st in pend:
                st()
            pend = [p for p in new_pend if p is not None]
        b.step()
        lastg = groups[-1]
        lastbanks = [ps[3 * ((len(groups) - 1) % 2) + i] for i in range(len(lastg))]
        new_pend = []
        for i, (tt, o, n) in enumerate(lastg):
            new_pend.append(evac(lastbanks[i], tt, o, n))
        for st in pend:
            st()
        b.step()
        for st in new_pend:
            if st is not None:
                st()
        b.step()




def xT_chunks(x):
    out = []
    for core in range(NCORE):
        bb, cc = divmod(core, 4)
        out.append(np.ascontiguousarray(
            x[bb, cc * TC:(cc + 1) * TC, :].T.astype(np.float32)))
    return out


def run(b, in_maps):
    import os
    nc = b.finalize()
    r = run_bass_kernel_spmd(nc, in_maps, core_ids=list(range(NCORE)))
    if os.environ.get("KERNEL_TIMES") and getattr(r, "exec_time_ns", None):
        print(f"[launch] exec={r.exec_time_ns/1e3:.0f}us", flush=True)
    return r




def build_l0_proj():
    b = BSP()
    xT = b.din("xT", [D, TC], F32)
    wqkv = b.din("wqkv", [D, 3 * D], F32)
    qT_o = b.dout("qT", [D, TC], BF16)
    kT_o = b.dout("kT", [D, TC], BF16)
    v_o = b.dout("v", [TC, D], BF16)

    lay = Lay(b, f32_cols=16384 + 6144, bf_cols=16384 + 8 * 3072)
    fa, ba = lay.fa, lay.ba
    x = fa.alloc8(TC)
    m0 = fa.mark()
    mb = fa.alloc(TC); rb = fa.alloc(TC); tb = fa.alloc(TC)
    xln = ba.alloc8(TC)
    m1 = ba.mark()
    scr = ba.alloc8(TC)

    b.step()
    for k in range(DC):
        b.dma("sync", x[k], xT[k * 128:(k + 1) * 128, :])
    lay.ln(x, scr, mb, rb, tb, T=TC, out_bf=xln)

    ba.reset(m1)
    fa.reset(m0)
    w32 = fa.alloc(3 * D)
    wbf = lay.load_w_bf(b.dram["wqkv"], D, 3 * D, w32)

    lay.proj_f(wbf, xln, TC, D, qT_o, w_dout=3 * D, o_base=0, scale=1.0 / 32.0)
    lay.proj_f(wbf, xln, TC, D, kT_o, w_dout=3 * D, o_base=D)
    lay.proj_t(wbf, xln, TC, D, v_o, w_dout=3 * D, o_base=2 * D)

    kn2_o = b.dout("kn2loc", [1, 1], F32)
    ktmp = xln
    kn2b = mb
    ps = lay.ps
    b.step()
    for k in range(DC):
        b.dma(DMA_ENG[k % 3], ktmp[k], kT_o[k * 128:(k + 1) * 128, :])
    b.step()
    for k in range(DC):
        b.on("scalar", lambda e, k=k: e.activation(ktmp[k], ktmp[k], AF.Square))
    b.step()
    def mm_kn2(e):
        r = None
        for blk in range(4):
            for k in range(DC):
                r = e.matmul(ps[blk][:, 0:512], lay.ones_bf[:],
                             ktmp[k][:, blk * 512:(blk + 1) * 512],
                             start=(k == 0), stop=(k == DC - 1))
        return r
    b.on("tensor", mm_kn2)
    b.step()
    for blk in range(4):
        b.on("vector", lambda e, blk=blk: e.tensor_copy(
            kn2b[:, blk * 512:(blk + 1) * 512], ps[blk][:, 0:512]))
    kmax = rb[:, 0:1]
    b.step()
    b.on("vector", lambda e: e.reduce_max(kmax, kn2b[:, 0:TC], AX.X))
    b.step()
    b.dma("gpsimd", kn2_o[0:1, 0:1], kmax[0:1, 0:1])
    b.step()
    return b


def kernel(**inputs):
    raise NotImplementedError("under construction")




def build_l0_attend(n_qt=16, do_epi=True, sweep_stage=5, debug=False):
    b = BSP()
    qT = b.din("qT", [D, TC], BF16)
    kTf = b.din("kTf", [D, TSEQ], BF16)
    vf = b.din("vf", [TSEQ, D], BF16)
    xT = b.din("xT", [D, TC], F32)
    wo_t = b.din("wo_t", [D, D], F32)
    out_b = b.din("out_b", [D, 1], F32)
    kn24 = b.din("kn24", [1, 4], F32)
    x1_o = b.dout("x1T", [D, TC], F32)

    lay = Lay(b, f32_cols=1024 + 16384 + 3 * TC,
              bf_cols=16384 + 2 + 8 * D + 33280, n_ps=6, pt_cols=0, n_stg=1)
    fa, ba, ps = lay.fa, lay.ba, lay.ps
    ptA = ps[4][:, 0:512].bitcast(BF16)
    ptB = ps[5][:, 0:512].bitcast(BF16)

    sm = fa.alloc(1024)
    def smcol(i, n=1):
        return sm[:, i:i + n]
    kn2_sb = sm[0:1, 0:4]
    k2m = sm[0:1, 4:5]
    lnk_bf = ba.alloc(2)[0:1, 0:1]
    lnk_b = smcol(8)
    qn2tok = smcol(9)
    Bneg = smcol(10)
    ell = smcol(11)
    ellp = [smcol(12), smcol(13)]
    rec = smcol(14)
    outb_sb = sm[:, 16:24]

    mreg = fa.mark()
    w32 = fa.alloc(D)
    fa.reset(mreg)
    x = fa.alloc8(TC)
    mb = fa.alloc(TC); rb = fa.alloc(TC); tb = fa.alloc(TC)

    yT = ba.alloc8(TC)
    b.step()
    wo_bf = None
    msweep = None

    b.step()
    b.dma("sync", kn2_sb, kn24[0:1, 0:4])
    for oc in range(DC):
        b.dma("gpsimd", outb_sb[0:128, oc:oc + 1], out_b[oc * 128:(oc + 1) * 128, 0:1])
    b.step()
    b.on("vector", lambda e: e.reduce_max(k2m, kn2_sb, AX.X))
    b.step()
    b.on("scalar", lambda e: e.activation(k2m, k2m, AF.Ln))
    b.step()
    b.on("vector", lambda e: e.tensor_copy(lnk_bf, k2m))
    b.step()
    b.on("tensor", lambda e: e.matmul(ps[0][:, 0:1], lay.ones_bf[0:1, 0:128],
                                      lnk_bf, start=True, stop=True))
    b.step()
    b.on("vector", lambda e: e.tensor_copy(lnk_b, ps[0][:, 0:1]))

    wo_bf = lay.load_w_bf(b.dram["wo_t"], D, D, w32)

    msweep = ba.mark()
    qsl = ba.alloc8(128)
    qsq = ba.alloc8(128)
    kbuf = [[ba.alloc(512) for _ in range(DC)] for _ in range(2)]
    vbuf = [[ba.alloc(D) for _ in range(4)] for _ in range(3)]
    probs = [ba.alloc(512) for _ in range(2)]
    probsT = [ba.alloc(512) for _ in range(2)]
    ybf = ba.alloc(D)

    NKB = TSEQ // 512

    for qt in range(n_qt):
        qoff = qt * 128
        b.step()
        for k in range(DC):
            b.dma("sync", qsl[k], qT[k * 128:(k + 1) * 128, qoff:qoff + 128])
        b.on("vector", lambda e: e.memset(ell, 0.0))
        b.step()
        for k in range(DC):
            b.on("scalar", lambda e, k=k: e.activation(qsq[k], qsl[k], AF.Square))
        b.step()
        def mm_qn2(e):
            r = None
            for k in range(DC):
                r = e.matmul(ps[0][:, 0:128], lay.ones_bf[:], qsq[k],
                             start=(k == 0), stop=(k == DC - 1))
            return r
        b.on("tensor", mm_qn2)
        b.step()
        b.on("vector", lambda e: e.tensor_copy(mb[:, 0:128], ps[0][:, 0:128]))
        b.step()
        b.on("tensor", lambda e: e.transpose(ps[1][:, 0:128], mb[:, 0:128],
                                             lay.ident_f32[:]))
        b.step()
        b.on("vector", lambda e: e.tensor_copy(qn2tok, ps[1][:, 0:1]))
        b.step()
        b.on("scalar", lambda e: e.activation(qn2tok, qn2tok, AF.Ln))
        b.step()
        b.on("vector", lambda e: e.tensor_add(qn2tok, qn2tok, lnk_b))
        b.step()
        b.on("scalar", lambda e: e.activation(Bneg, qn2tok, AF.Exp, scale=0.5))
        b.step()
        b.on("vector", lambda e: e.tensor_scalar(Bneg, Bneg, -1.0, None, ALU.mult))

        for i in (range(NKB + 6) if sweep_stage >= 2 else []):
            b.step()
            if i < NKB:
                for k in range(DC):
                    b.dma("sync", kbuf[i % 2][k],
                          kTf[k * 128:(k + 1) * 128, i * 512:(i + 1) * 512])
            jv = i - 3
            if 0 <= jv < NKB:
                for c in range(4):
                    b.dma("gpsimd", vbuf[jv % 3][c],
                          vf[jv * 512 + c * 128: jv * 512 + (c + 1) * 128, :])
            def tensor_work(e, i=i):
                r = None
                js = i - 1
                if 0 <= js < NKB:
                    bank = ps[2 + js % 2]
                    for k in range(DC):
                        r = e.matmul(bank[:, 0:512], qsl[k], kbuf[js % 2][k],
                                     start=(k == 0), stop=(k == DC - 1))
                jt = i - 3
                if 0 <= jt < NKB:
                    ptX = ptA if jt % 2 == 0 else ptB
                    for c in range(4):
                        r = e.transpose(ptX[:, c * 128:(c + 1) * 128],
                                        probs[jt % 2][:, c * 128:(c + 1) * 128],
                                        lay.ident_bf[:])
                jp = i - 5
                if 0 <= jp < NKB:
                    for c in range(4):
                        for h in range(2):
                            r = e.matmul(ps[h][:, 0:512],
                                         probsT[jp % 2][:, c * 128:(c + 1) * 128],
                                         vbuf[jp % 3][c][:, h * 512:(h + 1) * 512],
                                         start=(jp == 0 and c == 0),
                                         stop=(jp == NKB - 1 and c == 3))
                return r
            b.on("tensor", tensor_work)
            je = i - 2
            if 0 <= je < NKB:
                b.on("scalar", lambda e, je=je: e.activation(
                    probs[je % 2], ps[2 + je % 2][:, 0:512], AF.Exp,
                    bias=Bneg, accum_out=ellp[je % 2]))
            jv2 = i - 4
            if 0 <= jv2 < NKB:
                ptX = ptA if jv2 % 2 == 0 else ptB
                b.on("vector", lambda e, jv2=jv2, ptX=ptX: e.tensor_copy(
                    probsT[jv2 % 2], ptX[:, 0:512]))
            ja = i - 3
            if 0 <= ja < NKB:
                b.on("vector", lambda e, ja=ja: e.tensor_add(ell, ell, ellp[ja % 2]))

        if debug and qt == 0:
            dbg_ell = b.dout("dbg_ell", [2, 128], F32)
            dbg_p0 = b.dout("dbg_p0", [128, 512], BF16)
            b.step()
            b.dma("sync", dbg_ell[0:1, :].rearrange("a b -> b a"), ell)
            b.dma("sync", dbg_ell[1:2, :].rearrange("a b -> b a"), Bneg)
            b.dma("gpsimd", dbg_p0[:, :], probs[0])
        if sweep_stage < 3:
            continue
        b.step()
        b.on("vector", lambda e: e.reciprocal(rec, ell))
        b.step()
        for h in range(2):
            b.on("vector", lambda e, h=h: e.tensor_scalar(
                ybf[:, h * 512:(h + 1) * 512], ps[h][:, 0:512], rec, None, ALU.mult))
        if sweep_stage < 4:
            continue
        if debug and qt == 0:
            dbg_ybf = b.dout("dbg_ybf", [128, D], BF16)
            b.step()
            b.dma("sync", dbg_ybf[:, :], ybf)
        b.step()
        def ytrans(e):
            r = None
            for c in range(DC):
                r = e.transpose(ptA[:, c * 128:(c + 1) * 128],
                                ybf[:, c * 128:(c + 1) * 128], lay.ident_bf[:])
            return r
        b.on("tensor", ytrans)
        if sweep_stage < 5:
            continue
        b.step()
        for c in range(DC):
            b.on("vector", lambda e, c=c, qoff=qoff: e.tensor_copy(
                yT[c][:, qoff:qoff + 128], ptA[:, c * 128:(c + 1) * 128]))

    if not do_epi:
        b.step()
        return b
    ba.reset(msweep)
    attn = ba.alloc8(TC)
    scr = ba.alloc8(TC)
    outb_list = [outb_sb[0:128, oc:oc + 1] for oc in range(DC)]
    if debug:
        yT_dbg = b.dout("yT_dbg", [D, TC], BF16)
        b.step()
        for k in range(DC):
            b.dma(DMA_ENG[k % 3], yT_dbg[k * 128:(k + 1) * 128, :], yT[k])
    lay.proj_f(wo_bf, yT, TC, D, attn, bias=outb_list)
    if debug:
        at_dbg = b.dout("at_dbg", [D, TC], BF16)
        b.step()
        for k in range(DC):
            b.dma(DMA_ENG[k % 3], at_dbg[k * 128:(k + 1) * 128, :], attn[k])
    b.step()
    for k in range(DC):
        b.dma(DMA_ENG[k % 3], x[k], xT[k * 128:(k + 1) * 128, :])
    lay.ln(attn, scr, mb, rb, tb, T=TC, x_is_bf=True, resid=x)
    b.step()
    for k in range(DC):
        b.dma(DMA_ENG[k % 3], x1_o[k * 128:(k + 1) * 128, :], x[k])
    b.step()
    return b




def build_l0_ffn(debug=False):
    b = BSP()
    xT = b.din("xT", [D, TC], F32)
    halo = b.din("halo", [D, 2], F32)
    dwk = b.din("dwk", [D, 3], F32)
    dwb = b.din("dwb", [D, 1], F32)
    pw_t = b.din("pw_t", [D, D], F32)
    pwb = b.din("pwb", [D, 1], F32)
    rw_t = b.din("rw_t", [D, 8], F32)
    rb_in = b.din("rb_in", [8, 1], F32)
    ew_t = b.din("ew_t", [8 * D, D], F32)
    eb_t = b.din("eb_t", [D, 8], F32)
    wout_t = b.din("wout_t", [D, D], F32)
    woutb = b.din("woutb", [D, 1], F32)
    nw_in = b.din("nw_in", [D, 1], F32)
    ffd_t = b.din("ffd_t", [D, 256], F32)
    ffdb = b.din("ffdb", [256, 1], F32)
    ffu_t = b.din("ffu_t", [256, D], F32)
    ffub = b.din("ffub", [D, 1], F32)
    xo = b.dout("xoT", [D, TC], F32)

    lay = Lay(b, f32_cols=9600, bf_cols=79500, n_ps=6, pt_cols=1024, n_stg=6)
    fa, ba, ps, pt = lay.fa, lay.ba, lay.ps, lay.pt

    sm = fa.alloc(16)
    mb = fa.alloc(TC); rb = fa.alloc(TC); tb_ = fa.alloc(TC)
    rwbc = fa.alloc(TC)
    beT = fa.alloc(64)
    pbias = fa.alloc(40)
    kert = fa.alloc(3 * DC)
    nw = fa.alloc(DC)
    hx = fa.alloc(2 * DC)
    ltokf = [fa.alloc(8) for _ in range(16)]
    rwtok = [fa.alloc(8) for _ in range(16)]
    m1 = fa.alloc(1); nm1 = fa.alloc(1); m2 = fa.alloc(1); den = fa.alloc(1)
    msk = fa.alloc(8)
    rrow = mb[0:1, 1024:1024 + TC // 2]
    rrow = tb_[0:1, 0:TC]
    xblk = ([rwbc[:, i * 512:(i + 1) * 512] for i in range(4)]
            + [tb_[:, i * 512:(i + 1) * 512] for i in range(4)])

    R1 = ba.alloc8(2 + TC)
    R2 = ba.alloc8(TC)
    WR = ba.alloc(8 * D)
    scr = ba.alloc8(TC)
    acc = ba.alloc8(TC)
    d1 = [ba.alloc(TC) for _ in range(2)]
    stg2 = [ba.alloc(512) for _ in range(3)]
    rrow_bf = d1[0][0:1, 0:TC]

    xext = R1
    xln = [R1[k][:, 2:2 + TC] for k in range(DC)]

    b.step()
    xbf = acc
    for k in range(DC):
        b.dma("gpsimd", xbf[k], xT[k * 128:(k + 1) * 128, :])
    b.dma("sync", sm[0:8, 0:1], rb_in[:, :])
    for k in range(DC):
        b.dma("gpsimd", kert[0:128, 3 * k:3 * k + 3], dwk[k * 128:(k + 1) * 128, :])
        b.dma("scalar", nw[0:128, k:k + 1], nw_in[k * 128:(k + 1) * 128, :])
        b.dma("gpsimd", pbias[0:128, k:k + 1], pwb[k * 128:(k + 1) * 128, :])
        b.dma("scalar", pbias[0:128, 8 + k:9 + k], woutb[k * 128:(k + 1) * 128, :])
        b.dma("gpsimd", pbias[0:128, 16 + k:17 + k], ffub[k * 128:(k + 1) * 128, :])
        b.dma("scalar", pbias[0:128, 24 + k:25 + k], dwb[k * 128:(k + 1) * 128, :])
        b.dma("sync", beT[0:128, k * 8:(k + 1) * 8], eb_t[k * 128:(k + 1) * 128, :])
        b.dma("sync", hx[0:128, 2 * k:2 * k + 2], halo[k * 128:(k + 1) * 128, :])
    for k in range(2):
        b.dma("sync", pbias[0:128, 32 + k:33 + k], ffdb[k * 128:(k + 1) * 128, :])

    lay.ln(xbf, scr, mb, rb, tb_, T=TC, x_is_bf=True, out_bf=xln)
    hxt = [hx[0:128, 2 * k:2 * k + 2] for k in range(DC)]
    hscr = [stg2[0][:, k * 2:k * 2 + 2] for k in range(DC)]
    lay.ln(hxt, hscr, mb, rb, tb_, T=2, out_bf=[xext[k][:, 0:2] for k in range(DC)])

    xc = R2
    for j in range(3):
        b.step()
        for k in range(DC):
            eng = "vector"
            kap = kert[0:128, 3 * k + j:3 * k + j + 1]
            if j == 0:
                b.on(eng, lambda e, k=k, kap=kap: e.tensor_scalar(
                    xc[k], xext[k][:, 0:TC], kap, None, ALU.mult))
            else:
                b.on(eng, lambda e, k=k, kap=kap, j=j: e.scalar_tensor_tensor(
                    xc[k], xext[k][:, j:j + TC], kap, xc[k], ALU.mult, ALU.add))
    b.step()
    for k in range(DC):
        b.on("vector",
             lambda e, k=k: e.tensor_scalar(xc[k], xc[k],
                                            pbias[0:128, 24 + k:25 + k], None, ALU.add))

    w32 = mb
    pw_bf = WR[:, 0:8 * D]
    b.step()
    for k in range(DC):
        b.dma("gpsimd", pw_bf[:, k * D:(k + 1) * D], pw_t[k * 128:(k + 1) * 128, :])
    x1i = scr
    pwb_list = [pbias[0:128, k:k + 1] for k in range(DC)]
    def pw_store(oc, o, n, stgap):
        b.on("vector", lambda e: e.tensor_add(x1i[oc][:, o:o + n],
                                              xln[oc][:, o:o + n], stgap))
    lay.proj_f(pw_bf, xc, TC, D, None, act=AF.Silu, bias=pwb_list, store_fn=pw_store)

    if debug:
        d_x1i = b.dout("d_x1i", [D, TC], BF16)
        b.step()
        for k in range(DC):
            b.dma(DMA_ENG[k % 3], d_x1i[k * 128:(k + 1) * 128, :], x1i[k])
    rW_bf = stg2[1][:, 0:64]
    b.step()
    for k in range(DC):
        b.dma("gpsimd", rW_bf[:, k * 8:(k + 1) * 8], rw_t[k * 128:(k + 1) * 128, :])
    lg = rb
    b.step()
    def mm_router(e):
        r = None
        for blk in range(4):
            for k in range(DC):
                r = e.matmul(ps[blk][0:8, 0:512], rW_bf[:, k * 8:(k + 1) * 8],
                             x1i[k][:, blk * 512:(blk + 1) * 512],
                             start=(k == 0), stop=(k == DC - 1))
        return r
    b.on("tensor", mm_router)
    b.step()
    for blk in range(4):
        b.on("vector", lambda e, blk=blk: e.tensor_scalar(
            lg[0:8, blk * 512:(blk + 1) * 512], ps[blk][0:8, 0:512],
            sm[0:8, 0:1], None, ALU.add))
    for tt in range(16):
        b.step()
        b.on("tensor", lambda e, tt=tt: e.transpose(
            ps[4][:, 0:8], lg[0:8, tt * 128:(tt + 1) * 128], lay.ident_f32[0:8, 0:8]))
        b.step()
        b.on("vector", lambda e, tt=tt: e.tensor_copy(ltokf[tt][0:128, 0:8], ps[4][:, 0:8]))
    BIGNEG = -1.0e30
    for tt in range(16):
        lt = ltokf[tt][0:128, 0:8]
        rw_ = rwtok[tt][0:128, 0:8]
        b.step()
        b.on("vector", lambda e, lt=lt: e.reduce_max(m1[:, 0:1], lt, AX.X))
        b.step()
        b.on("vector", lambda e: e.tensor_scalar(nm1[:, 0:1], m1[:, 0:1], -1.0, None, ALU.mult))
        b.on("vector", lambda e, lt=lt: e.tensor_scalar(msk[:, 0:8], lt, m1[:, 0:1], None, ALU.is_ge))
        b.step()
        b.on("vector", lambda e, lt=lt: e.scalar_tensor_tensor(
            msk[:, 0:8], msk[:, 0:8], BIGNEG, lt, ALU.mult, ALU.add))
        b.step()
        b.on("vector", lambda e: e.reduce_max(m2[:, 0:1], msk[:, 0:8], AX.X))
        b.step()
        b.on("vector", lambda e, lt=lt: e.tensor_scalar(msk[:, 0:8], lt, m2[:, 0:1], None, ALU.is_ge))
        b.step()
        b.on("scalar", lambda e: e.activation(den[:, 0:1], m2[:, 0:1], AF.Exp, bias=nm1[:, 0:1]))
        b.step()
        b.on("vector", lambda e: e.tensor_scalar(den[:, 0:1], den[:, 0:1], 1.0, None, ALU.add))
        b.step()
        b.on("vector", lambda e: e.reciprocal(den[:, 0:1], den[:, 0:1]))
        b.step()
        b.on("scalar", lambda e, lt=lt, rw_=rw_: e.activation(rw_, lt, AF.Exp, bias=nm1[:, 0:1]))
        b.step()
        b.on("vector", lambda e, rw_=rw_: e.tensor_mul(rw_, rw_, msk[:, 0:8]))
        b.step()
        b.on("vector", lambda e, rw_=rw_: e.tensor_scalar(rw_, rw_, den[:, 0:1], None, ALU.mult))
    for tt in range(16):
        b.step()
        b.on("tensor", lambda e, tt=tt: e.transpose(
            ps[4][0:8, 0:128], rwtok[tt][0:128, 0:8], lay.ident_f32[:, :]))
        b.step()
        b.on("vector", lambda e, tt=tt: e.tensor_copy(
            lg[0:8, tt * 128:(tt + 1) * 128], ps[4][0:8, 0:128]))

    if debug:
        d_lg = b.dout("d_lg", [8, TC], F32)
        b.step()
        b.dma("sync", d_lg[:, :], lg[0:8, 0:TC])
    b.step()
    for k in range(DC):
        b.on("vector" if k % 2 == 0 else "gpsimd", lambda e, k=k: e.memset(acc[k], 0.0))
    for ex in range(8):
        we_bf = WR[:, 0:8 * D]
        b.step()
        for k in range(DC):
            b.dma("gpsimd", we_bf[:, k * D:(k + 1) * D],
                  ew_t[ex * D + k * 128: ex * D + (k + 1) * 128, :])
        b.step()
        b.dma("gpsimd", rrow[0:1, 0:TC], lg[ex:ex + 1, 0:TC])
        b.step()
        b.on("vector", lambda e: e.tensor_copy(rrow_bf[0:1, 0:TC], rrow[0:1, 0:TC]))
        b.step()
        def mm_gate(e):
            r = None
            for blk in range(4):
                r = e.matmul(ps[blk][:, 0:512], lay.ones_bf[0:1, 0:128],
                             rrow_bf[0:1, blk * 512:(blk + 1) * 512],
                             start=True, stop=True)
            return r
        b.on("tensor", mm_gate)
        b.step()
        for blk in range(4):
            b.on("vector", lambda e, blk=blk: e.tensor_copy(
                rwbc[:, blk * 512:(blk + 1) * 512], ps[blk][:, 0:512]))
        bias_list = [beT[0:128, k * 8 + ex:k * 8 + ex + 1] for k in range(DC)]
        def est(oc, o, n, stgap, ex=ex):
            def fn(e):
                e.tensor_mul(stg2[oc % 3][:, 0:n], stgap, rwbc[:, o:o + n])
                return e.tensor_add(acc[oc][:, o:o + n], acc[oc][:, o:o + n],
                                    stg2[oc % 3][:, 0:n])
            b.on("vector", fn)
        lay.proj_f(we_bf, x1i, TC, D, None, act=AF.Silu, bias=bias_list, store_fn=est)

    if debug:
        d_acc = b.dout("d_acc", [D, TC], BF16)
        b.step()
        for k in range(DC):
            b.dma(DMA_ENG[k % 3], d_acc[k * 128:(k + 1) * 128, :], acc[k])
    wout_bf = WR[:, 0:8 * D]
    b.step()
    for k in range(DC):
        b.dma("gpsimd", wout_bf[:, k * D:(k + 1) * D], wout_t[k * 128:(k + 1) * 128, :])
    s_bf = xln
    wob_list = [pbias[0:128, 8 + k:9 + k] for k in range(DC)]
    def mo_store(oc, o, n, stgap):
        b.on("vector", lambda e: e.tensor_add(s_bf[oc][:, o:o + n],
                                              x1i[oc][:, o:o + n], stgap))
    lay.proj_f(wout_bf, acc, TC, D, None, bias=wob_list, act=AF.Identity,
               store_fn=mo_store)

    if debug:
        d_s = b.dout("d_s", [D, TC], BF16)
        b.step()
        for k in range(DC):
            b.dma(DMA_ENG[k % 3], d_s[k * 128:(k + 1) * 128, :], s_bf[k])
    b.step()
    for k in range(DC):
        b.on("scalar", lambda e, k=k: e.activation(R2[k], s_bf[k], AF.Square))
    b.step()
    def mm_rms(e):
        r = None
        for blk in range(4):
            for k in range(DC):
                r = e.matmul(ps[blk][:, 0:512], lay.ones_bf[:],
                             R2[k][:, blk * 512:(blk + 1) * 512],
                             start=(k == 0), stop=(k == DC - 1))
        return r
    b.on("tensor", mm_rms)
    epsap = lay.const(1e-6)
    b.step()
    for blk in range(4):
        b.on("vector", lambda e, blk=blk: e.tensor_scalar(
            rb[:, blk * 512:(blk + 1) * 512], ps[blk][:, 0:512], 1.0 / D, None, ALU.mult))
    b.step()
    b.on("scalar", lambda e: e.activation(rb[:, 0:TC], rb[:, 0:TC], AF.Ln, bias=epsap))
    b.step()
    b.on("scalar", lambda e: e.activation(rb[:, 0:TC], rb[:, 0:TC], AF.Exp, scale=-0.5))
    h_bf = R2
    b.step()
    for k in range(DC):
        b.on("vector",
             lambda e, k=k: e.scalar_tensor_tensor(
                 h_bf[k], s_bf[k], nw[0:128, k:k + 1], rb[:, 0:TC], ALU.mult, ALU.mult))

    if debug:
        d_h = b.dout("d_h", [D, TC], BF16)
        b.step()
        for k in range(DC):
            b.dma(DMA_ENG[k % 3], d_h[k * 128:(k + 1) * 128, :], h_bf[k])
    ffd_bf = WR[:, 0:2048]
    ffu_bf = WR[:, 2048:4096]
    b.step()
    for k in range(DC):
        b.dma("gpsimd", ffd_bf[:, k * 256:(k + 1) * 256], ffd_t[k * 128:(k + 1) * 128, :])
    for k in range(2):
        b.dma("gpsimd", ffu_bf[:, k * D:(k + 1) * D], ffu_t[k * 128:(k + 1) * 128, :])
    fdb_list = [pbias[0:128, 32 + k:33 + k] for k in range(2)]
    lay.proj_f(ffd_bf, h_bf, TC, 256, d1, w_dout=256, act=AF.Gelu, bias=fdb_list)
    out_sub = acc
    fub_list = [pbias[0:128, 16 + k:17 + k] for k in range(DC)]
    def fu_store(oc, o, n, stgap):
        b.on("vector", lambda e: e.tensor_add(out_sub[oc][:, o:o + n],
                                              h_bf[oc][:, o:o + n], stgap))
    lay.proj_f(ffu_bf, d1, TC, D, None, w_dout=D, din_c=2, bias=fub_list,
               act=AF.Identity, store_fn=fu_store)

    if debug:
        d_os = b.dout("d_os", [D, TC], BF16)
        b.step()
        for k in range(DC):
            b.dma(DMA_ENG[k % 3], d_os[k * 128:(k + 1) * 128, :], out_sub[k])
    lay.ln_resid_stream(out_sub, scr, mb, rb, tb_, T=TC, x_in=xT, x_out=xo,
                        xblk=xblk, x_is_bf=True)
    b.step()
    return b




def build_chanmix():
    b = BSP()
    xT = b.din("xT", [D, TC], F32)
    halo = b.din("halo", [D, 1], F32)
    wr_t = b.din("wr_t", [D, D], F32)
    wk_t = b.din("wk_t", [D, D], F32)
    wv_t = b.din("wv_t", [D, D], F32)
    mur = b.din("mur", [D, 1], F32)
    muk = b.din("muk", [D, 1], F32)
    xo = b.dout("xoT", [D, TC], F32)

    lay = Lay(b, f32_cols=9000, bf_cols=77000, n_ps=6, pt_cols=0, n_stg=6)
    fa, ba, ps = lay.fa, lay.ba, lay.ps

    mb = fa.alloc(TC); rb = fa.alloc(TC); tb_ = fa.alloc(TC)
    mu = fa.alloc(2 * DC)
    hx = fa.alloc(DC)
    w32 = mb
    xblk = ([rb[:, i * 512:(i + 1) * 512] for i in range(0)]
            + [tb_[:, i * 512:(i + 1) * 512] for i in range(4)]
            + [fa.alloc(512) for _ in range(4)])

    A = ba.alloc8(1 + TC)
    B_ = ba.alloc8(TC)
    Cb = ba.alloc8(TC)
    Db = ba.alloc8(TC)
    WR = ba.alloc(8 * D)
    mext = A
    m_t = [A[k][:, 1:1 + TC] for k in range(DC)]

    b.step()
    xbf = Cb
    for k in range(DC):
        b.dma("gpsimd", xbf[k], xT[k * 128:(k + 1) * 128, :])
    for k in range(DC):
        b.dma("sync", mu[0:128, k:k + 1], mur[k * 128:(k + 1) * 128, :])
        b.dma("sync", mu[0:128, DC + k:DC + k + 1], muk[k * 128:(k + 1) * 128, :])
        b.dma("scalar", hx[0:128, k:k + 1], halo[k * 128:(k + 1) * 128, :])

    lay.ln(xbf, B_, mb, rb, tb_, T=TC, x_is_bf=True, out_bf=m_t)
    hxt = [hx[0:128, k:k + 1] for k in range(DC)]
    hscr = [lay.stg[0][:, k:k + 1] for k in range(DC)]
    lay.ln(hxt, hscr, mb, rb, tb_, T=1, out_bf=[mext[k][:, 0:1] for k in range(DC)])

    def timemix(dst, mucol):
        b.step()
        for k in range(DC):
            b.on("vector", lambda e, k=k: e.tensor_sub(
                B_[k], m_t[k], mext[k][:, 0:TC]))
        b.step()
        for k in range(DC):
            b.on("vector", lambda e, k=k: e.scalar_tensor_tensor(
                dst[k], B_[k], mu[0:128, mucol + k:mucol + k + 1],
                mext[k][:, 0:TC], ALU.mult, ALU.add))
    timemix(Cb, 0)
    wrbf = WR[:, 0:8 * D]
    b.step()
    for k in range(DC):
        b.dma("gpsimd", wrbf[:, k * D:(k + 1) * D], wr_t[k * 128:(k + 1) * 128, :])
    lay.proj_f(wrbf, Cb, TC, D, Db, act=AF.Sigmoid)

    timemix(Cb, DC)
    b.step()
    for k in range(DC):
        b.dma("gpsimd", wrbf[:, k * D:(k + 1) * D], wk_t[k * 128:(k + 1) * 128, :])
    k2 = [A[k][:, 0:TC] for k in range(DC)]
    lay.proj_f(wrbf, Cb, TC, D, k2, act=AF.Relu)
    b.step()
    for k in range(DC):
        b.on("scalar", lambda e, k=k: e.activation(k2[k], k2[k], AF.Square))

    b.step()
    for k in range(DC):
        b.dma("gpsimd", wrbf[:, k * D:(k + 1) * D], wv_t[k * 128:(k + 1) * 128, :])
    out_sub = Cb
    def vstore(oc, o, n, stgap):
        b.on("vector", lambda e: e.tensor_mul(out_sub[oc][:, o:o + n],
                                              Db[oc][:, o:o + n], stgap))
    lay.proj_f(wrbf, k2, TC, D, None, store_fn=vstore)

    lay.ln_resid_stream(out_sub, B_, mb, rb, tb_, T=TC, x_in=xT, x_out=xo,
                        xblk=xblk, x_is_bf=True)
    b.step()
    return b




def build_l1a_A():
    b = BSP()
    xT = b.din("xT", [D, TC], F32)
    halo = b.din("halo", [D, 1], F32)
    wk_t = b.din("wk_t", [D, D], F32)
    wv_t = b.din("wv_t", [D, D], F32)
    wr_t = b.din("wr_t", [D, D], F32)
    tmk = b.din("tmk", [D, 1], F32)
    tmv = b.din("tmv", [D, 1], F32)
    tmr = b.din("tmr", [D, 1], F32)
    w_in = b.din("w_in", [D, 1], F32)
    ko = b.dout("kTT", [D, TC], F32)
    vo = b.dout("vTT", [D, TC], F32)
    ro = b.dout("rTT", [D, TC], F32)
    kmo = b.dout("kmaxl", [D, 1], F32)
    tlo = b.dout("tailsA", [D, 1], F32)
    tlo2 = b.dout("tailsB", [D, 1], F32)

    lay = Lay(b, f32_cols=16700, bf_cols=57400, n_ps=6, pt_cols=0, n_stg=6)
    fa, ba, ps = lay.fa, lay.ba, lay.ps

    mb = fa.alloc(TC); rb = fa.alloc(TC); tb_ = fa.alloc(TC)
    tm = fa.alloc(4 * DC)
    hx = fa.alloc(DC)
    sm = fa.alloc(16)
    zc = fa.alloc(TC); vc = fa.alloc(TC); sa = fa.alloc(TC); sb2 = fa.alloc(TC)
    gam = fa.alloc(TC)
    onesf = tb_
    w32 = mb

    A = ba.alloc8(1 + TC)
    B_ = ba.alloc8(TC)
    Cb = ba.alloc8(TC)
    WR = ba.alloc(8 * D)
    mext = A
    m_t = [A[k][:, 1:1 + TC] for k in range(DC)]

    b.step()
    xbf = Cb
    for k in range(DC):
        b.dma("gpsimd", xbf[k], xT[k * 128:(k + 1) * 128, :])
    for k in range(DC):
        b.dma("sync", tm[0:128, k:k + 1], tmk[k * 128:(k + 1) * 128, :])
        b.dma("sync", tm[0:128, DC + k:DC + k + 1], tmv[k * 128:(k + 1) * 128, :])
        b.dma("sync", tm[0:128, 2 * DC + k:2 * DC + k + 1], tmr[k * 128:(k + 1) * 128, :])
        b.dma("sync", tm[0:128, 3 * DC + k:3 * DC + k + 1], w_in[k * 128:(k + 1) * 128, :])
        b.dma("scalar", hx[0:128, k:k + 1], halo[k * 128:(k + 1) * 128, :])

    lay.ln(xbf, B_, mb, rb, tb_, T=TC, x_is_bf=True, out_bf=m_t)
    hxt = [hx[0:128, k:k + 1] for k in range(DC)]
    hscr = [lay.stg[0][:, k:k + 1] for k in range(DC)]
    lay.ln(hxt, hscr, mb, rb, tb_, T=1, out_bf=[mext[k][:, 0:1] for k in range(DC)])

    def timemix(dst, col0):
        b.step()
        for k in range(DC):
            b.on("vector", lambda e, k=k: e.tensor_sub(B_[k], m_t[k], mext[k][:, 0:TC]))
        b.step()
        for k in range(DC):
            b.on("vector", lambda e, k=k, col0=col0: e.scalar_tensor_tensor(
                dst[k], B_[k], tm[0:128, col0 + k:col0 + k + 1],
                mext[k][:, 0:TC], ALU.mult, ALU.add))

    def loadw(dram):
        b.step()
        for k in range(DC):
            b.dma("gpsimd", WR[:, k * D:(k + 1) * D], dram[k * 128:(k + 1) * 128, :])
        return WR[:, 0:8 * D]

    b.step()
    b.on("vector", lambda e: e.memset(onesf[:, 0:TC], 1.0))
    timemix(Cb, 0)
    wbf = loadw(wk_t)
    lay.proj_f(wbf, Cb, TC, D, ko, out_f32=True)
    timemix(Cb, DC)
    wbf = loadw(wv_t)
    lay.proj_f(wbf, Cb, TC, D, vo, out_f32=True)
    timemix(Cb, 2 * DC)
    wbf = loadw(wr_t)
    lay.proj_f(wbf, Cb, TC, D, ro, out_f32=True)

    for k in range(DC):
        b.step()
        b.dma("sync", zc[:, 0:TC], ko[k * 128:(k + 1) * 128, :])
        b.dma("gpsimd", vc[:, 0:TC], vo[k * 128:(k + 1) * 128, :])
        b.step()
        b.on("vector", lambda e: e.reduce_max(sm[:, 0:1], zc[:, 0:TC], AX.X))
        b.step()
        b.on("vector", lambda e: e.tensor_scalar(sm[:, 1:2], sm[:, 0:1], -1.0, None, ALU.mult))
        b.on("scalar", lambda e, k=k: e.activation(sm[:, 2:3], tm[0:128, 3 * DC + k:3 * DC + k + 1], AF.Exp))
        b.step()
        b.on("scalar", lambda e: e.activation(zc[:, 0:TC], zc[:, 0:TC], AF.Exp, bias=sm[:, 1:2]))
        b.on("vector", lambda e: e.tensor_scalar(sm[:, 3:4], sm[:, 2:3], -1.0, None, ALU.mult))
        b.step()
        b.on("scalar", lambda e: e.activation(sm[:, 4:5], sm[:, 3:4], AF.Exp))
        b.step()
        b.on("vector", lambda e: e.tensor_mul(vc[:, 0:TC], zc[:, 0:TC], vc[:, 0:TC]))
        b.on("vector", lambda e: e.tensor_scalar(gam[:, 0:TC], onesf[:, 0:TC], sm[:, 4:5], None, ALU.mult))
        b.step()
        b.on("vector", lambda e: e.tensor_tensor_scan(sa[:, 0:TC], gam[:, 0:TC], vc[:, 0:TC], 0.0, ALU.mult, ALU.add))
        b.step()
        b.on("vector", lambda e: e.tensor_tensor_scan(sb2[:, 0:TC], gam[:, 0:TC], zc[:, 0:TC], 0.0, ALU.mult, ALU.add))
        b.step()
        b.dma("sync", kmo[k * 128:(k + 1) * 128, 0:1], sm[:, 0:1])
        b.dma("gpsimd", tlo[k * 128:(k + 1) * 128, 0:1], sa[:, TC - 1:TC])
        b.dma("scalar", tlo2[k * 128:(k + 1) * 128, 0:1], sb2[:, TC - 1:TC])
    b.step()
    return b




def build_l1a_C(debug=False):
    b = BSP()
    xT = b.din("xT", [D, TC], F32)
    kTT = b.din("kTT", [D, TC], F32)
    vTT = b.din("vTT", [D, TC], F32)
    rTT = b.din("rTT", [D, TC], F32)
    kmax4 = b.din("kmax4", [D, 4], F32)
    tails8 = b.din("tails8", [D, 8], F32)
    tjs = b.din("tjs", [128, 4], F32)
    msk4 = b.din("msk4", [128, 4], F32)
    w_in = b.din("w_in", [D, 1], F32)
    u_in = b.din("u_in", [D, 1], F32)
    wo_t = b.din("wo_t", [D, D], F32)
    xo = b.dout("xoT", [D, TC], F32)

    lay = Lay(b, f32_cols=22800, bf_cols=41000, n_ps=6, pt_cols=0, n_stg=6)
    fa, ba, ps = lay.fa, lay.ba, lay.ps

    mb = fa.alloc(TC); rb = fa.alloc(TC); tb_ = fa.alloc(TC)
    wu = fa.alloc(2 * DC)
    km = fa.alloc(4 * DC)
    tl = fa.alloc(8 * DC)
    tjc = fa.alloc(4); mkc = fa.alloc(4)
    sm = fa.alloc(16)
    zc = fa.alloc(TC); vc = fa.alloc(TC)
    saf = fa.alloc(1 + TC); sbf = fa.alloc(1 + TC)
    gam = fa.alloc(TC)
    onesf = tb_
    rampg = fa.alloc(TC)
    num = fa.alloc(TC); den = fa.alloc(TC)
    xblk = ([rampg[:, i * 512:(i + 1) * 512] for i in range(4)]
            + [num[:, i * 512:(i + 1) * 512] for i in range(4)])
    w32 = mb

    rwkv = ba.alloc8(TC)
    scr = ba.alloc8(TC)
    WR = ba.alloc(8 * D)

    b.step()
    b.dma("sync", tjc[0:128, 0:4], tjs[:, :])
    b.dma("sync", mkc[0:128, 0:4], msk4[:, :])
    for k in range(DC):
        b.dma("sync", wu[0:128, k:k + 1], w_in[k * 128:(k + 1) * 128, :])
        b.dma("sync", wu[0:128, DC + k:DC + k + 1], u_in[k * 128:(k + 1) * 128, :])
        b.dma("gpsimd", km[0:128, 4 * k:4 * k + 4], kmax4[k * 128:(k + 1) * 128, :])
        b.dma("scalar", tl[0:128, 8 * k:8 * k + 8], tails8[k * 128:(k + 1) * 128, :])
    b.on("vector", lambda e: e.memset(onesf[:, 0:TC], 1.0))
    b.on("gpsimd", lambda e: e.iota(gam[:, 0:TC], [[1, TC]], channel_multiplier=0,
                                    allow_small_or_imprecise_dtypes=True))

    wbf = WR[:, 0:8 * D]
    b.step()
    for k in range(DC):
        b.dma("gpsimd", wbf[:, k * D:(k + 1) * D], wo_t[k * 128:(k + 1) * 128, :])
    rampi = rb
    b.step()
    b.on("vector", lambda e: e.tensor_copy(rampi[:, 0:TC], gam[:, 0:TC]))

    for k in range(DC):
        wcol = wu[0:128, k:k + 1]
        ucol = wu[0:128, DC + k:DC + k + 1]
        b.step()
        b.dma("sync", zc[:, 0:TC], kTT[k * 128:(k + 1) * 128, :])
        b.dma("gpsimd", vc[:, 0:TC], vTT[k * 128:(k + 1) * 128, :])
        b.on("vector", lambda e, k=k: e.reduce_max(sm[:, 0:1], km[0:128, 4 * k:4 * k + 4], AX.X))
        b.step()
        b.on("vector", lambda e: e.tensor_scalar(sm[:, 1:2], sm[:, 0:1], -1.0, None, ALU.mult))
        b.on("scalar", lambda e, wcol=wcol: e.activation(sm[:, 2:3], wcol, AF.Exp))
        b.step()
        b.on("scalar", lambda e: e.activation(zc[:, 0:TC], zc[:, 0:TC], AF.Exp, bias=sm[:, 1:2]))
        b.on("vector", lambda e: e.tensor_scalar(sm[:, 3:4], sm[:, 2:3], -1.0, None, ALU.mult))
        b.step()
        b.on("scalar", lambda e: e.activation(sm[:, 4:5], sm[:, 3:4], AF.Exp))
        b.on("scalar", lambda e, ucol=ucol: e.activation(sm[:, 5:6], ucol, AF.Exp))
        b.step()
        b.on("vector", lambda e: e.tensor_mul(vc[:, 0:TC], zc[:, 0:TC], vc[:, 0:TC]))
        b.on("vector", lambda e: e.tensor_scalar(gam[:, 0:TC], onesf[:, 0:TC], sm[:, 4:5], None, ALU.mult))
        b.on("vector", lambda e: e.memset(sm[:, 6:8], 0.0))
        t8 = sm[:, 8:12]
        t9 = sm[:, 12:16]
        b.step()
        b.on("vector", lambda e: e.tensor_scalar(t8, tjc[0:128, 0:4], sm[:, 2:3], None, ALU.mult))
        b.on("vector", lambda e, k=k: e.tensor_scalar(
            t9, km[0:128, 4 * k:4 * k + 4], sm[:, 0:1], None, ALU.subtract))
        b.step()
        b.on("scalar", lambda e: e.activation(t8, t8, AF.Exp))
        b.on("scalar", lambda e: e.activation(t9, t9, AF.Exp))
        b.step()
        b.on("vector", lambda e: e.tensor_mul(t8, t8, t9))
        b.step()
        b.on("vector", lambda e: e.tensor_mul(t8, t8, mkc[0:128, 0:4]))
        b.step()
        b.on("vector", lambda e, k=k: e.tensor_mul(t9, t8, tl[0:128, 8 * k:8 * k + 4]))
        b.step()
        b.on("vector", lambda e: e.reduce_sum(sm[:, 6:7], t9, AX.X))
        b.on("vector", lambda e, k=k: e.tensor_mul(t9, t8, tl[0:128, 8 * k + 4:8 * k + 8]))
        b.step()
        b.on("vector", lambda e: e.reduce_sum(sm[:, 7:8], t9, AX.X))
        b.step()
        b.on("vector", lambda e: e.tensor_tensor_scan(saf[:, 1:1 + TC], gam[:, 0:TC],
                                                      vc[:, 0:TC], 0.0, ALU.mult, ALU.add))
        b.step()
        b.on("vector", lambda e: e.tensor_tensor_scan(sbf[:, 1:1 + TC], gam[:, 0:TC],
                                                      zc[:, 0:TC], 0.0, ALU.mult, ALU.add))
        b.step()
        b.on("vector", lambda e: e.tensor_scalar(rampg[:, 0:TC], rampi[:, 0:TC],
                                                 sm[:, 3:4], None, ALU.mult))
        b.step()
        b.on("scalar", lambda e: e.activation(rampg[:, 0:TC], rampg[:, 0:TC], AF.Exp,
                                              bias=sm[:, 3:4]))
        b.step()
        b.on("vector", lambda e: e.scalar_tensor_tensor(
            saf[:, 1:1 + TC], rampg[:, 0:TC], sm[:, 6:7], saf[:, 1:1 + TC],
            ALU.mult, ALU.add))
        b.on("vector", lambda e: e.tensor_copy(saf[:, 0:1], sm[:, 6:7]))
        b.step()
        b.on("vector", lambda e: e.scalar_tensor_tensor(
            sbf[:, 1:1 + TC], rampg[:, 0:TC], sm[:, 7:8], sbf[:, 1:1 + TC],
            ALU.mult, ALU.add))
        b.on("vector", lambda e: e.tensor_copy(sbf[:, 0:1], sm[:, 7:8]))
        b.step()
        b.on("vector", lambda e: e.scalar_tensor_tensor(
            num[:, 0:TC], vc[:, 0:TC], sm[:, 5:6], saf[:, 0:TC], ALU.mult, ALU.add))
        b.on("vector", lambda e: e.scalar_tensor_tensor(
            den[:, 0:TC], zc[:, 0:TC], sm[:, 5:6], sbf[:, 0:TC], ALU.mult, ALU.add))
        b.step()
        b.on("vector", lambda e: e.reciprocal(den[:, 0:TC], den[:, 0:TC]))
        b.step()
        b.on("vector", lambda e: e.tensor_mul(num[:, 0:TC], num[:, 0:TC], den[:, 0:TC]))
        if debug and k == 0:
            d_z = b.dout("d_z", [128, TC], F32)
            d_sa = b.dout("d_sa", [128, 1 + TC], F32)
            d_num = b.dout("d_num", [128, TC], F32)
            d_ramp = b.dout("d_ramp", [128, TC], F32)
            b.step()
            b.dma("sync", d_z[:, :], zc[:, 0:TC])
            b.dma("gpsimd", d_sa[:, :], saf[:, 0:1 + TC])
            b.dma("scalar", d_num[:, :], num[:, 0:TC])
            b.dma("sync", d_ramp[:, :], rampi[:, 0:TC])
        b.step()
        b.dma("sync", zc[:, 0:TC], rTT[k * 128:(k + 1) * 128, :])
        b.step()
        b.on("scalar", lambda e: e.activation(zc[:, 0:TC], zc[:, 0:TC], AF.Sigmoid))
        b.step()
        b.on("vector", lambda e, k=k: e.tensor_mul(rwkv[k], zc[:, 0:TC], num[:, 0:TC]))

    if debug:
        d_rw = b.dout("d_rw", [D, TC], BF16)
        b.step()
        for k in range(DC):
            b.dma(DMA_ENG[k % 3], d_rw[k * 128:(k + 1) * 128, :], rwkv[k])
    out_sub = scr
    def wstore(oc, o, n, stgap):
        b.on("vector", lambda e: e.tensor_copy(out_sub[oc][:, o:o + n], stgap))
    lay.proj_f(wbf, rwkv, TC, D, out_sub)
    lay.ln_resid_stream(out_sub, rwkv, mb, rb, tb_, T=TC, x_in=xT, x_out=xo,
                        xblk=xblk, x_is_bf=True)
    b.step()
    return b




def build_l2a_A(debug=False):
    b = BSP()
    xT = b.din("xT", [D, TC], F32)
    wq_t = b.din("wq_t", [D, D], F32)
    wk_t = b.din("wk_t", [D, D], F32)
    wv_t = b.din("wv_t", [D, D], F32)
    bq_r = b.din("bq_r", [1, D], F32)
    bk_r = b.din("bk_r", [1, D], F32)
    bv_r = b.din("bv_r", [1, D], F32)
    cs16 = b.din("cs16", [TC, 512], F32)
    sn16 = b.din("sn16", [TC, 512], F32)
    qT_o = b.dout("qT", [D, TC], BF16)
    kT_o = b.dout("kT", [D, TC], BF16)
    v_o = b.dout("v", [TC, D], BF16)

    lay = Lay(b, f32_cols=16500, bf_cols=59500, n_ps=6, pt_cols=1024, n_stg=6)
    fa, ba, ps, pt = lay.fa, lay.ba, lay.ps, lay.pt

    mb = fa.alloc(TC); rb = fa.alloc(TC); tb_ = fa.alloc(TC)
    qf = fa.alloc(D); qr = fa.alloc(D); tmp1 = fa.alloc(512)
    cs_f = fa.alloc(512); sn_f = fa.alloc(512)
    bb3 = fa.alloc(3 * D)
    w32 = mb

    xln = ba.alloc8(TC)
    scr = ba.alloc8(TC)
    WR = ba.alloc(8 * D)
    qT_sb = ba.alloc8(TC)
    qrb = ba.alloc(D)
    brow = ba.alloc(D)

    b.step()
    xbf = qT_sb
    for k in range(DC):
        b.dma("gpsimd", xbf[k], xT[k * 128:(k + 1) * 128, :])
    lay.ln(xbf, scr, mb, rb, tb_, T=TC, x_is_bf=True, out_bf=xln)

    for bi, brd in enumerate((bq_r, bk_r, bv_r)):
        b.step()
        b.dma("sync", tmp1[0:1, 0:512], brd[0:1, 0:512])
        b.dma("gpsimd", cs_f[0:1, 0:512], brd[0:1, 512:1024])
        b.step()
        b.on("vector", lambda e: e.tensor_copy(brow[0:1, 0:512], tmp1[0:1, 0:512]))
        b.on("vector", lambda e: e.tensor_copy(brow[0:1, 512:1024], cs_f[0:1, 0:512]))
        b.step()
        def bmm(e):
            r = None
            for h in range(2):
                r = e.matmul(ps[h][:, 0:512], lay.ones_bf[0:1, 0:128],
                             brow[0:1, h * 512:(h + 1) * 512], start=True, stop=True)
            return r
        b.on("tensor", bmm)
        b.step()
        for h in range(2):
            b.on("vector", lambda e, bi=bi, h=h: e.tensor_copy(
                bb3[:, bi * D + h * 512: bi * D + (h + 1) * 512], ps[h][:, 0:512]))

    def loadw(dram):
        b.step()
        for k in range(DC):
            b.dma("gpsimd", WR[:, k * D:(k + 1) * D], dram[k * 128:(k + 1) * 128, :])
        return WR[:, 0:8 * D]

    wbf = loadw(wv_t)
    def vevac_store(tt, o, n, stgap):
        b.on("vector", lambda e, tt=tt, o=o, n=n, stgap=stgap: None)
    jobs = [(tt, ob) for tt in range(16) for ob in range(2)]
    groups = [jobs[i:i + 3] for i in range(0, len(jobs), 3)]
    pend = []
    for gi, g in enumerate(groups):
        banks = [ps[3 * (gi % 2) + i] for i in range(len(g))]
        b.step()
        def mms(e, g=g, banks=banks):
            r = None
            for i, (tt, ob) in enumerate(g):
                for k in range(DC):
                    r = e.matmul(banks[i][:, 0:512],
                                 xln[k][:, tt * 128:(tt + 1) * 128],
                                 wbf[k * D + ob * 512: k * D + ob * 512 + 512] if False else
                                 WR[:, k * D + ob * 512: k * D + ob * 512 + 512],
                                 start=(k == 0), stop=(k == DC - 1))
            return r
        b.on("tensor", mms)
        new_pend = []
        if gi > 0:
            prevg = groups[gi - 1]
            prevbanks = [ps[3 * ((gi - 1) % 2) + i] for i in range(len(prevg))]
            for i, (tt, ob) in enumerate(prevg):
                s = lay.stg_next()[:, 0:512]
                b.on("vector", lambda e, s=s, bank=prevbanks[i], ob=ob: e.scalar_tensor_tensor(
                    s, bank[:, 0:512], 1.0, bb3[:, 2 * D + ob * 512:2 * D + (ob + 1) * 512],
                    ALU.mult, ALU.add))
                new_pend.append((tt, ob, s))
        for (tt, ob, s) in pend:
            b.dma(DMA_ENG[(tt + ob) % 3], v_o[tt * 128:(tt + 1) * 128, ob * 512:(ob + 1) * 512], s)
        pend = new_pend
    b.step()
    lastg = groups[-1]
    lastbanks = [ps[3 * ((len(groups) - 1) % 2) + i] for i in range(len(lastg))]
    new_pend = []
    for i, (tt, ob) in enumerate(lastg):
        s = lay.stg_next()[:, 0:512]
        b.on("vector", lambda e, s=s, bank=lastbanks[i], ob=ob: e.scalar_tensor_tensor(
            s, bank[:, 0:512], 1.0, bb3[:, 2 * D + ob * 512:2 * D + (ob + 1) * 512],
            ALU.mult, ALU.add))
        new_pend.append((tt, ob, s))
    for (tt, ob, s) in pend:
        b.dma(DMA_ENG[(tt + ob) % 3], v_o[tt * 128:(tt + 1) * 128, ob * 512:(ob + 1) * 512], s)
    b.step()
    for (tt, ob, s) in new_pend:
        b.dma(DMA_ENG[(tt + ob) % 3], v_o[tt * 128:(tt + 1) * 128, ob * 512:(ob + 1) * 512], s)
    b.step()

    def qk_path(wdram, bias_off, scale, dst_dram):
        wbf = loadw(wdram)
        for tt in range(16):
            b.step()
            b.dma("sync", cs_f[:, 0:512], cs16[tt * 128:(tt + 1) * 128, :])
            b.dma("gpsimd", sn_f[:, 0:512], sn16[tt * 128:(tt + 1) * 128, :])
            def mmq(e, tt=tt):
                r = None
                for ob in range(2):
                    for k in range(DC):
                        r = e.matmul(ps[ob][:, 0:512],
                                     xln[k][:, tt * 128:(tt + 1) * 128],
                                     WR[:, k * D + ob * 512: k * D + ob * 512 + 512],
                                     start=(k == 0), stop=(k == DC - 1))
                return r
            b.on("tensor", mmq)
            b.step()
            for ob in range(2):
                b.on("vector", lambda e, ob=ob: e.scalar_tensor_tensor(
                    qf[:, ob * 512:(ob + 1) * 512], ps[ob][:, 0:512], 1.0,
                    bb3[:, bias_off + ob * 512:bias_off + (ob + 1) * 512],
                    ALU.mult, ALU.add))
            qe = qf.rearrange("p (h j two) -> p two h j", h=16, j=32, two=2)
            qev = qe[:, 0, :, :]
            qov = qe[:, 1, :, :]
            orr = qr.rearrange("p (h half j) -> p half h j", h=16, half=2, j=32)
            orre = orr[:, 0, :, :]
            orim = orr[:, 1, :, :]
            b.step()
            b.on("vector", lambda e: e.tensor_mul(orre, qev, cs_f[:, 0:512]))
            b.on("vector", lambda e: e.tensor_mul(tmp1[:, 0:512], qov, sn_f[:, 0:512]))
            b.step()
            b.on("vector", lambda e: e.tensor_sub(orre, orre, tmp1[:, 0:512]))
            b.on("vector", lambda e: e.tensor_mul(tmp1[:, 0:512], qov, cs_f[:, 0:512]))
            b.step()
            b.on("vector", lambda e: e.tensor_mul(orim, qev, sn_f[:, 0:512]))
            b.step()
            b.on("vector", lambda e: e.tensor_add(orim, orim, tmp1[:, 0:512]))
            if debug and tt == 0 and bias_off == 0:
                d_qf = b.dout("d_qf", [128, D], F32)
                d_qr = b.dout("d_qr", [128, D], F32)
                b.step()
                b.dma("sync", d_qf[:, :], qf[:, 0:D])
                b.dma("gpsimd", d_qr[:, :], qr[:, 0:D])
            b.step()
            b.on("vector", lambda e, scale=scale: e.tensor_scalar(
                qrb[:, 0:D], qr[:, 0:D], scale, None, ALU.mult))
            b.step()
            def qtr(e):
                r = None
                for c in range(DC):
                    r = e.transpose(pt[:, c * 128:(c + 1) * 128],
                                    qrb[:, c * 128:(c + 1) * 128], lay.ident_bf[:])
                return r
            b.on("tensor", qtr)
            b.step()
            for c in range(DC):
                b.on("vector", lambda e, c=c, tt=tt: e.tensor_copy(
                    qT_sb[c][:, tt * 128:(tt + 1) * 128], pt[:, c * 128:(c + 1) * 128]))
        b.step()
        for c in range(DC):
            b.dma(DMA_ENG[c % 3], dst_dram[c * 128:(c + 1) * 128, :], qT_sb[c])
        b.step()

    qk_path(wq_t, 0, 1.0 / 8.0, qT_o)
    qk_path(wk_t, D, 1.0, kT_o)
    b.step()
    return b




def build_l2a_B(debug=False):
    b = BSP()
    qTi = b.din("qTi", [D, TC], BF16)
    kTf = b.din("kTf", [D, TSEQ], BF16)
    vf = b.din("vf", [TSEQ, D], BF16)
    mtd = b.din("mtd", [128, 16 * 512], F32)
    wo_t = b.din("wo_t", [D, D], F32)
    bo_c = b.din("bo_c", [D, 1], F32)
    at_o = b.dout("attnTi", [D, TC], BF16)

    lay = Lay(b, f32_cols=10400, bf_cols=75800, n_ps=6, pt_cols=1024, n_stg=6)
    fa, ba, ps, pt = lay.fa, lay.ba, lay.ps, lay.pt

    mb = fa.alloc(TC)
    mtd_f = fa.alloc(16 * 512)
    boc = fa.alloc(DC)
    rec2 = fa.alloc(2)
    w32 = mb

    qT_sb = ba.alloc8(TC)
    kpair = ba.alloc(TSEQ)
    vpair = [ba.alloc(128) for _ in range(64)]
    probs = [ba.alloc(1024) for _ in range(2)]
    ytok = [ba.alloc(D) for _ in range(16)]
    yT = ba.alloc8(TC)
    WR = ba.alloc(8 * D)

    b.step()
    for k in range(DC):
        b.dma("gpsimd", qT_sb[k], qTi[k * 128:(k + 1) * 128, :])
        b.dma("sync", boc[0:128, k:k + 1], bo_c[k * 128:(k + 1) * 128, :])
    b.dma("scalar", mtd_f[:, :], mtd[:, :])
    b.step()
    for k in range(DC):
        b.dma("gpsimd", WR[:, k * D:(k + 1) * D], wo_t[k * 128:(k + 1) * 128, :])

    for pair in range(8):
        b.step()
        b.dma("sync", kpair[:, :], kTf[pair * 128:(pair + 1) * 128, :])
        for ch in range(64):
            b.dma(("gpsimd", "scalar")[ch % 2], vpair[ch],
                  vf[ch * 128:(ch + 1) * 128, pair * 128:(pair + 1) * 128])
        for j in range(16):
            ngrp = j + 1
            nfull = ngrp - 1
            qslA = qT_sb[pair][0:64, j * 128:(j + 1) * 128]
            qslB = qT_sb[pair][64:128, j * 128:(j + 1) * 128]

            def emit_pv(e, g, ngrp=ngrp):
                r = None
                pb = probs[g % 2]
                first = (g == 0)
                last = (g == ngrp - 1)
                for h in range(2):
                    for c in range(4):
                        ch = g * 4 + c
                        r = e.matmul(ps[0][:, h * 64:(h + 1) * 64],
                                     pb[:, h * 512 + c * 128:h * 512 + (c + 1) * 128],
                                     vpair[ch][:, h * 64:(h + 1) * 64],
                                     start=(first and c == 0 and h == 0),
                                     stop=(last and c == 3))
                for h in range(2):
                    for c in range(4):
                        r = e.matmul(ps[1][:, h:h + 1],
                                     pb[:, h * 512 + c * 128:h * 512 + (c + 1) * 128],
                                     lay.ones_bf[:, 0:1],
                                     start=(first and c == 0 and h == 0),
                                     stop=(last and c == 3))
                return r

            def emit_mms(e, g, qslA=qslA, qslB=qslB):
                r = None
                bank = lay.psL[1 + (g % 2)]
                for h, qsl in ((0, qslA), (1, qslB)):
                    for c in range(4):
                        kcol = (g * 4 + c) * 128
                        r = e.matmul(bank[:, h * 512 + c * 128:h * 512 + (c + 1) * 128],
                                     kpair[h * 64:(h + 1) * 64, kcol:kcol + 128],
                                     qsl, start=True, stop=True)
                return r

            for i in range(nfull + 2):
                b.step()
                def work(e, i=i, nfull=nfull, emit_mms=emit_mms, emit_pv=emit_pv):
                    r = None
                    if i < nfull:
                        r = emit_mms(e, i)
                    if 0 <= i - 2 < nfull:
                        r = emit_pv(e, i - 2)
                    return r
                b.on("tensor", work)
                ge = i - 1
                if 0 <= ge < nfull:
                    b.on("scalar", lambda e, ge=ge: e.activation(
                        probs[ge % 2], lay.psL[1 + (ge % 2)][:, 0:1024], AF.Exp))
            g = ngrp - 1
            b.step()
            b.on("tensor", lambda e, g=g, emit_mms=emit_mms: emit_mms(e, g))
            b.step()
            for h in range(2):
                b.on("vector", lambda e, h=h, g=g, j=j: e.tensor_tensor(
                    lay.psL[1 + (g % 2)][:, h * 512:(h + 1) * 512],
                    lay.psL[1 + (g % 2)][:, h * 512:(h + 1) * 512],
                    mtd_f[:, j * 512:(j + 1) * 512], ALU.add))
            b.step()
            b.on("scalar", lambda e, g=g: e.activation(
                probs[g % 2], lay.psL[1 + (g % 2)][:, 0:1024], AF.Exp))
            b.step()
            b.on("tensor", lambda e, g=g, emit_pv=emit_pv: emit_pv(e, g))
            b.step()
            b.on("vector", lambda e: e.tensor_copy(rec2[:, 0:2], ps[1][:, 0:2]))
            b.step()
            b.on("vector", lambda e: e.reciprocal(rec2[:, 0:2], rec2[:, 0:2]))
            b.step()
            for h in range(2):
                b.on("vector", lambda e, h=h, j=j, pair=pair: e.tensor_scalar(
                    ytok[j][:, pair * 128 + h * 64: pair * 128 + (h + 1) * 64],
                    ps[0][:, h * 64:(h + 1) * 64], rec2[:, h:h + 1], None, ALU.mult))
            b.step()

    for j in range(16):
        b.step()
        def ytr(e, j=j):
            r = None
            for c in range(DC):
                r = e.transpose(pt[:, c * 128:(c + 1) * 128],
                                ytok[j][:, c * 128:(c + 1) * 128], lay.ident_bf[:])
            return r
        b.on("tensor", ytr)
        b.step()
        for c in range(DC):
            b.on("vector", lambda e, c=c, j=j: e.tensor_copy(
                yT[c][:, j * 128:(j + 1) * 128], pt[:, c * 128:(c + 1) * 128]))
    boc_list = [boc[0:128, k:k + 1] for k in range(DC)]
    lay.proj_f(WR[:, 0:8 * D], yT, TC, D, at_o, bias=boc_list, act=AF.Identity)
    b.step()
    return b




def build_sandwich():
    b = BSP()
    xT = b.din("xT", [D, TC], F32)
    mixT = b.din("mixT", [D, TC], BF16)
    xo = b.dout("xoT", [D, TC], F32)
    lay = Lay(b, f32_cols=10300, bf_cols=33000, n_ps=6, pt_cols=0, n_stg=2)
    fa, ba = lay.fa, lay.ba
    mb = fa.alloc(TC); rb = fa.alloc(TC); tb_ = fa.alloc(TC)
    xblk = ([mb[:, i * 512:(i + 1) * 512] for i in range(0)]
            + [fa.alloc(512) for _ in range(8)])
    mix = ba.alloc8(TC)
    scr = ba.alloc8(TC)
    b.step()
    for k in range(DC):
        b.dma(DMA_ENG[k % 3], mix[k], mixT[k * 128:(k + 1) * 128, :])
    lay.ln_resid_stream(mix, scr, mb, rb, tb_, T=TC, x_in=xT, x_out=xo,
                        xblk=xblk, x_is_bf=True)
    b.step()
    return b


def build_l3a_A():
    b = BSP()
    xT = b.din("xT", [D, TC], F32)
    wq = b.din("wq", [D, D], F32)
    wv = b.din("wv", [D, D], F32)
    bq_r = b.din("bq_r", [1, D], F32)
    bv_r = b.din("bv_r", [1, D], F32)
    qsm_o = b.dout("qsmT", [D, TC], BF16)
    v_o = b.dout("vtok", [TC, D], BF16)
    ml_o = b.dout("mloc", [D, 1], F32)

    lay = Lay(b, f32_cols=16000, bf_cols=60000, n_ps=6, pt_cols=1024, n_stg=6)
    fa, ba, ps, pt = lay.fa, lay.ba, lay.ps, lay.pt
    SCONST = float(64.0 ** 0.5) ** (-0.25)

    mb = fa.alloc(TC); rb = fa.alloc(TC); tb_ = fa.alloc(TC)
    qf = fa.alloc(D)
    bb2 = fa.alloc(2 * D)
    sm = fa.alloc(8)
    w32 = mb

    xln = ba.alloc8(TC)
    scr = ba.alloc8(TC)
    WR = ba.alloc(8 * D)
    qT_sb = ba.alloc8(TC)
    qrb = ba.alloc(D)
    brow = ba.alloc(D)

    b.step()
    xbf = qT_sb
    for k in range(DC):
        b.dma("gpsimd", xbf[k], xT[k * 128:(k + 1) * 128, :])
    lay.ln(xbf, scr, mb, rb, tb_, T=TC, x_is_bf=True, out_bf=xln)

    for bi, brd in enumerate((bq_r, bv_r)):
        b.step()
        b.dma("sync", qf[0:1, 0:D], brd[0:1, :])
        b.step()
        b.on("vector", lambda e: e.tensor_copy(brow[0:1, 0:D], qf[0:1, 0:D]))
        b.step()
        def bmm(e):
            r = None
            for h in range(2):
                r = e.matmul(ps[h][:, 0:512], lay.ones_bf[0:1, 0:128],
                             brow[0:1, h * 512:(h + 1) * 512], start=True, stop=True)
            return r
        b.on("tensor", bmm)
        b.step()
        for h in range(2):
            b.on("vector", lambda e, bi=bi, h=h: e.tensor_copy(
                bb2[:, bi * D + h * 512: bi * D + (h + 1) * 512], ps[h][:, 0:512]))

    def loadw(dram):
        b.step()
        for k in range(DC):
            b.dma("gpsimd", WR[:, k * D:(k + 1) * D], dram[k * 128:(k + 1) * 128, :])
        return WR[:, 0:8 * D]

    wbf = loadw(wv)
    jobs = [(tt, ob) for tt in range(16) for ob in range(2)]
    groups = [jobs[i:i + 3] for i in range(0, len(jobs), 3)]
    pend = []
    for gi, g in enumerate(groups):
        banks = [ps[3 * (gi % 2) + i] for i in range(len(g))]
        b.step()
        def mms(e, g=g, banks=banks):
            r = None
            for i, (tt, ob) in enumerate(g):
                for k in range(DC):
                    r = e.matmul(banks[i][:, 0:512],
                                 xln[k][:, tt * 128:(tt + 1) * 128],
                                 WR[:, k * D + ob * 512: k * D + ob * 512 + 512],
                                 start=(k == 0), stop=(k == DC - 1))
            return r
        b.on("tensor", mms)
        new_pend = []
        if gi > 0:
            prevg = groups[gi - 1]
            prevbanks = [ps[3 * ((gi - 1) % 2) + i] for i in range(len(prevg))]
            for i, (tt, ob) in enumerate(prevg):
                s = lay.stg_next()[:, 0:512]
                b.on("vector", lambda e, s=s, bank=prevbanks[i], ob=ob: e.scalar_tensor_tensor(
                    s, bank[:, 0:512], 1.0, bb2[:, D + ob * 512:D + (ob + 1) * 512],
                    ALU.mult, ALU.add))
                new_pend.append((tt, ob, s))
        for (tt, ob, s) in pend:
            b.dma(DMA_ENG[(tt + ob) % 3], v_o[tt * 128:(tt + 1) * 128, ob * 512:(ob + 1) * 512], s)
        pend = new_pend
    b.step()
    lastg = groups[-1]
    lastbanks = [ps[3 * ((len(groups) - 1) % 2) + i] for i in range(len(lastg))]
    new_pend = []
    for i, (tt, ob) in enumerate(lastg):
        s = lay.stg_next()[:, 0:512]
        b.on("vector", lambda e, s=s, bank=lastbanks[i], ob=ob: e.scalar_tensor_tensor(
            s, bank[:, 0:512], 1.0, bb2[:, D + ob * 512:D + (ob + 1) * 512],
            ALU.mult, ALU.add))
        new_pend.append((tt, ob, s))
    for (tt, ob, s) in pend:
        b.dma(DMA_ENG[(tt + ob) % 3], v_o[tt * 128:(tt + 1) * 128, ob * 512:(ob + 1) * 512], s)
    b.step()
    for (tt, ob, s) in new_pend:
        b.dma(DMA_ENG[(tt + ob) % 3], v_o[tt * 128:(tt + 1) * 128, ob * 512:(ob + 1) * 512], s)
    b.step()

    wbf = loadw(wq)
    for tt in range(16):
        b.step()
        def mmq(e, tt=tt):
            r = None
            for ob in range(2):
                for k in range(DC):
                    r = e.matmul(ps[ob][:, 0:512],
                                 xln[k][:, tt * 128:(tt + 1) * 128],
                                 WR[:, k * D + ob * 512: k * D + ob * 512 + 512],
                                 start=(k == 0), stop=(k == DC - 1))
            return r
        b.on("tensor", mmq)
        b.step()
        for ob in range(2):
            b.on("vector", lambda e, ob=ob: e.scalar_tensor_tensor(
                qf[:, ob * 512:(ob + 1) * 512], ps[ob][:, 0:512], 1.0,
                bb2[:, ob * 512:(ob + 1) * 512], ALU.mult, ALU.add))
        for h in range(16):
            hs = qf[:, h * 64:(h + 1) * 64]
            b.step()
            b.on("vector", lambda e, hs=hs: e.reduce_max(sm[:, 0:1], hs, AX.X))
            b.step()
            b.on("vector", lambda e: e.tensor_scalar(sm[:, 1:2], sm[:, 0:1], -1.0, None, ALU.mult))
            b.step()
            b.on("scalar", lambda e, hs=hs: e.activation(hs, hs, AF.Exp, bias=sm[:, 1:2],
                                                         accum_out=sm[:, 2:3]))
            b.step()
            b.on("vector", lambda e: e.reciprocal(sm[:, 2:3], sm[:, 2:3]))
            b.step()
            b.on("vector", lambda e, hs=hs: e.tensor_scalar(hs, hs, sm[:, 2:3],
                                                            SCONST, ALU.mult, ALU.mult))
        b.step()
        b.on("vector", lambda e: e.tensor_copy(qrb[:, 0:D], qf[:, 0:D]))
        b.step()
        def qtr(e):
            r = None
            for c in range(DC):
                r = e.transpose(pt[:, c * 128:(c + 1) * 128],
                                qrb[:, c * 128:(c + 1) * 128], lay.ident_bf[:])
            return r
        b.on("tensor", qtr)
        b.step()
        for c in range(DC):
            b.on("vector", lambda e, c=c, tt=tt: e.tensor_copy(
                qT_sb[c][:, tt * 128:(tt + 1) * 128], pt[:, c * 128:(c + 1) * 128]))
    b.step()
    for c in range(DC):
        b.dma(DMA_ENG[c % 3], qsm_o[c * 128:(c + 1) * 128, :], qT_sb[c])
    b.step()
    for c in range(DC):
        b.on("vector", lambda e, c=c: e.reduce_max(sm[:, 3:4], qT_sb[c], AX.X))
        b.on("vector", lambda e, c=c: e.tensor_copy(mb[:, c:c + 1], sm[:, 3:4])) if False else None
    b.step()
    for c in range(DC):
        b.on("vector", lambda e, c=c: e.reduce_max(mb[:, c:c + 1], qT_sb[c], AX.X))
    b.step()
    for c in range(DC):
        b.dma(DMA_ENG[c % 3], ml_o[c * 128:(c + 1) * 128, 0:1], mb[:, c:c + 1])
    b.step()
    return b


def build_l3a_B():
    b = BSP()
    qsmT = b.din("qsmT", [D, TC], BF16)
    vtok = b.din("vtok", [TC, D], BF16)
    ml4 = b.din("mloc4", [D, 4], F32)
    s_o = b.dout("spart", [D, 1], F32)
    cx_o = b.dout("ctxp", [D, 64], F32)

    lay = Lay(b, f32_cols=8000, bf_cols=70000, n_ps=6, pt_cols=1024, n_stg=6)
    fa, ba, ps, pt = lay.fa, lay.ba, lay.ps, lay.pt

    m4 = fa.alloc(4 * DC)
    sm = fa.alloc(8)
    cstg = fa.alloc(64)
    qT = ba.alloc8(TC)
    ek = ba.alloc8(TC)
    ektok = [ba.alloc(D) for _ in range(16)]
    vt = [ba.alloc(D) for _ in range(16)]

    b.step()
    for k in range(DC):
        b.dma("gpsimd", qT[k], qsmT[k * 128:(k + 1) * 128, :])
        b.dma("sync", m4[0:128, 4 * k:4 * k + 4], ml4[k * 128:(k + 1) * 128, :])
    for tt in range(16):
        b.dma("scalar", vt[tt], vtok[tt * 128:(tt + 1) * 128, :])
    for k in range(DC):
        b.step()
        b.on("vector", lambda e, k=k: e.reduce_max(sm[:, 0:1], m4[0:128, 4 * k:4 * k + 4], AX.X))
        b.step()
        b.on("vector", lambda e: e.tensor_scalar(sm[:, 1:2], sm[:, 0:1], -1.0, None, ALU.mult))
        b.step()
        b.on("scalar", lambda e, k=k: e.activation(ek[k], qT[k], AF.Exp, bias=sm[:, 1:2]))
        b.step()
        b.on("vector", lambda e, k=k: e.reduce_sum(sm[:, 2:3], ek[k], AX.X))
        b.step()
        b.dma("sync", s_o[k * 128:(k + 1) * 128, 0:1], sm[:, 2:3])
    for tt in range(16):
        b.step()
        def etr(e, tt=tt):
            r = None
            for c in range(DC):
                r = e.transpose(pt[:, c * 128:(c + 1) * 128],
                                ek[c][:, tt * 128:(tt + 1) * 128], lay.ident_bf[:])
            return r
        b.on("tensor", etr)
        b.step()
        for c in range(DC):
            b.on("vector", lambda e, c=c, tt=tt: e.tensor_copy(
                ektok[tt][:, c * 128:(c + 1) * 128], pt[:, c * 128:(c + 1) * 128]))
    for p2 in range(8):
        b.step()
        def cmm(e, p2=p2):
            r = None
            for hh in range(2):
                h = 2 * p2 + hh
                for tt in range(16):
                    r = e.matmul(ps[hh][0:64, 0:64],
                                 ektok[tt][:, h * 64:(h + 1) * 64],
                                 vt[tt][:, h * 64:(h + 1) * 64],
                                 start=(tt == 0), stop=(tt == 15))
            return r
        b.on("tensor", cmm)
        b.step()
        for hh in range(2):
            b.on("vector", lambda e, hh=hh: e.tensor_copy(cstg[0:64, 0:64], ps[hh][0:64, 0:64])) \
                if False else None
        for hh in range(2):
            b.on("vector" if hh == 0 else "scalar",
                 (lambda e, hh=hh: e.tensor_copy(cstg[0:64, 0:64], ps[hh][0:64, 0:64])) if hh == 0
                 else (lambda e, hh=hh: e.copy(cstg[64:128, 0:64], ps[hh][0:64, 0:64])))
        b.step()
        b.dma("sync", cx_o[(2 * p2) * 64:(2 * p2) * 64 + 64, :], cstg[0:64, 0:64])
        b.dma("gpsimd", cx_o[(2 * p2 + 1) * 64:(2 * p2 + 1) * 64 + 64, :], cstg[64:128, 0:64])
    b.step()
    return b


def build_l3a_C():
    b = BSP()
    xT = b.din("xT", [D, TC], F32)
    qsmT = b.din("qsmT", [D, TC], BF16)
    cx4 = b.din("ctxp4", [D, 4 * 64], F32)
    s4 = b.din("spart4", [D, 4], F32)
    ml4 = b.din("mloc4", [D, 4], F32)
    wo = b.din("wo", [D, D], F32)
    bo_c = b.din("bo_c", [D, 1], F32)
    xo = b.dout("xoT", [D, TC], F32)

    lay = Lay(b, f32_cols=14000, bf_cols=60000, n_ps=6, pt_cols=0, n_stg=6)
    fa, ba, ps = lay.fa, lay.ba, lay.ps
    SCONST = float(64.0 ** 0.5) ** (-0.25)

    mb = fa.alloc(TC); rb = fa.alloc(TC); tb_ = fa.alloc(TC)
    cxf = fa.alloc(4 * 64)
    s4f = fa.alloc(4 * DC)
    m4f = fa.alloc(4 * DC)
    boc = fa.alloc(DC)
    sm = fa.alloc(8)
    xblk = ([rb[:, i * 512:(i + 1) * 512] for i in range(0)]
            + [fa.alloc(512) for _ in range(8)])
    w32 = mb

    qT = ba.alloc8(TC)
    scr = ba.alloc8(TC)
    attnT = ba.alloc8(TC)
    ctxbf = ba.alloc(64)
    WR = ba.alloc(8 * D)

    b.step()
    for k in range(DC):
        b.dma("gpsimd", qT[k], qsmT[k * 128:(k + 1) * 128, :])
        b.dma("sync", s4f[0:128, 4 * k:4 * k + 4], s4[k * 128:(k + 1) * 128, :])
        b.dma("sync", m4f[0:128, 4 * k:4 * k + 4], ml4[k * 128:(k + 1) * 128, :])
        b.dma("scalar", boc[0:128, k:k + 1], bo_c[k * 128:(k + 1) * 128, :])
    b.step()
    for k in range(DC):
        b.dma("gpsimd", WR[:, k * D:(k + 1) * D], wo[k * 128:(k + 1) * 128, :])

    for k in range(DC):
        b.step()
        b.dma("sync", cxf[:, 0:256], cx4[k * 128:(k + 1) * 128, :].rearrange("p (r e) -> p (r e)", r=4) if False else cx4[k * 128:(k + 1) * 128, :])
        b.step()
        b.on("vector", lambda e, k=k: e.reduce_sum(sm[:, 0:1], s4f[0:128, 4 * k:4 * k + 4], AX.X))
        b.step()
        b.on("vector", lambda e: e.reciprocal(sm[:, 0:1], sm[:, 0:1]))
        b.step()
        b.on("vector", lambda e: e.tensor_add(cxf[:, 0:64], cxf[:, 0:64], cxf[:, 64:128]))
        b.step()
        b.on("vector", lambda e: e.tensor_add(cxf[:, 128:192], cxf[:, 128:192], cxf[:, 192:256]))
        b.step()
        b.on("vector", lambda e: e.tensor_add(cxf[:, 0:64], cxf[:, 0:64], cxf[:, 128:192]))
        b.step()
        b.on("vector", lambda e: e.tensor_scalar(cxf[:, 0:64], cxf[:, 0:64], sm[:, 0:1],
                                                 SCONST, ALU.mult, ALU.mult))
        b.step()
        b.on("vector", lambda e: e.tensor_copy(ctxbf[:, 0:64], cxf[:, 0:64]))
        b.step()
        def amm(e, k=k):
            r = None
            for blk in range(4):
                for hh in range(2):
                    r = e.matmul(ps[blk][hh * 64:(hh + 1) * 64, 0:512],
                                 ctxbf[hh * 64:(hh + 1) * 64, 0:64],
                                 qT[k][hh * 64:(hh + 1) * 64, blk * 512:(blk + 1) * 512],
                                 start=True, stop=True)
            return r
        b.on("tensor", amm)
        b.step()
        for blk in range(4):
            b.on("vector", lambda e, k=k, blk=blk: e.tensor_copy(
                attnT[k][:, blk * 512:(blk + 1) * 512], ps[blk][:, 0:512]))
    mixo = scr
    boc_list = [boc[0:128, k:k + 1] for k in range(DC)]
    lay.proj_f(WR[:, 0:8 * D], attnT, TC, D, mixo, bias=boc_list, act=AF.Identity)
    xsc = qT
    lay.ln_resid_stream(mixo, xsc, mb, rb, tb_, T=TC, x_in=xT, x_out=xo,
                        xblk=xblk, x_is_bf=True)
    b.step()
    return b


def build_l3f_1():
    b = BSP()
    xT = b.din("xT", [D, TC], F32)
    g1 = b.din("g1", [D, 1], F32)
    b1 = b.din("b1c", [D, 1], F32)
    wq = b.din("wq", [D, D], F32)
    wk = b.din("wk", [D, D], F32)
    wv = b.din("wv", [D, D], F32)
    qT_o = b.dout("qT", [D, TC], BF16)
    kT_o = b.dout("kT", [D, TC], BF16)
    v_o = b.dout("v", [TC, D], BF16)
    m0_o = b.dout("m0T", [D, TC], BF16)

    lay = Lay(b, f32_cols=13000, bf_cols=59500, n_ps=6, pt_cols=0, n_stg=6)
    fa, ba, ps = lay.fa, lay.ba, lay.ps

    mb = fa.alloc(TC); rb = fa.alloc(TC); tb_ = fa.alloc(TC)
    gb = fa.alloc(2 * DC)
    w32 = mb

    m0 = ba.alloc8(TC)
    scr = ba.alloc8(TC)
    xbf = ba.alloc8(TC)
    h = xbf
    WR = ba.alloc(8 * D)

    b.step()
    for k in range(DC):
        b.dma("gpsimd", xbf[k], xT[k * 128:(k + 1) * 128, :])
        b.dma("sync", gb[0:128, k:k + 1], g1[k * 128:(k + 1) * 128, :])
        b.dma("sync", gb[0:128, DC + k:DC + k + 1], b1[k * 128:(k + 1) * 128, :])
    lay.ln(xbf, scr, mb, rb, tb_, T=TC, x_is_bf=True, out_bf=m0)
    gl = [gb[0:128, k:k + 1] for k in range(DC)]
    bl = [gb[0:128, DC + k:DC + k + 1] for k in range(DC)]
    lay.ln(m0, scr, mb, rb, tb_, T=TC, x_is_bf=True, out_bf=h, gain_bias=(gl, bl))
    b.step()
    for k in range(DC):
        b.dma(DMA_ENG[k % 3], m0_o[k * 128:(k + 1) * 128, :], m0[k])

    def loadw(dram):
        b.step()
        for k in range(DC):
            b.dma("gpsimd", WR[:, k * D:(k + 1) * D], dram[k * 128:(k + 1) * 128, :])
        return WR[:, 0:8 * D]

    wbf = loadw(wq)
    lay.proj_f(wbf, h, TC, D, qT_o, scale=1.0 / 8.0)
    wbf = loadw(wk)
    lay.proj_f(wbf, h, TC, D, kT_o)
    wbf = loadw(wv)
    lay.proj_t(wbf, h, TC, D, v_o)
    b.step()
    return b


def build_l3f_2():
    b = BSP()
    qT_i = b.din("qT", [D, TC], BF16)
    kTf = b.din("kTf", [D, TSEQ], BF16)
    vf = b.din("vf", [TSEQ, D], BF16)
    m0T = b.din("m0T", [D, TC], BF16)
    wo = b.din("wo", [D, D], F32)
    xi_o = b.dout("xintT", [D, TC], BF16)

    lay = Lay(b, f32_cols=5200, bf_cols=79900, n_ps=6, pt_cols=1024, n_stg=6)
    fa, ba, ps, pt = lay.fa, lay.ba, lay.ps, lay.pt

    mb = fa.alloc(TC)
    rec2 = fa.alloc(2)
    w32 = mb

    m0blk = [ba.alloc(512) for _ in range(DC)]
    qT_sb = ba.alloc8(TC)
    kpair = ba.alloc(TSEQ)
    vpair = [ba.alloc(128) for _ in range(64)]
    probs = [ba.alloc(1024) for _ in range(2)]
    yT = ba.alloc8(TC)
    WR = ba.alloc(8 * D)
    mk = ba.mark()
    ytok = [ba.alloc(D) for _ in range(16)]

    b.step()
    for k in range(DC):
        b.dma("gpsimd", qT_sb[k], qT_i[k * 128:(k + 1) * 128, :])
    b.step()
    for k in range(DC):
        b.dma("gpsimd", WR[:, k * D:(k + 1) * D], wo[k * 128:(k + 1) * 128, :])

    NG = 16
    for pair in range(8):
        b.step()
        b.dma("sync", kpair[:, :], kTf[pair * 128:(pair + 1) * 128, :])
        for ch in range(64):
            b.dma(("gpsimd", "scalar")[ch % 2], vpair[ch],
                  vf[ch * 128:(ch + 1) * 128, pair * 128:(pair + 1) * 128])
        for j in range(16):
            qslA = qT_sb[pair][0:64, j * 128:(j + 1) * 128]
            qslB = qT_sb[pair][64:128, j * 128:(j + 1) * 128]
            for i in range(NG + 2):
                b.step()
                def work(e, i=i, qslA=qslA, qslB=qslB):
                    r = None
                    if i < NG:
                        bank = lay.psL[1 + (i % 2)]
                        for h, qsl in ((0, qslA), (1, qslB)):
                            for c in range(4):
                                kcol = (i * 4 + c) * 128
                                r = e.matmul(bank[:, h * 512 + c * 128:h * 512 + (c + 1) * 128],
                                             kpair[h * 64:(h + 1) * 64, kcol:kcol + 128],
                                             qsl, start=True, stop=True)
                    g = i - 2
                    if 0 <= g:
                        pb = probs[g % 2]
                        first = (g == 0)
                        last = (g == NG - 1)
                        for h in range(2):
                            for c in range(4):
                                ch = g * 4 + c
                                r = e.matmul(ps[0][:, h * 64:(h + 1) * 64],
                                             pb[:, h * 512 + c * 128:h * 512 + (c + 1) * 128],
                                             vpair[ch][:, h * 64:(h + 1) * 64],
                                             start=(first and c == 0 and h == 0),
                                             stop=(last and c == 3))
                        for h in range(2):
                            for c in range(4):
                                r = e.matmul(ps[1][:, h:h + 1],
                                             pb[:, h * 512 + c * 128:h * 512 + (c + 1) * 128],
                                             lay.ones_bf[:, 0:1],
                                             start=(first and c == 0 and h == 0),
                                             stop=(last and c == 3))
                    return r
                b.on("tensor", work)
                ge = i - 1
                if 0 <= ge < NG:
                    b.on("scalar", lambda e, ge=ge: e.activation(
                        probs[ge % 2], lay.psL[1 + (ge % 2)][:, 0:1024], AF.Exp))
            b.step()
            b.on("vector", lambda e: e.tensor_copy(rec2[:, 0:2], ps[1][:, 0:2]))
            b.step()
            b.on("vector", lambda e: e.reciprocal(rec2[:, 0:2], rec2[:, 0:2]))
            b.step()
            for h in range(2):
                b.on("vector", lambda e, h=h, j=j, pair=pair: e.tensor_scalar(
                    ytok[j][:, pair * 128 + h * 64: pair * 128 + (h + 1) * 64],
                    ps[0][:, h * 64:(h + 1) * 64], rec2[:, h:h + 1], None, ALU.mult))
            b.step()

    for j in range(16):
        b.step()
        def ytr(e, j=j):
            r = None
            for c in range(DC):
                r = e.transpose(pt[:, c * 128:(c + 1) * 128],
                                ytok[j][:, c * 128:(c + 1) * 128], lay.ident_bf[:])
            return r
        b.on("tensor", ytr)
        b.step()
        for c in range(DC):
            b.on("vector", lambda e, c=c, j=j: e.tensor_copy(
                yT[c][:, j * 128:(j + 1) * 128], pt[:, c * 128:(c + 1) * 128]))
    ba.reset(mk)
    xint8 = ba.alloc8(TC)
    lay.proj_f(WR[:, 0:8 * D], yT, TC, D, xint8)
    for blk in range(4):
        b.step()
        for k in range(DC):
            b.dma(DMA_ENG[k % 3], m0blk[k][:, 0:512],
                  m0T[k * 128:(k + 1) * 128, blk * 512:(blk + 1) * 512])
        b.step()
        for k in range(DC):
            b.on("vector", lambda e, k=k, blk=blk: e.tensor_add(
                xint8[k][:, blk * 512:(blk + 1) * 512],
                xint8[k][:, blk * 512:(blk + 1) * 512], m0blk[k][:, 0:512]))
        b.step()
        for k in range(DC):
            b.dma(DMA_ENG[k % 3], xi_o[k * 128:(k + 1) * 128, blk * 512:(blk + 1) * 512],
                  xint8[k][:, blk * 512:(blk + 1) * 512])
    b.step()
    return b


def build_l3f_3(debug=False):
    b = BSP()
    xT = b.din("xT", [D, TC], F32)
    xintT = b.din("xintT", [D, TC], BF16)
    g2 = b.din("g2", [D, 1], F32)
    b2c = b.din("b2c", [D, 1], F32)
    wg = b.din("wg", [D, 8], F32)
    w1f = b.din("w1f", [8 * D, 4096], F32)
    b1f = b.din("b1f", [8 * 4096, 1], F32)
    w2f = b.din("w2f", [8 * 4096, D], F32)
    b2f = b.din("b2f", [8 * D, 1], F32)
    xo = b.dout("xoT", [D, TC], F32)

    lay = Lay(b, f32_cols=8600, bf_cols=82500, n_ps=6, pt_cols=1024, n_stg=6)
    fa, ba, ps, pt = lay.fa, lay.ba, lay.ps, lay.pt

    mb = fa.alloc(TC); rb = fa.alloc(TC); tb_ = fa.alloc(TC)
    gb = fa.alloc(2 * DC)
    rwbc = fa.alloc(TC)
    b1s = fa.alloc(16 * 2)
    b2s = fa.alloc(DC)
    ltokf = [fa.alloc(8) for _ in range(16)]
    rwtok = [fa.alloc(8) for _ in range(16)]
    m1 = fa.alloc(1); nm1 = fa.alloc(1); m2 = fa.alloc(1); den = fa.alloc(1)
    msk = fa.alloc(8)
    xblk = [tb_[:, i * 512:(i + 1) * 512] for i in range(4)] + \
           [rwbc[:, i * 512:(i + 1) * 512] for i in range(4)]
    rrow = tb_[0:1, 0:TC]
    w32 = mb

    m_bf = ba.alloc8(TC)
    acc = ba.alloc8(TC)
    WR = ba.alloc(2 * 16 * D)
    scr = ba.alloc8(TC)
    h1h = [scr[i // 4][:, (i % 4) * 512:(i % 4 + 1) * 512] for i in range(16)]
    yeac = [scr[4 + i // 4][:, (i % 4) * 512:(i % 4 + 1) * 512] for i in range(DC)]
    stg2 = [scr[6][:, i * 512:(i + 1) * 512] for i in range(3)]
    rrow_bf = scr[7][0:1, 0:512]
    lg = rb

    b.step()
    xibf = acc
    for k in range(DC):
        b.dma("gpsimd", xibf[k], xintT[k * 128:(k + 1) * 128, :])
        b.dma("sync", gb[0:128, k:k + 1], g2[k * 128:(k + 1) * 128, :])
        b.dma("sync", gb[0:128, DC + k:DC + k + 1], b2c[k * 128:(k + 1) * 128, :])
    gl = [gb[0:128, k:k + 1] for k in range(DC)]
    bl = [gb[0:128, DC + k:DC + k + 1] for k in range(DC)]
    lay.ln(xibf, scr, mb, rb, tb_, T=TC, x_is_bf=True, out_bf=m_bf,
           gain_bias=(gl, bl))

    b.step()
    for k in range(DC):
        b.dma("gpsimd", stg2[1][:, k * 8:(k + 1) * 8], wg[k * 128:(k + 1) * 128, :])
    b.step()
    def mm_router(e):
        r = None
        for blk in range(4):
            for k in range(DC):
                r = e.matmul(ps[blk][0:8, 0:512], stg2[1][:, k * 8:(k + 1) * 8],
                             m_bf[k][:, blk * 512:(blk + 1) * 512],
                             start=(k == 0), stop=(k == DC - 1))
        return r
    b.on("tensor", mm_router)
    b.step()
    for blk in range(4):
        b.on("vector", lambda e, blk=blk: e.tensor_copy(
            lg[0:8, blk * 512:(blk + 1) * 512], ps[blk][0:8, 0:512]))
    for tt in range(16):
        b.step()
        b.on("tensor", lambda e, tt=tt: e.transpose(
            ps[4][:, 0:8], lg[0:8, tt * 128:(tt + 1) * 128], lay.ident_f32[0:8, 0:8]))
        b.step()
        b.on("vector", lambda e, tt=tt: e.tensor_copy(ltokf[tt][0:128, 0:8], ps[4][:, 0:8]))
    BIGNEG = -1.0e30
    for tt in range(16):
        lt = ltokf[tt][0:128, 0:8]
        rw_ = rwtok[tt][0:128, 0:8]
        b.step()
        b.on("vector", lambda e, lt=lt: e.reduce_max(m1[:, 0:1], lt, AX.X))
        b.step()
        b.on("vector", lambda e: e.tensor_scalar(nm1[:, 0:1], m1[:, 0:1], -1.0, None, ALU.mult))
        b.on("vector", lambda e, lt=lt: e.tensor_scalar(msk[:, 0:8], lt, m1[:, 0:1], None, ALU.is_ge))
        b.step()
        b.on("vector", lambda e, lt=lt: e.scalar_tensor_tensor(
            msk[:, 0:8], msk[:, 0:8], BIGNEG, lt, ALU.mult, ALU.add))
        b.step()
        b.on("vector", lambda e: e.reduce_max(m2[:, 0:1], msk[:, 0:8], AX.X))
        b.step()
        b.on("vector", lambda e, lt=lt: e.tensor_scalar(msk[:, 0:8], lt, m2[:, 0:1], None, ALU.is_ge))
        b.step()
        b.on("scalar", lambda e: e.activation(den[:, 0:1], m2[:, 0:1], AF.Exp, bias=nm1[:, 0:1]))
        b.step()
        b.on("vector", lambda e: e.tensor_scalar(den[:, 0:1], den[:, 0:1], 1.0, None, ALU.add))
        b.step()
        b.on("vector", lambda e: e.reciprocal(den[:, 0:1], den[:, 0:1]))
        b.step()
        b.on("scalar", lambda e, lt=lt, rw_=rw_: e.activation(rw_, lt, AF.Exp, bias=nm1[:, 0:1]))
        b.step()
        b.on("vector", lambda e, rw_=rw_: e.tensor_mul(rw_, rw_, msk[:, 0:8]))
        b.step()
        b.on("vector", lambda e, rw_=rw_: e.tensor_scalar(rw_, rw_, den[:, 0:1], None, ALU.mult))
    for tt in range(16):
        b.step()
        b.on("tensor", lambda e, tt=tt: e.transpose(
            ps[4][0:8, 0:128], rwtok[tt][0:128, 0:8], lay.ident_f32[:, :]))
        b.step()
        b.on("vector", lambda e, tt=tt: e.tensor_copy(
            lg[0:8, tt * 128:(tt + 1) * 128], ps[4][0:8, 0:128]))

    if debug:
        d_m = b.dout("d_m", [D, TC], BF16)
        d_lg = b.dout("d_lg", [8, TC], F32)
        b.step()
        for k in range(DC):
            b.dma(DMA_ENG[k % 3], d_m[k * 128:(k + 1) * 128, :], m_bf[k])
        b.dma("sync", d_lg[:, :], lg[0:8, 0:TC])
    b.step()
    for k in range(DC):
        b.on("vector", lambda e, k=k: e.memset(acc[k], 0.0))
    W1h = WR[:, 0:16 * D]
    W2h = WR[:, 16 * D:32 * D]
    for ex in range(8):
        b.step()
        b.dma("gpsimd", rrow[0:1, 0:TC], lg[ex:ex + 1, 0:TC])
        for k in range(DC):
            b.dma("sync", b2s[0:128, k:k + 1], b2f[ex * D + k * 128: ex * D + (k + 1) * 128, :])
        b.step()
        b.on("vector", lambda e: e.tensor_copy(rrow_bf[0:1, 0:512], rrow[0:1, 0:512]))
        b.on("vector", lambda e: e.tensor_copy(h1h[1][0:1, 0:512], rrow[0:1, 512:1024]))
        b.on("vector", lambda e: e.tensor_copy(h1h[2][0:1, 0:512], rrow[0:1, 1024:1536]))
        b.on("vector", lambda e: e.tensor_copy(h1h[3][0:1, 0:512], rrow[0:1, 1536:2048]))
        b.step()
        def mm_gate(e):
            r = None
            srcs = [rrow_bf, h1h[1][0:1, 0:512], h1h[2][0:1, 0:512], h1h[3][0:1, 0:512]]
            for blk in range(4):
                r = e.matmul(ps[blk][:, 0:512], lay.ones_bf[0:1, 0:128],
                             srcs[blk][0:1, 0:512], start=True, stop=True)
            return r
        b.on("tensor", mm_gate)
        b.step()
        for blk in range(4):
            b.on("vector", lambda e, blk=blk: e.tensor_copy(
                rwbc[:, blk * 512:(blk + 1) * 512], ps[blk][:, 0:512]))
        for half in range(2):
            b.step()
            for k in range(DC):
                b.dma("gpsimd", W1h[:, k * 2048:(k + 1) * 2048],
                      w1f[ex * D + k * 128: ex * D + (k + 1) * 128,
                          half * 2048:(half + 1) * 2048])
            b.step()
            for k in range(16):
                b.dma("gpsimd", W2h[:, k * D:(k + 1) * D],
                      w2f[ex * 4096 + half * 2048 + k * 128:
                          ex * 4096 + half * 2048 + (k + 1) * 128, :])
            b.step()
            for hc in range(16):
                b.dma(DMA_ENG[hc % 3], b1s[0:128, hc:hc + 1],
                      b1f[ex * 4096 + half * 2048 + hc * 128:
                          ex * 4096 + half * 2048 + (hc + 1) * 128, :])
            b1l = [b1s[0:128, hc:hc + 1] for hc in range(16)]
            for tb in range(4):
                mslice = [m_bf[k][:, tb * 512:(tb + 1) * 512] for k in range(DC)]
                lay.proj_f(W1h, mslice, 512, 2048, h1h, w_dout=2048,
                           act=AF.Gelu, bias=b1l)
                if debug and ex == 0 and half == 0 and tb == 0:
                    d_h1 = b.dout("d_h1", [16 * 128, 512], BF16)
                    d_rw = b.dout("d_rw", [128, TC], F32)
                    b.step()
                    for hc in range(16):
                        b.dma(DMA_ENG[hc % 3], d_h1[hc * 128:(hc + 1) * 128, :], h1h[hc])
                    b.dma("sync", d_rw[:, :], rwbc[:, 0:TC])
                def yst(oc, o, n, stgap, tb=tb):
                    def fn(e, oc=oc, o=o, n=n, stgap=stgap):
                        e.tensor_mul(stg2[0][:, 0:n], stgap,
                                     rwbc[:, tb * 512 + o: tb * 512 + o + n])
                        return e.tensor_add(acc[oc][:, tb * 512 + o: tb * 512 + o + n],
                                            acc[oc][:, tb * 512 + o: tb * 512 + o + n],
                                            stg2[0][:, 0:n])
                    b.on("vector", fn)
                lay.proj_f(W2h, h1h, 512, D, None, w_dout=D, din_c=16, store_fn=yst)
        b.step()
        for k in range(DC):
            b.on("vector", lambda e, k=k: e.scalar_tensor_tensor(
                acc[k], rwbc[:, 0:TC], b2s[0:128, k:k + 1], acc[k], ALU.mult, ALU.add))

    if debug:
        d_acc = b.dout("d_acc", [D, TC], BF16)
        b.step()
        for k in range(DC):
            b.dma(DMA_ENG[k % 3], d_acc[k * 128:(k + 1) * 128, :], acc[k])
    for blk in range(4):
        b.step()
        for k in range(DC):
            b.dma("gpsimd", xblk[k][:, 0:512],
                  xintT[k * 128:(k + 1) * 128, blk * 512:(blk + 1) * 512])
        b.step()
        for k in range(DC):
            b.on("vector", lambda e, k=k, blk=blk: e.tensor_add(
                acc[k][:, blk * 512:(blk + 1) * 512],
                acc[k][:, blk * 512:(blk + 1) * 512], xblk[k][:, 0:512]))
    lay.ln_resid_stream(acc, scr, mb, rb, tb_, T=TC, x_in=xT, x_out=xo,
                        xblk=xblk, x_is_bf=True)
    b.step()
    return b


def _rope_tables(pos0):
    j = np.arange(32)
    theta = 10000.0 ** (-2 * j / 64.0)
    pos = pos0 + np.arange(TC)
    f = pos[:, None] * theta[None, :]
    return (np.ascontiguousarray(np.tile(np.cos(f), (1, 16)).astype(np.float32)),
            np.ascontiguousarray(np.tile(np.sin(f), (1, 16)).astype(np.float32)))


def kernel(**I):
    C = np.ascontiguousarray
    f32 = np.float32
    const = Lay.const_inputs()
    col = lambda a: C(np.asarray(a).reshape(-1, 1).astype(f32))
    row = lambda a: C(np.asarray(a).reshape(1, -1).astype(f32))
    mat = lambda a: C(np.asarray(a).astype(f32))
    matT = lambda a: C(np.asarray(a).T.astype(f32))

    x = np.asarray(I["L0_attn_x"], dtype=f32)
    xts = xT_chunks(x)

    def halos(xts_list, n):
        out = []
        for c in range(NCORE):
            bb, cc = divmod(c, 4)
            if cc == 0:
                out.append(np.zeros((D, n), f32))
            else:
                out.append(C(xts_list[c - 1][:, TC - n:TC]))
        return out

    wqkv = mat(I["L0_attn_weight_qkv"])
    im = [{"xT": xts[c], "wqkv": wqkv, **const} for c in range(NCORE)]
    r = run(build_l0_proj(), im)
    qT = [np.asarray(r.results[c]["qT"]) for c in range(NCORE)]
    kT = [np.asarray(r.results[c]["kT"]) for c in range(NCORE)]
    v = [np.asarray(r.results[c]["v"]) for c in range(NCORE)]
    kn2 = [np.asarray(r.results[c]["kn2loc"]) for c in range(NCORE)]
    kTf = [np.concatenate(kT[bb * 4:(bb + 1) * 4], axis=1) for bb in range(2)]
    vf = [np.concatenate(v[bb * 4:(bb + 1) * 4], axis=0) for bb in range(2)]
    kn24 = [C(np.concatenate(kn2[bb * 4:(bb + 1) * 4], axis=1)) for bb in range(2)]
    wo_t = matT(I["L0_attn_out_w"])
    outb = col(I["L0_attn_out_b"])
    im = [{"qT": qT[c], "kTf": kTf[c // 4], "vf": vf[c // 4], "xT": xts[c],
           "wo_t": wo_t, "out_b": outb, "kn24": kn24[c // 4], **const}
          for c in range(NCORE)]
    r = run(build_l0_attend(), im)
    xts = [C(np.asarray(r.results[c]["x1T"])) for c in range(NCORE)]

    w = {"dwk": C(I["L0_ffn_dw_kernel"][:, 0, :].astype(f32)),
         "dwb": col(I["L0_ffn_dw_bias"]),
         "pw_t": matT(I["L0_ffn_pw_w"]), "pwb": col(I["L0_ffn_pw_b"]),
         "rw_t": matT(I["L0_ffn_router_w"]), "rb_in": col(I["L0_ffn_router_b"]),
         "ew_t": C(np.asarray(I["L0_ffn_expert_w"]).transpose(0, 2, 1)
                   .reshape(8 * D, D).astype(f32)),
         "eb_t": matT(I["L0_ffn_expert_b"]),
         "wout_t": matT(I["L0_ffn_out_w"]), "woutb": col(I["L0_ffn_out_b"]),
         "nw_in": col(I["L0_ffn_norm_w"]),
         "ffd_t": matT(I["L0_ffn_ff_down_w"]), "ffdb": col(I["L0_ffn_ff_down_b"]),
         "ffu_t": matT(I["L0_ffn_ff_up_w"]), "ffub": col(I["L0_ffn_ff_up_b"])}
    hl = halos(xts, 2)
    im = [{"xT": xts[c], "halo": hl[c], **w, **const} for c in range(NCORE)]
    r = run(build_l0_ffn(), im)
    xts = [C(np.asarray(r.results[c]["xoT"])) for c in range(NCORE)]

    wA = {"wk_t": matT(I["L1_attn_Wk"]), "wv_t": matT(I["L1_attn_Wv"]),
          "wr_t": matT(I["L1_attn_Wr"]),
          "tmk": col(I["L1_attn_time_mix_k"]), "tmv": col(I["L1_attn_time_mix_v"]),
          "tmr": col(I["L1_attn_time_mix_r"]), "w_in": col(I["L1_attn_w"])}
    hl = halos(xts, 1)
    im = [{"xT": xts[c], "halo": hl[c], **wA, **const} for c in range(NCORE)]
    r = run(build_l1a_A(), im)
    kTT = [np.asarray(r.results[c]["kTT"]) for c in range(NCORE)]
    vTT = [np.asarray(r.results[c]["vTT"]) for c in range(NCORE)]
    rTT = [np.asarray(r.results[c]["rTT"]) for c in range(NCORE)]
    kmx = [np.asarray(r.results[c]["kmaxl"]) for c in range(NCORE)]
    tls = [np.concatenate([np.asarray(r.results[c]["tailsA"]),
                           np.asarray(r.results[c]["tailsB"])], axis=1)
           for c in range(NCORE)]
    wo1 = matT(I["L1_attn_Wo"])
    u_in = col(I["L1_attn_u"])
    im = []
    for c in range(NCORE):
        bb, cc = divmod(c, 4)
        kmax4 = C(np.concatenate([kmx[bb * 4 + j] for j in range(4)], axis=1))
        tails8 = C(np.concatenate([tls[bb * 4 + j][:, 0:1] for j in range(4)]
                                  + [tls[bb * 4 + j][:, 1:2] for j in range(4)], axis=1))
        tjs = np.zeros((128, 4), f32)
        msk = np.zeros((128, 4), f32)
        for j in range(4):
            if j < cc:
                tjs[:, j] = -2048.0 * (cc - 1 - j)
                msk[:, j] = 1.0
        im.append({"xT": xts[c], "kTT": kTT[c], "vTT": vTT[c], "rTT": rTT[c],
                   "kmax4": kmax4, "tails8": tails8, "tjs": tjs, "msk4": msk,
                   "w_in": wA["w_in"], "u_in": u_in, "wo_t": wo1, **const})
    r = run(build_l1a_C(), im)
    xts = [C(np.asarray(r.results[c]["xoT"])) for c in range(NCORE)]

    w = {"wr_t": matT(I["L1_ffn_W_r"]), "wk_t": matT(I["L1_ffn_W_k"]),
         "wv_t": matT(I["L1_ffn_W_v"]),
         "mur": col(I["L1_ffn_mu_r"]), "muk": col(I["L1_ffn_mu_k"])}
    hl = halos(xts, 1)
    im = [{"xT": xts[c], "halo": hl[c], **w, **const} for c in range(NCORE)]
    r = run(build_chanmix(), im)
    xts = [C(np.asarray(r.results[c]["xoT"])) for c in range(NCORE)]

    wA = {"wq_t": matT(I["L2_attn_Wq"]), "wk_t": matT(I["L2_attn_Wk"]),
          "wv_t": matT(I["L2_attn_Wv"]),
          "bq_r": row(I["L2_attn_bq"]), "bk_r": row(I["L2_attn_bk"]),
          "bv_r": row(I["L2_attn_bv"])}
    im = []
    for c in range(NCORE):
        bb, cc = divmod(c, 4)
        cs, sn = _rope_tables(cc * TC)
        im.append({"xT": xts[c], "cs16": cs, "sn16": sn, **wA, **const})
    r = run(build_l2a_A(), im)
    qT = [np.asarray(r.results[c]["qT"]) for c in range(NCORE)]
    kT = [np.asarray(r.results[c]["kT"]) for c in range(NCORE)]
    v = [np.asarray(r.results[c]["v"]) for c in range(NCORE)]
    kTf = [np.concatenate(kT[bb * 4:(bb + 1) * 4], axis=1) for bb in range(2)]
    vf = [np.concatenate(v[bb * 4:(bb + 1) * 4], axis=0) for bb in range(2)]
    wo2 = matT(I["L2_attn_Wo"])
    bo2 = col(I["L2_attn_bo"])
    im = []
    for c in range(NCORE):
        bb, cc = divmod(c, 4)
        cols = []
        for j in range(16):
            g = 4 * j + cc
            src = bb * 4 + g // 16
            lt = g % 16
            cols.append(qT[src][:, lt * 128:(lt + 1) * 128])
        qTi = C(np.concatenate(cols, axis=1))
        mtd = np.zeros((128, 16 * 512), f32)
        for j in range(16):
            i = 4 * j + cc
            for cch in range(4):
                key = 512 * j + cch * 128 + np.arange(128)[:, None]
                qq = i * 128 + np.arange(128)[None, :]
                mtd[:, j * 512 + cch * 128: j * 512 + (cch + 1) * 128] = \
                    np.where(key <= qq, 0.0, -1e9)
        im.append({"qTi": qTi, "kTf": kTf[bb], "vf": vf[bb], "mtd": mtd,
                   "wo_t": wo2, "bo_c": bo2, **const})
    r = run(build_l2a_B(), im)
    at = [np.asarray(r.results[c]["attnTi"]) for c in range(NCORE)]
    im = []
    for c in range(NCORE):
        bb, cc = divmod(c, 4)
        cols = []
        for lt in range(16):
            g = cc * 16 + lt
            src = bb * 4 + g % 4
            j = g // 4
            cols.append(at[src][:, j * 128:(j + 1) * 128])
        attnT = C(np.concatenate(cols, axis=1))
        im.append({"xT": xts[c], "mixT": attnT, **const})
    r = run(build_sandwich(), im)
    xts = [C(np.asarray(r.results[c]["xoT"])) for c in range(NCORE)]

    w = {"wr_t": matT(I["L2_ffn_W_r"]), "wk_t": matT(I["L2_ffn_W_k"]),
         "wv_t": matT(I["L2_ffn_W_v"]),
         "mur": col(I["L2_ffn_mu_r"]), "muk": col(I["L2_ffn_mu_k"])}
    hl = halos(xts, 1)
    im = [{"xT": xts[c], "halo": hl[c], **w, **const} for c in range(NCORE)]
    r = run(build_chanmix(), im)
    xts = [C(np.asarray(r.results[c]["xoT"])) for c in range(NCORE)]

    wA = {"wq": mat(I["L3_attn_Wq"]), "wv": mat(I["L3_attn_Wv"]),
          "bq_r": row(I["L3_attn_bq"]), "bv_r": row(I["L3_attn_bv"])}
    im = [{"xT": xts[c], **wA, **const} for c in range(NCORE)]
    r = run(build_l3a_A(), im)
    qsm = [np.asarray(r.results[c]["qsmT"]) for c in range(NCORE)]
    vtk = [np.asarray(r.results[c]["vtok"]) for c in range(NCORE)]
    ml = [np.asarray(r.results[c]["mloc"]) for c in range(NCORE)]
    im = []
    for c in range(NCORE):
        bb, cc = divmod(c, 4)
        ml4 = C(np.concatenate([ml[bb * 4 + j] for j in range(4)], axis=1))
        im.append({"qsmT": qsm[c], "vtok": vtk[c], "mloc4": ml4, **const})
    r = run(build_l3a_B(), im)
    sp = [np.asarray(r.results[c]["spart"]) for c in range(NCORE)]
    cxp = [np.asarray(r.results[c]["ctxp"]) for c in range(NCORE)]
    im = []
    for c in range(NCORE):
        bb, cc = divmod(c, 4)
        ml4 = C(np.concatenate([ml[bb * 4 + j] for j in range(4)], axis=1))
        s4 = C(np.concatenate([sp[bb * 4 + j] for j in range(4)], axis=1))
        cx4 = C(np.concatenate([cxp[bb * 4 + j] for j in range(4)], axis=1))
        im.append({"xT": xts[c], "qsmT": qsm[c], "ctxp4": cx4, "spart4": s4,
                   "mloc4": ml4, "wo": mat(I["L3_attn_Wo"]),
                   "bo_c": col(I["L3_attn_bo"]), **const})
    r = run(build_l3a_C(), im)
    xts = [C(np.asarray(r.results[c]["xoT"])) for c in range(NCORE)]

    w1 = {"g1": col(I["L3_ffn_ln1_g"]), "b1c": col(I["L3_ffn_ln1_b"]),
          "wq": mat(I["L3_ffn_Wq"]), "wk": mat(I["L3_ffn_Wk"]),
          "wv": mat(I["L3_ffn_Wv"])}
    im = [{"xT": xts[c], **w1, **const} for c in range(NCORE)]
    r = run(build_l3f_1(), im)
    qT = [np.asarray(r.results[c]["qT"]) for c in range(NCORE)]
    kT = [np.asarray(r.results[c]["kT"]) for c in range(NCORE)]
    v = [np.asarray(r.results[c]["v"]) for c in range(NCORE)]
    m0 = [np.asarray(r.results[c]["m0T"]) for c in range(NCORE)]
    kTf = [np.concatenate(kT[bb * 4:(bb + 1) * 4], axis=1) for bb in range(2)]
    vf = [np.concatenate(v[bb * 4:(bb + 1) * 4], axis=0) for bb in range(2)]
    im = [{"qT": qT[c], "kTf": kTf[c // 4], "vf": vf[c // 4], "m0T": m0[c],
           "wo": mat(I["L3_ffn_Wo"]), **const} for c in range(NCORE)]
    r = run(build_l3f_2(), im)
    xint = [np.asarray(r.results[c]["xintT"]) for c in range(NCORE)]
    im = [{"xT": xts[c], "xintT": xint[c],
           "g2": col(I["L3_ffn_ln2_g"]), "b2c": col(I["L3_ffn_ln2_b"]),
           "wg": mat(I["L3_ffn_Wg"]),
           "w1f": C(np.asarray(I["L3_ffn_W1"]).reshape(8 * D, 4096).astype(f32)),
           "b1f": col(I["L3_ffn_b1"]),
           "w2f": C(np.asarray(I["L3_ffn_W2"]).reshape(8 * 4096, D).astype(f32)),
           "b2f": col(I["L3_ffn_b2"]), **const} for c in range(NCORE)]
    r = run(build_l3f_3(), im)

    out = np.empty((NB, TSEQ, D), dtype=f32)
    for c in range(NCORE):
        bb, cc = divmod(c, 4)
        out[bb, cc * TC:(cc + 1) * TC, :] = np.asarray(r.results[c]["xoT"]).T
    return out


# version 03
# speedup vs baseline: 1.2922x; 1.1746x over previous
import contextlib

import numpy as np
import ml_dtypes

import concourse.bass as bass
import concourse.mybir as mybir
from concourse.bass_utils import run_bass_kernel_spmd

F32 = mybir.dt.float32
BF16 = mybir.dt.bfloat16
AF = mybir.ActivationFunctionType
ALU = mybir.AluOpType
AX = mybir.AxisListType
BF = ml_dtypes.bfloat16

D = 1024
DC = 8
TSEQ = 8192
NB = 2
TC = 2048
NCORE = 8
ENGINES = ("sync", "tensor", "vector", "scalar", "gpsimd")
DMA_ENG = ("sync", "gpsimd", "scalar")



class BSP:

    def __init__(self):
        self.nc = bass.Bass("TRN2", target_bir_lowering=False, debug=False)
        self._es = contextlib.ExitStack()
        self.steps = []
        self._cur = None
        self.dram = {}

    def sbuf(self, name, shape, dtype):
        return self._es.enter_context(self.nc.sbuf_tensor(name, list(shape), dtype))

    def psum(self, name, shape, dtype=F32):
        return self._es.enter_context(self.nc.psum_tensor(name, list(shape), dtype))

    def din(self, name, shape, dtype=F32):
        ap = self.nc.dram_tensor(name, list(shape), dtype, kind="ExternalInput").ap()
        self.dram[name] = ap
        return ap

    def dout(self, name, shape, dtype=F32):
        ap = self.nc.dram_tensor(name, list(shape), dtype, kind="ExternalOutput").ap()
        self.dram[name] = ap
        return ap

    def step(self):
        self._cur = {e: [] for e in ENGINES}
        self.steps.append(self._cur)

    def on(self, eng, fn):
        assert self._cur is not None
        self._cur[eng].append(("op", fn))

    def dma(self, eng, out, in_):
        assert self._cur is not None
        self._cur[eng].append(("dma", (out, in_)))

    def finalize(self):
        nc = self.nc
        with self._es:
            bar = self._es.enter_context(nc.semaphore("bar"))
            dsems = {e: self._es.enter_context(nc.semaphore(f"ds_{e}")) for e in ENGINES}
            dcnt = {e: 0 for e in ENGINES}
            with nc.Block() as block:
                def run_engine(e, eng):
                    for k, st in enumerate(self.steps):
                        if k > 0:
                            e.wait_ge(bar, len(ENGINES) * k)
                        last = None
                        had_dma = False
                        for kind, payload in st[eng]:
                            if kind == "op":
                                r = payload(e)
                                if r is not None:
                                    last = r
                            else:
                                out, in_ = payload
                                e.dma_start(out=out, in_=in_).then_inc(dsems[eng], 16)
                                dcnt[eng] += 16
                                had_dma = True
                        if had_dma:
                            e.wait_ge(dsems[eng], dcnt[eng])
                            e.nop().then_inc(bar, 1)
                        elif last is not None:
                            last.then_inc(bar, 1)
                        else:
                            e.nop().then_inc(bar, 1)
                for eng in ENGINES:
                    getattr(block, eng)(lambda e, eng=eng: run_engine(e, eng))
        return nc


class Arena:

    def __init__(self, b, name, cols, dtype):
        self.t = b.sbuf(name, [128, cols], dtype)
        self.cols = cols
        self.off = 0
        self.name = name
        self.hi = 0

    def alloc(self, n, parts=128):
        assert self.off + n <= self.cols, \
            f"{self.name} arena overflow: {self.off}+{n}>{self.cols}"
        ap = self.t[0:parts, self.off:self.off + n]
        self.off += n
        self.hi = max(self.hi, self.off)
        return ap

    def alloc8(self, n):
        return [self.alloc(n) for _ in range(DC)]

    def mark(self):
        return self.off

    def reset(self, mark):
        self.off = mark




class Lay:
    def __init__(self, b: BSP, f32_cols, bf_cols, n_ps=6, pt_cols=512, n_stg=6,
                 pq_cols=0):
        self.b = b
        self.fa = Arena(b, "af32", f32_cols, F32)
        self.ba = Arena(b, "abf", bf_cols, BF16)
        self.psL = [b.psum(f"psL{i}", [128, 1024], F32) for i in range((n_ps + 1) // 2)]
        self.ps = [self.psL[i // 2][:, (i % 2) * 512:(i % 2 + 1) * 512]
                   for i in range(n_ps)]
        self.pt = b.psum("pst", [128, pt_cols], BF16) if pt_cols else None
        self.pq = b.psum("psq", [128, pq_cols], F32) if pq_cols else None
        self.ones_bf = b.sbuf("ones_bf", [128, 128], BF16)
        self.ident_bf = b.sbuf("ident_bf", [128, 128], BF16)
        self.stg = [b.sbuf(f"stg{i}", [128, 512], BF16)[:, :] for i in range(max(n_stg, 1))]
        self._stg_i = 0
        self.cb = b.sbuf("constb", [128, 16], F32)
        self._consts = {}
        c_ident = b.din("c_ident", [128, 128], F32)
        self.ident_f32 = b.sbuf("identf", [128, 128], F32)
        b.step()
        b.dma("sync", self.ident_f32[:, :], c_ident[:, :])
        b.on("gpsimd", lambda e: e.memset(self.ones_bf[:], 1.0))
        b.step()
        b.on("vector", lambda e: e.tensor_copy(self.ident_bf[:], self.ident_f32[:, :]))

    def stg_next(self):
        s = self.stg[self._stg_i % len(self.stg)]
        self._stg_i += 1
        return s

    def stgf_next(self):
        if not hasattr(self, "stgf"):
            self.stgf = [self.b.sbuf(f"stgf{i}", [128, 512], F32)[:, :] for i in range(6)]
            self._stgf_i = 0
        s = self.stgf[self._stgf_i % len(self.stgf)]
        self._stgf_i += 1
        return s

    def const(self, v):
        v = float(v)
        if v in self._consts:
            return self._consts[v]
        i = len(self._consts)
        ap = self.cb[0:128, i:i + 1]
        self.b.on("gpsimd", lambda e: e.memset(ap, v))
        self._consts[v] = ap
        return ap

    @staticmethod
    def const_inputs():
        return {"c_ident": np.eye(128, dtype=np.float32)}

    def load_w_bf(self, dram_ap, din, dout, w32):
        b = self.b
        kc = din // 128
        wbf = self.ba.alloc(kc * dout)
        b.step()
        for k in range(kc):
            b.dma("gpsimd", wbf[:, k * dout:(k + 1) * dout],
                  dram_ap[k * 128:(k + 1) * 128, :])
        return wbf

    def wslice(self, wbf, w_dout, k, o0, on):
        return wbf[:, k * w_dout + o0: k * w_dout + o0 + on]

    def ln(self, x_tiles, scr_bf, mb, rb, tb, *, T, x_is_bf=False, eps=1e-5,
           out_bf=None, out_f32=None, resid=None, gain_bias=None):
        b, ps = self.b, self.ps
        blks = [(i, min(512, T - i)) for i in range(0, T, 512)]
        assert len(blks) <= 4
        epsap = self.const(eps)
        if not x_is_bf:
            b.step()
            for k in range(DC):
                b.on("vector" if k % 2 == 0 else "gpsimd",
                     lambda e, k=k: e.tensor_copy(scr_bf[k], x_tiles[k]))
            src = scr_bf
        else:
            src = x_tiles
        b.step()
        def mm_sums(e):
            r = None
            for bi, (o, n) in enumerate(blks):
                for k in range(DC):
                    r = e.matmul(ps[bi][:, 0:n], self.ones_bf[:],
                                 src[k][:, o:o + n],
                                 start=(k == 0), stop=(k == DC - 1))
            return r
        b.on("tensor", mm_sums)
        b.step()
        for bi, (o, n) in enumerate(blks):
            b.on("vector", lambda e, bi=bi, o=o, n=n: e.tensor_scalar(
                mb[:, o:o + n], ps[bi][:, 0:n], 1.0 / D, None, ALU.mult))
        for k in range(DC):
            b.on("scalar", lambda e, k=k, s=src[k]: e.activation(scr_bf[k], s, AF.Square))
        for wave in range(2):
            wb = blks[wave * 2:wave * 2 + 2]
            if not wb:
                break
            b.step()
            def mm_sq(e, wb=wb, wave=wave):
                r = None
                for bi, (o, n) in enumerate(wb):
                    for k in range(DC):
                        r = e.matmul(ps[4 + bi][:, 0:n], self.ones_bf[:],
                                     scr_bf[k][:, o:o + n],
                                     start=(k == 0), stop=(k == DC - 1))
                return r
            b.on("tensor", mm_sq)
            b.step()
            for bi, (o, n) in enumerate(wb):
                b.on("vector", lambda e, bi=bi, o=o, n=n: e.tensor_scalar(
                    rb[:, o:o + n], ps[4 + bi][:, 0:n], 1.0 / D, None, ALU.mult))
        b.step()
        b.on("vector", lambda e: e.tensor_mul(tb[:, 0:T], mb[:, 0:T], mb[:, 0:T]))
        b.step()
        b.on("vector", lambda e: e.tensor_sub(rb[:, 0:T], rb[:, 0:T], tb[:, 0:T]))
        b.step()
        b.on("scalar", lambda e: e.activation(rb[:, 0:T], rb[:, 0:T], AF.Ln, bias=epsap))
        b.step()
        b.on("scalar", lambda e: e.activation(rb[:, 0:T], rb[:, 0:T], AF.Exp, scale=-0.5))
        b.step()
        for k in range(DC):
            b.on("vector" if k % 2 == 0 else "gpsimd",
                 lambda e, k=k: e.tensor_sub(scr_bf[k], x_tiles[k], mb[:, 0:T]))
        b.step()
        for k in range(DC):
            eng = "vector" if k % 2 == 0 else "gpsimd"
            dst = (out_bf or out_f32 or scr_bf)[k]
            b.on(eng, lambda e, k=k, dst=dst: e.tensor_mul(dst, scr_bf[k], rb[:, 0:T]))
        if gain_bias is not None:
            gl, bl = gain_bias
            b.step()
            for k in range(DC):
                dst = (out_bf or out_f32 or scr_bf)[k]
                b.on("vector", lambda e, k=k, dst=dst: e.tensor_scalar(
                    dst, dst, gl[k], bl[k], ALU.mult, ALU.add))
        if resid is not None:
            assert out_bf is None and out_f32 is None
            b.step()
            for k in range(DC):
                b.on("vector" if k % 2 == 0 else "gpsimd",
                     lambda e, k=k: e.tensor_add(resid[k], resid[k], scr_bf[k]))

    def ln_resid_stream(self, x_tiles, scr_bf, mb, rb, tb, *, T, x_in, x_out,
                        xblk, x_is_bf=True, eps=1e-5):
        self.ln(x_tiles, scr_bf, mb, rb, tb, T=T, x_is_bf=x_is_bf, eps=eps)
        b = self.b
        blks = [(i, min(512, T - i)) for i in range(0, T, 512)]
        for (o, n) in blks:
            b.step()
            for k in range(DC):
                b.dma(DMA_ENG[k % 3], xblk[k][:, 0:n],
                      x_in[k * 128:(k + 1) * 128, o:o + n])
            b.step()
            for k in range(DC):
                b.on("vector" if k % 2 == 0 else "gpsimd",
                     lambda e, k=k, o=o, n=n: e.tensor_add(
                         xblk[k][:, 0:n], xblk[k][:, 0:n], scr_bf[k][:, o:o + n]))
            b.step()
            for k in range(DC):
                b.dma(DMA_ENG[k % 3], x_out[k * 128:(k + 1) * 128, o:o + n],
                      xblk[k][:, 0:n])

    def proj_f(self, wbf, x_bf, T, dout, dst, *, w_dout=None, o_base=0,
               din_c=DC, scale=1.0, bias=None, act=None, store_fn=None,
               out_f32=False):
        b, ps = self.b, self.ps
        w_dout = w_dout or dout
        oc_n = dout // 128
        resident = isinstance(dst, list)
        blks = [(i, min(512, T - i)) for i in range(0, T, 512)]
        jobs = [(o, n, oc) for (o, n) in blks for oc in range(oc_n)]
        groups = [jobs[i:i + 3] for i in range(0, len(jobs), 3)]
        pend = []
        def evac(bank, o, n, oc):
            if resident:
                tgt = dst[oc][:, o:o + n]
            elif out_f32:
                tgt = self.stgf_next()[:, 0:n]
            else:
                tgt = self.stg_next()[:, 0:n]
            bias_ap = bias[oc] if bias is not None else None
            if act is not None or bias_ap is not None:
                fn = act if act is not None else AF.Identity
                kw = {}
                if bias_ap is not None:
                    kw["bias"] = bias_ap
                if scale != 1.0:
                    kw["scale"] = scale
                b.on("scalar", lambda e, tgt=tgt, bank=bank, fn=fn, kw=kw, n=n:
                     e.activation(tgt, bank[:, 0:n], fn, **kw))
            elif scale != 1.0:
                b.on("vector", lambda e, tgt=tgt, bank=bank, n=n: e.tensor_scalar(
                    tgt, bank[:, 0:n], scale, None, ALU.mult))
            else:
                b.on("vector", lambda e, tgt=tgt, bank=bank, n=n: e.tensor_copy(
                    tgt, bank[:, 0:n]))
            if resident:
                return None
            if store_fn is not None:
                return lambda: store_fn(oc, o, n, tgt)
            dram_slice = dst[oc * 128:(oc + 1) * 128, o:o + n]
            eng = DMA_ENG[oc % 2]
            return lambda: b.dma(eng, dram_slice, tgt)
        for gi, g in enumerate(groups):
            banks = [ps[3 * (gi % 2) + i] for i in range(len(g))]
            b.step()
            def mms(e, g=g, banks=banks):
                r = None
                for i, (o, n, oc) in enumerate(g):
                    for k in range(din_c):
                        r = e.matmul(banks[i][:, 0:n],
                                     self.wslice(wbf, w_dout, k, o_base + oc * 128, 128),
                                     x_bf[k][:, o:o + n],
                                     start=(k == 0), stop=(k == din_c - 1))
                return r
            b.on("tensor", mms)
            new_pend = []
            if gi > 0:
                prevg = groups[gi - 1]
                prevbanks = [ps[3 * ((gi - 1) % 2) + i] for i in range(len(prevg))]
                for i, (o, n, oc) in enumerate(prevg):
                    new_pend.append(evac(prevbanks[i], o, n, oc))
            for st in pend:
                st()
            pend = [p for p in new_pend if p is not None]
        b.step()
        lastg = groups[-1]
        lastbanks = [ps[3 * ((len(groups) - 1) % 2) + i] for i in range(len(lastg))]
        new_pend = []
        for i, (o, n, oc) in enumerate(lastg):
            new_pend.append(evac(lastbanks[i], o, n, oc))
        for st in pend:
            st()
        b.step()
        for st in new_pend:
            if st is not None:
                st()
        b.step()

    def proj_t(self, wbf, x_bf, T, dout, dst, *, w_dout=None, o_base=0,
               din_c=DC, scale=1.0):
        b, ps = self.b, self.ps
        w_dout = w_dout or dout
        resident = isinstance(dst, list)
        obs = [(i, min(512, dout - i)) for i in range(0, dout, 512)]
        jobs = [(tt, o, n) for tt in range(T // 128) for (o, n) in obs]
        groups = [jobs[i:i + 3] for i in range(0, len(jobs), 3)]
        pend = []
        def evac(bank, tt, o, n):
            if resident:
                tgt = dst[tt][:, o:o + n]
            else:
                tgt = self.stg_next()[:, 0:n]
            if scale != 1.0:
                b.on("vector", lambda e, tgt=tgt, bank=bank, n=n: e.tensor_scalar(
                    tgt, bank[:, 0:n], scale, None, ALU.mult))
            else:
                b.on("vector", lambda e, tgt=tgt, bank=bank, n=n: e.tensor_copy(
                    tgt, bank[:, 0:n]))
            if resident:
                return None
            dram_slice = dst[tt * 128:(tt + 1) * 128, o:o + n]
            eng = DMA_ENG[tt % 2]
            return lambda: b.dma(eng, dram_slice, tgt)
        for gi, g in enumerate(groups):
            banks = [ps[3 * (gi % 2) + i] for i in range(len(g))]
            b.step()
            def mms(e, g=g, banks=banks):
                r = None
                for i, (tt, o, n) in enumerate(g):
                    for k in range(din_c):
                        r = e.matmul(banks[i][:, 0:n],
                                     x_bf[k][:, tt * 128:(tt + 1) * 128],
                                     self.wslice(wbf, w_dout, k, o_base + o, n),
                                     start=(k == 0), stop=(k == din_c - 1))
                return r
            b.on("tensor", mms)
            new_pend = []
            if gi > 0:
                prevg = groups[gi - 1]
                prevbanks = [ps[3 * ((gi - 1) % 2) + i] for i in range(len(prevg))]
                for i, (tt, o, n) in enumerate(prevg):
                    new_pend.append(evac(prevbanks[i], tt, o, n))
            for st in pend:
                st()
            pend = [p for p in new_pend if p is not None]
        b.step()
        lastg = groups[-1]
        lastbanks = [ps[3 * ((len(groups) - 1) % 2) + i] for i in range(len(lastg))]
        new_pend = []
        for i, (tt, o, n) in enumerate(lastg):
            new_pend.append(evac(lastbanks[i], tt, o, n))
        for st in pend:
            st()
        b.step()
        for st in new_pend:
            if st is not None:
                st()
        b.step()




def xT_chunks(x):
    out = []
    for core in range(NCORE):
        bb, cc = divmod(core, 4)
        out.append(np.ascontiguousarray(
            x[bb, cc * TC:(cc + 1) * TC, :].T.astype(np.float32)))
    return out


def run(b, in_maps):
    import os
    nc = b.finalize()
    tr = bool(os.environ.get("KERNEL_TIMES"))
    r = run_bass_kernel_spmd(nc, in_maps, core_ids=list(range(NCORE)), trace=tr)
    if tr:
        t = getattr(r, "exec_time_ns", None)
        print(f"[launch] exec={(t or 0)/1e3:.0f}us", flush=True)
    return r




def build_l0_proj():
    b = BSP()
    xT = b.din("xT", [D, TC], F32)
    wqkv = b.din("wqkv", [D, 3 * D], F32)
    qT_o = b.dout("qT", [D, TC], BF16)
    kT_o = b.dout("kT", [D, TC], BF16)
    v_o = b.dout("v", [TC, D], BF16)

    lay = Lay(b, f32_cols=16384 + 6144, bf_cols=16384 + 8 * 3072)
    fa, ba = lay.fa, lay.ba
    x = fa.alloc8(TC)
    m0 = fa.mark()
    mb = fa.alloc(TC); rb = fa.alloc(TC); tb = fa.alloc(TC)
    xln = ba.alloc8(TC)
    m1 = ba.mark()
    scr = ba.alloc8(TC)

    b.step()
    for k in range(DC):
        b.dma("sync", x[k], xT[k * 128:(k + 1) * 128, :])
    lay.ln(x, scr, mb, rb, tb, T=TC, out_bf=xln)

    ba.reset(m1)
    fa.reset(m0)
    w32 = fa.alloc(3 * D)
    wbf = lay.load_w_bf(b.dram["wqkv"], D, 3 * D, w32)

    lay.proj_f(wbf, xln, TC, D, qT_o, w_dout=3 * D, o_base=0, scale=1.0 / 32.0)
    lay.proj_f(wbf, xln, TC, D, kT_o, w_dout=3 * D, o_base=D)
    lay.proj_t(wbf, xln, TC, D, v_o, w_dout=3 * D, o_base=2 * D)

    kn2_o = b.dout("kn2loc", [1, 1], F32)
    ktmp = xln
    kn2b = mb
    ps = lay.ps
    b.step()
    for k in range(DC):
        b.dma(DMA_ENG[k % 3], ktmp[k], kT_o[k * 128:(k + 1) * 128, :])
    b.step()
    for k in range(DC):
        b.on("scalar", lambda e, k=k: e.activation(ktmp[k], ktmp[k], AF.Square))
    b.step()
    def mm_kn2(e):
        r = None
        for blk in range(4):
            for k in range(DC):
                r = e.matmul(ps[blk][:, 0:512], lay.ones_bf[:],
                             ktmp[k][:, blk * 512:(blk + 1) * 512],
                             start=(k == 0), stop=(k == DC - 1))
        return r
    b.on("tensor", mm_kn2)
    b.step()
    for blk in range(4):
        b.on("vector", lambda e, blk=blk: e.tensor_copy(
            kn2b[:, blk * 512:(blk + 1) * 512], ps[blk][:, 0:512]))
    kmax = rb[:, 0:1]
    b.step()
    b.on("vector", lambda e: e.reduce_max(kmax, kn2b[:, 0:TC], AX.X))
    b.step()
    b.dma("gpsimd", kn2_o[0:1, 0:1], kmax[0:1, 0:1])
    b.step()
    return b


def kernel(**inputs):
    raise NotImplementedError("under construction")




def build_l0_attend(n_qt=16, do_epi=True, sweep_stage=5, debug=False):
    b = BSP()
    qT = b.din("qT", [D, TC], BF16)
    kTf = b.din("kTf", [D, TSEQ], BF16)
    vf = b.din("vf", [TSEQ, D], BF16)
    xT = b.din("xT", [D, TC], F32)
    wo_t = b.din("wo_t", [D, D], F32)
    out_b = b.din("out_b", [D, 1], F32)
    kn24 = b.din("kn24", [1, 4], F32)
    x1_o = b.dout("x1T", [D, TC], F32)

    lay = Lay(b, f32_cols=1024 + 16384 + 3 * TC,
              bf_cols=16384 + 2 + 8 * D + 33280, n_ps=6, pt_cols=0, n_stg=1)
    fa, ba, ps = lay.fa, lay.ba, lay.ps
    ptA = ps[4][:, 0:512].bitcast(BF16)
    ptB = ps[5][:, 0:512].bitcast(BF16)

    sm = fa.alloc(1024)
    def smcol(i, n=1):
        return sm[:, i:i + n]
    kn2_sb = sm[0:1, 0:4]
    k2m = sm[0:1, 4:5]
    lnk_bf = ba.alloc(2)[0:1, 0:1]
    lnk_b = smcol(8)
    qn2tok = smcol(9)
    Bneg = smcol(10)
    ell = smcol(11)
    ellp = [smcol(12), smcol(13)]
    rec = smcol(14)
    outb_sb = sm[:, 16:24]

    mreg = fa.mark()
    w32 = fa.alloc(D)
    fa.reset(mreg)
    x = fa.alloc8(TC)
    mb = fa.alloc(TC); rb = fa.alloc(TC); tb = fa.alloc(TC)

    yT = ba.alloc8(TC)
    b.step()
    wo_bf = None
    msweep = None

    b.step()
    b.dma("sync", kn2_sb, kn24[0:1, 0:4])
    for oc in range(DC):
        b.dma("gpsimd", outb_sb[0:128, oc:oc + 1], out_b[oc * 128:(oc + 1) * 128, 0:1])
    b.step()
    b.on("vector", lambda e: e.reduce_max(k2m, kn2_sb, AX.X))
    b.step()
    b.on("scalar", lambda e: e.activation(k2m, k2m, AF.Ln))
    b.step()
    b.on("vector", lambda e: e.tensor_copy(lnk_bf, k2m))
    b.step()
    b.on("tensor", lambda e: e.matmul(ps[0][:, 0:1], lay.ones_bf[0:1, 0:128],
                                      lnk_bf, start=True, stop=True))
    b.step()
    b.on("vector", lambda e: e.tensor_copy(lnk_b, ps[0][:, 0:1]))

    wo_bf = lay.load_w_bf(b.dram["wo_t"], D, D, w32)

    msweep = ba.mark()
    qsl = ba.alloc8(128)
    qsq = ba.alloc8(128)
    kbuf = [[ba.alloc(512) for _ in range(DC)] for _ in range(2)]
    vbuf = [[ba.alloc(D) for _ in range(4)] for _ in range(3)]
    probs = [ba.alloc(512) for _ in range(2)]
    probsT = [ba.alloc(512) for _ in range(2)]
    ybf = ba.alloc(D)

    NKB = TSEQ // 512

    for qt in range(n_qt):
        qoff = qt * 128
        b.step()
        for k in range(DC):
            b.dma("sync", qsl[k], qT[k * 128:(k + 1) * 128, qoff:qoff + 128])
        b.on("vector", lambda e: e.memset(ell, 0.0))
        b.step()
        for k in range(DC):
            b.on("scalar", lambda e, k=k: e.activation(qsq[k], qsl[k], AF.Square))
        b.step()
        def mm_qn2(e):
            r = None
            for k in range(DC):
                r = e.matmul(ps[0][:, 0:128], lay.ones_bf[:], qsq[k],
                             start=(k == 0), stop=(k == DC - 1))
            return r
        b.on("tensor", mm_qn2)
        b.step()
        b.on("vector", lambda e: e.tensor_copy(mb[:, 0:128], ps[0][:, 0:128]))
        b.step()
        b.on("tensor", lambda e: e.transpose(ps[1][:, 0:128], mb[:, 0:128],
                                             lay.ident_f32[:]))
        b.step()
        b.on("vector", lambda e: e.tensor_copy(qn2tok, ps[1][:, 0:1]))
        b.step()
        b.on("scalar", lambda e: e.activation(qn2tok, qn2tok, AF.Ln))
        b.step()
        b.on("vector", lambda e: e.tensor_add(qn2tok, qn2tok, lnk_b))
        b.step()
        b.on("scalar", lambda e: e.activation(Bneg, qn2tok, AF.Exp, scale=0.5))
        b.step()
        b.on("vector", lambda e: e.tensor_scalar(Bneg, Bneg, -1.0, None, ALU.mult))

        for i in (range(NKB + 6) if sweep_stage >= 2 else []):
            b.step()
            if i < NKB:
                for k in range(DC):
                    b.dma("sync", kbuf[i % 2][k],
                          kTf[k * 128:(k + 1) * 128, i * 512:(i + 1) * 512])
            jv = i - 3
            if 0 <= jv < NKB:
                for c in range(4):
                    b.dma("gpsimd", vbuf[jv % 3][c],
                          vf[jv * 512 + c * 128: jv * 512 + (c + 1) * 128, :])
            def tensor_work(e, i=i):
                r = None
                js = i - 1
                if 0 <= js < NKB:
                    bank = ps[2 + js % 2]
                    for k in range(DC):
                        r = e.matmul(bank[:, 0:512], qsl[k], kbuf[js % 2][k],
                                     start=(k == 0), stop=(k == DC - 1))
                jt = i - 3
                if 0 <= jt < NKB:
                    ptX = ptA if jt % 2 == 0 else ptB
                    for c in range(4):
                        r = e.transpose(ptX[:, c * 128:(c + 1) * 128],
                                        probs[jt % 2][:, c * 128:(c + 1) * 128],
                                        lay.ident_bf[:])
                jp = i - 5
                if 0 <= jp < NKB:
                    for c in range(4):
                        for h in range(2):
                            r = e.matmul(ps[h][:, 0:512],
                                         probsT[jp % 2][:, c * 128:(c + 1) * 128],
                                         vbuf[jp % 3][c][:, h * 512:(h + 1) * 512],
                                         start=(jp == 0 and c == 0),
                                         stop=(jp == NKB - 1 and c == 3))
                return r
            b.on("tensor", tensor_work)
            je = i - 2
            if 0 <= je < NKB:
                b.on("scalar", lambda e, je=je: e.activation(
                    probs[je % 2], ps[2 + je % 2][:, 0:512], AF.Exp,
                    bias=Bneg, accum_out=ellp[je % 2]))
            jv2 = i - 4
            if 0 <= jv2 < NKB:
                ptX = ptA if jv2 % 2 == 0 else ptB
                b.on("vector", lambda e, jv2=jv2, ptX=ptX: e.tensor_copy(
                    probsT[jv2 % 2], ptX[:, 0:512]))
            ja = i - 3
            if 0 <= ja < NKB:
                b.on("vector", lambda e, ja=ja: e.tensor_add(ell, ell, ellp[ja % 2]))

        if debug and qt == 0:
            dbg_ell = b.dout("dbg_ell", [2, 128], F32)
            dbg_p0 = b.dout("dbg_p0", [128, 512], BF16)
            b.step()
            b.dma("sync", dbg_ell[0:1, :].rearrange("a b -> b a"), ell)
            b.dma("sync", dbg_ell[1:2, :].rearrange("a b -> b a"), Bneg)
            b.dma("gpsimd", dbg_p0[:, :], probs[0])
        if sweep_stage < 3:
            continue
        b.step()
        b.on("vector", lambda e: e.reciprocal(rec, ell))
        b.step()
        for h in range(2):
            b.on("vector", lambda e, h=h: e.tensor_scalar(
                ybf[:, h * 512:(h + 1) * 512], ps[h][:, 0:512], rec, None, ALU.mult))
        if sweep_stage < 4:
            continue
        if debug and qt == 0:
            dbg_ybf = b.dout("dbg_ybf", [128, D], BF16)
            b.step()
            b.dma("sync", dbg_ybf[:, :], ybf)
        b.step()
        def ytrans(e):
            r = None
            for c in range(DC):
                r = e.transpose(ptA[:, c * 128:(c + 1) * 128],
                                ybf[:, c * 128:(c + 1) * 128], lay.ident_bf[:])
            return r
        b.on("tensor", ytrans)
        if sweep_stage < 5:
            continue
        b.step()
        for c in range(DC):
            b.on("vector", lambda e, c=c, qoff=qoff: e.tensor_copy(
                yT[c][:, qoff:qoff + 128], ptA[:, c * 128:(c + 1) * 128]))

    if not do_epi:
        b.step()
        return b
    ba.reset(msweep)
    attn = ba.alloc8(TC)
    scr = ba.alloc8(TC)
    outb_list = [outb_sb[0:128, oc:oc + 1] for oc in range(DC)]
    if debug:
        yT_dbg = b.dout("yT_dbg", [D, TC], BF16)
        b.step()
        for k in range(DC):
            b.dma(DMA_ENG[k % 3], yT_dbg[k * 128:(k + 1) * 128, :], yT[k])
    lay.proj_f(wo_bf, yT, TC, D, attn, bias=outb_list)
    if debug:
        at_dbg = b.dout("at_dbg", [D, TC], BF16)
        b.step()
        for k in range(DC):
            b.dma(DMA_ENG[k % 3], at_dbg[k * 128:(k + 1) * 128, :], attn[k])
    b.step()
    for k in range(DC):
        b.dma(DMA_ENG[k % 3], x[k], xT[k * 128:(k + 1) * 128, :])
    lay.ln(attn, scr, mb, rb, tb, T=TC, x_is_bf=True, resid=x)
    b.step()
    for k in range(DC):
        b.dma(DMA_ENG[k % 3], x1_o[k * 128:(k + 1) * 128, :], x[k])
    b.step()
    return b




def build_l0_ffn(debug=False):
    b = BSP()
    xT = b.din("xT", [D, TC], F32)
    halo = b.din("halo", [D, 2], F32)
    dwk = b.din("dwk", [D, 3], F32)
    dwb = b.din("dwb", [D, 1], F32)
    pw_t = b.din("pw_t", [D, D], F32)
    pwb = b.din("pwb", [D, 1], F32)
    rw_t = b.din("rw_t", [D, 8], F32)
    rb_in = b.din("rb_in", [8, 1], F32)
    ew_t = b.din("ew_t", [8 * D, D], F32)
    eb_t = b.din("eb_t", [D, 8], F32)
    wout_t = b.din("wout_t", [D, D], F32)
    woutb = b.din("woutb", [D, 1], F32)
    nw_in = b.din("nw_in", [D, 1], F32)
    ffd_t = b.din("ffd_t", [D, 256], F32)
    ffdb = b.din("ffdb", [256, 1], F32)
    ffu_t = b.din("ffu_t", [256, D], F32)
    ffub = b.din("ffub", [D, 1], F32)
    xo = b.dout("xoT", [D, TC], F32)

    lay = Lay(b, f32_cols=9600, bf_cols=79500, n_ps=6, pt_cols=1024, n_stg=6)
    fa, ba, ps, pt = lay.fa, lay.ba, lay.ps, lay.pt

    sm = fa.alloc(16)
    mb = fa.alloc(TC); rb = fa.alloc(TC); tb_ = fa.alloc(TC)
    rwbc = fa.alloc(TC)
    beT = fa.alloc(64)
    pbias = fa.alloc(40)
    kert = fa.alloc(3 * DC)
    nw = fa.alloc(DC)
    hx = fa.alloc(2 * DC)
    ltokf = [fa.alloc(8) for _ in range(16)]
    rwtok = [fa.alloc(8) for _ in range(16)]
    m1 = fa.alloc(1); nm1 = fa.alloc(1); m2 = fa.alloc(1); den = fa.alloc(1)
    msk = fa.alloc(8)
    rrow = mb[0:1, 1024:1024 + TC // 2]
    rrow = tb_[0:1, 0:TC]
    xblk = ([rwbc[:, i * 512:(i + 1) * 512] for i in range(4)]
            + [tb_[:, i * 512:(i + 1) * 512] for i in range(4)])

    R1 = ba.alloc8(2 + TC)
    R2 = ba.alloc8(TC)
    WR = ba.alloc(8 * D)
    scr = ba.alloc8(TC)
    acc = ba.alloc8(TC)
    d1 = [ba.alloc(TC) for _ in range(2)]
    stg2 = [ba.alloc(512) for _ in range(3)]
    rrow_bf = d1[0][0:1, 0:TC]

    xext = R1
    xln = [R1[k][:, 2:2 + TC] for k in range(DC)]

    b.step()
    xbf = acc
    for k in range(DC):
        b.dma("gpsimd", xbf[k], xT[k * 128:(k + 1) * 128, :])
    b.dma("sync", sm[0:8, 0:1], rb_in[:, :])
    for k in range(DC):
        b.dma("gpsimd", kert[0:128, 3 * k:3 * k + 3], dwk[k * 128:(k + 1) * 128, :])
        b.dma("scalar", nw[0:128, k:k + 1], nw_in[k * 128:(k + 1) * 128, :])
        b.dma("gpsimd", pbias[0:128, k:k + 1], pwb[k * 128:(k + 1) * 128, :])
        b.dma("scalar", pbias[0:128, 8 + k:9 + k], woutb[k * 128:(k + 1) * 128, :])
        b.dma("gpsimd", pbias[0:128, 16 + k:17 + k], ffub[k * 128:(k + 1) * 128, :])
        b.dma("scalar", pbias[0:128, 24 + k:25 + k], dwb[k * 128:(k + 1) * 128, :])
        b.dma("sync", beT[0:128, k * 8:(k + 1) * 8], eb_t[k * 128:(k + 1) * 128, :])
        b.dma("sync", hx[0:128, 2 * k:2 * k + 2], halo[k * 128:(k + 1) * 128, :])
    for k in range(2):
        b.dma("sync", pbias[0:128, 32 + k:33 + k], ffdb[k * 128:(k + 1) * 128, :])

    lay.ln(xbf, scr, mb, rb, tb_, T=TC, x_is_bf=True, out_bf=xln)
    hxt = [hx[0:128, 2 * k:2 * k + 2] for k in range(DC)]
    hscr = [stg2[0][:, k * 2:k * 2 + 2] for k in range(DC)]
    lay.ln(hxt, hscr, mb, rb, tb_, T=2, out_bf=[xext[k][:, 0:2] for k in range(DC)])

    xc = R2
    for j in range(3):
        b.step()
        for k in range(DC):
            eng = "vector"
            kap = kert[0:128, 3 * k + j:3 * k + j + 1]
            if j == 0:
                b.on(eng, lambda e, k=k, kap=kap: e.tensor_scalar(
                    xc[k], xext[k][:, 0:TC], kap, None, ALU.mult))
            else:
                b.on(eng, lambda e, k=k, kap=kap, j=j: e.scalar_tensor_tensor(
                    xc[k], xext[k][:, j:j + TC], kap, xc[k], ALU.mult, ALU.add))
    b.step()
    for k in range(DC):
        b.on("vector",
             lambda e, k=k: e.tensor_scalar(xc[k], xc[k],
                                            pbias[0:128, 24 + k:25 + k], None, ALU.add))

    w32 = mb
    pw_bf = WR[:, 0:8 * D]
    b.step()
    for k in range(DC):
        b.dma("gpsimd", pw_bf[:, k * D:(k + 1) * D], pw_t[k * 128:(k + 1) * 128, :])
    x1i = scr
    pwb_list = [pbias[0:128, k:k + 1] for k in range(DC)]
    def pw_store(oc, o, n, stgap):
        b.on("vector", lambda e: e.tensor_add(x1i[oc][:, o:o + n],
                                              xln[oc][:, o:o + n], stgap))
    lay.proj_f(pw_bf, xc, TC, D, None, act=AF.Silu, bias=pwb_list, store_fn=pw_store)

    if debug:
        d_x1i = b.dout("d_x1i", [D, TC], BF16)
        b.step()
        for k in range(DC):
            b.dma(DMA_ENG[k % 3], d_x1i[k * 128:(k + 1) * 128, :], x1i[k])
    rW_bf = stg2[1][:, 0:64]
    b.step()
    for k in range(DC):
        b.dma("gpsimd", rW_bf[:, k * 8:(k + 1) * 8], rw_t[k * 128:(k + 1) * 128, :])
    lg = rb
    b.step()
    def mm_router(e):
        r = None
        for blk in range(4):
            for k in range(DC):
                r = e.matmul(ps[blk][0:8, 0:512], rW_bf[:, k * 8:(k + 1) * 8],
                             x1i[k][:, blk * 512:(blk + 1) * 512],
                             start=(k == 0), stop=(k == DC - 1))
        return r
    b.on("tensor", mm_router)
    b.step()
    for blk in range(4):
        b.on("vector", lambda e, blk=blk: e.tensor_scalar(
            lg[0:8, blk * 512:(blk + 1) * 512], ps[blk][0:8, 0:512],
            sm[0:8, 0:1], None, ALU.add))
    for tt in range(16):
        b.step()
        b.on("tensor", lambda e, tt=tt: e.transpose(
            ps[4][:, 0:8], lg[0:8, tt * 128:(tt + 1) * 128], lay.ident_f32[0:8, 0:8]))
        b.step()
        b.on("vector", lambda e, tt=tt: e.tensor_copy(ltokf[tt][0:128, 0:8], ps[4][:, 0:8]))
    BIGNEG = -1.0e30
    for tt in range(16):
        lt = ltokf[tt][0:128, 0:8]
        rw_ = rwtok[tt][0:128, 0:8]
        b.step()
        b.on("vector", lambda e, lt=lt: e.reduce_max(m1[:, 0:1], lt, AX.X))
        b.step()
        b.on("vector", lambda e: e.tensor_scalar(nm1[:, 0:1], m1[:, 0:1], -1.0, None, ALU.mult))
        b.on("vector", lambda e, lt=lt: e.tensor_scalar(msk[:, 0:8], lt, m1[:, 0:1], None, ALU.is_ge))
        b.step()
        b.on("vector", lambda e, lt=lt: e.scalar_tensor_tensor(
            msk[:, 0:8], msk[:, 0:8], BIGNEG, lt, ALU.mult, ALU.add))
        b.step()
        b.on("vector", lambda e: e.reduce_max(m2[:, 0:1], msk[:, 0:8], AX.X))
        b.step()
        b.on("vector", lambda e, lt=lt: e.tensor_scalar(msk[:, 0:8], lt, m2[:, 0:1], None, ALU.is_ge))
        b.step()
        b.on("scalar", lambda e: e.activation(den[:, 0:1], m2[:, 0:1], AF.Exp, bias=nm1[:, 0:1]))
        b.step()
        b.on("vector", lambda e: e.tensor_scalar(den[:, 0:1], den[:, 0:1], 1.0, None, ALU.add))
        b.step()
        b.on("vector", lambda e: e.reciprocal(den[:, 0:1], den[:, 0:1]))
        b.step()
        b.on("scalar", lambda e, lt=lt, rw_=rw_: e.activation(rw_, lt, AF.Exp, bias=nm1[:, 0:1]))
        b.step()
        b.on("vector", lambda e, rw_=rw_: e.tensor_mul(rw_, rw_, msk[:, 0:8]))
        b.step()
        b.on("vector", lambda e, rw_=rw_: e.tensor_scalar(rw_, rw_, den[:, 0:1], None, ALU.mult))
    for tt in range(16):
        b.step()
        b.on("tensor", lambda e, tt=tt: e.transpose(
            ps[4][0:8, 0:128], rwtok[tt][0:128, 0:8], lay.ident_f32[:, :]))
        b.step()
        b.on("vector", lambda e, tt=tt: e.tensor_copy(
            lg[0:8, tt * 128:(tt + 1) * 128], ps[4][0:8, 0:128]))

    if debug:
        d_lg = b.dout("d_lg", [8, TC], F32)
        b.step()
        b.dma("sync", d_lg[:, :], lg[0:8, 0:TC])
    b.step()
    for k in range(DC):
        b.on("vector" if k % 2 == 0 else "gpsimd", lambda e, k=k: e.memset(acc[k], 0.0))
    for ex in range(8):
        we_bf = WR[:, 0:8 * D]
        b.step()
        for k in range(DC):
            b.dma("gpsimd", we_bf[:, k * D:(k + 1) * D],
                  ew_t[ex * D + k * 128: ex * D + (k + 1) * 128, :])
        b.step()
        b.dma("gpsimd", rrow[0:1, 0:TC], lg[ex:ex + 1, 0:TC])
        b.step()
        b.on("vector", lambda e: e.tensor_copy(rrow_bf[0:1, 0:TC], rrow[0:1, 0:TC]))
        b.step()
        def mm_gate(e):
            r = None
            for blk in range(4):
                r = e.matmul(ps[blk][:, 0:512], lay.ones_bf[0:1, 0:128],
                             rrow_bf[0:1, blk * 512:(blk + 1) * 512],
                             start=True, stop=True)
            return r
        b.on("tensor", mm_gate)
        b.step()
        for blk in range(4):
            b.on("vector", lambda e, blk=blk: e.tensor_copy(
                rwbc[:, blk * 512:(blk + 1) * 512], ps[blk][:, 0:512]))
        bias_list = [beT[0:128, k * 8 + ex:k * 8 + ex + 1] for k in range(DC)]
        def est(oc, o, n, stgap, ex=ex):
            def fn(e):
                e.tensor_mul(stg2[oc % 3][:, 0:n], stgap, rwbc[:, o:o + n])
                return e.tensor_add(acc[oc][:, o:o + n], acc[oc][:, o:o + n],
                                    stg2[oc % 3][:, 0:n])
            b.on("vector", fn)
        lay.proj_f(we_bf, x1i, TC, D, None, act=AF.Silu, bias=bias_list, store_fn=est)

    if debug:
        d_acc = b.dout("d_acc", [D, TC], BF16)
        b.step()
        for k in range(DC):
            b.dma(DMA_ENG[k % 3], d_acc[k * 128:(k + 1) * 128, :], acc[k])
    wout_bf = WR[:, 0:8 * D]
    b.step()
    for k in range(DC):
        b.dma("gpsimd", wout_bf[:, k * D:(k + 1) * D], wout_t[k * 128:(k + 1) * 128, :])
    s_bf = xln
    wob_list = [pbias[0:128, 8 + k:9 + k] for k in range(DC)]
    def mo_store(oc, o, n, stgap):
        b.on("vector", lambda e: e.tensor_add(s_bf[oc][:, o:o + n],
                                              x1i[oc][:, o:o + n], stgap))
    lay.proj_f(wout_bf, acc, TC, D, None, bias=wob_list, act=AF.Identity,
               store_fn=mo_store)

    if debug:
        d_s = b.dout("d_s", [D, TC], BF16)
        b.step()
        for k in range(DC):
            b.dma(DMA_ENG[k % 3], d_s[k * 128:(k + 1) * 128, :], s_bf[k])
    b.step()
    for k in range(DC):
        b.on("scalar", lambda e, k=k: e.activation(R2[k], s_bf[k], AF.Square))
    b.step()
    def mm_rms(e):
        r = None
        for blk in range(4):
            for k in range(DC):
                r = e.matmul(ps[blk][:, 0:512], lay.ones_bf[:],
                             R2[k][:, blk * 512:(blk + 1) * 512],
                             start=(k == 0), stop=(k == DC - 1))
        return r
    b.on("tensor", mm_rms)
    epsap = lay.const(1e-6)
    b.step()
    for blk in range(4):
        b.on("vector", lambda e, blk=blk: e.tensor_scalar(
            rb[:, blk * 512:(blk + 1) * 512], ps[blk][:, 0:512], 1.0 / D, None, ALU.mult))
    b.step()
    b.on("scalar", lambda e: e.activation(rb[:, 0:TC], rb[:, 0:TC], AF.Ln, bias=epsap))
    b.step()
    b.on("scalar", lambda e: e.activation(rb[:, 0:TC], rb[:, 0:TC], AF.Exp, scale=-0.5))
    h_bf = R2
    b.step()
    for k in range(DC):
        b.on("vector",
             lambda e, k=k: e.scalar_tensor_tensor(
                 h_bf[k], s_bf[k], nw[0:128, k:k + 1], rb[:, 0:TC], ALU.mult, ALU.mult))

    if debug:
        d_h = b.dout("d_h", [D, TC], BF16)
        b.step()
        for k in range(DC):
            b.dma(DMA_ENG[k % 3], d_h[k * 128:(k + 1) * 128, :], h_bf[k])
    ffd_bf = WR[:, 0:2048]
    ffu_bf = WR[:, 2048:4096]
    b.step()
    for k in range(DC):
        b.dma("gpsimd", ffd_bf[:, k * 256:(k + 1) * 256], ffd_t[k * 128:(k + 1) * 128, :])
    for k in range(2):
        b.dma("gpsimd", ffu_bf[:, k * D:(k + 1) * D], ffu_t[k * 128:(k + 1) * 128, :])
    fdb_list = [pbias[0:128, 32 + k:33 + k] for k in range(2)]
    lay.proj_f(ffd_bf, h_bf, TC, 256, d1, w_dout=256, act=AF.Gelu, bias=fdb_list)
    out_sub = acc
    fub_list = [pbias[0:128, 16 + k:17 + k] for k in range(DC)]
    def fu_store(oc, o, n, stgap):
        b.on("vector", lambda e: e.tensor_add(out_sub[oc][:, o:o + n],
                                              h_bf[oc][:, o:o + n], stgap))
    lay.proj_f(ffu_bf, d1, TC, D, None, w_dout=D, din_c=2, bias=fub_list,
               act=AF.Identity, store_fn=fu_store)

    if debug:
        d_os = b.dout("d_os", [D, TC], BF16)
        b.step()
        for k in range(DC):
            b.dma(DMA_ENG[k % 3], d_os[k * 128:(k + 1) * 128, :], out_sub[k])
    lay.ln_resid_stream(out_sub, scr, mb, rb, tb_, T=TC, x_in=xT, x_out=xo,
                        xblk=xblk, x_is_bf=True)
    b.step()
    return b




def build_chanmix():
    b = BSP()
    xT = b.din("xT", [D, TC], F32)
    halo = b.din("halo", [D, 1], F32)
    wr_t = b.din("wr_t", [D, D], F32)
    wk_t = b.din("wk_t", [D, D], F32)
    wv_t = b.din("wv_t", [D, D], F32)
    mur = b.din("mur", [D, 1], F32)
    muk = b.din("muk", [D, 1], F32)
    xo = b.dout("xoT", [D, TC], F32)

    lay = Lay(b, f32_cols=9000, bf_cols=77000, n_ps=6, pt_cols=0, n_stg=6)
    fa, ba, ps = lay.fa, lay.ba, lay.ps

    mb = fa.alloc(TC); rb = fa.alloc(TC); tb_ = fa.alloc(TC)
    mu = fa.alloc(2 * DC)
    hx = fa.alloc(DC)
    w32 = mb
    xblk = ([rb[:, i * 512:(i + 1) * 512] for i in range(0)]
            + [tb_[:, i * 512:(i + 1) * 512] for i in range(4)]
            + [fa.alloc(512) for _ in range(4)])

    A = ba.alloc8(1 + TC)
    B_ = ba.alloc8(TC)
    Cb = ba.alloc8(TC)
    Db = ba.alloc8(TC)
    WR = ba.alloc(8 * D)
    mext = A
    m_t = [A[k][:, 1:1 + TC] for k in range(DC)]

    b.step()
    xbf = Cb
    for k in range(DC):
        b.dma("gpsimd", xbf[k], xT[k * 128:(k + 1) * 128, :])
    for k in range(DC):
        b.dma("sync", mu[0:128, k:k + 1], mur[k * 128:(k + 1) * 128, :])
        b.dma("sync", mu[0:128, DC + k:DC + k + 1], muk[k * 128:(k + 1) * 128, :])
        b.dma("scalar", hx[0:128, k:k + 1], halo[k * 128:(k + 1) * 128, :])

    lay.ln(xbf, B_, mb, rb, tb_, T=TC, x_is_bf=True, out_bf=m_t)
    hxt = [hx[0:128, k:k + 1] for k in range(DC)]
    hscr = [lay.stg[0][:, k:k + 1] for k in range(DC)]
    lay.ln(hxt, hscr, mb, rb, tb_, T=1, out_bf=[mext[k][:, 0:1] for k in range(DC)])

    def timemix(dst, mucol):
        b.step()
        for k in range(DC):
            b.on("vector", lambda e, k=k: e.tensor_sub(
                B_[k], m_t[k], mext[k][:, 0:TC]))
        b.step()
        for k in range(DC):
            b.on("vector", lambda e, k=k: e.scalar_tensor_tensor(
                dst[k], B_[k], mu[0:128, mucol + k:mucol + k + 1],
                mext[k][:, 0:TC], ALU.mult, ALU.add))
    timemix(Cb, 0)
    wrbf = WR[:, 0:8 * D]
    b.step()
    for k in range(DC):
        b.dma("gpsimd", wrbf[:, k * D:(k + 1) * D], wr_t[k * 128:(k + 1) * 128, :])
    lay.proj_f(wrbf, Cb, TC, D, Db, act=AF.Sigmoid)

    timemix(Cb, DC)
    b.step()
    for k in range(DC):
        b.dma("gpsimd", wrbf[:, k * D:(k + 1) * D], wk_t[k * 128:(k + 1) * 128, :])
    k2 = [A[k][:, 0:TC] for k in range(DC)]
    lay.proj_f(wrbf, Cb, TC, D, k2, act=AF.Relu)
    b.step()
    for k in range(DC):
        b.on("scalar", lambda e, k=k: e.activation(k2[k], k2[k], AF.Square))

    b.step()
    for k in range(DC):
        b.dma("gpsimd", wrbf[:, k * D:(k + 1) * D], wv_t[k * 128:(k + 1) * 128, :])
    out_sub = Cb
    def vstore(oc, o, n, stgap):
        b.on("vector", lambda e: e.tensor_mul(out_sub[oc][:, o:o + n],
                                              Db[oc][:, o:o + n], stgap))
    lay.proj_f(wrbf, k2, TC, D, None, store_fn=vstore)

    lay.ln_resid_stream(out_sub, B_, mb, rb, tb_, T=TC, x_in=xT, x_out=xo,
                        xblk=xblk, x_is_bf=True)
    b.step()
    return b




def build_l1a_A():
    b = BSP()
    xT = b.din("xT", [D, TC], F32)
    halo = b.din("halo", [D, 1], F32)
    wk_t = b.din("wk_t", [D, D], F32)
    wv_t = b.din("wv_t", [D, D], F32)
    wr_t = b.din("wr_t", [D, D], F32)
    tmk = b.din("tmk", [D, 1], F32)
    tmv = b.din("tmv", [D, 1], F32)
    tmr = b.din("tmr", [D, 1], F32)
    w_in = b.din("w_in", [D, 1], F32)
    ko = b.dout("kTT", [D, TC], F32)
    vo = b.dout("vTT", [D, TC], F32)
    ro = b.dout("rTT", [D, TC], F32)
    kmo = b.dout("kmaxl", [D, 1], F32)
    tlo = b.dout("tailsA", [D, 1], F32)
    tlo2 = b.dout("tailsB", [D, 1], F32)

    lay = Lay(b, f32_cols=16700, bf_cols=57400, n_ps=6, pt_cols=0, n_stg=6)
    fa, ba, ps = lay.fa, lay.ba, lay.ps

    mb = fa.alloc(TC); rb = fa.alloc(TC); tb_ = fa.alloc(TC)
    tm = fa.alloc(4 * DC)
    hx = fa.alloc(DC)
    sm = fa.alloc(16)
    zc = fa.alloc(TC); vc = fa.alloc(TC); sa = fa.alloc(TC); sb2 = fa.alloc(TC)
    gam = fa.alloc(TC)
    onesf = tb_
    w32 = mb

    A = ba.alloc8(1 + TC)
    B_ = ba.alloc8(TC)
    Cb = ba.alloc8(TC)
    WR = ba.alloc(8 * D)
    mext = A
    m_t = [A[k][:, 1:1 + TC] for k in range(DC)]

    b.step()
    xbf = Cb
    for k in range(DC):
        b.dma("gpsimd", xbf[k], xT[k * 128:(k + 1) * 128, :])
    for k in range(DC):
        b.dma("sync", tm[0:128, k:k + 1], tmk[k * 128:(k + 1) * 128, :])
        b.dma("sync", tm[0:128, DC + k:DC + k + 1], tmv[k * 128:(k + 1) * 128, :])
        b.dma("sync", tm[0:128, 2 * DC + k:2 * DC + k + 1], tmr[k * 128:(k + 1) * 128, :])
        b.dma("sync", tm[0:128, 3 * DC + k:3 * DC + k + 1], w_in[k * 128:(k + 1) * 128, :])
        b.dma("scalar", hx[0:128, k:k + 1], halo[k * 128:(k + 1) * 128, :])

    lay.ln(xbf, B_, mb, rb, tb_, T=TC, x_is_bf=True, out_bf=m_t)
    hxt = [hx[0:128, k:k + 1] for k in range(DC)]
    hscr = [lay.stg[0][:, k:k + 1] for k in range(DC)]
    lay.ln(hxt, hscr, mb, rb, tb_, T=1, out_bf=[mext[k][:, 0:1] for k in range(DC)])

    def timemix(dst, col0):
        b.step()
        for k in range(DC):
            b.on("vector", lambda e, k=k: e.tensor_sub(B_[k], m_t[k], mext[k][:, 0:TC]))
        b.step()
        for k in range(DC):
            b.on("vector", lambda e, k=k, col0=col0: e.scalar_tensor_tensor(
                dst[k], B_[k], tm[0:128, col0 + k:col0 + k + 1],
                mext[k][:, 0:TC], ALU.mult, ALU.add))

    def loadw(dram):
        b.step()
        for k in range(DC):
            b.dma("gpsimd", WR[:, k * D:(k + 1) * D], dram[k * 128:(k + 1) * 128, :])
        return WR[:, 0:8 * D]

    b.step()
    b.on("vector", lambda e: e.memset(onesf[:, 0:TC], 1.0))
    timemix(Cb, 0)
    wbf = loadw(wk_t)
    lay.proj_f(wbf, Cb, TC, D, ko, out_f32=True)
    timemix(Cb, DC)
    wbf = loadw(wv_t)
    lay.proj_f(wbf, Cb, TC, D, vo, out_f32=True)
    timemix(Cb, 2 * DC)
    wbf = loadw(wr_t)
    lay.proj_f(wbf, Cb, TC, D, ro, out_f32=True)

    for k in range(DC):
        b.step()
        b.dma("sync", zc[:, 0:TC], ko[k * 128:(k + 1) * 128, :])
        b.dma("gpsimd", vc[:, 0:TC], vo[k * 128:(k + 1) * 128, :])
        b.step()
        b.on("vector", lambda e: e.reduce_max(sm[:, 0:1], zc[:, 0:TC], AX.X))
        b.step()
        b.on("vector", lambda e: e.tensor_scalar(sm[:, 1:2], sm[:, 0:1], -1.0, None, ALU.mult))
        b.on("scalar", lambda e, k=k: e.activation(sm[:, 2:3], tm[0:128, 3 * DC + k:3 * DC + k + 1], AF.Exp))
        b.step()
        b.on("scalar", lambda e: e.activation(zc[:, 0:TC], zc[:, 0:TC], AF.Exp, bias=sm[:, 1:2]))
        b.on("vector", lambda e: e.tensor_scalar(sm[:, 3:4], sm[:, 2:3], -1.0, None, ALU.mult))
        b.step()
        b.on("scalar", lambda e: e.activation(sm[:, 4:5], sm[:, 3:4], AF.Exp))
        b.step()
        b.on("vector", lambda e: e.tensor_mul(vc[:, 0:TC], zc[:, 0:TC], vc[:, 0:TC]))
        b.on("vector", lambda e: e.tensor_scalar(gam[:, 0:TC], onesf[:, 0:TC], sm[:, 4:5], None, ALU.mult))
        b.step()
        b.on("vector", lambda e: e.tensor_tensor_scan(sa[:, 0:TC], gam[:, 0:TC], vc[:, 0:TC], 0.0, ALU.mult, ALU.add))
        b.step()
        b.on("vector", lambda e: e.tensor_tensor_scan(sb2[:, 0:TC], gam[:, 0:TC], zc[:, 0:TC], 0.0, ALU.mult, ALU.add))
        b.step()
        b.dma("sync", kmo[k * 128:(k + 1) * 128, 0:1], sm[:, 0:1])
        b.dma("gpsimd", tlo[k * 128:(k + 1) * 128, 0:1], sa[:, TC - 1:TC])
        b.dma("scalar", tlo2[k * 128:(k + 1) * 128, 0:1], sb2[:, TC - 1:TC])
    b.step()
    return b




def build_l1a_C(debug=False):
    b = BSP()
    xT = b.din("xT", [D, TC], F32)
    kTT = b.din("kTT", [D, TC], F32)
    vTT = b.din("vTT", [D, TC], F32)
    rTT = b.din("rTT", [D, TC], F32)
    kmax4 = b.din("kmax4", [D, 4], F32)
    tails8 = b.din("tails8", [D, 8], F32)
    tjs = b.din("tjs", [128, 4], F32)
    msk4 = b.din("msk4", [128, 4], F32)
    w_in = b.din("w_in", [D, 1], F32)
    u_in = b.din("u_in", [D, 1], F32)
    wo_t = b.din("wo_t", [D, D], F32)
    xo = b.dout("xoT", [D, TC], F32)

    lay = Lay(b, f32_cols=22800, bf_cols=41000, n_ps=6, pt_cols=0, n_stg=6)
    fa, ba, ps = lay.fa, lay.ba, lay.ps

    mb = fa.alloc(TC); rb = fa.alloc(TC); tb_ = fa.alloc(TC)
    wu = fa.alloc(2 * DC)
    km = fa.alloc(4 * DC)
    tl = fa.alloc(8 * DC)
    tjc = fa.alloc(4); mkc = fa.alloc(4)
    sm = fa.alloc(16)
    zc = fa.alloc(TC); vc = fa.alloc(TC)
    saf = fa.alloc(1 + TC); sbf = fa.alloc(1 + TC)
    gam = fa.alloc(TC)
    onesf = tb_
    rampg = fa.alloc(TC)
    num = fa.alloc(TC); den = fa.alloc(TC)
    xblk = ([rampg[:, i * 512:(i + 1) * 512] for i in range(4)]
            + [num[:, i * 512:(i + 1) * 512] for i in range(4)])
    w32 = mb

    rwkv = ba.alloc8(TC)
    scr = ba.alloc8(TC)
    WR = ba.alloc(8 * D)

    b.step()
    b.dma("sync", tjc[0:128, 0:4], tjs[:, :])
    b.dma("sync", mkc[0:128, 0:4], msk4[:, :])
    for k in range(DC):
        b.dma("sync", wu[0:128, k:k + 1], w_in[k * 128:(k + 1) * 128, :])
        b.dma("sync", wu[0:128, DC + k:DC + k + 1], u_in[k * 128:(k + 1) * 128, :])
        b.dma("gpsimd", km[0:128, 4 * k:4 * k + 4], kmax4[k * 128:(k + 1) * 128, :])
        b.dma("scalar", tl[0:128, 8 * k:8 * k + 8], tails8[k * 128:(k + 1) * 128, :])
    b.on("vector", lambda e: e.memset(onesf[:, 0:TC], 1.0))
    b.on("gpsimd", lambda e: e.iota(gam[:, 0:TC], [[1, TC]], channel_multiplier=0,
                                    allow_small_or_imprecise_dtypes=True))

    wbf = WR[:, 0:8 * D]
    b.step()
    for k in range(DC):
        b.dma("gpsimd", wbf[:, k * D:(k + 1) * D], wo_t[k * 128:(k + 1) * 128, :])
    rampi = rb
    b.step()
    b.on("vector", lambda e: e.tensor_copy(rampi[:, 0:TC], gam[:, 0:TC]))

    for k in range(DC):
        wcol = wu[0:128, k:k + 1]
        ucol = wu[0:128, DC + k:DC + k + 1]
        b.step()
        b.dma("sync", zc[:, 0:TC], kTT[k * 128:(k + 1) * 128, :])
        b.dma("gpsimd", vc[:, 0:TC], vTT[k * 128:(k + 1) * 128, :])
        b.on("vector", lambda e, k=k: e.reduce_max(sm[:, 0:1], km[0:128, 4 * k:4 * k + 4], AX.X))
        b.step()
        b.on("vector", lambda e: e.tensor_scalar(sm[:, 1:2], sm[:, 0:1], -1.0, None, ALU.mult))
        b.on("scalar", lambda e, wcol=wcol: e.activation(sm[:, 2:3], wcol, AF.Exp))
        b.step()
        b.on("scalar", lambda e: e.activation(zc[:, 0:TC], zc[:, 0:TC], AF.Exp, bias=sm[:, 1:2]))
        b.on("vector", lambda e: e.tensor_scalar(sm[:, 3:4], sm[:, 2:3], -1.0, None, ALU.mult))
        b.step()
        b.on("scalar", lambda e: e.activation(sm[:, 4:5], sm[:, 3:4], AF.Exp))
        b.on("scalar", lambda e, ucol=ucol: e.activation(sm[:, 5:6], ucol, AF.Exp))
        b.step()
        b.on("vector", lambda e: e.tensor_mul(vc[:, 0:TC], zc[:, 0:TC], vc[:, 0:TC]))
        b.on("vector", lambda e: e.tensor_scalar(gam[:, 0:TC], onesf[:, 0:TC], sm[:, 4:5], None, ALU.mult))
        b.on("vector", lambda e: e.memset(sm[:, 6:8], 0.0))
        t8 = sm[:, 8:12]
        t9 = sm[:, 12:16]
        b.step()
        b.on("vector", lambda e: e.tensor_scalar(t8, tjc[0:128, 0:4], sm[:, 2:3], None, ALU.mult))
        b.on("vector", lambda e, k=k: e.tensor_scalar(
            t9, km[0:128, 4 * k:4 * k + 4], sm[:, 0:1], None, ALU.subtract))
        b.step()
        b.on("scalar", lambda e: e.activation(t8, t8, AF.Exp))
        b.on("scalar", lambda e: e.activation(t9, t9, AF.Exp))
        b.step()
        b.on("vector", lambda e: e.tensor_mul(t8, t8, t9))
        b.step()
        b.on("vector", lambda e: e.tensor_mul(t8, t8, mkc[0:128, 0:4]))
        b.step()
        b.on("vector", lambda e, k=k: e.tensor_mul(t9, t8, tl[0:128, 8 * k:8 * k + 4]))
        b.step()
        b.on("vector", lambda e: e.reduce_sum(sm[:, 6:7], t9, AX.X))
        b.on("vector", lambda e, k=k: e.tensor_mul(t9, t8, tl[0:128, 8 * k + 4:8 * k + 8]))
        b.step()
        b.on("vector", lambda e: e.reduce_sum(sm[:, 7:8], t9, AX.X))
        b.step()
        b.on("vector", lambda e: e.tensor_tensor_scan(saf[:, 1:1 + TC], gam[:, 0:TC],
                                                      vc[:, 0:TC], 0.0, ALU.mult, ALU.add))
        b.step()
        b.on("vector", lambda e: e.tensor_tensor_scan(sbf[:, 1:1 + TC], gam[:, 0:TC],
                                                      zc[:, 0:TC], 0.0, ALU.mult, ALU.add))
        b.step()
        b.on("vector", lambda e: e.tensor_scalar(rampg[:, 0:TC], rampi[:, 0:TC],
                                                 sm[:, 3:4], None, ALU.mult))
        b.step()
        b.on("scalar", lambda e: e.activation(rampg[:, 0:TC], rampg[:, 0:TC], AF.Exp,
                                              bias=sm[:, 3:4]))
        b.step()
        b.on("vector", lambda e: e.scalar_tensor_tensor(
            saf[:, 1:1 + TC], rampg[:, 0:TC], sm[:, 6:7], saf[:, 1:1 + TC],
            ALU.mult, ALU.add))
        b.on("vector", lambda e: e.tensor_copy(saf[:, 0:1], sm[:, 6:7]))
        b.step()
        b.on("vector", lambda e: e.scalar_tensor_tensor(
            sbf[:, 1:1 + TC], rampg[:, 0:TC], sm[:, 7:8], sbf[:, 1:1 + TC],
            ALU.mult, ALU.add))
        b.on("vector", lambda e: e.tensor_copy(sbf[:, 0:1], sm[:, 7:8]))
        b.step()
        b.on("vector", lambda e: e.scalar_tensor_tensor(
            num[:, 0:TC], vc[:, 0:TC], sm[:, 5:6], saf[:, 0:TC], ALU.mult, ALU.add))
        b.on("vector", lambda e: e.scalar_tensor_tensor(
            den[:, 0:TC], zc[:, 0:TC], sm[:, 5:6], sbf[:, 0:TC], ALU.mult, ALU.add))
        b.step()
        b.on("vector", lambda e: e.reciprocal(den[:, 0:TC], den[:, 0:TC]))
        b.step()
        b.on("vector", lambda e: e.tensor_mul(num[:, 0:TC], num[:, 0:TC], den[:, 0:TC]))
        if debug and k == 0:
            d_z = b.dout("d_z", [128, TC], F32)
            d_sa = b.dout("d_sa", [128, 1 + TC], F32)
            d_num = b.dout("d_num", [128, TC], F32)
            d_ramp = b.dout("d_ramp", [128, TC], F32)
            b.step()
            b.dma("sync", d_z[:, :], zc[:, 0:TC])
            b.dma("gpsimd", d_sa[:, :], saf[:, 0:1 + TC])
            b.dma("scalar", d_num[:, :], num[:, 0:TC])
            b.dma("sync", d_ramp[:, :], rampi[:, 0:TC])
        b.step()
        b.dma("sync", zc[:, 0:TC], rTT[k * 128:(k + 1) * 128, :])
        b.step()
        b.on("scalar", lambda e: e.activation(zc[:, 0:TC], zc[:, 0:TC], AF.Sigmoid))
        b.step()
        b.on("vector", lambda e, k=k: e.tensor_mul(rwkv[k], zc[:, 0:TC], num[:, 0:TC]))

    if debug:
        d_rw = b.dout("d_rw", [D, TC], BF16)
        b.step()
        for k in range(DC):
            b.dma(DMA_ENG[k % 3], d_rw[k * 128:(k + 1) * 128, :], rwkv[k])
    out_sub = scr
    def wstore(oc, o, n, stgap):
        b.on("vector", lambda e: e.tensor_copy(out_sub[oc][:, o:o + n], stgap))
    lay.proj_f(wbf, rwkv, TC, D, out_sub)
    lay.ln_resid_stream(out_sub, rwkv, mb, rb, tb_, T=TC, x_in=xT, x_out=xo,
                        xblk=xblk, x_is_bf=True)
    b.step()
    return b




def build_l2a_A(debug=False):
    b = BSP()
    xT = b.din("xT", [D, TC], F32)
    wq_t = b.din("wq_t", [D, D], F32)
    wk_t = b.din("wk_t", [D, D], F32)
    wv_t = b.din("wv_t", [D, D], F32)
    bq_r = b.din("bq_r", [1, D], F32)
    bk_r = b.din("bk_r", [1, D], F32)
    bv_r = b.din("bv_r", [1, D], F32)
    cs16 = b.din("cs16", [TC, 512], F32)
    sn16 = b.din("sn16", [TC, 512], F32)
    qT_o = b.dout("qT", [D, TC], BF16)
    kT_o = b.dout("kT", [D, TC], BF16)
    v_o = b.dout("v", [TC, D], BF16)

    lay = Lay(b, f32_cols=16500, bf_cols=59500, n_ps=6, pt_cols=1024, n_stg=6)
    fa, ba, ps, pt = lay.fa, lay.ba, lay.ps, lay.pt

    mb = fa.alloc(TC); rb = fa.alloc(TC); tb_ = fa.alloc(TC)
    qf = fa.alloc(D); qr = fa.alloc(D); tmp1 = fa.alloc(512)
    cs_f = fa.alloc(512); sn_f = fa.alloc(512)
    bb3 = fa.alloc(3 * D)
    w32 = mb

    xln = ba.alloc8(TC)
    scr = ba.alloc8(TC)
    WR = ba.alloc(8 * D)
    qT_sb = ba.alloc8(TC)
    qrb = ba.alloc(D)
    brow = ba.alloc(D)

    b.step()
    xbf = qT_sb
    for k in range(DC):
        b.dma("gpsimd", xbf[k], xT[k * 128:(k + 1) * 128, :])
    lay.ln(xbf, scr, mb, rb, tb_, T=TC, x_is_bf=True, out_bf=xln)

    for bi, brd in enumerate((bq_r, bk_r, bv_r)):
        b.step()
        b.dma("sync", tmp1[0:1, 0:512], brd[0:1, 0:512])
        b.dma("gpsimd", cs_f[0:1, 0:512], brd[0:1, 512:1024])
        b.step()
        b.on("vector", lambda e: e.tensor_copy(brow[0:1, 0:512], tmp1[0:1, 0:512]))
        b.on("vector", lambda e: e.tensor_copy(brow[0:1, 512:1024], cs_f[0:1, 0:512]))
        b.step()
        def bmm(e):
            r = None
            for h in range(2):
                r = e.matmul(ps[h][:, 0:512], lay.ones_bf[0:1, 0:128],
                             brow[0:1, h * 512:(h + 1) * 512], start=True, stop=True)
            return r
        b.on("tensor", bmm)
        b.step()
        for h in range(2):
            b.on("vector", lambda e, bi=bi, h=h: e.tensor_copy(
                bb3[:, bi * D + h * 512: bi * D + (h + 1) * 512], ps[h][:, 0:512]))

    def loadw(dram):
        b.step()
        for k in range(DC):
            b.dma("gpsimd", WR[:, k * D:(k + 1) * D], dram[k * 128:(k + 1) * 128, :])
        return WR[:, 0:8 * D]

    wbf = loadw(wv_t)
    def vevac_store(tt, o, n, stgap):
        b.on("vector", lambda e, tt=tt, o=o, n=n, stgap=stgap: None)
    jobs = [(tt, ob) for tt in range(16) for ob in range(2)]
    groups = [jobs[i:i + 3] for i in range(0, len(jobs), 3)]
    pend = []
    for gi, g in enumerate(groups):
        banks = [ps[3 * (gi % 2) + i] for i in range(len(g))]
        b.step()
        def mms(e, g=g, banks=banks):
            r = None
            for i, (tt, ob) in enumerate(g):
                for k in range(DC):
                    r = e.matmul(banks[i][:, 0:512],
                                 xln[k][:, tt * 128:(tt + 1) * 128],
                                 wbf[k * D + ob * 512: k * D + ob * 512 + 512] if False else
                                 WR[:, k * D + ob * 512: k * D + ob * 512 + 512],
                                 start=(k == 0), stop=(k == DC - 1))
            return r
        b.on("tensor", mms)
        new_pend = []
        if gi > 0:
            prevg = groups[gi - 1]
            prevbanks = [ps[3 * ((gi - 1) % 2) + i] for i in range(len(prevg))]
            for i, (tt, ob) in enumerate(prevg):
                s = lay.stg_next()[:, 0:512]
                b.on("vector", lambda e, s=s, bank=prevbanks[i], ob=ob: e.scalar_tensor_tensor(
                    s, bank[:, 0:512], 1.0, bb3[:, 2 * D + ob * 512:2 * D + (ob + 1) * 512],
                    ALU.mult, ALU.add))
                new_pend.append((tt, ob, s))
        for (tt, ob, s) in pend:
            b.dma(DMA_ENG[(tt + ob) % 3], v_o[tt * 128:(tt + 1) * 128, ob * 512:(ob + 1) * 512], s)
        pend = new_pend
    b.step()
    lastg = groups[-1]
    lastbanks = [ps[3 * ((len(groups) - 1) % 2) + i] for i in range(len(lastg))]
    new_pend = []
    for i, (tt, ob) in enumerate(lastg):
        s = lay.stg_next()[:, 0:512]
        b.on("vector", lambda e, s=s, bank=lastbanks[i], ob=ob: e.scalar_tensor_tensor(
            s, bank[:, 0:512], 1.0, bb3[:, 2 * D + ob * 512:2 * D + (ob + 1) * 512],
            ALU.mult, ALU.add))
        new_pend.append((tt, ob, s))
    for (tt, ob, s) in pend:
        b.dma(DMA_ENG[(tt + ob) % 3], v_o[tt * 128:(tt + 1) * 128, ob * 512:(ob + 1) * 512], s)
    b.step()
    for (tt, ob, s) in new_pend:
        b.dma(DMA_ENG[(tt + ob) % 3], v_o[tt * 128:(tt + 1) * 128, ob * 512:(ob + 1) * 512], s)
    b.step()

    def qk_path(wdram, bias_off, scale, dst_dram):
        wbf = loadw(wdram)
        for tt in range(16):
            b.step()
            b.dma("sync", cs_f[:, 0:512], cs16[tt * 128:(tt + 1) * 128, :])
            b.dma("gpsimd", sn_f[:, 0:512], sn16[tt * 128:(tt + 1) * 128, :])
            def mmq(e, tt=tt):
                r = None
                for ob in range(2):
                    for k in range(DC):
                        r = e.matmul(ps[ob][:, 0:512],
                                     xln[k][:, tt * 128:(tt + 1) * 128],
                                     WR[:, k * D + ob * 512: k * D + ob * 512 + 512],
                                     start=(k == 0), stop=(k == DC - 1))
                return r
            b.on("tensor", mmq)
            b.step()
            for ob in range(2):
                b.on("vector", lambda e, ob=ob: e.scalar_tensor_tensor(
                    qf[:, ob * 512:(ob + 1) * 512], ps[ob][:, 0:512], 1.0,
                    bb3[:, bias_off + ob * 512:bias_off + (ob + 1) * 512],
                    ALU.mult, ALU.add))
            qe = qf.rearrange("p (h j two) -> p two h j", h=16, j=32, two=2)
            qev = qe[:, 0, :, :]
            qov = qe[:, 1, :, :]
            orr = qr.rearrange("p (h half j) -> p half h j", h=16, half=2, j=32)
            orre = orr[:, 0, :, :]
            orim = orr[:, 1, :, :]
            b.step()
            b.on("vector", lambda e: e.tensor_mul(orre, qev, cs_f[:, 0:512]))
            b.on("vector", lambda e: e.tensor_mul(tmp1[:, 0:512], qov, sn_f[:, 0:512]))
            b.step()
            b.on("vector", lambda e: e.tensor_sub(orre, orre, tmp1[:, 0:512]))
            b.on("vector", lambda e: e.tensor_mul(tmp1[:, 0:512], qov, cs_f[:, 0:512]))
            b.step()
            b.on("vector", lambda e: e.tensor_mul(orim, qev, sn_f[:, 0:512]))
            b.step()
            b.on("vector", lambda e: e.tensor_add(orim, orim, tmp1[:, 0:512]))
            if debug and tt == 0 and bias_off == 0:
                d_qf = b.dout("d_qf", [128, D], F32)
                d_qr = b.dout("d_qr", [128, D], F32)
                b.step()
                b.dma("sync", d_qf[:, :], qf[:, 0:D])
                b.dma("gpsimd", d_qr[:, :], qr[:, 0:D])
            b.step()
            b.on("vector", lambda e, scale=scale: e.tensor_scalar(
                qrb[:, 0:D], qr[:, 0:D], scale, None, ALU.mult))
            b.step()
            def qtr(e):
                r = None
                for c in range(DC):
                    r = e.transpose(pt[:, c * 128:(c + 1) * 128],
                                    qrb[:, c * 128:(c + 1) * 128], lay.ident_bf[:])
                return r
            b.on("tensor", qtr)
            b.step()
            for c in range(DC):
                b.on("vector", lambda e, c=c, tt=tt: e.tensor_copy(
                    qT_sb[c][:, tt * 128:(tt + 1) * 128], pt[:, c * 128:(c + 1) * 128]))
        b.step()
        for c in range(DC):
            b.dma(DMA_ENG[c % 3], dst_dram[c * 128:(c + 1) * 128, :], qT_sb[c])
        b.step()

    qk_path(wq_t, 0, 1.0 / 8.0, qT_o)
    qk_path(wk_t, D, 1.0, kT_o)
    b.step()
    return b




def build_l2a_B(debug=False):
    b = BSP()
    qTi = b.din("qTi", [D, TC], BF16)
    kTf = b.din("kTf", [D, TSEQ], BF16)
    vf = b.din("vf", [TSEQ, D], BF16)
    mtd = b.din("mtd", [128, 16 * 512], F32)
    wo_t = b.din("wo_t", [D, D], F32)
    bo_c = b.din("bo_c", [D, 1], F32)
    at_o = b.dout("attnTi", [D, TC], BF16)

    lay = Lay(b, f32_cols=10400, bf_cols=75800, n_ps=6, pt_cols=1024, n_stg=6)
    fa, ba, ps, pt = lay.fa, lay.ba, lay.ps, lay.pt

    mb = fa.alloc(TC)
    mtd_f = fa.alloc(16 * 512)
    boc = fa.alloc(DC)
    rec2 = fa.alloc(2)
    w32 = mb

    qT_sb = ba.alloc8(TC)
    kpair = ba.alloc(TSEQ)
    vpair = [ba.alloc(128) for _ in range(64)]
    probs = [ba.alloc(1024) for _ in range(2)]
    ytok = [ba.alloc(D) for _ in range(16)]
    yT = ba.alloc8(TC)
    WR = ba.alloc(8 * D)

    b.step()
    for k in range(DC):
        b.dma("gpsimd", qT_sb[k], qTi[k * 128:(k + 1) * 128, :])
        b.dma("sync", boc[0:128, k:k + 1], bo_c[k * 128:(k + 1) * 128, :])
    b.dma("scalar", mtd_f[:, :], mtd[:, :])
    b.step()
    for k in range(DC):
        b.dma("gpsimd", WR[:, k * D:(k + 1) * D], wo_t[k * 128:(k + 1) * 128, :])

    for pair in range(8):
        b.step()
        b.dma("sync", kpair[:, :], kTf[pair * 128:(pair + 1) * 128, :])
        for ch in range(64):
            b.dma(("gpsimd", "scalar")[ch % 2], vpair[ch],
                  vf[ch * 128:(ch + 1) * 128, pair * 128:(pair + 1) * 128])
        for j in range(16):
            ngrp = j + 1
            nfull = ngrp - 1
            qslA = qT_sb[pair][0:64, j * 128:(j + 1) * 128]
            qslB = qT_sb[pair][64:128, j * 128:(j + 1) * 128]

            def emit_pv(e, g, ngrp=ngrp):
                r = None
                pb = probs[g % 2]
                first = (g == 0)
                last = (g == ngrp - 1)
                for h in range(2):
                    for c in range(4):
                        ch = g * 4 + c
                        r = e.matmul(ps[0][:, h * 64:(h + 1) * 64],
                                     pb[:, h * 512 + c * 128:h * 512 + (c + 1) * 128],
                                     vpair[ch][:, h * 64:(h + 1) * 64],
                                     start=(first and c == 0 and h == 0),
                                     stop=(last and c == 3))
                for h in range(2):
                    for c in range(4):
                        r = e.matmul(ps[1][:, h:h + 1],
                                     pb[:, h * 512 + c * 128:h * 512 + (c + 1) * 128],
                                     lay.ones_bf[:, 0:1],
                                     start=(first and c == 0 and h == 0),
                                     stop=(last and c == 3))
                return r

            def emit_mms(e, g, qslA=qslA, qslB=qslB):
                r = None
                bank = lay.psL[1 + (g % 2)]
                for h, qsl in ((0, qslA), (1, qslB)):
                    for c in range(4):
                        kcol = (g * 4 + c) * 128
                        r = e.matmul(bank[:, h * 512 + c * 128:h * 512 + (c + 1) * 128],
                                     kpair[h * 64:(h + 1) * 64, kcol:kcol + 128],
                                     qsl, start=True, stop=True)
                return r

            for i in range(nfull + 2):
                b.step()
                def work(e, i=i, nfull=nfull, emit_mms=emit_mms, emit_pv=emit_pv):
                    r = None
                    if i < nfull:
                        r = emit_mms(e, i)
                    if 0 <= i - 2 < nfull:
                        r = emit_pv(e, i - 2)
                    return r
                b.on("tensor", work)
                ge = i - 1
                if 0 <= ge < nfull:
                    b.on("scalar", lambda e, ge=ge: e.activation(
                        probs[ge % 2], lay.psL[1 + (ge % 2)][:, 0:1024], AF.Exp))
            g = ngrp - 1
            b.step()
            b.on("tensor", lambda e, g=g, emit_mms=emit_mms: emit_mms(e, g))
            b.step()
            for h in range(2):
                b.on("vector", lambda e, h=h, g=g, j=j: e.tensor_tensor(
                    lay.psL[1 + (g % 2)][:, h * 512:(h + 1) * 512],
                    lay.psL[1 + (g % 2)][:, h * 512:(h + 1) * 512],
                    mtd_f[:, j * 512:(j + 1) * 512], ALU.add))
            b.step()
            b.on("scalar", lambda e, g=g: e.activation(
                probs[g % 2], lay.psL[1 + (g % 2)][:, 0:1024], AF.Exp))
            b.step()
            b.on("tensor", lambda e, g=g, emit_pv=emit_pv: emit_pv(e, g))
            b.step()
            b.on("vector", lambda e: e.tensor_copy(rec2[:, 0:2], ps[1][:, 0:2]))
            b.step()
            b.on("vector", lambda e: e.reciprocal(rec2[:, 0:2], rec2[:, 0:2]))
            b.step()
            for h in range(2):
                b.on("vector", lambda e, h=h, j=j, pair=pair: e.tensor_scalar(
                    ytok[j][:, pair * 128 + h * 64: pair * 128 + (h + 1) * 64],
                    ps[0][:, h * 64:(h + 1) * 64], rec2[:, h:h + 1], None, ALU.mult))
            b.step()

    for j in range(16):
        b.step()
        def ytr(e, j=j):
            r = None
            for c in range(DC):
                r = e.transpose(pt[:, c * 128:(c + 1) * 128],
                                ytok[j][:, c * 128:(c + 1) * 128], lay.ident_bf[:])
            return r
        b.on("tensor", ytr)
        b.step()
        for c in range(DC):
            b.on("vector", lambda e, c=c, j=j: e.tensor_copy(
                yT[c][:, j * 128:(j + 1) * 128], pt[:, c * 128:(c + 1) * 128]))
    boc_list = [boc[0:128, k:k + 1] for k in range(DC)]
    lay.proj_f(WR[:, 0:8 * D], yT, TC, D, at_o, bias=boc_list, act=AF.Identity)
    b.step()
    return b




def build_sandwich(precast=False):
    b = BSP()
    xT = b.din("xT", [D, TC], F32)
    mixT = b.din("mixT", [D, TC], BF16)
    xo = b.dout("xoT", [D, TC], F32)
    if precast:
        w1e = b.din("w1e", [D, 4096], F32)
        w2e = b.din("w2e", [4096, D], F32)
        w1b_o = b.dout("w1b", [D, 4096], BF16)
        w2b_o = b.dout("w2b", [4096, D], BF16)
    lay = Lay(b, f32_cols=10300, bf_cols=41400 if precast else 33000,
              n_ps=6, pt_cols=0, n_stg=2)
    fa, ba = lay.fa, lay.ba
    mb = fa.alloc(TC); rb = fa.alloc(TC); tb_ = fa.alloc(TC)
    xblk = ([mb[:, i * 512:(i + 1) * 512] for i in range(0)]
            + [fa.alloc(512) for _ in range(8)])
    mix = ba.alloc8(TC)
    scr = ba.alloc8(TC)
    b.step()
    for k in range(DC):
        b.dma(DMA_ENG[k % 3], mix[k], mixT[k * 128:(k + 1) * 128, :])
    if precast:
        buf = [ba.alloc(4096), ba.alloc(4096)]
        jobs = [[(w1e[k * 128:(k + 1) * 128, :], w1b_o[k * 128:(k + 1) * 128, :],
                  0, 4096)] for k in range(8)]
        jobs += [[(w2e[(g * 4 + j) * 128:(g * 4 + j + 1) * 128, :],
                   w2b_o[(g * 4 + j) * 128:(g * 4 + j + 1) * 128, :],
                   j * 1024, 1024) for j in range(4)] for g in range(8)]
        prev = None
        for i, job in enumerate(jobs):
            b.step()
            bu = buf[i % 2]
            for (din_, dout_, off, n) in job:
                b.dma("gpsimd", bu[:, off:off + n], din_)
            if prev is not None:
                pj, pbuf = prev
                for qi, (pdin, pdout, off, n) in enumerate(pj):
                    b.dma(("sync", "scalar")[qi % 2], pdout, pbuf[:, off:off + n])
            prev = (job, bu)
        b.step()
        pj, pbuf = prev
        for qi, (pdin, pdout, off, n) in enumerate(pj):
            b.dma(("sync", "scalar")[qi % 2], pdout, pbuf[:, off:off + n])
    lay.ln_resid_stream(mix, scr, mb, rb, tb_, T=TC, x_in=xT, x_out=xo,
                        xblk=xblk, x_is_bf=True)
    b.step()
    return b


def build_l3a_A():
    b = BSP()
    xT = b.din("xT", [D, TC], F32)
    wq = b.din("wq", [D, D], F32)
    wv = b.din("wv", [D, D], F32)
    bq_r = b.din("bq_r", [1, D], F32)
    bv_r = b.din("bv_r", [1, D], F32)
    qsm_o = b.dout("qsmT", [D, TC], BF16)
    v_o = b.dout("vtok", [TC, D], BF16)
    ml_o = b.dout("mloc", [D, 1], F32)

    lay = Lay(b, f32_cols=16000, bf_cols=60000, n_ps=6, pt_cols=1024, n_stg=6)
    fa, ba, ps, pt = lay.fa, lay.ba, lay.ps, lay.pt
    SCONST = float(64.0 ** 0.5) ** (-0.25)

    mb = fa.alloc(TC); rb = fa.alloc(TC); tb_ = fa.alloc(TC)
    qf = fa.alloc(D)
    bb2 = fa.alloc(2 * D)
    sm = fa.alloc(8)
    w32 = mb

    xln = ba.alloc8(TC)
    scr = ba.alloc8(TC)
    WR = ba.alloc(8 * D)
    qT_sb = ba.alloc8(TC)
    qrb = ba.alloc(D)
    brow = ba.alloc(D)

    b.step()
    xbf = qT_sb
    for k in range(DC):
        b.dma("gpsimd", xbf[k], xT[k * 128:(k + 1) * 128, :])
    lay.ln(xbf, scr, mb, rb, tb_, T=TC, x_is_bf=True, out_bf=xln)

    for bi, brd in enumerate((bq_r, bv_r)):
        b.step()
        b.dma("sync", qf[0:1, 0:D], brd[0:1, :])
        b.step()
        b.on("vector", lambda e: e.tensor_copy(brow[0:1, 0:D], qf[0:1, 0:D]))
        b.step()
        def bmm(e):
            r = None
            for h in range(2):
                r = e.matmul(ps[h][:, 0:512], lay.ones_bf[0:1, 0:128],
                             brow[0:1, h * 512:(h + 1) * 512], start=True, stop=True)
            return r
        b.on("tensor", bmm)
        b.step()
        for h in range(2):
            b.on("vector", lambda e, bi=bi, h=h: e.tensor_copy(
                bb2[:, bi * D + h * 512: bi * D + (h + 1) * 512], ps[h][:, 0:512]))

    def loadw(dram):
        b.step()
        for k in range(DC):
            b.dma("gpsimd", WR[:, k * D:(k + 1) * D], dram[k * 128:(k + 1) * 128, :])
        return WR[:, 0:8 * D]

    wbf = loadw(wv)
    jobs = [(tt, ob) for tt in range(16) for ob in range(2)]
    groups = [jobs[i:i + 3] for i in range(0, len(jobs), 3)]
    pend = []
    for gi, g in enumerate(groups):
        banks = [ps[3 * (gi % 2) + i] for i in range(len(g))]
        b.step()
        def mms(e, g=g, banks=banks):
            r = None
            for i, (tt, ob) in enumerate(g):
                for k in range(DC):
                    r = e.matmul(banks[i][:, 0:512],
                                 xln[k][:, tt * 128:(tt + 1) * 128],
                                 WR[:, k * D + ob * 512: k * D + ob * 512 + 512],
                                 start=(k == 0), stop=(k == DC - 1))
            return r
        b.on("tensor", mms)
        new_pend = []
        if gi > 0:
            prevg = groups[gi - 1]
            prevbanks = [ps[3 * ((gi - 1) % 2) + i] for i in range(len(prevg))]
            for i, (tt, ob) in enumerate(prevg):
                s = lay.stg_next()[:, 0:512]
                b.on("vector", lambda e, s=s, bank=prevbanks[i], ob=ob: e.scalar_tensor_tensor(
                    s, bank[:, 0:512], 1.0, bb2[:, D + ob * 512:D + (ob + 1) * 512],
                    ALU.mult, ALU.add))
                new_pend.append((tt, ob, s))
        for (tt, ob, s) in pend:
            b.dma(DMA_ENG[(tt + ob) % 3], v_o[tt * 128:(tt + 1) * 128, ob * 512:(ob + 1) * 512], s)
        pend = new_pend
    b.step()
    lastg = groups[-1]
    lastbanks = [ps[3 * ((len(groups) - 1) % 2) + i] for i in range(len(lastg))]
    new_pend = []
    for i, (tt, ob) in enumerate(lastg):
        s = lay.stg_next()[:, 0:512]
        b.on("vector", lambda e, s=s, bank=lastbanks[i], ob=ob: e.scalar_tensor_tensor(
            s, bank[:, 0:512], 1.0, bb2[:, D + ob * 512:D + (ob + 1) * 512],
            ALU.mult, ALU.add))
        new_pend.append((tt, ob, s))
    for (tt, ob, s) in pend:
        b.dma(DMA_ENG[(tt + ob) % 3], v_o[tt * 128:(tt + 1) * 128, ob * 512:(ob + 1) * 512], s)
    b.step()
    for (tt, ob, s) in new_pend:
        b.dma(DMA_ENG[(tt + ob) % 3], v_o[tt * 128:(tt + 1) * 128, ob * 512:(ob + 1) * 512], s)
    b.step()

    wbf = loadw(wq)
    for tt in range(16):
        b.step()
        def mmq(e, tt=tt):
            r = None
            for ob in range(2):
                for k in range(DC):
                    r = e.matmul(ps[ob][:, 0:512],
                                 xln[k][:, tt * 128:(tt + 1) * 128],
                                 WR[:, k * D + ob * 512: k * D + ob * 512 + 512],
                                 start=(k == 0), stop=(k == DC - 1))
            return r
        b.on("tensor", mmq)
        b.step()
        for ob in range(2):
            b.on("vector", lambda e, ob=ob: e.scalar_tensor_tensor(
                qf[:, ob * 512:(ob + 1) * 512], ps[ob][:, 0:512], 1.0,
                bb2[:, ob * 512:(ob + 1) * 512], ALU.mult, ALU.add))
        for h in range(16):
            hs = qf[:, h * 64:(h + 1) * 64]
            b.step()
            b.on("vector", lambda e, hs=hs: e.reduce_max(sm[:, 0:1], hs, AX.X))
            b.step()
            b.on("vector", lambda e: e.tensor_scalar(sm[:, 1:2], sm[:, 0:1], -1.0, None, ALU.mult))
            b.step()
            b.on("scalar", lambda e, hs=hs: e.activation(hs, hs, AF.Exp, bias=sm[:, 1:2],
                                                         accum_out=sm[:, 2:3]))
            b.step()
            b.on("vector", lambda e: e.reciprocal(sm[:, 2:3], sm[:, 2:3]))
            b.step()
            b.on("vector", lambda e, hs=hs: e.tensor_scalar(hs, hs, sm[:, 2:3],
                                                            SCONST, ALU.mult, ALU.mult))
        b.step()
        b.on("vector", lambda e: e.tensor_copy(qrb[:, 0:D], qf[:, 0:D]))
        b.step()
        def qtr(e):
            r = None
            for c in range(DC):
                r = e.transpose(pt[:, c * 128:(c + 1) * 128],
                                qrb[:, c * 128:(c + 1) * 128], lay.ident_bf[:])
            return r
        b.on("tensor", qtr)
        b.step()
        for c in range(DC):
            b.on("vector", lambda e, c=c, tt=tt: e.tensor_copy(
                qT_sb[c][:, tt * 128:(tt + 1) * 128], pt[:, c * 128:(c + 1) * 128]))
    b.step()
    for c in range(DC):
        b.dma(DMA_ENG[c % 3], qsm_o[c * 128:(c + 1) * 128, :], qT_sb[c])
    b.step()
    for c in range(DC):
        b.on("vector", lambda e, c=c: e.reduce_max(sm[:, 3:4], qT_sb[c], AX.X))
        b.on("vector", lambda e, c=c: e.tensor_copy(mb[:, c:c + 1], sm[:, 3:4])) if False else None
    b.step()
    for c in range(DC):
        b.on("vector", lambda e, c=c: e.reduce_max(mb[:, c:c + 1], qT_sb[c], AX.X))
    b.step()
    for c in range(DC):
        b.dma(DMA_ENG[c % 3], ml_o[c * 128:(c + 1) * 128, 0:1], mb[:, c:c + 1])
    b.step()
    return b


def build_l3a_B():
    b = BSP()
    qsmT = b.din("qsmT", [D, TC], BF16)
    vtok = b.din("vtok", [TC, D], BF16)
    ml4 = b.din("mloc4", [D, 4], F32)
    s_o = b.dout("spart", [D, 1], F32)
    cx_o = b.dout("ctxp", [D, 64], F32)

    lay = Lay(b, f32_cols=8000, bf_cols=70000, n_ps=6, pt_cols=1024, n_stg=6)
    fa, ba, ps, pt = lay.fa, lay.ba, lay.ps, lay.pt

    m4 = fa.alloc(4 * DC)
    sm = fa.alloc(8)
    cstg = fa.alloc(64)
    qT = ba.alloc8(TC)
    ek = ba.alloc8(TC)
    ektok = [ba.alloc(D) for _ in range(16)]
    vt = [ba.alloc(D) for _ in range(16)]

    b.step()
    for k in range(DC):
        b.dma("gpsimd", qT[k], qsmT[k * 128:(k + 1) * 128, :])
        b.dma("sync", m4[0:128, 4 * k:4 * k + 4], ml4[k * 128:(k + 1) * 128, :])
    for tt in range(16):
        b.dma("scalar", vt[tt], vtok[tt * 128:(tt + 1) * 128, :])
    for k in range(DC):
        b.step()
        b.on("vector", lambda e, k=k: e.reduce_max(sm[:, 0:1], m4[0:128, 4 * k:4 * k + 4], AX.X))
        b.step()
        b.on("vector", lambda e: e.tensor_scalar(sm[:, 1:2], sm[:, 0:1], -1.0, None, ALU.mult))
        b.step()
        b.on("scalar", lambda e, k=k: e.activation(ek[k], qT[k], AF.Exp, bias=sm[:, 1:2]))
        b.step()
        b.on("vector", lambda e, k=k: e.reduce_sum(sm[:, 2:3], ek[k], AX.X))
        b.step()
        b.dma("sync", s_o[k * 128:(k + 1) * 128, 0:1], sm[:, 2:3])
    for tt in range(16):
        b.step()
        def etr(e, tt=tt):
            r = None
            for c in range(DC):
                r = e.transpose(pt[:, c * 128:(c + 1) * 128],
                                ek[c][:, tt * 128:(tt + 1) * 128], lay.ident_bf[:])
            return r
        b.on("tensor", etr)
        b.step()
        for c in range(DC):
            b.on("vector", lambda e, c=c, tt=tt: e.tensor_copy(
                ektok[tt][:, c * 128:(c + 1) * 128], pt[:, c * 128:(c + 1) * 128]))
    for p2 in range(8):
        b.step()
        def cmm(e, p2=p2):
            r = None
            for hh in range(2):
                h = 2 * p2 + hh
                for tt in range(16):
                    r = e.matmul(ps[hh][0:64, 0:64],
                                 ektok[tt][:, h * 64:(h + 1) * 64],
                                 vt[tt][:, h * 64:(h + 1) * 64],
                                 start=(tt == 0), stop=(tt == 15))
            return r
        b.on("tensor", cmm)
        b.step()
        for hh in range(2):
            b.on("vector", lambda e, hh=hh: e.tensor_copy(cstg[0:64, 0:64], ps[hh][0:64, 0:64])) \
                if False else None
        for hh in range(2):
            b.on("vector" if hh == 0 else "scalar",
                 (lambda e, hh=hh: e.tensor_copy(cstg[0:64, 0:64], ps[hh][0:64, 0:64])) if hh == 0
                 else (lambda e, hh=hh: e.copy(cstg[64:128, 0:64], ps[hh][0:64, 0:64])))
        b.step()
        b.dma("sync", cx_o[(2 * p2) * 64:(2 * p2) * 64 + 64, :], cstg[0:64, 0:64])
        b.dma("gpsimd", cx_o[(2 * p2 + 1) * 64:(2 * p2 + 1) * 64 + 64, :], cstg[64:128, 0:64])
    b.step()
    return b


def build_l3a_C():
    b = BSP()
    xT = b.din("xT", [D, TC], F32)
    qsmT = b.din("qsmT", [D, TC], BF16)
    cx4 = b.din("ctxp4", [D, 4 * 64], F32)
    s4 = b.din("spart4", [D, 4], F32)
    ml4 = b.din("mloc4", [D, 4], F32)
    wo = b.din("wo", [D, D], F32)
    bo_c = b.din("bo_c", [D, 1], F32)
    xo = b.dout("xoT", [D, TC], F32)

    lay = Lay(b, f32_cols=14000, bf_cols=60000, n_ps=6, pt_cols=0, n_stg=6)
    fa, ba, ps = lay.fa, lay.ba, lay.ps
    SCONST = float(64.0 ** 0.5) ** (-0.25)

    mb = fa.alloc(TC); rb = fa.alloc(TC); tb_ = fa.alloc(TC)
    cxf = fa.alloc(4 * 64)
    s4f = fa.alloc(4 * DC)
    m4f = fa.alloc(4 * DC)
    boc = fa.alloc(DC)
    sm = fa.alloc(8)
    xblk = ([rb[:, i * 512:(i + 1) * 512] for i in range(0)]
            + [fa.alloc(512) for _ in range(8)])
    w32 = mb

    qT = ba.alloc8(TC)
    scr = ba.alloc8(TC)
    attnT = ba.alloc8(TC)
    ctxbf = ba.alloc(64)
    WR = ba.alloc(8 * D)

    b.step()
    for k in range(DC):
        b.dma("gpsimd", qT[k], qsmT[k * 128:(k + 1) * 128, :])
        b.dma("sync", s4f[0:128, 4 * k:4 * k + 4], s4[k * 128:(k + 1) * 128, :])
        b.dma("sync", m4f[0:128, 4 * k:4 * k + 4], ml4[k * 128:(k + 1) * 128, :])
        b.dma("scalar", boc[0:128, k:k + 1], bo_c[k * 128:(k + 1) * 128, :])
    b.step()
    for k in range(DC):
        b.dma("gpsimd", WR[:, k * D:(k + 1) * D], wo[k * 128:(k + 1) * 128, :])

    for k in range(DC):
        b.step()
        b.dma("sync", cxf[:, 0:256], cx4[k * 128:(k + 1) * 128, :].rearrange("p (r e) -> p (r e)", r=4) if False else cx4[k * 128:(k + 1) * 128, :])
        b.step()
        b.on("vector", lambda e, k=k: e.reduce_sum(sm[:, 0:1], s4f[0:128, 4 * k:4 * k + 4], AX.X))
        b.step()
        b.on("vector", lambda e: e.reciprocal(sm[:, 0:1], sm[:, 0:1]))
        b.step()
        b.on("vector", lambda e: e.tensor_add(cxf[:, 0:64], cxf[:, 0:64], cxf[:, 64:128]))
        b.step()
        b.on("vector", lambda e: e.tensor_add(cxf[:, 128:192], cxf[:, 128:192], cxf[:, 192:256]))
        b.step()
        b.on("vector", lambda e: e.tensor_add(cxf[:, 0:64], cxf[:, 0:64], cxf[:, 128:192]))
        b.step()
        b.on("vector", lambda e: e.tensor_scalar(cxf[:, 0:64], cxf[:, 0:64], sm[:, 0:1],
                                                 SCONST, ALU.mult, ALU.mult))
        b.step()
        b.on("vector", lambda e: e.tensor_copy(ctxbf[:, 0:64], cxf[:, 0:64]))
        b.step()
        def amm(e, k=k):
            r = None
            for blk in range(4):
                for hh in range(2):
                    r = e.matmul(ps[blk][hh * 64:(hh + 1) * 64, 0:512],
                                 ctxbf[hh * 64:(hh + 1) * 64, 0:64],
                                 qT[k][hh * 64:(hh + 1) * 64, blk * 512:(blk + 1) * 512],
                                 start=True, stop=True)
            return r
        b.on("tensor", amm)
        b.step()
        for blk in range(4):
            b.on("vector", lambda e, k=k, blk=blk: e.tensor_copy(
                attnT[k][:, blk * 512:(blk + 1) * 512], ps[blk][:, 0:512]))
    mixo = scr
    boc_list = [boc[0:128, k:k + 1] for k in range(DC)]
    lay.proj_f(WR[:, 0:8 * D], attnT, TC, D, mixo, bias=boc_list, act=AF.Identity)
    xsc = qT
    lay.ln_resid_stream(mixo, xsc, mb, rb, tb_, T=TC, x_in=xT, x_out=xo,
                        xblk=xblk, x_is_bf=True)
    b.step()
    return b


def build_l3f_1():
    b = BSP()
    xT = b.din("xT", [D, TC], F32)
    g1 = b.din("g1", [D, 1], F32)
    b1 = b.din("b1c", [D, 1], F32)
    wq = b.din("wq", [D, D], F32)
    wk = b.din("wk", [D, D], F32)
    wv = b.din("wv", [D, D], F32)
    qT_o = b.dout("qT", [D, TC], BF16)
    kT_o = b.dout("kT", [D, TC], BF16)
    v_o = b.dout("v", [TC, D], BF16)
    m0_o = b.dout("m0T", [D, TC], BF16)

    lay = Lay(b, f32_cols=13000, bf_cols=59500, n_ps=6, pt_cols=0, n_stg=6)
    fa, ba, ps = lay.fa, lay.ba, lay.ps

    mb = fa.alloc(TC); rb = fa.alloc(TC); tb_ = fa.alloc(TC)
    gb = fa.alloc(2 * DC)
    w32 = mb

    m0 = ba.alloc8(TC)
    scr = ba.alloc8(TC)
    xbf = ba.alloc8(TC)
    h = xbf
    WR = ba.alloc(8 * D)

    b.step()
    for k in range(DC):
        b.dma("gpsimd", xbf[k], xT[k * 128:(k + 1) * 128, :])
        b.dma("sync", gb[0:128, k:k + 1], g1[k * 128:(k + 1) * 128, :])
        b.dma("sync", gb[0:128, DC + k:DC + k + 1], b1[k * 128:(k + 1) * 128, :])
    lay.ln(xbf, scr, mb, rb, tb_, T=TC, x_is_bf=True, out_bf=m0)
    gl = [gb[0:128, k:k + 1] for k in range(DC)]
    bl = [gb[0:128, DC + k:DC + k + 1] for k in range(DC)]
    lay.ln(m0, scr, mb, rb, tb_, T=TC, x_is_bf=True, out_bf=h, gain_bias=(gl, bl))
    b.step()
    for k in range(DC):
        b.dma(DMA_ENG[k % 3], m0_o[k * 128:(k + 1) * 128, :], m0[k])

    def loadw(dram):
        b.step()
        for k in range(DC):
            b.dma("gpsimd", WR[:, k * D:(k + 1) * D], dram[k * 128:(k + 1) * 128, :])
        return WR[:, 0:8 * D]

    wbf = loadw(wq)
    lay.proj_f(wbf, h, TC, D, qT_o, scale=1.0 / 8.0)
    wbf = loadw(wk)
    lay.proj_f(wbf, h, TC, D, kT_o)
    wbf = loadw(wv)
    lay.proj_t(wbf, h, TC, D, v_o)
    b.step()
    return b


def build_l3f_2():
    b = BSP()
    qT_i = b.din("qT", [D, TC], BF16)
    kTf = b.din("kTf", [D, TSEQ], BF16)
    vf = b.din("vf", [TSEQ, D], BF16)
    m0T = b.din("m0T", [D, TC], BF16)
    wo = b.din("wo", [D, D], F32)
    xi_o = b.dout("xintT", [D, TC], BF16)

    lay = Lay(b, f32_cols=5200, bf_cols=79900, n_ps=6, pt_cols=1024, n_stg=6)
    fa, ba, ps, pt = lay.fa, lay.ba, lay.ps, lay.pt

    mb = fa.alloc(TC)
    rec2 = fa.alloc(2)
    w32 = mb

    m0blk = [ba.alloc(512) for _ in range(DC)]
    qT_sb = ba.alloc8(TC)
    kpair = ba.alloc(TSEQ)
    vpair = [ba.alloc(128) for _ in range(64)]
    probs = [ba.alloc(1024) for _ in range(2)]
    yT = ba.alloc8(TC)
    WR = ba.alloc(8 * D)
    mk = ba.mark()
    ytok = [ba.alloc(D) for _ in range(16)]

    b.step()
    for k in range(DC):
        b.dma("gpsimd", qT_sb[k], qT_i[k * 128:(k + 1) * 128, :])
    b.step()
    for k in range(DC):
        b.dma("gpsimd", WR[:, k * D:(k + 1) * D], wo[k * 128:(k + 1) * 128, :])

    NG = 16
    for pair in range(8):
        b.step()
        b.dma("sync", kpair[:, :], kTf[pair * 128:(pair + 1) * 128, :])
        for ch in range(64):
            b.dma(("gpsimd", "scalar")[ch % 2], vpair[ch],
                  vf[ch * 128:(ch + 1) * 128, pair * 128:(pair + 1) * 128])
        for j in range(16):
            qslA = qT_sb[pair][0:64, j * 128:(j + 1) * 128]
            qslB = qT_sb[pair][64:128, j * 128:(j + 1) * 128]
            for i in range(NG + 2):
                b.step()
                def work(e, i=i, qslA=qslA, qslB=qslB):
                    r = None
                    if i < NG:
                        bank = lay.psL[1 + (i % 2)]
                        for h, qsl in ((0, qslA), (1, qslB)):
                            for c in range(4):
                                kcol = (i * 4 + c) * 128
                                r = e.matmul(bank[:, h * 512 + c * 128:h * 512 + (c + 1) * 128],
                                             kpair[h * 64:(h + 1) * 64, kcol:kcol + 128],
                                             qsl, start=True, stop=True)
                    g = i - 2
                    if 0 <= g:
                        pb = probs[g % 2]
                        first = (g == 0)
                        last = (g == NG - 1)
                        for h in range(2):
                            for c in range(4):
                                ch = g * 4 + c
                                r = e.matmul(ps[0][:, h * 64:(h + 1) * 64],
                                             pb[:, h * 512 + c * 128:h * 512 + (c + 1) * 128],
                                             vpair[ch][:, h * 64:(h + 1) * 64],
                                             start=(first and c == 0 and h == 0),
                                             stop=(last and c == 3))
                        for h in range(2):
                            for c in range(4):
                                r = e.matmul(ps[1][:, h:h + 1],
                                             pb[:, h * 512 + c * 128:h * 512 + (c + 1) * 128],
                                             lay.ones_bf[:, 0:1],
                                             start=(first and c == 0 and h == 0),
                                             stop=(last and c == 3))
                    return r
                b.on("tensor", work)
                ge = i - 1
                if 0 <= ge < NG:
                    b.on("scalar", lambda e, ge=ge: e.activation(
                        probs[ge % 2], lay.psL[1 + (ge % 2)][:, 0:1024], AF.Exp))
            b.step()
            b.on("vector", lambda e: e.tensor_copy(rec2[:, 0:2], ps[1][:, 0:2]))
            b.step()
            b.on("vector", lambda e: e.reciprocal(rec2[:, 0:2], rec2[:, 0:2]))
            b.step()
            for h in range(2):
                b.on("vector", lambda e, h=h, j=j, pair=pair: e.tensor_scalar(
                    ytok[j][:, pair * 128 + h * 64: pair * 128 + (h + 1) * 64],
                    ps[0][:, h * 64:(h + 1) * 64], rec2[:, h:h + 1], None, ALU.mult))
            b.step()

    for j in range(16):
        b.step()
        def ytr(e, j=j):
            r = None
            for c in range(DC):
                r = e.transpose(pt[:, c * 128:(c + 1) * 128],
                                ytok[j][:, c * 128:(c + 1) * 128], lay.ident_bf[:])
            return r
        b.on("tensor", ytr)
        b.step()
        for c in range(DC):
            b.on("vector", lambda e, c=c, j=j: e.tensor_copy(
                yT[c][:, j * 128:(j + 1) * 128], pt[:, c * 128:(c + 1) * 128]))
    ba.reset(mk)
    xint8 = ba.alloc8(TC)
    lay.proj_f(WR[:, 0:8 * D], yT, TC, D, xint8)
    for blk in range(4):
        b.step()
        for k in range(DC):
            b.dma(DMA_ENG[k % 3], m0blk[k][:, 0:512],
                  m0T[k * 128:(k + 1) * 128, blk * 512:(blk + 1) * 512])
        b.step()
        for k in range(DC):
            b.on("vector", lambda e, k=k, blk=blk: e.tensor_add(
                xint8[k][:, blk * 512:(blk + 1) * 512],
                xint8[k][:, blk * 512:(blk + 1) * 512], m0blk[k][:, 0:512]))
        b.step()
        for k in range(DC):
            b.dma(DMA_ENG[k % 3], xi_o[k * 128:(k + 1) * 128, blk * 512:(blk + 1) * 512],
                  xint8[k][:, blk * 512:(blk + 1) * 512])
    b.step()
    return b


def build_l3f_3(debug=False):
    b = BSP()
    xT = b.din("xT", [D, TC], F32)
    xintT = b.din("xintT", [D, TC], BF16)
    g2 = b.din("g2", [D, 1], F32)
    b2c = b.din("b2c", [D, 1], F32)
    wg = b.din("wg", [D, 8], F32)
    w1f = b.din("w1f", [8 * D, 4096], BF16)
    b1f = b.din("b1f", [8 * 4096, 1], F32)
    w2f = b.din("w2f", [8 * 4096, D], BF16)
    b2f = b.din("b2f", [8 * D, 1], F32)
    xo = b.dout("xoT", [D, TC], F32)

    lay = Lay(b, f32_cols=8600, bf_cols=82500, n_ps=6, pt_cols=1024, n_stg=6)
    fa, ba, ps, pt = lay.fa, lay.ba, lay.ps, lay.pt

    mb = fa.alloc(TC); rb = fa.alloc(TC); tb_ = fa.alloc(TC)
    gb = fa.alloc(2 * DC)
    rwbc = fa.alloc(TC)
    b1s = fa.alloc(16 * 2)
    b2s = fa.alloc(DC)
    ltokf = [fa.alloc(8) for _ in range(16)]
    rwtok = [fa.alloc(8) for _ in range(16)]
    m1 = fa.alloc(1); nm1 = fa.alloc(1); m2 = fa.alloc(1); den = fa.alloc(1)
    msk = fa.alloc(8)
    xblk = [tb_[:, i * 512:(i + 1) * 512] for i in range(4)] + \
           [rwbc[:, i * 512:(i + 1) * 512] for i in range(4)]
    rrow = tb_[0:1, 0:TC]
    w32 = mb

    m_bf = ba.alloc8(TC)
    acc = ba.alloc8(TC)
    WR = ba.alloc(2 * 16 * D)
    scr = ba.alloc8(TC)
    h1h = [scr[i // 4][:, (i % 4) * 512:(i % 4 + 1) * 512] for i in range(16)]
    yeac = [scr[4 + i // 4][:, (i % 4) * 512:(i % 4 + 1) * 512] for i in range(DC)]
    stg2 = [scr[6][:, i * 512:(i + 1) * 512] for i in range(3)]
    rrow_bf = scr[7][0:1, 0:512]
    lg = rb

    b.step()
    xibf = acc
    for k in range(DC):
        b.dma("gpsimd", xibf[k], xintT[k * 128:(k + 1) * 128, :])
        b.dma("sync", gb[0:128, k:k + 1], g2[k * 128:(k + 1) * 128, :])
        b.dma("sync", gb[0:128, DC + k:DC + k + 1], b2c[k * 128:(k + 1) * 128, :])
    gl = [gb[0:128, k:k + 1] for k in range(DC)]
    bl = [gb[0:128, DC + k:DC + k + 1] for k in range(DC)]
    lay.ln(xibf, scr, mb, rb, tb_, T=TC, x_is_bf=True, out_bf=m_bf,
           gain_bias=(gl, bl))

    b.step()
    for k in range(DC):
        b.dma("gpsimd", stg2[1][:, k * 8:(k + 1) * 8], wg[k * 128:(k + 1) * 128, :])
    b.step()
    def mm_router(e):
        r = None
        for blk in range(4):
            for k in range(DC):
                r = e.matmul(ps[blk][0:8, 0:512], stg2[1][:, k * 8:(k + 1) * 8],
                             m_bf[k][:, blk * 512:(blk + 1) * 512],
                             start=(k == 0), stop=(k == DC - 1))
        return r
    b.on("tensor", mm_router)
    b.step()
    for blk in range(4):
        b.on("vector", lambda e, blk=blk: e.tensor_copy(
            lg[0:8, blk * 512:(blk + 1) * 512], ps[blk][0:8, 0:512]))
    for tt in range(16):
        b.step()
        b.on("tensor", lambda e, tt=tt: e.transpose(
            ps[4][:, 0:8], lg[0:8, tt * 128:(tt + 1) * 128], lay.ident_f32[0:8, 0:8]))
        b.step()
        b.on("vector", lambda e, tt=tt: e.tensor_copy(ltokf[tt][0:128, 0:8], ps[4][:, 0:8]))
    BIGNEG = -1.0e30
    for tt in range(16):
        lt = ltokf[tt][0:128, 0:8]
        rw_ = rwtok[tt][0:128, 0:8]
        b.step()
        b.on("vector", lambda e, lt=lt: e.reduce_max(m1[:, 0:1], lt, AX.X))
        b.step()
        b.on("vector", lambda e: e.tensor_scalar(nm1[:, 0:1], m1[:, 0:1], -1.0, None, ALU.mult))
        b.on("vector", lambda e, lt=lt: e.tensor_scalar(msk[:, 0:8], lt, m1[:, 0:1], None, ALU.is_ge))
        b.step()
        b.on("vector", lambda e, lt=lt: e.scalar_tensor_tensor(
            msk[:, 0:8], msk[:, 0:8], BIGNEG, lt, ALU.mult, ALU.add))
        b.step()
        b.on("vector", lambda e: e.reduce_max(m2[:, 0:1], msk[:, 0:8], AX.X))
        b.step()
        b.on("vector", lambda e, lt=lt: e.tensor_scalar(msk[:, 0:8], lt, m2[:, 0:1], None, ALU.is_ge))
        b.step()
        b.on("scalar", lambda e: e.activation(den[:, 0:1], m2[:, 0:1], AF.Exp, bias=nm1[:, 0:1]))
        b.step()
        b.on("vector", lambda e: e.tensor_scalar(den[:, 0:1], den[:, 0:1], 1.0, None, ALU.add))
        b.step()
        b.on("vector", lambda e: e.reciprocal(den[:, 0:1], den[:, 0:1]))
        b.step()
        b.on("scalar", lambda e, lt=lt, rw_=rw_: e.activation(rw_, lt, AF.Exp, bias=nm1[:, 0:1]))
        b.step()
        b.on("vector", lambda e, rw_=rw_: e.tensor_mul(rw_, rw_, msk[:, 0:8]))
        b.step()
        b.on("vector", lambda e, rw_=rw_: e.tensor_scalar(rw_, rw_, den[:, 0:1], None, ALU.mult))
    for tt in range(16):
        b.step()
        b.on("tensor", lambda e, tt=tt: e.transpose(
            ps[4][0:8, 0:128], rwtok[tt][0:128, 0:8], lay.ident_f32[:, :]))
        b.step()
        b.on("vector", lambda e, tt=tt: e.tensor_copy(
            lg[0:8, tt * 128:(tt + 1) * 128], ps[4][0:8, 0:128]))

    if debug:
        d_m = b.dout("d_m", [D, TC], BF16)
        d_lg = b.dout("d_lg", [8, TC], F32)
        b.step()
        for k in range(DC):
            b.dma(DMA_ENG[k % 3], d_m[k * 128:(k + 1) * 128, :], m_bf[k])
        b.dma("sync", d_lg[:, :], lg[0:8, 0:TC])
    b.step()
    for k in range(DC):
        b.on("vector", lambda e, k=k: e.memset(acc[k], 0.0))
    W1h = WR[:, 0:16 * D]
    W2h = WR[:, 16 * D:32 * D]
    for ex in range(8):
        b.step()
        b.dma("gpsimd", rrow[0:1, 0:TC], lg[ex:ex + 1, 0:TC])
        for k in range(DC):
            b.dma("sync", b2s[0:128, k:k + 1], b2f[ex * D + k * 128: ex * D + (k + 1) * 128, :])
        b.step()
        b.on("vector", lambda e: e.tensor_copy(rrow_bf[0:1, 0:512], rrow[0:1, 0:512]))
        b.on("vector", lambda e: e.tensor_copy(h1h[1][0:1, 0:512], rrow[0:1, 512:1024]))
        b.on("vector", lambda e: e.tensor_copy(h1h[2][0:1, 0:512], rrow[0:1, 1024:1536]))
        b.on("vector", lambda e: e.tensor_copy(h1h[3][0:1, 0:512], rrow[0:1, 1536:2048]))
        b.step()
        def mm_gate(e):
            r = None
            srcs = [rrow_bf, h1h[1][0:1, 0:512], h1h[2][0:1, 0:512], h1h[3][0:1, 0:512]]
            for blk in range(4):
                r = e.matmul(ps[blk][:, 0:512], lay.ones_bf[0:1, 0:128],
                             srcs[blk][0:1, 0:512], start=True, stop=True)
            return r
        b.on("tensor", mm_gate)
        b.step()
        for blk in range(4):
            b.on("vector", lambda e, blk=blk: e.tensor_copy(
                rwbc[:, blk * 512:(blk + 1) * 512], ps[blk][:, 0:512]))
        for half in range(2):
            b.step()
            qi = 0
            for k in range(DC):
                b.dma(DMA_ENG[qi % 3], W1h[:, k * 2048:(k + 1) * 2048],
                      w1f[ex * D + k * 128: ex * D + (k + 1) * 128,
                          half * 2048:(half + 1) * 2048])
                qi += 1
            for k in range(16):
                b.dma(DMA_ENG[qi % 3], W2h[:, k * D:(k + 1) * D],
                      w2f[ex * 4096 + half * 2048 + k * 128:
                          ex * 4096 + half * 2048 + (k + 1) * 128, :])
                qi += 1
            for hc in range(16):
                b.dma(DMA_ENG[qi % 3], b1s[0:128, hc:hc + 1],
                      b1f[ex * 4096 + half * 2048 + hc * 128:
                          ex * 4096 + half * 2048 + (hc + 1) * 128, :])
                qi += 1
            b1l = [b1s[0:128, hc:hc + 1] for hc in range(16)]
            for tb in range(4):
                mslice = [m_bf[k][:, tb * 512:(tb + 1) * 512] for k in range(DC)]
                lay.proj_f(W1h, mslice, 512, 2048, h1h, w_dout=2048,
                           act=AF.Gelu, bias=b1l)
                if debug and ex == 0 and half == 0 and tb == 0:
                    d_h1 = b.dout("d_h1", [16 * 128, 512], BF16)
                    d_rw = b.dout("d_rw", [128, TC], F32)
                    b.step()
                    for hc in range(16):
                        b.dma(DMA_ENG[hc % 3], d_h1[hc * 128:(hc + 1) * 128, :], h1h[hc])
                    b.dma("sync", d_rw[:, :], rwbc[:, 0:TC])
                def yst(oc, o, n, stgap, tb=tb):
                    def fn(e, oc=oc, o=o, n=n, stgap=stgap):
                        e.tensor_mul(stg2[0][:, 0:n], stgap,
                                     rwbc[:, tb * 512 + o: tb * 512 + o + n])
                        return e.tensor_add(acc[oc][:, tb * 512 + o: tb * 512 + o + n],
                                            acc[oc][:, tb * 512 + o: tb * 512 + o + n],
                                            stg2[0][:, 0:n])
                    b.on("vector", fn)
                lay.proj_f(W2h, h1h, 512, D, None, w_dout=D, din_c=16, store_fn=yst)
        b.step()
        for k in range(DC):
            b.on("vector", lambda e, k=k: e.scalar_tensor_tensor(
                acc[k], rwbc[:, 0:TC], b2s[0:128, k:k + 1], acc[k], ALU.mult, ALU.add))

    if debug:
        d_acc = b.dout("d_acc", [D, TC], BF16)
        b.step()
        for k in range(DC):
            b.dma(DMA_ENG[k % 3], d_acc[k * 128:(k + 1) * 128, :], acc[k])
    for blk in range(4):
        b.step()
        for k in range(DC):
            b.dma("gpsimd", xblk[k][:, 0:512],
                  xintT[k * 128:(k + 1) * 128, blk * 512:(blk + 1) * 512])
        b.step()
        for k in range(DC):
            b.on("vector", lambda e, k=k, blk=blk: e.tensor_add(
                acc[k][:, blk * 512:(blk + 1) * 512],
                acc[k][:, blk * 512:(blk + 1) * 512], xblk[k][:, 0:512]))
    lay.ln_resid_stream(acc, scr, mb, rb, tb_, T=TC, x_in=xT, x_out=xo,
                        xblk=xblk, x_is_bf=True)
    b.step()
    return b


def _rope_tables(pos0):
    j = np.arange(32)
    theta = 10000.0 ** (-2 * j / 64.0)
    pos = pos0 + np.arange(TC)
    f = pos[:, None] * theta[None, :]
    return (np.ascontiguousarray(np.tile(np.cos(f), (1, 16)).astype(np.float32)),
            np.ascontiguousarray(np.tile(np.sin(f), (1, 16)).astype(np.float32)))


def kernel(**I):
    C = np.ascontiguousarray
    f32 = np.float32
    const = Lay.const_inputs()
    col = lambda a: C(np.asarray(a).reshape(-1, 1).astype(f32))
    row = lambda a: C(np.asarray(a).reshape(1, -1).astype(f32))
    mat = lambda a: C(np.asarray(a).astype(f32))
    matT = lambda a: C(np.asarray(a).T.astype(f32))

    x = np.asarray(I["L0_attn_x"], dtype=f32)
    xts = xT_chunks(x)

    def halos(xts_list, n):
        out = []
        for c in range(NCORE):
            bb, cc = divmod(c, 4)
            if cc == 0:
                out.append(np.zeros((D, n), f32))
            else:
                out.append(C(xts_list[c - 1][:, TC - n:TC]))
        return out

    wqkv = mat(I["L0_attn_weight_qkv"])
    im = [{"xT": xts[c], "wqkv": wqkv, **const} for c in range(NCORE)]
    r = run(build_l0_proj(), im)
    qT = [np.asarray(r.results[c]["qT"]) for c in range(NCORE)]
    kT = [np.asarray(r.results[c]["kT"]) for c in range(NCORE)]
    v = [np.asarray(r.results[c]["v"]) for c in range(NCORE)]
    kn2 = [np.asarray(r.results[c]["kn2loc"]) for c in range(NCORE)]
    kTf = [np.concatenate(kT[bb * 4:(bb + 1) * 4], axis=1) for bb in range(2)]
    vf = [np.concatenate(v[bb * 4:(bb + 1) * 4], axis=0) for bb in range(2)]
    kn24 = [C(np.concatenate(kn2[bb * 4:(bb + 1) * 4], axis=1)) for bb in range(2)]
    wo_t = matT(I["L0_attn_out_w"])
    outb = col(I["L0_attn_out_b"])
    im = [{"qT": qT[c], "kTf": kTf[c // 4], "vf": vf[c // 4], "xT": xts[c],
           "wo_t": wo_t, "out_b": outb, "kn24": kn24[c // 4], **const}
          for c in range(NCORE)]
    r = run(build_l0_attend(), im)
    xts = [C(np.asarray(r.results[c]["x1T"])) for c in range(NCORE)]

    w = {"dwk": C(I["L0_ffn_dw_kernel"][:, 0, :].astype(f32)),
         "dwb": col(I["L0_ffn_dw_bias"]),
         "pw_t": matT(I["L0_ffn_pw_w"]), "pwb": col(I["L0_ffn_pw_b"]),
         "rw_t": matT(I["L0_ffn_router_w"]), "rb_in": col(I["L0_ffn_router_b"]),
         "ew_t": C(np.asarray(I["L0_ffn_expert_w"]).transpose(0, 2, 1)
                   .reshape(8 * D, D).astype(f32)),
         "eb_t": matT(I["L0_ffn_expert_b"]),
         "wout_t": matT(I["L0_ffn_out_w"]), "woutb": col(I["L0_ffn_out_b"]),
         "nw_in": col(I["L0_ffn_norm_w"]),
         "ffd_t": matT(I["L0_ffn_ff_down_w"]), "ffdb": col(I["L0_ffn_ff_down_b"]),
         "ffu_t": matT(I["L0_ffn_ff_up_w"]), "ffub": col(I["L0_ffn_ff_up_b"])}
    hl = halos(xts, 2)
    im = [{"xT": xts[c], "halo": hl[c], **w, **const} for c in range(NCORE)]
    r = run(build_l0_ffn(), im)
    xts = [C(np.asarray(r.results[c]["xoT"])) for c in range(NCORE)]

    wA = {"wk_t": matT(I["L1_attn_Wk"]), "wv_t": matT(I["L1_attn_Wv"]),
          "wr_t": matT(I["L1_attn_Wr"]),
          "tmk": col(I["L1_attn_time_mix_k"]), "tmv": col(I["L1_attn_time_mix_v"]),
          "tmr": col(I["L1_attn_time_mix_r"]), "w_in": col(I["L1_attn_w"])}
    hl = halos(xts, 1)
    im = [{"xT": xts[c], "halo": hl[c], **wA, **const} for c in range(NCORE)]
    r = run(build_l1a_A(), im)
    kTT = [np.asarray(r.results[c]["kTT"]) for c in range(NCORE)]
    vTT = [np.asarray(r.results[c]["vTT"]) for c in range(NCORE)]
    rTT = [np.asarray(r.results[c]["rTT"]) for c in range(NCORE)]
    kmx = [np.asarray(r.results[c]["kmaxl"]) for c in range(NCORE)]
    tls = [np.concatenate([np.asarray(r.results[c]["tailsA"]),
                           np.asarray(r.results[c]["tailsB"])], axis=1)
           for c in range(NCORE)]
    wo1 = matT(I["L1_attn_Wo"])
    u_in = col(I["L1_attn_u"])
    im = []
    for c in range(NCORE):
        bb, cc = divmod(c, 4)
        kmax4 = C(np.concatenate([kmx[bb * 4 + j] for j in range(4)], axis=1))
        tails8 = C(np.concatenate([tls[bb * 4 + j][:, 0:1] for j in range(4)]
                                  + [tls[bb * 4 + j][:, 1:2] for j in range(4)], axis=1))
        tjs = np.zeros((128, 4), f32)
        msk = np.zeros((128, 4), f32)
        for j in range(4):
            if j < cc:
                tjs[:, j] = -2048.0 * (cc - 1 - j)
                msk[:, j] = 1.0
        im.append({"xT": xts[c], "kTT": kTT[c], "vTT": vTT[c], "rTT": rTT[c],
                   "kmax4": kmax4, "tails8": tails8, "tjs": tjs, "msk4": msk,
                   "w_in": wA["w_in"], "u_in": u_in, "wo_t": wo1, **const})
    r = run(build_l1a_C(), im)
    xts = [C(np.asarray(r.results[c]["xoT"])) for c in range(NCORE)]

    w = {"wr_t": matT(I["L1_ffn_W_r"]), "wk_t": matT(I["L1_ffn_W_k"]),
         "wv_t": matT(I["L1_ffn_W_v"]),
         "mur": col(I["L1_ffn_mu_r"]), "muk": col(I["L1_ffn_mu_k"])}
    hl = halos(xts, 1)
    im = [{"xT": xts[c], "halo": hl[c], **w, **const} for c in range(NCORE)]
    r = run(build_chanmix(), im)
    xts = [C(np.asarray(r.results[c]["xoT"])) for c in range(NCORE)]

    wA = {"wq_t": matT(I["L2_attn_Wq"]), "wk_t": matT(I["L2_attn_Wk"]),
          "wv_t": matT(I["L2_attn_Wv"]),
          "bq_r": row(I["L2_attn_bq"]), "bk_r": row(I["L2_attn_bk"]),
          "bv_r": row(I["L2_attn_bv"])}
    im = []
    for c in range(NCORE):
        bb, cc = divmod(c, 4)
        cs, sn = _rope_tables(cc * TC)
        im.append({"xT": xts[c], "cs16": cs, "sn16": sn, **wA, **const})
    r = run(build_l2a_A(), im)
    qT = [np.asarray(r.results[c]["qT"]) for c in range(NCORE)]
    kT = [np.asarray(r.results[c]["kT"]) for c in range(NCORE)]
    v = [np.asarray(r.results[c]["v"]) for c in range(NCORE)]
    kTf = [np.concatenate(kT[bb * 4:(bb + 1) * 4], axis=1) for bb in range(2)]
    vf = [np.concatenate(v[bb * 4:(bb + 1) * 4], axis=0) for bb in range(2)]
    wo2 = matT(I["L2_attn_Wo"])
    bo2 = col(I["L2_attn_bo"])
    im = []
    for c in range(NCORE):
        bb, cc = divmod(c, 4)
        cols = []
        for j in range(16):
            g = 4 * j + cc
            src = bb * 4 + g // 16
            lt = g % 16
            cols.append(qT[src][:, lt * 128:(lt + 1) * 128])
        qTi = C(np.concatenate(cols, axis=1))
        mtd = np.zeros((128, 16 * 512), f32)
        for j in range(16):
            i = 4 * j + cc
            for cch in range(4):
                key = 512 * j + cch * 128 + np.arange(128)[:, None]
                qq = i * 128 + np.arange(128)[None, :]
                mtd[:, j * 512 + cch * 128: j * 512 + (cch + 1) * 128] = \
                    np.where(key <= qq, 0.0, -1e9)
        im.append({"qTi": qTi, "kTf": kTf[bb], "vf": vf[bb], "mtd": mtd,
                   "wo_t": wo2, "bo_c": bo2, **const})
    r = run(build_l2a_B(), im)
    at = [np.asarray(r.results[c]["attnTi"]) for c in range(NCORE)]
    w1r = np.asarray(I["L3_ffn_W1"]).reshape(8 * D, 4096).astype(f32)
    w2r = np.asarray(I["L3_ffn_W2"]).reshape(8 * 4096, D).astype(f32)
    im = []
    for c in range(NCORE):
        bb, cc = divmod(c, 4)
        cols = []
        for lt in range(16):
            g = cc * 16 + lt
            src = bb * 4 + g % 4
            j = g // 4
            cols.append(at[src][:, j * 128:(j + 1) * 128])
        attnT = C(np.concatenate(cols, axis=1))
        im.append({"xT": xts[c], "mixT": attnT,
                   "w1e": C(w1r[c * D:(c + 1) * D, :]),
                   "w2e": C(w2r[c * 4096:(c + 1) * 4096, :]), **const})
    r = run(build_sandwich(precast=True), im)
    xts = [C(np.asarray(r.results[c]["xoT"])) for c in range(NCORE)]
    w1bf = C(np.concatenate([np.asarray(r.results[c]["w1b"])
                             for c in range(NCORE)], axis=0))
    w2bf = C(np.concatenate([np.asarray(r.results[c]["w2b"])
                             for c in range(NCORE)], axis=0))

    w = {"wr_t": matT(I["L2_ffn_W_r"]), "wk_t": matT(I["L2_ffn_W_k"]),
         "wv_t": matT(I["L2_ffn_W_v"]),
         "mur": col(I["L2_ffn_mu_r"]), "muk": col(I["L2_ffn_mu_k"])}
    hl = halos(xts, 1)
    im = [{"xT": xts[c], "halo": hl[c], **w, **const} for c in range(NCORE)]
    r = run(build_chanmix(), im)
    xts = [C(np.asarray(r.results[c]["xoT"])) for c in range(NCORE)]

    wA = {"wq": mat(I["L3_attn_Wq"]), "wv": mat(I["L3_attn_Wv"]),
          "bq_r": row(I["L3_attn_bq"]), "bv_r": row(I["L3_attn_bv"])}
    im = [{"xT": xts[c], **wA, **const} for c in range(NCORE)]
    r = run(build_l3a_A(), im)
    qsm = [np.asarray(r.results[c]["qsmT"]) for c in range(NCORE)]
    vtk = [np.asarray(r.results[c]["vtok"]) for c in range(NCORE)]
    ml = [np.asarray(r.results[c]["mloc"]) for c in range(NCORE)]
    im = []
    for c in range(NCORE):
        bb, cc = divmod(c, 4)
        ml4 = C(np.concatenate([ml[bb * 4 + j] for j in range(4)], axis=1))
        im.append({"qsmT": qsm[c], "vtok": vtk[c], "mloc4": ml4, **const})
    r = run(build_l3a_B(), im)
    sp = [np.asarray(r.results[c]["spart"]) for c in range(NCORE)]
    cxp = [np.asarray(r.results[c]["ctxp"]) for c in range(NCORE)]
    im = []
    for c in range(NCORE):
        bb, cc = divmod(c, 4)
        ml4 = C(np.concatenate([ml[bb * 4 + j] for j in range(4)], axis=1))
        s4 = C(np.concatenate([sp[bb * 4 + j] for j in range(4)], axis=1))
        cx4 = C(np.concatenate([cxp[bb * 4 + j] for j in range(4)], axis=1))
        im.append({"xT": xts[c], "qsmT": qsm[c], "ctxp4": cx4, "spart4": s4,
                   "mloc4": ml4, "wo": mat(I["L3_attn_Wo"]),
                   "bo_c": col(I["L3_attn_bo"]), **const})
    r = run(build_l3a_C(), im)
    xts = [C(np.asarray(r.results[c]["xoT"])) for c in range(NCORE)]

    w1 = {"g1": col(I["L3_ffn_ln1_g"]), "b1c": col(I["L3_ffn_ln1_b"]),
          "wq": mat(I["L3_ffn_Wq"]), "wk": mat(I["L3_ffn_Wk"]),
          "wv": mat(I["L3_ffn_Wv"])}
    im = [{"xT": xts[c], **w1, **const} for c in range(NCORE)]
    r = run(build_l3f_1(), im)
    qT = [np.asarray(r.results[c]["qT"]) for c in range(NCORE)]
    kT = [np.asarray(r.results[c]["kT"]) for c in range(NCORE)]
    v = [np.asarray(r.results[c]["v"]) for c in range(NCORE)]
    m0 = [np.asarray(r.results[c]["m0T"]) for c in range(NCORE)]
    kTf = [np.concatenate(kT[bb * 4:(bb + 1) * 4], axis=1) for bb in range(2)]
    vf = [np.concatenate(v[bb * 4:(bb + 1) * 4], axis=0) for bb in range(2)]
    im = [{"qT": qT[c], "kTf": kTf[c // 4], "vf": vf[c // 4], "m0T": m0[c],
           "wo": mat(I["L3_ffn_Wo"]), **const} for c in range(NCORE)]
    r = run(build_l3f_2(), im)
    xint = [np.asarray(r.results[c]["xintT"]) for c in range(NCORE)]
    im = [{"xT": xts[c], "xintT": xint[c],
           "g2": col(I["L3_ffn_ln2_g"]), "b2c": col(I["L3_ffn_ln2_b"]),
           "wg": mat(I["L3_ffn_Wg"]),
           "w1f": w1bf, "b1f": col(I["L3_ffn_b1"]),
           "w2f": w2bf, "b2f": col(I["L3_ffn_b2"]), **const}
          for c in range(NCORE)]
    r = run(build_l3f_3(), im)

    out = np.empty((NB, TSEQ, D), dtype=f32)
    for c in range(NCORE):
        bb, cc = divmod(c, 4)
        out[bb, cc * TC:(cc + 1) * TC, :] = np.asarray(r.results[c]["xoT"]).T
    return out
